# version 14
# baseline (speedup 1.0000x reference)
import sys, os
if '/opt/trn_rl_repo' not in sys.path:
    sys.path.insert(0, '/opt/trn_rl_repo')
import numpy as np
import ml_dtypes
import concourse.bass as bass
import concourse.mybir as mybir
from concourse.bass_utils import run_bass_kernel_spmd
from concourse import dve_ops

F32 = mybir.dt.float32
BF16 = mybir.dt.bfloat16
AF = mybir.ActivationFunctionType
ALU = mybir.AluOpType
BF = ml_dtypes.bfloat16

NCORES = 8
S = 8192
D = 2048
TOK = S // NCORES
NCH = D // 128
NTG = TOK // 512
HA = 16
NBK = 4
HB = 32
HBL = HB // NCORES
HD = 64
FH = 8192
FH2 = 16384
LNEPS = 1e-5
RMSEPS = 1.1920929e-7
NEGBIG = -1.0e9


def _c(a):
    return np.ascontiguousarray(a)


def host_inputs(inp, core):
    d = {}
    x = np.asarray(inp['L0_attn_a_hidden_states'])
    sl = slice(TOK * core, TOK * (core + 1))
    d['xT'] = _c(x[0, sl, :].T)

    d['WqaT'] = _c(np.asarray(inp['L0_attn_a_Wq']).T)
    d['WoaT'] = _c(np.asarray(inp['L0_attn_a_Wo']).T)
    mem = np.asarray(inp['L0_attn_a_memories'])
    d['memH'] = _c(mem.transpose(1, 0, 2, 3))
    mn = np.asarray(inp['L0_attn_a_memory_norms'])
    d['mnT'] = _c(mn.transpose(1, 2, 0).reshape(128 * HA // 128, 128, NBK)
                  .reshape(HA, 128, NBK))
    d['mnFlat'] = _c(mn.reshape(NBK, HA * 128))
    bw = np.asarray(inp['L0_attn_a_bank_weights'])
    d['bwT'] = _c(bw.T)

    hsl = slice(HBL * HD * core, HBL * HD * (core + 1))
    d['Wqkvf'] = _c(np.concatenate([
        np.asarray(inp['L0_attn_b_Wq'])[hsl, :].T,
        np.asarray(inp['L0_attn_b_Wk'])[hsl, :].T,
        np.asarray(inp['L0_attn_b_Wv'])[hsl, :].T,
        np.asarray(inp['L0_attn_b_Wf'])[HBL * core: HBL * (core + 1), :].T,
    ], axis=1))
    bfv = np.asarray(inp['L0_attn_b_bf'])[HBL * core: HBL * (core + 1)]
    bff = np.zeros((128, 1), dtype=np.float32)
    for grp in range(4):
        bff[32 * grp:32 * grp + 4, 0] = bfv
    d['bf_full'] = bff
    d['WobT'] = _c(np.asarray(inp['L0_attn_b_Wo']).T)

    wguT = np.asarray(inp['L0_ffn_w_gate_up']).T
    d['wguS'] = _c(wguT[256 * core: 256 * (core + 1), :])
    wpT = np.asarray(inp['L0_ffn_w_proj']).T
    d['wpS'] = _c(wpT[1024 * core: 1024 * (core + 1), :])

    d['ident_f32'] = np.eye(128, dtype=np.float32)
    d['ones_f32'] = np.ones((128, 512), dtype=np.float32)
    d['ones_bf'] = np.ones((128, 512), dtype=BF)
    tri = np.tril(np.ones((128, 128), dtype=np.float32), -1)
    d['Utri_bf'] = tri.astype(BF)
    d['negI_bf'] = (NEGBIG * np.eye(128, dtype=np.float32)).astype(BF)
    d['trif'] = NEGBIG * tri
    sel = np.zeros((4, 512), dtype=np.float32)
    for n in range(4):
        sel[n, 128 * n:128 * (n + 1)] = 1.0
    d['selG'] = sel.astype(BF)
    ind2 = np.zeros((128, 2), dtype=np.float32)
    ind2[:64, 0] = 1.0
    ind2[64:, 1] = 1.0
    d['ind2'] = ind2.astype(BF)
    e2 = np.zeros((2, 128), dtype=np.float32)
    e2[0, :64] = 1.0
    e2[1, 64:] = 1.0
    d['e2_bf'] = e2.astype(BF)
    d['e2_f32'] = e2
    cst = np.zeros((128, 8), dtype=np.float32)
    cst[:, 0] = LNEPS
    cst[:, 1] = 1.0
    cst[:, 2] = RMSEPS
    d['cst'] = cst
    return d


class P:
    def __init__(self, nc):
        self.nc = nc
        self.streams = {k: [] for k in ('tensor', 'vector', 'scalar', 'sync', 'gpsimd')}
        self.sem = {}
        self.cnt = {}
        from contextlib import ExitStack
        self.stack = ExitStack()

    def mksem(self, *names):
        for n in names:
            self.sem[n] = self.stack.enter_context(self.nc.semaphore(n))
            self.cnt[n] = 0

    def emit(self, eng, f):
        self.streams[eng].append(f)

    def wait(self, eng, sname, val=None):
        v = self.cnt[sname] if val is None else val
        s = self.sem[sname]
        if v > 0:
            self.emit(eng, lambda e, s=s, v=v: e.wait_ge(s, v))
        return v

    def op(self, eng, f, inc=None, n=1, drain=True):
        if drain and eng in ('vector', 'scalar'):
            self.emit(eng, lambda e: e.drain())
        if inc is not None:
            self.cnt[inc] += n
            s = self.sem[inc]
            self.emit(eng, lambda e, f=f, s=s, n=n: f(e).then_inc(s, n))
        else:
            self.emit(eng, lambda e, f=f: f(e))

    def dma(self, q, out, in_, sname):
        self.op(q, lambda e, o=out, i=in_: e.dma_start(out=o, in_=i), inc=sname, n=16,
                drain=False)
        return self.cnt[sname]

    def build(self):
        nc = self.nc
        with nc.Block() as block:
            @block.tensor
            def _(e):
                for f in self.streams['tensor']:
                    f(e)

            @block.vector
            def _(e):
                for f in self.streams['vector']:
                    f(e)

            @block.scalar
            def _(e):
                for f in self.streams['scalar']:
                    f(e)

            @block.sync
            def _(e):
                for f in self.streams['sync']:
                    f(e)

            @block.gpsimd
            def _(e):
                for f in self.streams['gpsimd']:
                    f(e)
        self.stack.close()


def build_nc(phases=('q', 'a', 'f'), debug_taps=()):
    nc = bass.Bass("TRN2", target_bir_lowering=False, debug=False)
    p = P(nc)
    p.mksem('dh', 'dha', 'dhb', 'dga', 'dgb', 'dgc', 'dsa', 'dsb', 'dso', 'te', 've', 'sc', 'cc', 'st', 'dq')

    io = {}
    def din(name, shape, dtype=F32):
        io[name] = nc.dram_tensor(name, list(shape), dtype, kind="ExternalInput").ap()
        return io[name]

    xT = din('xT', [D, TOK])
    WqaT = din('WqaT', [D, D]); WoaT = din('WoaT', [D, D])
    memH = din('memH', [HA, NBK, 128, 128]); mnT = din('mnT', [HA, 128, NBK])
    mnFlat = din('mnFlat', [NBK, HA * 128]); bwT = din('bwT', [NBK, HA])
    Wqkvf = din('Wqkvf', [D, 772]); bf_full = din('bf_full', [128, 1])
    WobT = din('WobT', [D, D])
    wguS = din('wguS', [256, FH2]); wpS = din('wpS', [1024, D])
    ident_f32 = din('ident_f32', [128, 128])
    ones_f32 = din('ones_f32', [128, 512]); ones_bf = din('ones_bf', [128, 512], BF16)
    Utri_bf = din('Utri_bf', [128, 128], BF16); negI_bf = din('negI_bf', [128, 128], BF16)
    trif = din('trif', [128, 128])
    selG = din('selG', [4, 512], BF16)
    ind2 = din('ind2', [128, 2], BF16); e2_bf = din('e2_bf', [2, 128], BF16)
    e2_f32 = din('e2_f32', [2, 128]); cst = din('cst', [128, 8])

    y = nc.dram_tensor('y', [D, TOK], F32, kind="ExternalOutput").ap()
    wqgu_sh = nc.dram_tensor('wqgu_sh', [128, 128, 2, 128], BF16).ap()
    wqgu_all = nc.dram_tensor('wqgu_all', [8 * 128, 128, 2, 128], BF16, addr_space='Shared').ap()
    wqp_sh = nc.dram_tensor('wqp_sh', [8, 4, 128, 512], BF16).ap()
    wqp_all = nc.dram_tensor('wqp_all', [64, 4, 128, 512], BF16, addr_space='Shared').ap()
    xsha = nc.dram_tensor('xsha', [D, TOK // 2], BF16).ap()
    xshb = nc.dram_tensor('xshb', [D, TOK // 2], BF16).ap()
    xga = nc.dram_tensor('xga', [8 * D, TOK // 2], BF16, addr_space='Shared').ap()
    xgb = nc.dram_tensor('xgb', [8 * D, TOK // 2], BF16, addr_space='Shared').ap()
    obnc = nc.dram_tensor('obnc', [256, S], BF16).ap()
    ogath = nc.dram_tensor('ogath', [8 * 256, S], BF16, addr_space='Shared').ap()
    wqkvf_bf = nc.dram_tensor('wqkvf_bf', [D, 772], BF16).ap()
    wqa_bf = nc.dram_tensor('wqa_bf', [HA, D, 128], BF16).ap()
    woa_bf = nc.dram_tensor('woa_bf', [NCH, D, 128], BF16).ap()
    taps = {}
    for t in debug_taps:
        pass

    def mktap(name, shape, dtype=F32):
        if name in debug_taps:
            taps[name] = nc.dram_tensor('tap_' + name, list(shape), dtype,
                                        kind="ExternalOutput").ap()
            return taps[name]
        return None

    SB = {}
    SBASE = 16512
    def sb(name, shape, dtype, off):
        h = nc.alloc_sbuf_tensor_at(name, list(shape), dtype, offset=SBASE + off)
        SB[name] = h
        return h.ap()

    R = sb('R', [128, NCH * TOK], F32, 0)
    xln = sb('xln', [128, NCH * TOK], BF16, 65536)
    comb = sb('comb', [128, HA * TOK], BF16, 98304)
    hbuf = sb('hbuf', [128, (FH // 128) * 512], BF16, 98304)
    Ksb = [sb(f'Ksb{i}', [128, S], BF16, 98816 + 16384 * i) for i in range(2)]
    Qsb = [sb(f'Qsb{i}', [128, S], BF16, 131584 + 16384 * i) for i in range(2)]
    Vsb = [sb(f'Vsb{i}', [128, (S // 128) * 130], BF16, 65536 + 16640 * i) for i in range(2)]
    W0 = 163840
    wab = [sb(f'wab{i}', [128, NCH * 128], BF16, W0 + 4096 * i) for i in range(2)]
    mem_sb = [sb(f'mem_sb{i}', [128, NBK * 128], BF16, W0 + 8192 + 1024 * i) for i in range(2)]
    mnT_sb = sb('mnT_sb', [128, HA * NBK], BF16, W0 + 10240)
    smallc = W0 + 10496
    bw_sb = sb('bw_sb', [4, HA], F32, smallc)
    wact4 = sb('wact4', [4, HA], F32, smallc + 128)
    act_sb = sb('act_sb', [4, 1], F32, smallc + 320)
    g_sb = sb('g_sb', [4, 512], BF16, smallc + 384)
    ident_sb = sb('ident_sb', [128, 128], F32, smallc + 1408)
    onesf_sb = sb('onesf_sb', [128, 512], F32, smallc + 1920)
    onesb_sb = sb('onesb_sb', [128, 512], BF16, smallc + 3968)
    selG_sb = sb('selG_sb', [4, 512], BF16, smallc + 4992)
    utri_sb = sb('utri_sb', [128, 128], BF16, smallc + 6144)
    negI_sb = sb('negI_sb', [128, 128], BF16, smallc + 6400)
    trif_sb = sb('trif_sb', [128, 128], F32, 172288)
    cst_sb = sb('cst_sb', [128, 8], F32, smallc + 6016)
    lnc = smallc + 6656
    mnf_sb = sb('mnf_sb', [4, HA * 128], F32, lnc)
    xb_sb = [sb(f'xb_sb{i}', [128, 512], BF16, lnc + 1024 * i) for i in range(2)]
    xsq_sb = [sb(f'xsq_sb{i}', [128, 512], BF16, lnc + 2048 + 1024 * i) for i in range(2)]
    sigp = [sb(f'sigp{i}', [128, 512], BF16, lnc + 10240 + 1024 * i) for i in range(2)]
    row_sb = sb('row_sb', [1, 512 * 5], F32, lnc + 4096)
    tmpa = sb('tmpa', [128, 512], F32, lnc + 14336)
    tmpb = sb('tmpb', [128, 512], F32, lnc + 16384)
    qwb = [sb(f'qwb{i}', [128, 2048], BF16, 131840 + 4096 * i) for i in range(2)]
    qt1 = sb('qt1', [128, 512], BF16, 131840 + 8192)
    qt2 = [sb('qt2a', [128, 512], BF16, 131840 + 9216), sb('qt2b', [128, 512], BF16, 131840 + 22016)]
    qt3 = sb('qt3', [128, 512], BF16, 131840 + 10240)
    qsd = sb('qsd', [2, 512], BF16, 131840 + 11264)
    qrcp = sb('qrcp', [2, 512], F32, 131840 + 12288)
    qsdf = sb('qsdf', [2, 512], F32, 131840 + 15872)
    qscr = sb('qscr', [2, 512], F32, 131840 + 17920)
    ascr = sb('ascr', [128, 512], F32, 131840 + 19968)
    ind2_sb = sb('ind2_sb', [128, 2], BF16, 131840 + 14336)
    e2b_sb = sb('e2b_sb', [2, 128], BF16, 131840 + 14848)
    e2f_sb = sb('e2f_sb', [2, 128], F32, 131840 + 15360)
    fgw = [sb('fgw0', [128, 2048], BF16, 0 + 163840), sb('fgw1', [128, 2048], BF16, 4096 + 163840),
           sb('fgw2', [128, 2048], BF16, 203520), sb('fgw3', [128, 2048], BF16, 207616)]
    fpw = [sb('fpw0', [128, 512], BF16, 172032), sb('fpw1', [128, 512], BF16, 173056)]
    sil_sb = [sb('sil_sb0', [128, 512], BF16, lnc + 18432), sb('sil_sb1', [128, 512], BF16, lnc + 19456)]
    hsq_sb = [sb('hsq_sb0', [128, 512], BF16, lnc + 20480), sb('hsq_sb1', [128, 512], BF16, lnc + 21504)]
    xat = [sb(f'xat{i}', [128, 512], BF16, lnc + 1024 * i) for i in range(4)]
    Ptil = sb('Ptil', [128, 2048], F32, lnc + 14336)
    zz8 = sb('zz8', [128, 2048], F32, lnc + 4096)
    pcum = sb('pcum', [128, 2048], F32, 203520)
    wst = [sb(f'wst{i}', [128, 772], BF16, lnc + 4096 + 1664 * i) for i in range(2)]
    ptb = [sb(f'ptb{i}', [128, 512], BF16, lnc + 10240 + 1024 * i) for i in range(6)]
    rank1A = sb('rank1A', [128, 512], F32, 167936)
    rank1B = sb('rank1B', [128, 512], F32, 170240)
    fqbw = sb('fqbw', [128, 1024], F32, lnc + 16384)
    fqbA = fqbw[:, 0:512]
    fqbB = fqbw[:, 512:1024]
    rcpb = sb('rcpb', [128, 512], F32, lnc)
    rcpbB = sb('rcpbB', [128, 512], F32, lnc + 2048)
    dvb = sb('dvb', [128, 512], F32, lnc + 4096)
    dvb2 = sb('dvb2', [128, 512], F32, lnc + 6144)
    onrm_sb = sb('onrm_sb', [128, 512], BF16, lnc + 8192)
    onrm2_sb = sb('onrm2_sb', [128, 512], BF16, lnc + 9216)
    fct = sb('fct', [128, 256], F32, 211712)
    bfc_sb = sb('bfc_sb', [128, 1], F32, 212736)
    initc = sb('initc', [128, 1], F32, 212768)
    ftmp = sb('ftmp', [128, 4], F32, 174080)
    ft2 = sb('ft2', [4, 128], F32, lnc + 12288)
    oown = sb('oown', [128, NCH * 1024], BF16, 65536)
    assert 16512 + 212800 <= 229344

    PB = [nc.alloc_psum_tensor(f'PB{i}', [128, 1024], F32) for i in range(3)]
    PS = [nc.alloc_psum_tensor(f'PS{i}', [128, 512], F32) for i in range(2)]
    PSa = [PB[0].ap()[:, 0:512], PB[0].ap()[:, 512:1024],
           PB[1].ap()[:, 0:512], PB[1].ap()[:, 512:1024],
           PB[2].ap()[:, 0:512], PB[2].ap()[:, 512:1024]] + [t.ap() for t in PS]
    PBa = [t.ap() for t in PB]

    for dst, src in ((ident_sb, ident_f32), (onesf_sb, ones_f32), (onesb_sb, ones_bf),
                     (selG_sb, selG), (bw_sb, bwT), (mnf_sb, mnFlat), (cst_sb, cst),
                     (utri_sb, Utri_bf), (negI_sb, negI_bf)):
        p.dma('sync', dst[:, :], src[:, :], 'dh')
    for h in range(HA):
        p.dma('gpsimd', mnT_sb[:, NBK * h:NBK * (h + 1)], mnT[h, :, :], 'dgc')
    p.dma('gpsimd', bfc_sb[:, :], bf_full[:, :], 'dgc')
    p.dma('gpsimd', ind2_sb[:, :], ind2[:, :], 'dgc')
    p.dma('gpsimd', e2b_sb[:, :], e2_bf[:, :], 'dgc')
    p.dma('gpsimd', e2f_sb[:, :], e2_f32[:, :], 'dgc')
    for c in range(NCH):
        p.dma('sync', R[:, TOK * c:TOK * (c + 1)], xT[128 * c:128 * (c + 1), :], 'dh')
    c_loaded = p.cnt['dh']
    g_loaded = p.cnt['dgc']

    def rcp_fast(out, in_, scratch, inc=None):
        p.op('vector', lambda e: e.reciprocal_approx_fast(out=scratch, in_=in_))
        p.op('vector', lambda e: e._custom_dve(
            dve_ops.RECIPROCAL_APPROX_NR, out=out, in0=in_, in1=scratch, s0=2.0), inc=inc)

    def bank_weights_prep():
        p.wait('vector', 'dh', c_loaded)
        p.wait('vector', 'dgc', g_loaded)
        p.op('vector', lambda e: e.reduce_sum(tmpb[0:4, 0:1], mnf_sb[:, :], mybir.AxisListType.X))
        p.op('vector', lambda e: e.tensor_single_scalar(act_sb[:, :], tmpb[0:4, 0:1], 1e-6, ALU.is_ge), inc='ve')
        p.wait('scalar', 'dh', c_loaded)
        p.op('scalar', lambda e: e.activation(tmpa[0:4, 0:HA], bw_sb[:, :], AF.Exp), inc='sc')
        p.wait('tensor', 'sc')
        p.op('tensor', lambda e: e.matmul(PSa[6][0:1, 0:HA], onesf_sb[0:4, 0:1], tmpa[0:4, 0:HA],
                                          start=True, stop=True), inc='te')
        p.wait('vector', 'te')
        p.op('vector', lambda e: e.reciprocal(row_sb[0:1, 0:HA], PSa[6][0:1, 0:HA]), inc='ve')
        p.wait('tensor', 've')
        p.op('tensor', lambda e: e.matmul(PSa[7][0:4, 0:HA], onesf_sb[0:1, 0:4], row_sb[0:1, 0:HA],
                                          start=True, stop=True), inc='te')
        p.wait('vector', 'te')
        p.op('vector', lambda e: e.scalar_tensor_tensor(wact4[:, :], tmpa[0:4, 0:HA], act_sb[:, :],
                                                        PSa[7][0:4, 0:HA], ALU.mult, ALU.mult), inc='ve')

    def layernorm():
        te0 = p.cnt['te']
        p.wait('vector', 'dhb', p.cnt['dhb'])
        for g in range(NTG):
            gv0 = p.cnt['ve']
            for c in range(NCH):
                b = c % 2
                p.wait('vector', 'te', p.cnt['te'])
                p.op('vector', lambda e, c=c, g=g, b=b: e.tensor_copy(
                    xb_sb[b][:, :], R[:, TOK * c + 512 * g: TOK * c + 512 * (g + 1)]), inc='ve')
                p.wait('scalar', 've')
                p.op('scalar', lambda e, b=b: e.activation(xsq_sb[b][:, :], xb_sb[b][:, :], AF.Square), inc='sc')
                p.wait('tensor', 've')
                p.wait('tensor', 'sc')
                p.op('tensor', lambda e, b=b, c=c: e.matmul(
                    PSa[6][0:1, :], onesb_sb[:, 0:1], xb_sb[b][:, :], start=(c == 0), stop=(c == NCH - 1)))
                p.op('tensor', lambda e, b=b, c=c: e.matmul(
                    PSa[7][0:1, :], onesb_sb[:, 0:1], xsq_sb[b][:, :], start=(c == 0), stop=(c == NCH - 1)),
                    inc='te')
            p.wait('vector', 'te')
            m = row_sb[0:1, 0:512]; rstd = row_sb[0:1, 512:1024]; mr = row_sb[0:1, 1024:1536]
            var = row_sb[0:1, 1536:2048]; m2 = row_sb[0:1, 2048:2560]
            p.op('vector', lambda e: e.tensor_scalar_mul(m, PSa[6][0:1, :], 1.0 / D))
            p.op('vector', lambda e: e.tensor_scalar_mul(var, PSa[7][0:1, :], 1.0 / D))
            p.op('vector', lambda e: e.tensor_mul(m2, m, m))
            p.op('vector', lambda e: e.tensor_sub(var, var, m2), inc='ve')
            p.wait('scalar', 've')
            p.op('scalar', lambda e: e.activation(var, var, AF.Ln, bias=cst_sb[0:1, 0:1]))
            p.op('scalar', lambda e: e.activation(rstd, var, AF.Exp, scale=-0.5), inc='sc')
            p.wait('vector', 'sc')
            p.op('vector', lambda e: e.tensor_mul(mr, m, rstd), inc='ve')
            p.wait('tensor', 've')
            p.op('tensor', lambda e: e.matmul(PSa[4][:, :], onesf_sb[0:1, 0:128], rstd, start=True, stop=True))
            p.op('tensor', lambda e: e.matmul(PSa[5][:, :], onesf_sb[0:1, 0:128], mr, start=True, stop=True),
                 inc='te')
            p.wait('vector', 'te')
            for c in range(NCH):
                p.op('vector', lambda e, c=c, g=g: e.tensor_mul(
                    tmpa[:, :], R[:, TOK * c + 512 * g: TOK * c + 512 * (g + 1)], PSa[4][:, :]))
                p.op('vector', lambda e, c=c, g=g: e.tensor_sub(
                    xln[:, TOK * c + 512 * g: TOK * c + 512 * (g + 1)], tmpa[:, :], PSa[5][:, :]), inc='ve')

    def attn_a(qsteps=()):
        qsteps = list(qsteps)
        def load_w(dst, srcB, h, sem):
            p.dma('sync', dst[:, :],
                  srcB[h, :, :].rearrange('(c p) w -> p c w', p=128)[:, :, :], sem)

        def load_mem(dst, h, sem):
            for n in range(NBK):
                p.dma('gpsimd', dst[:, 128 * n:128 * (n + 1)], memH[h, n, :, :], sem)

        p.wait('sync', 'dhb', wqa_done)
        p.wait('sync', 'dha', p.cnt['dha'])
        load_w(wab[0], wqa_bf, 0, 'dha')
        p.wait('gpsimd', 'dga', p.cnt['dga'])
        load_mem(mem_sb[0], 0, 'dga')
        xln_ready = p.cnt['ve']
        prev_te = 0
        sig_done = {}
        for h in range(HA):
            b = h % 2
            sem_b = 'dha' if b == 0 else 'dhb'
            gsem_b = 'dga' if b == 0 else 'dgb'
            need = p.cnt[sem_b]
            gneed = p.cnt[gsem_b]
            if h + 1 < HA:
                nsem = 'dhb' if b == 0 else 'dha'
                gsem = 'dgb' if b == 0 else 'dga'
                p.wait('sync', 'te', prev_te)
                p.wait('sync', nsem, p.cnt[nsem])
                load_w(wab[1 - b], wqa_bf, h + 1, nsem)
                p.wait('gpsimd', 'te', prev_te)
                p.wait('gpsimd', gsem, p.cnt[gsem])
                load_mem(mem_sb[1 - b], h + 1, gsem)
            p.wait('tensor', sem_b, need)
            p.wait('tensor', gsem_b, gneed)
            p.wait('tensor', 've', max(sig_done.get(h - 1, 0), xln_ready))
            for g in range(NTG):
                for c in range(NCH):
                    p.op('tensor', lambda e, g=g, c=c, b=b: e.matmul(
                        PSa[g][:, :], wab[b][:, 128 * c:128 * (c + 1)],
                        xln[:, TOK * c + 512 * g: TOK * c + 512 * (g + 1)],
                        start=(c == 0), stop=(c == NCH - 1)),
                        inc=('te' if (g == NTG - 1 and c == NCH - 1) else None))
            te_q = p.cnt['te']
            for g in range(NTG):
                p.wait('vector', 'te', te_q)
                p.op('vector', lambda e, g=g: e.tensor_single_scalar(tmpa[:, :], PSa[g][:, :], 0.0, ALU.min),
                     inc='ve')
                p.wait('scalar', 've')
                p.op('scalar', lambda e: e.activation(tmpb[:, :], tmpa[:, :], AF.Exp), inc='sc')
                p.wait('vector', 'sc')
                p.op('vector', lambda e, g=g: e.scalar_tensor_tensor(
                    xb_sb[g][:, :], PSa[g][:, :], 0.0, tmpb[:, :], ALU.max, ALU.add), inc='ve')
                ve_sig = p.cnt['ve']
                if g == NTG - 1:
                    sig_done[h] = ve_sig
                p.wait('tensor', 've', ve_sig)
                p.op('tensor', lambda e, g=g, h=h: e.matmul(
                    PSa[6][0:4, :], mnT_sb[:, NBK * h: NBK * (h + 1)], xb_sb[g][:, :],
                    start=True, stop=True), inc='te')
                te_n = p.cnt['te']
                p.wait('vector', 'te', te_n)
                p.op('vector', lambda e: e.tensor_single_scalar(tmpa[0:4, :], PSa[6][0:4, :], 1e-6, ALU.max),
                     inc='ve')
                p.wait('scalar', 've')
                p.op('scalar', lambda e: e.activation(ascr[0:4, :], tmpa[0:4, :], AF.Ln))
                p.op('scalar', lambda e: e.activation(tmpb[0:4, :], ascr[0:4, :], AF.Exp,
                                                      scale=-1.0), inc='sc')
                p.wait('vector', 'sc')
                p.op('vector', lambda e, h=h: e.tensor_scalar_mul(
                    g_sb[:, :], tmpb[0:4, :], wact4[0:4, h:h + 1]), inc='ve')
                ve_g = p.cnt['ve']
                sigp_ve = {}
                for n in range(NBK):
                    pb = 4 + (n % 2)
                    p.wait('tensor', 've', ve_g if n == 0 else sigp_ve.get(n - 2, 0))
                    p.op('tensor', lambda e, n=n, pb=pb: e.matmul(
                        PSa[pb][:, :], selG_sb[:, 128 * n:128 * (n + 1)], g_sb[:, :],
                        start=True, stop=True), inc='te')
                    p.wait('vector', 'te')
                    p.op('vector', lambda e, g=g, pb=pb, n=n: e.tensor_mul(
                        sigp[n % 2][:, :], xb_sb[g][:, :], PSa[pb][:, :]), inc='ve')
                    sigp_ve[n] = p.cnt['ve']
                    p.wait('tensor', 've', sigp_ve[n])
                    p.op('tensor', lambda e, g=g, n=n, b=b: e.matmul(
                        PSa[2 + g][:, :], mem_sb[b][:, 128 * n:128 * (n + 1)], sigp[n % 2][:, :],
                        start=(n == 0), stop=(n == NBK - 1)), inc='te')
                p.wait('vector', 'te')
                p.op('vector', lambda e, g=g, h=h: e.tensor_copy(
                    comb[:, TOK * h + 512 * g: TOK * h + 512 * (g + 1)], PSa[2 + g][:, :]), inc='ve')
            prev_te = p.cnt['te']
            for _ in range(2):
                if qsteps:
                    qsteps.pop(0)()
        while qsteps:
            qsteps.pop(0)()
        comb_done = p.cnt['ve']
        p.wait('sync', 'te', p.cnt['te'])
        p.wait('sync', 'dha', p.cnt['dha'])
        load_w(wab[0], woa_bf, 0, 'dha')
        prev_te = 0
        for oc in range(NCH):
            b = oc % 2
            sem_b = 'dha' if b == 0 else 'dhb'
            need = p.cnt[sem_b]
            if oc + 1 < NCH:
                nsem = 'dhb' if b == 0 else 'dha'
                p.wait('sync', 'te', prev_te)
                p.wait('sync', nsem, p.cnt[nsem])
                load_w(wab[1 - b], woa_bf, oc + 1, nsem)
            p.wait('tensor', sem_b, need)
            p.wait('tensor', 've', comb_done)
            for g in range(NTG):
                p.wait('tensor', 've', p.cnt['ve'])
                for c in range(NCH):
                    p.op('tensor', lambda e, g=g, c=c, b=b: e.matmul(
                        PSa[g][:, :], wab[b][:, 128 * c:128 * (c + 1)],
                        comb[:, TOK * c + 512 * g: TOK * c + 512 * (g + 1)],
                        start=(c == 0), stop=(c == NCH - 1)),
                        inc=('te' if c == NCH - 1 else None))
                p.wait('vector', 'te')
                p.op('vector', lambda e, oc=oc, g=g: e.tensor_add(
                    R[:, TOK * oc + 512 * g: TOK * oc + 512 * (g + 1)],
                    R[:, TOK * oc + 512 * g: TOK * oc + 512 * (g + 1)], PSa[g][:, :]), inc='ve')
            prev_te = p.cnt['te']

    def quant_steps():
        RG = [list(range(NCORES))]
        steps = []

        def do_piece(bufb, out_fn):
            dso_mark = {-1: p.cnt['dso']}
            for st in range(4):
                qt2c = qt2[st % 2]
                wsl = qwb[bufb][:, 512 * st:512 * (st + 1)]
                p.wait('vector', 'dso', dso_mark.get(st - 2, dso_mark.get(-1, 0)))
                p.op('vector', lambda e: e.nop(), inc='ve')
                p.wait('scalar', 've')
                p.wait('scalar', 'te', p.cnt['te'])
                p.op('scalar', lambda e, wsl=wsl: e.activation(qt1[:, :], wsl, AF.Abs), inc='sc')
                p.wait('tensor', 'sc')
                p.op('tensor', lambda e: e.matmul(PSa[6][0:2, :], ind2_sb[:, :], qt1[:, :],
                                                  start=True, stop=True), inc='te')
                p.wait('vector', 'te')
                p.op('vector', lambda e: e.tensor_scalar(
                    qsd[:, :], PSa[6][0:2, :], 1.0 / 64, 1e-8, ALU.mult, ALU.max), inc='ve')
                p.wait('scalar', 've')
                p.op('scalar', lambda e: e.activation(qsdf[:, :], qsd[:, :], AF.Ln))
                p.op('scalar', lambda e: e.activation(qrcp[:, :], qsdf[:, :], AF.Exp,
                                                      scale=-1.0), inc='sc')
                p.wait('tensor', 've')
                p.wait('tensor', 'sc')
                p.op('tensor', lambda e: e.matmul(PSa[4][:, :], e2f_sb[:, :], qrcp[:, :],
                                                  start=True, stop=True))
                p.op('tensor', lambda e: e.matmul(PSa[5][:, :], e2b_sb[:, :], qsd[:, :],
                                                  start=True, stop=True), inc='te')
                p.wait('vector', 'te')
                p.op('vector', lambda e, wsl=wsl, qt2c=qt2c: e.tensor_mul(qt2c[:, :], wsl, PSa[4][:, :]))
                p.op('vector', lambda e, qt2c=qt2c: e.tensor_single_scalar(
                    qt1[:, :], qt2c[:, :], -0.5, ALU.is_lt))
                p.op('vector', lambda e, qt2c=qt2c: e.scalar_tensor_tensor(
                    qt3[:, :], qt2c[:, :], 0.5, qt1[:, :], ALU.is_gt, ALU.subtract))
                p.op('vector', lambda e, qt2c=qt2c: e.tensor_mul(
                    qt2c[:, :], qt3[:, :], PSa[5][:, :]), inc='ve')
                p.wait('scalar', 've')
                p.wait('scalar', 'dso', p.cnt['dso'])
                out_fn(st, qt2c)
                dso_mark[st] = p.cnt['dso']

        def gu_out(rc, cp, st, qt2c):
            hc0 = (2048 * cp + 512 * st) // 128
            dst = wqgu_sh[hc0:hc0 + 4, :, rc, :].transpose([1, 0, 2])
            p.dma('scalar', dst, qt2c.rearrange('p (a h) -> p a h', h=128)[:, :, :], 'dso')

        def p_out(rc, st, qt2c):
            p.dma('scalar', wqp_sh[rc, st, :, :], qt2c[:, :], 'dso')

        def precast(c0, c1):
            p.wait('gpsimd', 'dh', p.cnt['dh'])
            p.wait('gpsimd', 've', p.cnt['ve'])
            for c in range(c0, c1):
                b = c % 2
                sem = 'dga' if b == 0 else 'dgb'
                p.wait('gpsimd', sem, p.cnt[sem])
                p.wait('gpsimd', 'dhb', p.cnt['dhb'])
                p.dma('gpsimd', wst[b][:, :], Wqkvf[128 * c:128 * (c + 1), :], sem)
                p.wait('sync', sem, p.cnt[sem])
                p.dma('sync', wqkvf_bf[128 * c:128 * (c + 1), :], wst[b][:, :], 'dhb')
        steps.append(lambda: precast(0, 8))
        steps.append(lambda: precast(8, 16))

        pieces = [(rc, cp) for rc in range(2) for cp in range(8)]

        def gu_step(i):
            rc, cp = pieces[i]
            b = i % 2
            sem_b = 'dga' if b == 0 else 'dgb'
            if i == 0:
                p.wait('tensor', 'dgc', g_loaded)
                p.wait('gpsimd', 'dga', p.cnt['dga'])
                p.wait('gpsimd', 'dhb', p.cnt['dhb'])
                p.dma('gpsimd', qwb[0][:, :], wguS[0:128, 0:2048], 'dga')
            need = p.cnt[sem_b]
            if i + 1 < len(pieces):
                nrc, ncp = pieces[i + 1]
                nsem = 'dgb' if b == 0 else 'dga'
                p.wait('gpsimd', 've', p.cnt['ve'])
                p.wait('gpsimd', nsem, p.cnt[nsem])
                p.dma('gpsimd', qwb[1 - b][:, :],
                      wguS[128 * nrc:128 * (nrc + 1), 2048 * ncp:2048 * (ncp + 1)],
                      nsem)
            p.wait('vector', sem_b, need)
            do_piece(b, lambda st, qt2c, rc=rc, cp=cp: gu_out(rc, cp, st, qt2c))

        for i in range(len(pieces)):
            steps.append(lambda i=i: gu_step(i))

        def ag_gu():
            p.wait('gpsimd', 'dso', p.cnt['dso'])
            p.op('gpsimd', lambda e: e.collective_compute(
                "AllGather", ALU.bypass, replica_groups=RG,
                ins=[wqgu_sh[:, :, :, :].opt()], outs=[wqgu_all[:, :, :, :].opt()]), inc='cc')
        steps.append(ag_gu)

        def p_step(rc):
            b = rc % 2
            sem_b = 'dga' if b == 0 else 'dgb'
            if rc == 0:
                p.wait('gpsimd', 'dga', p.cnt['dga'])
                p.dma('gpsimd', qwb[0][:, :], wpS[0:128, :], 'dga')
            need = p.cnt[sem_b]
            if rc + 1 < 8:
                nsem = 'dgb' if b == 0 else 'dga'
                p.wait('gpsimd', 've', p.cnt['ve'])
                p.wait('gpsimd', nsem, p.cnt[nsem])
                p.dma('gpsimd', qwb[1 - b][:, :], wpS[128 * (rc + 1):128 * (rc + 2), :], nsem)
            p.wait('vector', sem_b, need)
            do_piece(b, lambda st, qt2c, rc=rc: p_out(rc, st, qt2c))

        for rc in range(8):
            steps.append(lambda rc=rc: p_step(rc))

        def ag_p():
            p.wait('gpsimd', 'dso', p.cnt['dso'])
            p.op('gpsimd', lambda e: e.collective_compute(
                "AllGather", ALU.bypass, replica_groups=RG,
                ins=[wqp_sh[:, :, :, :].opt()], outs=[wqp_all[:, :, :, :].opt()]), inc='cc')
        steps.append(ag_p)
        return steps

    def ffn():
        p.wait('sync', 'cc', 1)
        p.wait('gpsimd', 'cc', 1)
        p.wait('sync', 'te', p.cnt['te'])
        p.wait('gpsimd', 'te', p.cnt['te'])
        p.wait('vector', 'dhb', p.cnt['dhb'])
        p.wait('scalar', 'dhb', p.cnt['dhb'])
        p.wait('sync', 've', p.cnt['ve'])
        p.wait('gpsimd', 've', p.cnt['ve'])
        for T in range(NTG):
            ve_xln = p.cnt['ve']
            def load_gu(hcl, gbuf, ubuf, sems):
                for r in range(8):
                    q, sem = (('sync', sems[0]) if r < 3 else
                              ('scalar', sems[2]) if r < 6 else ('gpsimd', sems[1]))
                    p.dma(q, gbuf[:, 256 * r:256 * (r + 1)],
                          wqgu_all[128 * r + hcl, :, :, :], sem)
                    p.dma(q, ubuf[:, 256 * r:256 * (r + 1)],
                          wqgu_all[128 * r + hcl + 64, :, :, :], sem)

            sc_mark = {}
            ve_mark = {}
            rms_te = {}
            p.wait('sync', 'dha', p.cnt['dha'])
            p.wait('gpsimd', 'dga', p.cnt['dga'])
            p.wait('scalar', 'dsa', p.cnt['dsa'])
            load_gu(0, fgw[0], fgw[1], ('dha', 'dga', 'dsa'))
            for hcl in range(64):
                b = hcl % 2
                sems = ('dha', 'dga', 'dsa') if b == 0 else ('dhb', 'dgb', 'dsb')
                nh = p.cnt[sems[0]]
                ng = p.cnt[sems[1]]
                ns = p.cnt[sems[2]]
                if hcl + 1 < 64:
                    nsems = ('dhb', 'dgb', 'dsb') if b == 0 else ('dha', 'dga', 'dsa')
                    p.wait('sync', 'te', p.cnt['te'])
                    p.wait('gpsimd', 'te', p.cnt['te'])
                    p.wait('scalar', 'te', p.cnt['te'])
                    p.wait('sync', nsems[0], p.cnt[nsems[0]])
                    p.wait('gpsimd', nsems[1], p.cnt[nsems[1]])
                    p.wait('scalar', nsems[2], p.cnt[nsems[2]])
                    load_gu(hcl + 1, fgw[2 * (1 - b)], fgw[2 * (1 - b) + 1], nsems)
                p.wait('tensor', sems[0], nh)
                p.wait('tensor', sems[1], ng)
                p.wait('tensor', sems[2], ns)
                p.wait('tensor', 'sc', sc_mark[hcl - 2] if hcl >= 2 else 0)
                p.wait('tensor', 've', ve_mark[hcl - 2] if hcl >= 2 else ve_xln)
                for c in range(NCH):
                    p.op('tensor', lambda e, c=c, b=b, T=T: e.matmul(
                        PSa[2 * b][:, :], fgw[2 * b][:, 128 * c:128 * (c + 1)],
                        xln[:, TOK * c + 512 * T: TOK * c + 512 * (T + 1)],
                        start=(c == 0), stop=(c == NCH - 1)))
                for c in range(NCH):
                    p.op('tensor', lambda e, c=c, b=b, T=T: e.matmul(
                        PSa[2 * b + 1][:, :], fgw[2 * b + 1][:, 128 * c:128 * (c + 1)],
                        xln[:, TOK * c + 512 * T: TOK * c + 512 * (T + 1)],
                        start=(c == 0), stop=(c == NCH - 1)), inc=('te' if c == NCH - 1 else None))
                p.wait('scalar', 'te')
                if hcl >= 2:
                    p.wait('scalar', 've', ve_mark[hcl - 2])
                p.op('scalar', lambda e, b=b: e.activation(sil_sb[b][:, :], PSa[2 * b][:, :], AF.Silu),
                     inc='sc', drain=False)
                sc_mark[hcl] = p.cnt['sc']
                p.wait('vector', 'sc')
                p.op('vector', lambda e, b=b, hcl=hcl: e.tensor_mul(
                    hbuf[:, 512 * hcl:512 * (hcl + 1)], sil_sb[b][:, :], PSa[2 * b + 1][:, :]), drain=False)
                if hcl >= 2:
                    p.wait('vector', 'te', rms_te.get(hcl - 2, 0))
                p.op('vector', lambda e, b=b, hcl=hcl: e.tensor_mul(
                    hsq_sb[b][:, :], hbuf[:, 512 * hcl:512 * (hcl + 1)],
                    hbuf[:, 512 * hcl:512 * (hcl + 1)]), inc='ve')
                ve_mark[hcl] = p.cnt['ve']
                if hcl > 0:
                    p.wait('tensor', 've', ve_mark[hcl - 1])
                    p.op('tensor', lambda e, hcl=hcl: e.matmul(
                        PSa[6][0:1, :], onesb_sb[:, 0:1], hsq_sb[(hcl - 1) % 2][:, :],
                        start=(hcl - 1 == 0), stop=False), inc='te')
                    rms_te[hcl - 1] = p.cnt['te']
                if hcl == 63:
                    p.wait('tensor', 've', ve_mark[63])
                    p.op('tensor', lambda e: e.matmul(
                        PSa[6][0:1, :], onesb_sb[:, 0:1], hsq_sb[63 % 2][:, :],
                        start=False, stop=True), inc='te')
            p.wait('vector', 'te')
            rr = row_sb[0:1, 0:512]
            p.op('vector', lambda e: e.tensor_scalar_mul(rr, PSa[6][0:1, :], 1.0 / FH), inc='ve')
            p.wait('scalar', 've')
            p.op('scalar', lambda e: e.activation(rr, rr, AF.Ln, bias=cst_sb[0:1, 2:3]))
            p.op('scalar', lambda e: e.activation(rr, rr, AF.Exp, scale=-0.5), inc='sc')
            p.wait('tensor', 'sc')
            p.op('tensor', lambda e: e.matmul(PSa[7][:, :], onesf_sb[0:1, 0:128], rr,
                                              start=True, stop=True), inc='te')
            p.wait('vector', 'te')
            p.op('vector', lambda e: e.tensor_copy(tmpb[:, :], PSa[7][:, :]), inc='ve')
            p.wait('sync', 'cc', 2)
            p.wait('gpsimd', 'cc', 2)

            def load_p(dq, hc, buf):
                q, sem = ('sync', 'dha') if hc % 2 == 0 else ('gpsimd', 'dga')
                p.dma(q, buf[:, :], wqp_all[hc, dq, :, :], sem)

            p.wait('sync', 'dha', p.cnt['dha'])
            load_p(0, 0, fpw[0])
            for dq in range(4):
                for hc in range(64):
                    b = hc % 2
                    sem_b = 'dha' if b == 0 else 'dga'
                    need = p.cnt[sem_b]
                    if dq * 64 + hc + 1 < 256:
                        ndq, nhc = (dq, hc + 1) if hc + 1 < 64 else (dq + 1, 0)
                        nq, nsem = ('sync', 'dha') if nhc % 2 == 0 else ('gpsimd', 'dga')
                        p.wait('sync', 'te', p.cnt['te'])
                        p.wait('gpsimd', 'te', p.cnt['te'])
                        p.wait(nq, nsem, p.cnt[nsem])
                        load_p(ndq, nhc, fpw[1 - b])
                    p.wait('tensor', sem_b, need)
                    if hc == 0:
                        p.wait('tensor', 've', p.cnt['ve'])
                    for cc in range(4):
                        p.op('tensor', lambda e, cc=cc, b=b, hc=hc: e.matmul(
                            PSa[cc][:, :], fpw[b][:, 128 * cc:128 * (cc + 1)],
                            hbuf[:, 512 * hc:512 * (hc + 1)],
                            start=(hc == 0), stop=(hc == 63)),
                            inc=('te' if cc == 3 else None))
                p.wait('vector', 'te')
                for cc in range(4):
                    dchunk = 4 * dq + cc
                    p.op('vector', lambda e, cc=cc: e.tensor_mul(tmpa[:, :], PSa[cc][:, :], tmpb[:, :]))
                    p.op('vector', lambda e, dchunk=dchunk, T=T: e.tensor_add(
                        R[:, TOK * dchunk + 512 * T: TOK * dchunk + 512 * (T + 1)],
                        R[:, TOK * dchunk + 512 * T: TOK * dchunk + 512 * (T + 1)],
                        tmpa[:, :]), inc='ve')

    def attn_b():
        RG = [list(range(NCORES))]
        p.wait('sync', 've', p.cnt['ve'])
        p.wait('sync', 'te', p.cnt['te'])
        p.dma('sync', trif_sb[:, :], trif[:, :], 'dhb')
        for c in range(NCH):
            p.dma('sync', xsha[128 * c:128 * (c + 1), :], xln[:, TOK * c:TOK * c + 512], 'dhb')
            p.dma('sync', xshb[128 * c:128 * (c + 1), :], xln[:, TOK * c + 512:TOK * (c + 1)], 'dhb')
        bounce_done = p.cnt['dhb']
        p.wait('gpsimd', 'dhb', bounce_done)
        p.op('gpsimd', lambda e: e.collective_compute(
            "AllGather", ALU.bypass, replica_groups=RG,
            ins=[xsha[:, :].opt()], outs=[xga[:, :].opt()]), inc='cc')
        cc_xa = p.cnt['cc']
        p.op('gpsimd', lambda e: e.collective_compute(
            "AllGather", ALU.bypass, replica_groups=RG,
            ins=[xshb[:, :].opt()], outs=[xgb[:, :].opt()]), inc='cc')
        cc_x = p.cnt['cc']
        p.wait('sync', 'cc', cc_xa)
        p.wait('gpsimd', 'cc', cc_xa)
        ve_proj0 = p.cnt['ve']

        def load_xa(g, hc, buf, q, sem):
            r, gg = g // 2, g % 2
            src = xga if gg == 0 else xgb
            p.dma(q, buf[:, :], src[2048 * r + 128 * hc:2048 * r + 128 * (hc + 1), :], sem)

        def load_wst(hc, buf, sem, q='gpsimd'):
            p.dma(q, buf[:, :], wqkvf_bf[128 * hc:128 * (hc + 1), :], sem)

        p.wait('vector', 'dhb', bounce_done)
        for pair in range(2):
            p.op('vector', lambda e, pair=pair: e.memset(
                Vsb[pair][:, 0:8320].rearrange('p (c w) -> p c w', w=130)[:, :, 64:65], 1.0))
            p.op('vector', lambda e, pair=pair: e.memset(
                Vsb[pair][:, 0:8320].rearrange('p (c w) -> p c w', w=130)[:, :, 129:130], 1.0), inc='ve')
        p.wait('sync', 'dhb', p.cnt['dhb'])
        load_xa(0, 0, xat[0], 'sync', 'dhb')
        p.wait('gpsimd', 'dhb', p.cnt['dhb'])
        p.wait('gpsimd', 'dga', p.cnt['dga'])
        load_wst(0, wst[0], 'dga')
        gorder = [g for g in range(16) if g % 2 == 0] + [g for g in range(16) if g % 2 == 1]
        for gi, g in enumerate(gorder):
            for hc in range(NCH):
                i = gi * NCH + hc
                b = i % 2
                xsem = 'dhb' if b == 0 else 'dha'
                wsem = 'dga' if b == 0 else 'dsa'
                nx = p.cnt[xsem]
                nw = p.cnt[wsem]
                if i + 1 < 16 * NCH:
                    ng, nhc = (g, hc + 1) if hc + 1 < NCH else (gorder[gi + 1], 0)
                    nxs = 'dha' if b == 0 else 'dhb'
                    nws = 'dsa' if b == 0 else 'dga'
                    nwq = 'scalar' if b == 0 else 'gpsimd'
                    p.wait('sync', 'te', p.cnt['te'])
                    p.wait('sync', nxs, p.cnt[nxs])
                    if ng % 2 == 1 and g % 2 == 0:
                        p.wait('sync', 'cc', cc_x)
                    load_xa(ng, nhc, xat[1 - b], 'sync', nxs)
                    p.wait(nwq, 'te', p.cnt['te'])
                    p.wait(nwq, nws, p.cnt[nws])
                    load_wst(nhc, wst[1 - b], nws, nwq)
                p.wait('tensor', xsem, nx)
                p.wait('tensor', wsem, nw)
                if hc == 0:
                    p.wait('tensor', 've', ve_proj0 if g == 0 else p.cnt['ve'])
                for si in range(4):
                    p.op('tensor', lambda e, si=si, b=b, hc=hc: e.matmul(
                        PSa[si][:, :], wst[b][:, 128 * si:128 * (si + 1)], xat[b][:, :],
                        start=(hc == 0), stop=(hc == NCH - 1)))
                for si in range(4):
                    p.op('tensor', lambda e, si=si, b=b, hc=hc: e.matmul(
                        PSa[4 + si][0:128, 0:260], xat[b][:, 128 * si:128 * (si + 1)],
                        wst[b][:, 512:772],
                        start=(hc == 0), stop=(hc == NCH - 1)),
                        inc=('te' if si == 3 else None))
            p.wait('vector', 'te')
            pr, gg = g // 2, g % 2
            for pair in range(2):
                p.op('vector', lambda e, pair=pair, g=g: e.tensor_scalar_mul(
                    Qsb[pair][:, 512 * g:512 * (g + 1)], PSa[pair][:, :], 0.125),
                    drain=(pair == 0))
                p.op('vector', lambda e, pair=pair, g=g: e.tensor_copy(
                    Ksb[pair][:, 512 * g:512 * (g + 1)], PSa[2 + pair][:, :]), drain=False)
            for si in range(4):
                ch = 4 * g + si
                for pair in range(2):
                    p.op('vector', lambda e, si=si, pair=pair, ch=ch: e.tensor_copy(
                        Vsb[pair][:, 130 * ch:130 * ch + 64],
                        PSa[4 + si][0:128, 128 * pair:128 * pair + 64]), drain=False)
                    p.op('vector', lambda e, si=si, pair=pair, ch=ch: e.tensor_copy(
                        Vsb[pair][:, 130 * ch + 65:130 * ch + 129],
                        PSa[4 + si][0:128, 128 * pair + 64:128 * (pair + 1)]), drain=False)
                p.op('vector', lambda e, si=si: e.tensor_copy(
                    ftmp[:, :], PSa[4 + si][0:128, 256:260]), inc='ve')
                pgrp = 32 * (g // 4)
                p.wait('tensor', 've')
                p.op('tensor', lambda e: e.transpose(
                    PSa[0][0:4, 0:128], ftmp[:, :], ident_sb[:, :]), inc='te')
                p.wait('vector', 'te')
                p.wait('vector', 'dq', p.cnt['dq'])
                p.op('vector', lambda e: e.tensor_copy(ft2[:, :], PSa[0][0:4, 0:128]), inc='ve')
                p.wait('gpsimd', 've')
                col0 = (512 * g + 128 * si) % 2048
                p.dma('gpsimd', Ptil[pgrp:pgrp + 4, col0:col0 + 128], ft2[:, :], 'dq')
        p.wait('vector', 'dgc', g_loaded)
        p.wait('vector', 'dq', p.cnt['dq'])
        p.wait('scalar', 'dq', p.cnt['dq'])
        p.wait('vector', 'te', p.cnt['te'])
        p.op('vector', lambda e: e.memzero(zz8[:, :]))
        p.op('vector', lambda e: e.memzero(initc[:, :]), inc='ve')
        for grp in range(4):
            rs = slice(32 * grp, 32 * grp + 4)
            p.op('vector', lambda e, rs=rs: e.tensor_scalar(
                pcum[rs, :], Ptil[rs, :], bfc_sb[rs, 0:1], 0.0, ALU.add, ALU.min), inc='ve')
            p.wait('scalar', 've')
            p.op('scalar', lambda e, rs=rs: e.activation(
                Ptil[rs, :], Ptil[rs, :], AF.Abs, bias=bfc_sb[rs, 0:1]))
            p.op('scalar', lambda e, rs=rs: e.activation(
                Ptil[rs, :], Ptil[rs, :], AF.Exp, scale=-1.0))
            p.op('scalar', lambda e, rs=rs: e.activation(
                Ptil[rs, :], Ptil[rs, :], AF.Ln, bias=cst_sb[rs, 1:2]), inc='sc')
            p.wait('vector', 'sc')
            p.op('vector', lambda e, rs=rs: e.tensor_sub(
                Ptil[rs, :], Ptil[rs, :], pcum[rs, :]), inc='ve')
        for grp in range(4):
            rs = slice(32 * grp, 32 * grp + 4)
            if grp > 0:
                p.wait('gpsimd', 've', p.cnt['ve'])
                p.dma('gpsimd', initc[32 * grp:32 * grp + 4, 0:1],
                      pcum[32 * (grp - 1):32 * (grp - 1) + 4, 2047:2048], 'dq')
                p.wait('vector', 'dq', p.cnt['dq'])
            p.op('vector', lambda e, rs=rs: e.tensor_tensor_scan(
                pcum[rs, :], Ptil[rs, :], zz8[rs, :], initc[rs, 0:1],
                ALU.add, ALU.add), inc='ve')
        p.wait('tensor', 've')
        for k in range(64):
            grp, col = k // 16, (k % 16) * 128
            p.op('tensor', lambda e, grp=grp, col=col: e.transpose(
                PSa[1][0:128, 0:4], pcum[32 * grp:32 * grp + 4, col:col + 128],
                ident_sb[32 * grp:32 * grp + 4, 32 * grp:32 * grp + 4],
                tile_position=(32 * grp, 0)), inc='te')
            p.wait('vector', 'te')
            p.op('vector', lambda e, k=k: e.tensor_copy(fct[:, 4 * k:4 * (k + 1)],
                                                        PSa[1][0:128, 0:4]), inc='ve')
            p.wait('tensor', 've')
        p.wait('scalar', 've', p.cnt['ve'])
        for pair in range(2):
            for j in range(16):
                grp, col = j // 4, (j % 4) * 512
                p.wait('gpsimd', 'te', p.cnt['te'])
                p.wait('gpsimd', 'dq', p.cnt['dq'])
                p.dma('gpsimd', rank1A[0:1, :],
                      pcum[32 * grp + 2 * pair:32 * grp + 2 * pair + 1, col:col + 512], 'dq')
                p.dma('gpsimd', rank1B[0:1, :],
                      pcum[32 * grp + 2 * pair + 1:32 * grp + 2 * pair + 2, col:col + 512], 'dq')
                dq_r = p.cnt['dq']
                p.wait('tensor', 'dq', dq_r)
                p.wait('tensor', 've', p.cnt['ve'])
                p.op('tensor', lambda e: e.matmul(PSa[4][:, :], onesf_sb[0:1, 0:128],
                                                  rank1A[0:1, :], start=True, stop=True))
                p.op('tensor', lambda e: e.matmul(PSa[5][:, :], onesf_sb[0:1, 0:128],
                                                  rank1B[0:1, :], start=True, stop=True), inc='te')
                p.wait('vector', 'te')
                p.op('vector', lambda e: e.tensor_copy(fqbA[:, :], PSa[4][:, :]))
                p.op('vector', lambda e: e.tensor_copy(fqbB[:, :], PSa[5][:, :]), inc='ve')
                r1_ve = p.cnt['ve']
                nk = 4 * (j + 1)
                p.wait('tensor', 've', r1_ve)
                sc_mark_f = {}
                ve_qb = p.cnt['ve']

                def emit_lpv(k, nk):
                    kpar = k % 3
                    mc = 128 * (k - 4 * j) if k >= 4 * j else 0
                    p.op('tensor', lambda e, pair=pair, kpar=kpar, k=k, mc=mc, nk=nk: e.matmul(
                        PSa[6][0:65, mc:512], Vsb[pair][:, 130 * k:130 * k + 65],
                        ptb[kpar][:, mc:512], start=(k == 0), stop=(k == nk - 1)))
                    p.op('tensor', lambda e, pair=pair, kpar=kpar, k=k, mc=mc, nk=nk: e.matmul(
                        PSa[7][0:65, mc:512], Vsb[pair][:, 130 * k + 65:130 * (k + 1)],
                        ptb[3 + kpar][:, mc:512],
                        start=(k == 0), stop=(k == nk - 1)), inc='te')

                for k in range(nk + 2):
                    if k < nk:
                        kpar = k % 3
                        STW = PBa[kpar]
                        STA, STB = PSa[kpar * 2], PSa[kpar * 2 + 1]
                        if k == 2:
                            p.wait('tensor', 've', r1_ve)
                        mc = 128 * (k - 4 * j) if k >= 4 * j else 0
                        p.op('tensor', lambda e, STA=STA, pair=pair, k=k, j=j, mc=mc: e.matmul(
                            STA[:, mc:512], Ksb[pair][0:64, 128 * k:128 * (k + 1)],
                            Qsb[pair][0:64, 512 * j + mc:512 * (j + 1)],
                            start=True, stop=True))
                        p.op('tensor', lambda e, STB=STB, pair=pair, k=k, j=j, mc=mc: e.matmul(
                            STB[:, mc:512], Ksb[pair][64:128, 128 * k:128 * (k + 1)],
                            Qsb[pair][64:128, 512 * j + mc:512 * (j + 1)],
                            tile_position=(64, 0), start=True, stop=True), inc='te')
                    if k >= 2:
                        p.wait('tensor', 'sc', sc_mark_f[k - 2])
                        if k - 2 == 0:
                            p.wait('tensor', 've', ve_qb)
                        emit_lpv(k - 2, nk)
                    if k >= nk:
                        continue
                    p.wait('vector', 'te')
                    if k >= 4 * j:
                        p.op('vector', lambda e, STA=STA, mc=mc: e.tensor_add(
                            STA[:, mc:mc + 128], STA[:, mc:mc + 128], trif_sb[:, :]))
                        p.op('vector', lambda e, STB=STB, mc=mc: e.tensor_add(
                            STB[:, mc:mc + 128], STB[:, mc:mc + 128], trif_sb[:, :]), drain=False)
                        p.op('vector', lambda e, STW=STW, mc=mc: e.tensor_sub(
                            STW[:, mc:512], STW[:, mc:512], fqbw[:, mc:512]))
                        p.op('vector', lambda e, STW=STW, mc=mc: e.tensor_sub(
                            STW[:, 512 + mc:1024], STW[:, 512 + mc:1024],
                            fqbw[:, 512 + mc:1024]), inc='ve', drain=False)
                    else:
                        p.op('vector', lambda e, STW=STW: e.tensor_sub(
                            STW[:, :], STW[:, :], fqbw[:, :]), inc='ve', drain=False)
                    ve_f = p.cnt['ve']
                    p.wait('scalar', 've', ve_f)
                    p.op('scalar', lambda e, STA=STA, kpar=kpar, k=k, pair=pair, mc=mc: e.activation(
                        ptb[kpar][:, mc:512], STA[:, mc:512], AF.Exp,
                        bias=fct[:, 4 * k + 2 * pair:4 * k + 2 * pair + 1]))
                    p.op('scalar', lambda e, STB=STB, kpar=kpar, k=k, pair=pair, mc=mc: e.activation(
                        ptb[3 + kpar][:, mc:512], STB[:, mc:512], AF.Exp,
                        bias=fct[:, 4 * k + 2 * pair + 1:4 * k + 2 * pair + 2]), inc='sc', drain=False)
                    sc_mark_f[k] = p.cnt['sc']
                p.wait('vector', 'te')
                p.wait('vector', 'dq', p.cnt['dq'])
                p.wait('scalar', 'te', p.cnt['te'])
                p.op('scalar', lambda e: e.activation(dvb[64:65, :], PSa[6][64:65, :], AF.Ln))
                p.op('scalar', lambda e: e.activation(rcpb[64:65, :], dvb[64:65, :], AF.Exp,
                                                      scale=-1.0))
                p.op('scalar', lambda e: e.activation(dvb2[64:65, :], PSa[7][64:65, :], AF.Ln, drain=False)
                     if False else e.activation(dvb2[64:65, :], PSa[7][64:65, :], AF.Ln))
                p.op('scalar', lambda e: e.activation(rcpbB[64:65, :], dvb2[64:65, :], AF.Exp,
                                                      scale=-1.0), inc='sc')
                p.wait('vector', 'sc')
                p.op('vector', lambda e: e.nop(), inc='ve')
                p.wait('gpsimd', 've')
                p.dma('gpsimd', rcpb[0:1, :], rcpb[64:65, :], 'dq')
                p.dma('gpsimd', rcpbB[0:1, :], rcpbB[64:65, :], 'dq')
                p.wait('tensor', 'dq', p.cnt['dq'])
                p.wait('tensor', 've', p.cnt['ve'])
                p.wait('tensor', 'sc', p.cnt['sc'])
                p.op('tensor', lambda e: e.matmul(PSa[0][0:64, :], onesf_sb[0:1, 0:64],
                                                  rcpb[0:1, :], start=True, stop=True))
                p.op('tensor', lambda e: e.matmul(PSa[1][0:64, :], onesf_sb[0:1, 0:64],
                                                  rcpbB[0:1, :], start=True, stop=True), inc='te')
                p.wait('vector', 'te')
                p.op('vector', lambda e: e.tensor_copy(dvb[0:64, :], PSa[0][0:64, :]))
                p.op('vector', lambda e: e.tensor_copy(dvb2[0:64, :], PSa[1][0:64, :]))
                p.op('vector', lambda e: e.tensor_mul(onrm_sb[0:64, :], PSa[6][0:64, :], dvb[0:64, :]))
                p.op('vector', lambda e: e.tensor_mul(onrm2_sb[0:64, :], PSa[7][0:64, :],
                                                      dvb2[0:64, :]), inc='ve')
                p.wait('gpsimd', 've')
                p.dma('gpsimd', obnc[128 * pair:128 * pair + 64, 512 * j:512 * (j + 1)],
                      onrm_sb[0:64, :], 'dq')
                p.dma('gpsimd', obnc[128 * pair + 64:128 * (pair + 1), 512 * j:512 * (j + 1)],
                      onrm2_sb[0:64, :], 'dq')
        p.wait('gpsimd', 'dq', p.cnt['dq'])
        p.op('gpsimd', lambda e: e.collective_compute(
            "AllGather", ALU.bypass, replica_groups=RG,
            ins=[obnc[:, :].opt()], outs=[ogath[:, :].opt()]), inc='cc')
        cc_o = p.cnt['cc']
        p.wait('sync', 'cc', cc_o)
        p.wait('sync', 'te', p.cnt['te'])
        p.wait('sync', 'dhb', p.cnt['dhb'])
        cell = {}
        p.emit('sync', lambda e: cell.__setitem__('off', e.partition_id()))
        for c in range(NCH):
            def oload(e, c=c):
                src = ogath[128 * c:128 * (c + 1), bass.ts(cell['off'], TOK)]
                return e.dma_start(out=oown[:, TOK * c:TOK * (c + 1)], in_=src)
            p.op('sync', oload, inc='dhb', n=16)
        oo_done = p.cnt['dhb']

        def load_wo(dst, h, sem):
            for c in range(NCH):
                p.dma('gpsimd', dst[:, 128 * c:128 * (c + 1)],
                      WobT[128 * c:128 * (c + 1), 128 * h:128 * (h + 1)], sem)

        p.wait('gpsimd', 'te', p.cnt['te'])
        p.wait('gpsimd', 'dga', p.cnt['dga'])
        load_wo(fgw[0], 0, 'dga')
        prev_te = p.cnt['te']
        for oc in range(NCH):
            b = oc % 2
            sem_b = 'dga' if b == 0 else 'dgb'
            need = p.cnt[sem_b]
            if oc + 1 < NCH:
                nsem = 'dgb' if b == 0 else 'dga'
                p.wait('gpsimd', 'te', prev_te)
                p.wait('gpsimd', nsem, p.cnt[nsem])
                load_wo(fgw[1 - b], oc + 1, nsem)
            p.wait('tensor', sem_b, need)
            p.wait('tensor', 'dhb', oo_done)
            for g in range(NTG):
                p.wait('tensor', 've', p.cnt['ve'])
                for c in range(NCH):
                    p.op('tensor', lambda e, g=g, c=c, b=b: e.matmul(
                        PSa[g][:, :], fgw[b][:, 128 * c:128 * (c + 1)],
                        oown[:, TOK * c + 512 * g: TOK * c + 512 * (g + 1)],
                        start=(c == 0), stop=(c == NCH - 1)),
                        inc=('te' if c == NCH - 1 else None))
                p.wait('vector', 'te')
                p.op('vector', lambda e, oc=oc, g=g: e.tensor_add(
                    R[:, TOK * oc + 512 * g: TOK * oc + 512 * (g + 1)],
                    R[:, TOK * oc + 512 * g: TOK * oc + 512 * (g + 1)], PSa[g][:, :]), inc='ve')
            prev_te = p.cnt['te']

    for c in range(NCH):
        b = c % 2
        gsem = 'dga' if b == 0 else 'dgb'
        p.wait('gpsimd', gsem, p.cnt[gsem])
        p.wait('gpsimd', 'dhb', p.cnt['dhb'])
        p.dma('gpsimd', qwb[b][:, :], WqaT[128 * c:128 * (c + 1), :], gsem)
        p.wait('sync', gsem, p.cnt[gsem])
        p.dma('sync', wqa_bf[:, 128 * c:128 * (c + 1), :].transpose([1, 0, 2]),
              qwb[b].rearrange('p (h w) -> p h w', w=128)[:, :, :], 'dhb')
    for c in range(NCH):
        b = c % 2
        gsem = 'dga' if b == 0 else 'dgb'
        p.wait('gpsimd', gsem, p.cnt[gsem])
        p.wait('gpsimd', 'dhb', p.cnt['dhb'])
        p.dma('gpsimd', qwb[b][:, :], WoaT[128 * c:128 * (c + 1), :], gsem)
        p.wait('sync', gsem, p.cnt[gsem])
        p.dma('sync', woa_bf[:, 128 * c:128 * (c + 1), :].transpose([1, 0, 2]),
              qwb[b].rearrange('p (h w) -> p h w', w=128)[:, :, :], 'dhb')
    wqa_done = p.cnt['dhb']
    qsteps = quant_steps() if 'q' in phases else []
    bank_weights_prep()
    layernorm()
    tap_xln = mktap('xln', [128, NCH * TOK], BF16)
    if tap_xln is not None:
        p.wait('gpsimd', 've')
        p.dma('gpsimd', tap_xln[:, :], xln[:, :], 'st')
    if 'a' in phases:
        attn_a(qsteps)
    else:
        for st in qsteps:
            st()
    if 'f' in phases:
        layernorm()
        ffn()
    if 'b' in phases:
        layernorm()
        attn_b()
    if 'g' in phases:
        layernorm()
        ffn()

    p.wait('sync', 've')
    p.wait('sync', 'te')
    for c in range(NCH):
        p.dma('sync', y[128 * c:128 * (c + 1), :], R[:, TOK * c:TOK * (c + 1)], 'st')
    p.wait('gpsimd', 'st')
    p.build()
    return nc


def kernel(**inputs):
    nc = build_nc(phases=('q', 'a', 'f', 'b', 'g'))
    in_maps = [host_inputs(inputs, c) for c in range(NCORES)]
    res = run_bass_kernel_spmd(nc, in_maps, core_ids=list(range(NCORES)))
    out = np.concatenate([r['y'].T for r in res.results], axis=0)
    return out.reshape(1, S, D).astype(np.float32)


# version 15
# speedup vs baseline: 1.1410x; 1.1410x over previous
import sys, os
if '/opt/trn_rl_repo' not in sys.path:
    sys.path.insert(0, '/opt/trn_rl_repo')
import numpy as np
import ml_dtypes
import concourse.bass as bass
import concourse.mybir as mybir
from concourse.bass_utils import run_bass_kernel_spmd
from concourse import dve_ops

F32 = mybir.dt.float32
BF16 = mybir.dt.bfloat16
AF = mybir.ActivationFunctionType
ALU = mybir.AluOpType
BF = ml_dtypes.bfloat16

NCORES = 8
S = 8192
D = 2048
TOK = S // NCORES
NCH = D // 128
NTG = TOK // 512
HA = 16
NBK = 4
HB = 32
HBL = HB // NCORES
HD = 64
FH = 8192
FH2 = 16384
LNEPS = 1e-5
RMSEPS = 1.1920929e-7
NEGBIG = -1.0e9


def _c(a):
    return np.ascontiguousarray(a)


def host_inputs(inp, core):
    d = {}
    x = np.asarray(inp['L0_attn_a_hidden_states'])
    sl = slice(TOK * core, TOK * (core + 1))
    d['xT'] = _c(x[0, sl, :].T)

    d['WqaT'] = _c(np.asarray(inp['L0_attn_a_Wq']).T)
    d['WoaT'] = _c(np.asarray(inp['L0_attn_a_Wo']).T)
    mem = np.asarray(inp['L0_attn_a_memories'])
    d['memH'] = _c(mem.transpose(1, 0, 2, 3))
    mn = np.asarray(inp['L0_attn_a_memory_norms'])
    d['mnT'] = _c(mn.transpose(1, 2, 0).reshape(128 * HA // 128, 128, NBK)
                  .reshape(HA, 128, NBK))
    d['mnFlat'] = _c(mn.reshape(NBK, HA * 128))
    bw = np.asarray(inp['L0_attn_a_bank_weights'])
    d['bwT'] = _c(bw.T)

    hsl = slice(HBL * HD * core, HBL * HD * (core + 1))
    d['Wqkvf'] = _c(np.concatenate([
        np.asarray(inp['L0_attn_b_Wq'])[hsl, :].T,
        np.asarray(inp['L0_attn_b_Wk'])[hsl, :].T,
        np.asarray(inp['L0_attn_b_Wv'])[hsl, :].T,
        np.asarray(inp['L0_attn_b_Wf'])[HBL * core: HBL * (core + 1), :].T,
    ], axis=1))
    bfv = np.asarray(inp['L0_attn_b_bf'])[HBL * core: HBL * (core + 1)]
    bff = np.zeros((128, 1), dtype=np.float32)
    for grp in range(4):
        bff[32 * grp:32 * grp + 4, 0] = bfv
    d['bf_full'] = bff
    d['WobT'] = _c(np.asarray(inp['L0_attn_b_Wo']).T)

    wguT = np.asarray(inp['L0_ffn_w_gate_up']).T
    d['wguS'] = _c(wguT[256 * core: 256 * (core + 1), :])
    wpT = np.asarray(inp['L0_ffn_w_proj']).T
    d['wpS'] = _c(wpT[1024 * core: 1024 * (core + 1), :])

    d['ident_f32'] = np.eye(128, dtype=np.float32)
    d['ones_f32'] = np.ones((128, 512), dtype=np.float32)
    d['ones_bf'] = np.ones((128, 512), dtype=BF)
    tri = np.tril(np.ones((128, 128), dtype=np.float32), -1)
    d['Utri_bf'] = tri.astype(BF)
    d['negI_bf'] = (NEGBIG * np.eye(128, dtype=np.float32)).astype(BF)
    d['trif'] = NEGBIG * tri
    sel = np.zeros((4, 512), dtype=np.float32)
    for n in range(4):
        sel[n, 128 * n:128 * (n + 1)] = 1.0
    d['selG'] = sel.astype(BF)
    ind2 = np.zeros((128, 2), dtype=np.float32)
    ind2[:64, 0] = 1.0
    ind2[64:, 1] = 1.0
    d['ind2'] = ind2.astype(BF)
    e2 = np.zeros((2, 128), dtype=np.float32)
    e2[0, :64] = 1.0
    e2[1, 64:] = 1.0
    d['e2_bf'] = e2.astype(BF)
    d['e2_f32'] = e2
    cst = np.zeros((128, 8), dtype=np.float32)
    cst[:, 0] = LNEPS
    cst[:, 1] = 1.0
    cst[:, 2] = RMSEPS
    d['cst'] = cst
    return d


class P:
    def __init__(self, nc):
        self.nc = nc
        self.streams = {k: [] for k in ('tensor', 'vector', 'scalar', 'sync', 'gpsimd')}
        self.sem = {}
        self.cnt = {}
        from contextlib import ExitStack
        self.stack = ExitStack()

    def mksem(self, *names):
        for n in names:
            self.sem[n] = self.stack.enter_context(self.nc.semaphore(n))
            self.cnt[n] = 0

    def emit(self, eng, f):
        self.streams[eng].append(f)

    def wait(self, eng, sname, val=None):
        v = self.cnt[sname] if val is None else val
        s = self.sem[sname]
        if v > 0:
            self.emit(eng, lambda e, s=s, v=v: e.wait_ge(s, v))
        return v

    def op(self, eng, f, inc=None, n=1, drain=True):
        if drain and eng in ('vector', 'scalar'):
            self.emit(eng, lambda e: e.drain())
        if inc is not None:
            self.cnt[inc] += n
            s = self.sem[inc]
            self.emit(eng, lambda e, f=f, s=s, n=n: f(e).then_inc(s, n))
        else:
            self.emit(eng, lambda e, f=f: f(e))

    def dma(self, q, out, in_, sname):
        self.op(q, lambda e, o=out, i=in_: e.dma_start(out=o, in_=i), inc=sname, n=16,
                drain=False)
        return self.cnt[sname]

    def build(self):
        nc = self.nc
        with nc.Block() as block:
            @block.tensor
            def _(e):
                for f in self.streams['tensor']:
                    f(e)

            @block.vector
            def _(e):
                for f in self.streams['vector']:
                    f(e)

            @block.scalar
            def _(e):
                for f in self.streams['scalar']:
                    f(e)

            @block.sync
            def _(e):
                for f in self.streams['sync']:
                    f(e)

            @block.gpsimd
            def _(e):
                for f in self.streams['gpsimd']:
                    f(e)
        self.stack.close()


def build_nc(phases=('q', 'a', 'f'), debug_taps=()):
    nc = bass.Bass("TRN2", target_bir_lowering=False, debug=False)
    p = P(nc)
    p.mksem('dh', 'dha', 'dhb', 'dga', 'dgb', 'dgc', 'dsa', 'dsb', 'dso', 'te', 've', 'sc', 'cc', 'st', 'dq')

    io = {}
    def din(name, shape, dtype=F32):
        io[name] = nc.dram_tensor(name, list(shape), dtype, kind="ExternalInput").ap()
        return io[name]

    xT = din('xT', [D, TOK])
    WqaT = din('WqaT', [D, D]); WoaT = din('WoaT', [D, D])
    memH = din('memH', [HA, NBK, 128, 128]); mnT = din('mnT', [HA, 128, NBK])
    mnFlat = din('mnFlat', [NBK, HA * 128]); bwT = din('bwT', [NBK, HA])
    Wqkvf = din('Wqkvf', [D, 772]); bf_full = din('bf_full', [128, 1])
    WobT = din('WobT', [D, D])
    wguS = din('wguS', [256, FH2]); wpS = din('wpS', [1024, D])
    ident_f32 = din('ident_f32', [128, 128])
    ones_f32 = din('ones_f32', [128, 512]); ones_bf = din('ones_bf', [128, 512], BF16)
    Utri_bf = din('Utri_bf', [128, 128], BF16); negI_bf = din('negI_bf', [128, 128], BF16)
    trif = din('trif', [128, 128])
    selG = din('selG', [4, 512], BF16)
    ind2 = din('ind2', [128, 2], BF16); e2_bf = din('e2_bf', [2, 128], BF16)
    e2_f32 = din('e2_f32', [2, 128]); cst = din('cst', [128, 8])

    y = nc.dram_tensor('y', [D, TOK], F32, kind="ExternalOutput").ap()
    wqgu_sh = nc.dram_tensor('wqgu_sh', [128, 128, 2, 128], BF16).ap()
    wqgu_all = nc.dram_tensor('wqgu_all', [8 * 128, 128, 2, 128], BF16, addr_space='Shared').ap()
    wqp_sh = nc.dram_tensor('wqp_sh', [8, 4, 128, 512], BF16).ap()
    wqp_all = nc.dram_tensor('wqp_all', [64, 4, 128, 512], BF16, addr_space='Shared').ap()
    xsha = nc.dram_tensor('xsha', [D, TOK // 2], BF16).ap()
    xshb = nc.dram_tensor('xshb', [D, TOK // 2], BF16).ap()
    xga = nc.dram_tensor('xga', [8 * D, TOK // 2], BF16, addr_space='Shared').ap()
    xgb = nc.dram_tensor('xgb', [8 * D, TOK // 2], BF16, addr_space='Shared').ap()
    obnc = nc.dram_tensor('obnc', [256, S], BF16).ap()
    ogath = nc.dram_tensor('ogath', [8 * 256, S], BF16, addr_space='Shared').ap()
    wqkvf_bf = nc.dram_tensor('wqkvf_bf', [D, 772], BF16).ap()
    wqa_bf = nc.dram_tensor('wqa_bf', [HA, D, 128], BF16).ap()
    woa_bf = nc.dram_tensor('woa_bf', [NCH, D, 128], BF16).ap()
    taps = {}
    for t in debug_taps:
        pass

    def mktap(name, shape, dtype=F32):
        if name in debug_taps:
            taps[name] = nc.dram_tensor('tap_' + name, list(shape), dtype,
                                        kind="ExternalOutput").ap()
            return taps[name]
        return None

    SB = {}
    SBASE = 16512
    def sb(name, shape, dtype, off):
        h = nc.alloc_sbuf_tensor_at(name, list(shape), dtype, offset=SBASE + off)
        SB[name] = h
        return h.ap()

    R = sb('R', [128, NCH * TOK], F32, 0)
    xln = sb('xln', [128, NCH * TOK], BF16, 65536)
    comb = sb('comb', [128, HA * TOK], BF16, 98304)
    hbuf = sb('hbuf', [128, (FH // 128) * 512], BF16, 98304)
    Ksb = [sb(f'Ksb{i}', [128, S], BF16, 98816 + 16384 * i) for i in range(2)]
    Qsb = [sb(f'Qsb{i}', [128, S], BF16, 131584 + 16384 * i) for i in range(2)]
    Vsb = [sb(f'Vsb{i}', [128, (S // 128) * 130], BF16, 65536 + 16640 * i) for i in range(2)]
    W0 = 163840
    wab = [sb(f'wab{i}', [128, NCH * 128], BF16, W0 + 4096 * i) for i in range(2)]
    mem_sb = [sb(f'mem_sb{i}', [128, NBK * 128], BF16, W0 + 8192 + 1024 * i) for i in range(2)]
    mnT_sb = sb('mnT_sb', [128, HA * NBK], BF16, W0 + 10240)
    smallc = W0 + 10496
    bw_sb = sb('bw_sb', [4, HA], F32, smallc)
    wact4 = sb('wact4', [4, HA], F32, smallc + 128)
    act_sb = sb('act_sb', [4, 1], F32, smallc + 320)
    g_sb = sb('g_sb', [4, 512], BF16, smallc + 384)
    ident_sb = sb('ident_sb', [128, 128], F32, smallc + 1408)
    onesf_sb = sb('onesf_sb', [128, 512], F32, smallc + 1920)
    onesb_sb = sb('onesb_sb', [128, 512], BF16, smallc + 3968)
    selG_sb = sb('selG_sb', [4, 512], BF16, smallc + 4992)
    utri_sb = sb('utri_sb', [128, 128], BF16, smallc + 6144)
    negI_sb = sb('negI_sb', [128, 128], BF16, smallc + 6400)
    trif_sb = sb('trif_sb', [128, 128], F32, 172288)
    cst_sb = sb('cst_sb', [128, 8], F32, smallc + 6016)
    lnc = smallc + 6656
    mnf_sb = sb('mnf_sb', [4, HA * 128], F32, lnc)
    xb_sb = [sb(f'xb_sb{i}', [128, 512], BF16, lnc + 1024 * i) for i in range(2)]
    xsq_sb = [sb(f'xsq_sb{i}', [128, 512], BF16, lnc + 2048 + 1024 * i) for i in range(2)]
    sigp = [sb(f'sigp{i}', [128, 512], BF16, lnc + 10240 + 1024 * i) for i in range(2)]
    row_sb = sb('row_sb', [1, 512 * 5], F32, lnc + 4096)
    tmpa = sb('tmpa', [128, 512], F32, lnc + 14336)
    tmpb = sb('tmpb', [128, 512], F32, lnc + 16384)
    qwb = [sb(f'qwb{i}', [128, 2048], BF16, 131840 + 4096 * i) for i in range(2)]
    qt1 = sb('qt1', [128, 512], BF16, 131840 + 8192)
    qt2 = [sb('qt2a', [128, 512], BF16, 131840 + 9216), sb('qt2b', [128, 512], BF16, 131840 + 22016)]
    qt3 = sb('qt3', [128, 512], BF16, 131840 + 10240)
    qsd = sb('qsd', [2, 512], BF16, 131840 + 11264)
    qrcp = sb('qrcp', [2, 512], F32, 131840 + 12288)
    qsdf = sb('qsdf', [2, 512], F32, 131840 + 15872)
    qscr = sb('qscr', [2, 512], F32, 131840 + 17920)
    ascr = sb('ascr', [128, 512], F32, 131840 + 19968)
    ind2_sb = sb('ind2_sb', [128, 2], BF16, 131840 + 14336)
    e2b_sb = sb('e2b_sb', [2, 128], BF16, 131840 + 14848)
    e2f_sb = sb('e2f_sb', [2, 128], F32, 131840 + 15360)
    fgw = [sb('fgw0', [128, 2048], BF16, 0 + 163840), sb('fgw1', [128, 2048], BF16, 4096 + 163840),
           sb('fgw2', [128, 2048], BF16, 203520), sb('fgw3', [128, 2048], BF16, 207616)]
    fpw = [sb('fpw0', [128, 512], BF16, 172032), sb('fpw1', [128, 512], BF16, 173056)]
    sil_sb = [sb('sil_sb0', [128, 512], BF16, lnc + 18432), sb('sil_sb1', [128, 512], BF16, lnc + 19456)]
    hsq_sb = [sb('hsq_sb0', [128, 512], BF16, lnc + 20480), sb('hsq_sb1', [128, 512], BF16, lnc + 21504)]
    xat = [sb(f'xat{i}', [128, 512], BF16, lnc + 1024 * i) for i in range(4)]
    Ptil = sb('Ptil', [128, 2048], F32, lnc + 14336)
    zz8 = sb('zz8', [128, 2048], F32, lnc + 4096)
    pcum = sb('pcum', [128, 2048], F32, 203520)
    wst = [sb(f'wst{i}', [128, 772], BF16, lnc + 4096 + 1664 * i) for i in range(2)]
    ptb = [sb(f'ptb{i}', [128, 512], BF16, lnc + 10240 + 1024 * i) for i in range(6)]
    rank1A = sb('rank1A', [128, 512], F32, 167936)
    rank1B = sb('rank1B', [128, 512], F32, 170240)
    fq3w = [sb(f'fq3w{b}', [3, 1024], BF16, 164352 + 2048 * b) for b in range(2)]
    hiT = sb('hiT', [97, 512], BF16, 168448)
    miT = sb('miT', [97, 512], BF16, 169472)
    rcpb = sb('rcpb', [128, 512], F32, lnc)
    rcpbB = sb('rcpbB', [128, 512], F32, lnc + 2048)
    dvb = sb('dvb', [128, 512], F32, lnc + 4096)
    dvb2 = sb('dvb2', [128, 512], F32, lnc + 6144)
    onrm_sb = sb('onrm_sb', [128, 512], BF16, lnc + 8192)
    onrm2_sb = sb('onrm2_sb', [128, 512], BF16, lnc + 9216)
    fct = sb('fct', [128, 256], F32, 211712)
    bfc_sb = sb('bfc_sb', [128, 1], F32, 212736)
    initc = sb('initc', [128, 1], F32, 212768)
    ftmp = sb('ftmp', [128, 4], F32, 174080)
    ft2 = sb('ft2', [4, 128], F32, lnc + 12288)
    oown = sb('oown', [128, NCH * 1024], BF16, 65536)
    assert 16512 + 212800 <= 229344

    PB = [nc.alloc_psum_tensor(f'PB{i}', [128, 1024], F32) for i in range(3)]
    PS = [nc.alloc_psum_tensor(f'PS{i}', [128, 512], F32) for i in range(2)]
    PSa = [PB[0].ap()[:, 0:512], PB[0].ap()[:, 512:1024],
           PB[1].ap()[:, 0:512], PB[1].ap()[:, 512:1024],
           PB[2].ap()[:, 0:512], PB[2].ap()[:, 512:1024]] + [t.ap() for t in PS]
    PBa = [t.ap() for t in PB]

    for dst, src in ((ident_sb, ident_f32), (onesf_sb, ones_f32), (onesb_sb, ones_bf),
                     (selG_sb, selG), (bw_sb, bwT), (mnf_sb, mnFlat), (cst_sb, cst),
                     (utri_sb, Utri_bf), (negI_sb, negI_bf)):
        p.dma('sync', dst[:, :], src[:, :], 'dh')
    for h in range(HA):
        p.dma('gpsimd', mnT_sb[:, NBK * h:NBK * (h + 1)], mnT[h, :, :], 'dgc')
    p.dma('gpsimd', bfc_sb[:, :], bf_full[:, :], 'dgc')
    p.dma('gpsimd', ind2_sb[:, :], ind2[:, :], 'dgc')
    p.dma('gpsimd', e2b_sb[:, :], e2_bf[:, :], 'dgc')
    p.dma('gpsimd', e2f_sb[:, :], e2_f32[:, :], 'dgc')
    for c in range(NCH):
        p.dma('sync', R[:, TOK * c:TOK * (c + 1)], xT[128 * c:128 * (c + 1), :], 'dh')
    c_loaded = p.cnt['dh']
    g_loaded = p.cnt['dgc']

    def rcp_fast(out, in_, scratch, inc=None):
        p.op('vector', lambda e: e.reciprocal_approx_fast(out=scratch, in_=in_))
        p.op('vector', lambda e: e._custom_dve(
            dve_ops.RECIPROCAL_APPROX_NR, out=out, in0=in_, in1=scratch, s0=2.0), inc=inc)

    def bank_weights_prep():
        p.wait('vector', 'dh', c_loaded)
        p.wait('vector', 'dgc', g_loaded)
        p.op('vector', lambda e: e.reduce_sum(tmpb[0:4, 0:1], mnf_sb[:, :], mybir.AxisListType.X))
        p.op('vector', lambda e: e.tensor_single_scalar(act_sb[:, :], tmpb[0:4, 0:1], 1e-6, ALU.is_ge), inc='ve')
        p.wait('scalar', 'dh', c_loaded)
        p.op('scalar', lambda e: e.activation(tmpa[0:4, 0:HA], bw_sb[:, :], AF.Exp), inc='sc')
        p.wait('tensor', 'sc')
        p.op('tensor', lambda e: e.matmul(PSa[6][0:1, 0:HA], onesf_sb[0:4, 0:1], tmpa[0:4, 0:HA],
                                          start=True, stop=True), inc='te')
        p.wait('vector', 'te')
        p.op('vector', lambda e: e.reciprocal(row_sb[0:1, 0:HA], PSa[6][0:1, 0:HA]), inc='ve')
        p.wait('tensor', 've')
        p.op('tensor', lambda e: e.matmul(PSa[7][0:4, 0:HA], onesf_sb[0:1, 0:4], row_sb[0:1, 0:HA],
                                          start=True, stop=True), inc='te')
        p.wait('vector', 'te')
        p.op('vector', lambda e: e.scalar_tensor_tensor(wact4[:, :], tmpa[0:4, 0:HA], act_sb[:, :],
                                                        PSa[7][0:4, 0:HA], ALU.mult, ALU.mult), inc='ve')

    def layernorm():
        te0 = p.cnt['te']
        p.wait('vector', 'dhb', p.cnt['dhb'])
        for g in range(NTG):
            gv0 = p.cnt['ve']
            for c in range(NCH):
                b = c % 2
                p.wait('vector', 'te', p.cnt['te'])
                p.op('vector', lambda e, c=c, g=g, b=b: e.tensor_copy(
                    xb_sb[b][:, :], R[:, TOK * c + 512 * g: TOK * c + 512 * (g + 1)]), inc='ve')
                p.wait('scalar', 've')
                p.op('scalar', lambda e, b=b: e.activation(xsq_sb[b][:, :], xb_sb[b][:, :], AF.Square), inc='sc')
                p.wait('tensor', 've')
                p.wait('tensor', 'sc')
                p.op('tensor', lambda e, b=b, c=c: e.matmul(
                    PSa[6][0:1, :], onesb_sb[:, 0:1], xb_sb[b][:, :], start=(c == 0), stop=(c == NCH - 1)))
                p.op('tensor', lambda e, b=b, c=c: e.matmul(
                    PSa[7][0:1, :], onesb_sb[:, 0:1], xsq_sb[b][:, :], start=(c == 0), stop=(c == NCH - 1)),
                    inc='te')
            p.wait('vector', 'te')
            m = row_sb[0:1, 0:512]; rstd = row_sb[0:1, 512:1024]; mr = row_sb[0:1, 1024:1536]
            var = row_sb[0:1, 1536:2048]; m2 = row_sb[0:1, 2048:2560]
            p.op('vector', lambda e: e.tensor_scalar_mul(m, PSa[6][0:1, :], 1.0 / D))
            p.op('vector', lambda e: e.tensor_scalar_mul(var, PSa[7][0:1, :], 1.0 / D))
            p.op('vector', lambda e: e.tensor_mul(m2, m, m))
            p.op('vector', lambda e: e.tensor_sub(var, var, m2), inc='ve')
            p.wait('scalar', 've')
            p.op('scalar', lambda e: e.activation(var, var, AF.Ln, bias=cst_sb[0:1, 0:1]))
            p.op('scalar', lambda e: e.activation(rstd, var, AF.Exp, scale=-0.5), inc='sc')
            p.wait('vector', 'sc')
            p.op('vector', lambda e: e.tensor_mul(mr, m, rstd), inc='ve')
            p.wait('tensor', 've')
            p.op('tensor', lambda e: e.matmul(PSa[4][:, :], onesf_sb[0:1, 0:128], rstd, start=True, stop=True))
            p.op('tensor', lambda e: e.matmul(PSa[5][:, :], onesf_sb[0:1, 0:128], mr, start=True, stop=True),
                 inc='te')
            p.wait('vector', 'te')
            for c in range(NCH):
                p.op('vector', lambda e, c=c, g=g: e.tensor_mul(
                    tmpa[:, :], R[:, TOK * c + 512 * g: TOK * c + 512 * (g + 1)], PSa[4][:, :]))
                p.op('vector', lambda e, c=c, g=g: e.tensor_sub(
                    xln[:, TOK * c + 512 * g: TOK * c + 512 * (g + 1)], tmpa[:, :], PSa[5][:, :]), inc='ve')

    def attn_a(qsteps=()):
        qsteps = list(qsteps)
        def load_w(dst, srcB, h, sem):
            p.dma('sync', dst[:, :],
                  srcB[h, :, :].rearrange('(c p) w -> p c w', p=128)[:, :, :], sem)

        def load_mem(dst, h, sem):
            for n in range(NBK):
                p.dma('gpsimd', dst[:, 128 * n:128 * (n + 1)], memH[h, n, :, :], sem)

        p.wait('sync', 'dhb', wqa_done)
        p.wait('sync', 'dha', p.cnt['dha'])
        load_w(wab[0], wqa_bf, 0, 'dha')
        p.wait('gpsimd', 'dga', p.cnt['dga'])
        load_mem(mem_sb[0], 0, 'dga')
        xln_ready = p.cnt['ve']
        prev_te = 0
        sig_done = {}
        for h in range(HA):
            b = h % 2
            sem_b = 'dha' if b == 0 else 'dhb'
            gsem_b = 'dga' if b == 0 else 'dgb'
            need = p.cnt[sem_b]
            gneed = p.cnt[gsem_b]
            if h + 1 < HA:
                nsem = 'dhb' if b == 0 else 'dha'
                gsem = 'dgb' if b == 0 else 'dga'
                p.wait('sync', 'te', prev_te)
                p.wait('sync', nsem, p.cnt[nsem])
                load_w(wab[1 - b], wqa_bf, h + 1, nsem)
                p.wait('gpsimd', 'te', prev_te)
                p.wait('gpsimd', gsem, p.cnt[gsem])
                load_mem(mem_sb[1 - b], h + 1, gsem)
            p.wait('tensor', sem_b, need)
            p.wait('tensor', gsem_b, gneed)
            p.wait('tensor', 've', max(sig_done.get(h - 1, 0), xln_ready))
            for g in range(NTG):
                for c in range(NCH):
                    p.op('tensor', lambda e, g=g, c=c, b=b: e.matmul(
                        PSa[g][:, :], wab[b][:, 128 * c:128 * (c + 1)],
                        xln[:, TOK * c + 512 * g: TOK * c + 512 * (g + 1)],
                        start=(c == 0), stop=(c == NCH - 1)),
                        inc=('te' if (g == NTG - 1 and c == NCH - 1) else None))
            te_q = p.cnt['te']
            for g in range(NTG):
                p.wait('vector', 'te', te_q)
                p.op('vector', lambda e, g=g: e.tensor_single_scalar(tmpa[:, :], PSa[g][:, :], 0.0, ALU.min),
                     inc='ve')
                p.wait('scalar', 've')
                p.op('scalar', lambda e: e.activation(tmpb[:, :], tmpa[:, :], AF.Exp), inc='sc')
                p.wait('vector', 'sc')
                p.op('vector', lambda e, g=g: e.scalar_tensor_tensor(
                    xb_sb[g][:, :], PSa[g][:, :], 0.0, tmpb[:, :], ALU.max, ALU.add), inc='ve')
                ve_sig = p.cnt['ve']
                if g == NTG - 1:
                    sig_done[h] = ve_sig
                p.wait('tensor', 've', ve_sig)
                p.op('tensor', lambda e, g=g, h=h: e.matmul(
                    PSa[6][0:4, :], mnT_sb[:, NBK * h: NBK * (h + 1)], xb_sb[g][:, :],
                    start=True, stop=True), inc='te')
                te_n = p.cnt['te']
                p.wait('vector', 'te', te_n)
                p.op('vector', lambda e: e.tensor_single_scalar(tmpa[0:4, :], PSa[6][0:4, :], 1e-6, ALU.max),
                     inc='ve')
                p.wait('scalar', 've')
                p.op('scalar', lambda e: e.activation(ascr[0:4, :], tmpa[0:4, :], AF.Ln))
                p.op('scalar', lambda e: e.activation(tmpb[0:4, :], ascr[0:4, :], AF.Exp,
                                                      scale=-1.0), inc='sc')
                p.wait('vector', 'sc')
                p.op('vector', lambda e, h=h: e.tensor_scalar_mul(
                    g_sb[:, :], tmpb[0:4, :], wact4[0:4, h:h + 1]), inc='ve')
                ve_g = p.cnt['ve']
                sigp_ve = {}
                for n in range(NBK):
                    pb = 4 + (n % 2)
                    p.wait('tensor', 've', ve_g if n == 0 else sigp_ve.get(n - 2, 0))
                    p.op('tensor', lambda e, n=n, pb=pb: e.matmul(
                        PSa[pb][:, :], selG_sb[:, 128 * n:128 * (n + 1)], g_sb[:, :],
                        start=True, stop=True), inc='te')
                    p.wait('vector', 'te')
                    p.op('vector', lambda e, g=g, pb=pb, n=n: e.tensor_mul(
                        sigp[n % 2][:, :], xb_sb[g][:, :], PSa[pb][:, :]), inc='ve')
                    sigp_ve[n] = p.cnt['ve']
                    p.wait('tensor', 've', sigp_ve[n])
                    p.op('tensor', lambda e, g=g, n=n, b=b: e.matmul(
                        PSa[2 + g][:, :], mem_sb[b][:, 128 * n:128 * (n + 1)], sigp[n % 2][:, :],
                        start=(n == 0), stop=(n == NBK - 1)), inc='te')
                p.wait('vector', 'te')
                p.op('vector', lambda e, g=g, h=h: e.tensor_copy(
                    comb[:, TOK * h + 512 * g: TOK * h + 512 * (g + 1)], PSa[2 + g][:, :]), inc='ve')
            prev_te = p.cnt['te']
            for _ in range(2):
                if qsteps:
                    qsteps.pop(0)()
        while qsteps:
            qsteps.pop(0)()
        comb_done = p.cnt['ve']
        p.wait('sync', 'te', p.cnt['te'])
        p.wait('sync', 'dha', p.cnt['dha'])
        load_w(wab[0], woa_bf, 0, 'dha')
        prev_te = 0
        for oc in range(NCH):
            b = oc % 2
            sem_b = 'dha' if b == 0 else 'dhb'
            need = p.cnt[sem_b]
            if oc + 1 < NCH:
                nsem = 'dhb' if b == 0 else 'dha'
                p.wait('sync', 'te', prev_te)
                p.wait('sync', nsem, p.cnt[nsem])
                load_w(wab[1 - b], woa_bf, oc + 1, nsem)
            p.wait('tensor', sem_b, need)
            p.wait('tensor', 've', comb_done)
            for g in range(NTG):
                p.wait('tensor', 've', p.cnt['ve'])
                for c in range(NCH):
                    p.op('tensor', lambda e, g=g, c=c, b=b: e.matmul(
                        PSa[g][:, :], wab[b][:, 128 * c:128 * (c + 1)],
                        comb[:, TOK * c + 512 * g: TOK * c + 512 * (g + 1)],
                        start=(c == 0), stop=(c == NCH - 1)),
                        inc=('te' if c == NCH - 1 else None))
                p.wait('vector', 'te')
                p.op('vector', lambda e, oc=oc, g=g: e.tensor_add(
                    R[:, TOK * oc + 512 * g: TOK * oc + 512 * (g + 1)],
                    R[:, TOK * oc + 512 * g: TOK * oc + 512 * (g + 1)], PSa[g][:, :]), inc='ve')
            prev_te = p.cnt['te']

    def quant_steps():
        RG = [list(range(NCORES))]
        steps = []

        def do_piece(bufb, out_fn):
            dso_mark = {-1: p.cnt['dso']}
            for st in range(4):
                qt2c = qt2[st % 2]
                wsl = qwb[bufb][:, 512 * st:512 * (st + 1)]
                p.wait('vector', 'dso', dso_mark.get(st - 2, dso_mark.get(-1, 0)))
                p.op('vector', lambda e: e.nop(), inc='ve')
                p.wait('scalar', 've')
                p.wait('scalar', 'te', p.cnt['te'])
                p.op('scalar', lambda e, wsl=wsl: e.activation(qt1[:, :], wsl, AF.Abs), inc='sc')
                p.wait('tensor', 'sc')
                p.op('tensor', lambda e: e.matmul(PSa[6][0:2, :], ind2_sb[:, :], qt1[:, :],
                                                  start=True, stop=True), inc='te')
                p.wait('vector', 'te')
                p.op('vector', lambda e: e.tensor_scalar(
                    qsd[:, :], PSa[6][0:2, :], 1.0 / 64, 1e-8, ALU.mult, ALU.max), inc='ve')
                p.wait('scalar', 've')
                p.op('scalar', lambda e: e.activation(qsdf[:, :], qsd[:, :], AF.Ln))
                p.op('scalar', lambda e: e.activation(qrcp[:, :], qsdf[:, :], AF.Exp,
                                                      scale=-1.0), inc='sc')
                p.wait('tensor', 've')
                p.wait('tensor', 'sc')
                p.op('tensor', lambda e: e.matmul(PSa[4][:, :], e2f_sb[:, :], qrcp[:, :],
                                                  start=True, stop=True))
                p.op('tensor', lambda e: e.matmul(PSa[5][:, :], e2b_sb[:, :], qsd[:, :],
                                                  start=True, stop=True), inc='te')
                p.wait('vector', 'te')
                p.op('vector', lambda e, wsl=wsl, qt2c=qt2c: e.tensor_mul(qt2c[:, :], wsl, PSa[4][:, :]))
                p.op('vector', lambda e, qt2c=qt2c: e.tensor_single_scalar(
                    qt1[:, :], qt2c[:, :], -0.5, ALU.is_lt))
                p.op('vector', lambda e, qt2c=qt2c: e.scalar_tensor_tensor(
                    qt3[:, :], qt2c[:, :], 0.5, qt1[:, :], ALU.is_gt, ALU.subtract))
                p.op('vector', lambda e, qt2c=qt2c: e.tensor_mul(
                    qt2c[:, :], qt3[:, :], PSa[5][:, :]), inc='ve')
                p.wait('scalar', 've')
                p.wait('scalar', 'dso', p.cnt['dso'])
                out_fn(st, qt2c)
                dso_mark[st] = p.cnt['dso']

        def gu_out(rc, cp, st, qt2c):
            hc0 = (2048 * cp + 512 * st) // 128
            dst = wqgu_sh[hc0:hc0 + 4, :, rc, :].transpose([1, 0, 2])
            p.dma('scalar', dst, qt2c.rearrange('p (a h) -> p a h', h=128)[:, :, :], 'dso')

        def p_out(rc, st, qt2c):
            p.dma('scalar', wqp_sh[rc, st, :, :], qt2c[:, :], 'dso')

        def precast(c0, c1):
            p.wait('gpsimd', 'dh', p.cnt['dh'])
            p.wait('gpsimd', 've', p.cnt['ve'])
            for c in range(c0, c1):
                b = c % 2
                sem = 'dga' if b == 0 else 'dgb'
                p.wait('gpsimd', sem, p.cnt[sem])
                p.wait('gpsimd', 'dhb', p.cnt['dhb'])
                p.dma('gpsimd', wst[b][:, :], Wqkvf[128 * c:128 * (c + 1), :], sem)
                p.wait('sync', sem, p.cnt[sem])
                p.dma('sync', wqkvf_bf[128 * c:128 * (c + 1), :], wst[b][:, :], 'dhb')
        steps.append(lambda: precast(0, 8))
        steps.append(lambda: precast(8, 16))

        pieces = [(rc, cp) for rc in range(2) for cp in range(8)]

        def gu_step(i):
            rc, cp = pieces[i]
            b = i % 2
            sem_b = 'dga' if b == 0 else 'dgb'
            if i == 0:
                p.wait('tensor', 'dgc', g_loaded)
                p.wait('gpsimd', 'dga', p.cnt['dga'])
                p.wait('gpsimd', 'dhb', p.cnt['dhb'])
                p.dma('gpsimd', qwb[0][:, :], wguS[0:128, 0:2048], 'dga')
            need = p.cnt[sem_b]
            if i + 1 < len(pieces):
                nrc, ncp = pieces[i + 1]
                nsem = 'dgb' if b == 0 else 'dga'
                p.wait('gpsimd', 've', p.cnt['ve'])
                p.wait('gpsimd', nsem, p.cnt[nsem])
                p.dma('gpsimd', qwb[1 - b][:, :],
                      wguS[128 * nrc:128 * (nrc + 1), 2048 * ncp:2048 * (ncp + 1)],
                      nsem)
            p.wait('vector', sem_b, need)
            do_piece(b, lambda st, qt2c, rc=rc, cp=cp: gu_out(rc, cp, st, qt2c))

        for i in range(len(pieces)):
            steps.append(lambda i=i: gu_step(i))

        def ag_gu():
            p.wait('gpsimd', 'dso', p.cnt['dso'])
            p.op('gpsimd', lambda e: e.collective_compute(
                "AllGather", ALU.bypass, replica_groups=RG,
                ins=[wqgu_sh[:, :, :, :].opt()], outs=[wqgu_all[:, :, :, :].opt()]), inc='cc')
        steps.append(ag_gu)

        def p_step(rc):
            b = rc % 2
            sem_b = 'dga' if b == 0 else 'dgb'
            if rc == 0:
                p.wait('gpsimd', 'dga', p.cnt['dga'])
                p.dma('gpsimd', qwb[0][:, :], wpS[0:128, :], 'dga')
            need = p.cnt[sem_b]
            if rc + 1 < 8:
                nsem = 'dgb' if b == 0 else 'dga'
                p.wait('gpsimd', 've', p.cnt['ve'])
                p.wait('gpsimd', nsem, p.cnt[nsem])
                p.dma('gpsimd', qwb[1 - b][:, :], wpS[128 * (rc + 1):128 * (rc + 2), :], nsem)
            p.wait('vector', sem_b, need)
            do_piece(b, lambda st, qt2c, rc=rc: p_out(rc, st, qt2c))

        for rc in range(8):
            steps.append(lambda rc=rc: p_step(rc))

        def ag_p():
            p.wait('gpsimd', 'dso', p.cnt['dso'])
            p.op('gpsimd', lambda e: e.collective_compute(
                "AllGather", ALU.bypass, replica_groups=RG,
                ins=[wqp_sh[:, :, :, :].opt()], outs=[wqp_all[:, :, :, :].opt()]), inc='cc')
        steps.append(ag_p)
        return steps

    def ffn():
        p.wait('sync', 'cc', 1)
        p.wait('gpsimd', 'cc', 1)
        p.wait('sync', 'te', p.cnt['te'])
        p.wait('gpsimd', 'te', p.cnt['te'])
        p.wait('vector', 'dhb', p.cnt['dhb'])
        p.wait('scalar', 'dhb', p.cnt['dhb'])
        p.wait('sync', 've', p.cnt['ve'])
        p.wait('gpsimd', 've', p.cnt['ve'])
        for T in range(NTG):
            ve_xln = p.cnt['ve']
            def load_gu(hcl, gbuf, ubuf, sems):
                for r in range(8):
                    q, sem = (('sync', sems[0]) if r < 3 else
                              ('scalar', sems[2]) if r < 6 else ('gpsimd', sems[1]))
                    p.dma(q, gbuf[:, 256 * r:256 * (r + 1)],
                          wqgu_all[128 * r + hcl, :, :, :], sem)
                    p.dma(q, ubuf[:, 256 * r:256 * (r + 1)],
                          wqgu_all[128 * r + hcl + 64, :, :, :], sem)

            sc_mark = {}
            ve_mark = {}
            rms_te = {}
            p.wait('sync', 'dha', p.cnt['dha'])
            p.wait('gpsimd', 'dga', p.cnt['dga'])
            p.wait('scalar', 'dsa', p.cnt['dsa'])
            load_gu(0, fgw[0], fgw[1], ('dha', 'dga', 'dsa'))
            for hcl in range(64):
                b = hcl % 2
                sems = ('dha', 'dga', 'dsa') if b == 0 else ('dhb', 'dgb', 'dsb')
                nh = p.cnt[sems[0]]
                ng = p.cnt[sems[1]]
                ns = p.cnt[sems[2]]
                if hcl + 1 < 64:
                    nsems = ('dhb', 'dgb', 'dsb') if b == 0 else ('dha', 'dga', 'dsa')
                    p.wait('sync', 'te', p.cnt['te'])
                    p.wait('gpsimd', 'te', p.cnt['te'])
                    p.wait('scalar', 'te', p.cnt['te'])
                    p.wait('sync', nsems[0], p.cnt[nsems[0]])
                    p.wait('gpsimd', nsems[1], p.cnt[nsems[1]])
                    p.wait('scalar', nsems[2], p.cnt[nsems[2]])
                    load_gu(hcl + 1, fgw[2 * (1 - b)], fgw[2 * (1 - b) + 1], nsems)
                p.wait('tensor', sems[0], nh)
                p.wait('tensor', sems[1], ng)
                p.wait('tensor', sems[2], ns)
                p.wait('tensor', 'sc', sc_mark[hcl - 2] if hcl >= 2 else 0)
                p.wait('tensor', 've', ve_mark[hcl - 2] if hcl >= 2 else ve_xln)
                for c in range(NCH):
                    p.op('tensor', lambda e, c=c, b=b, T=T: e.matmul(
                        PSa[2 * b][:, :], fgw[2 * b][:, 128 * c:128 * (c + 1)],
                        xln[:, TOK * c + 512 * T: TOK * c + 512 * (T + 1)],
                        start=(c == 0), stop=(c == NCH - 1)))
                for c in range(NCH):
                    p.op('tensor', lambda e, c=c, b=b, T=T: e.matmul(
                        PSa[2 * b + 1][:, :], fgw[2 * b + 1][:, 128 * c:128 * (c + 1)],
                        xln[:, TOK * c + 512 * T: TOK * c + 512 * (T + 1)],
                        start=(c == 0), stop=(c == NCH - 1)), inc=('te' if c == NCH - 1 else None))
                p.wait('scalar', 'te')
                if hcl >= 2:
                    p.wait('scalar', 've', ve_mark[hcl - 2])
                p.op('scalar', lambda e, b=b: e.activation(sil_sb[b][:, :], PSa[2 * b][:, :], AF.Silu),
                     inc='sc', drain=False)
                sc_mark[hcl] = p.cnt['sc']
                p.wait('vector', 'sc')
                p.op('vector', lambda e, b=b, hcl=hcl: e.tensor_mul(
                    hbuf[:, 512 * hcl:512 * (hcl + 1)], sil_sb[b][:, :], PSa[2 * b + 1][:, :]), drain=False)
                if hcl >= 2:
                    p.wait('vector', 'te', rms_te.get(hcl - 2, 0))
                p.op('vector', lambda e, b=b, hcl=hcl: e.tensor_mul(
                    hsq_sb[b][:, :], hbuf[:, 512 * hcl:512 * (hcl + 1)],
                    hbuf[:, 512 * hcl:512 * (hcl + 1)]), inc='ve')
                ve_mark[hcl] = p.cnt['ve']
                if hcl > 0:
                    p.wait('tensor', 've', ve_mark[hcl - 1])
                    p.op('tensor', lambda e, hcl=hcl: e.matmul(
                        PSa[6][0:1, :], onesb_sb[:, 0:1], hsq_sb[(hcl - 1) % 2][:, :],
                        start=(hcl - 1 == 0), stop=False), inc='te')
                    rms_te[hcl - 1] = p.cnt['te']
                if hcl == 63:
                    p.wait('tensor', 've', ve_mark[63])
                    p.op('tensor', lambda e: e.matmul(
                        PSa[6][0:1, :], onesb_sb[:, 0:1], hsq_sb[63 % 2][:, :],
                        start=False, stop=True), inc='te')
            p.wait('vector', 'te')
            rr = row_sb[0:1, 0:512]
            p.op('vector', lambda e: e.tensor_scalar_mul(rr, PSa[6][0:1, :], 1.0 / FH), inc='ve')
            p.wait('scalar', 've')
            p.op('scalar', lambda e: e.activation(rr, rr, AF.Ln, bias=cst_sb[0:1, 2:3]))
            p.op('scalar', lambda e: e.activation(rr, rr, AF.Exp, scale=-0.5), inc='sc')
            p.wait('tensor', 'sc')
            p.op('tensor', lambda e: e.matmul(PSa[7][:, :], onesf_sb[0:1, 0:128], rr,
                                              start=True, stop=True), inc='te')
            p.wait('vector', 'te')
            p.op('vector', lambda e: e.tensor_copy(tmpb[:, :], PSa[7][:, :]), inc='ve')
            p.wait('sync', 'cc', 2)
            p.wait('gpsimd', 'cc', 2)

            def load_p(dq, hc, buf):
                q, sem = ('sync', 'dha') if hc % 2 == 0 else ('gpsimd', 'dga')
                p.dma(q, buf[:, :], wqp_all[hc, dq, :, :], sem)

            p.wait('sync', 'dha', p.cnt['dha'])
            load_p(0, 0, fpw[0])
            for dq in range(4):
                for hc in range(64):
                    b = hc % 2
                    sem_b = 'dha' if b == 0 else 'dga'
                    need = p.cnt[sem_b]
                    if dq * 64 + hc + 1 < 256:
                        ndq, nhc = (dq, hc + 1) if hc + 1 < 64 else (dq + 1, 0)
                        nq, nsem = ('sync', 'dha') if nhc % 2 == 0 else ('gpsimd', 'dga')
                        p.wait('sync', 'te', p.cnt['te'])
                        p.wait('gpsimd', 'te', p.cnt['te'])
                        p.wait(nq, nsem, p.cnt[nsem])
                        load_p(ndq, nhc, fpw[1 - b])
                    p.wait('tensor', sem_b, need)
                    if hc == 0:
                        p.wait('tensor', 've', p.cnt['ve'])
                    for cc in range(4):
                        p.op('tensor', lambda e, cc=cc, b=b, hc=hc: e.matmul(
                            PSa[cc][:, :], fpw[b][:, 128 * cc:128 * (cc + 1)],
                            hbuf[:, 512 * hc:512 * (hc + 1)],
                            start=(hc == 0), stop=(hc == 63)),
                            inc=('te' if cc == 3 else None))
                p.wait('vector', 'te')
                for cc in range(4):
                    dchunk = 4 * dq + cc
                    p.op('vector', lambda e, cc=cc: e.tensor_mul(tmpa[:, :], PSa[cc][:, :], tmpb[:, :]))
                    p.op('vector', lambda e, dchunk=dchunk, T=T: e.tensor_add(
                        R[:, TOK * dchunk + 512 * T: TOK * dchunk + 512 * (T + 1)],
                        R[:, TOK * dchunk + 512 * T: TOK * dchunk + 512 * (T + 1)],
                        tmpa[:, :]), inc='ve')

    def attn_b():
        RG = [list(range(NCORES))]
        p.wait('sync', 've', p.cnt['ve'])
        p.wait('sync', 'te', p.cnt['te'])
        p.dma('sync', trif_sb[:, :], trif[:, :], 'dhb')
        for c in range(NCH):
            p.dma('sync', xsha[128 * c:128 * (c + 1), :], xln[:, TOK * c:TOK * c + 512], 'dhb')
            p.dma('sync', xshb[128 * c:128 * (c + 1), :], xln[:, TOK * c + 512:TOK * (c + 1)], 'dhb')
        bounce_done = p.cnt['dhb']
        p.wait('gpsimd', 'dhb', bounce_done)
        p.op('gpsimd', lambda e: e.collective_compute(
            "AllGather", ALU.bypass, replica_groups=RG,
            ins=[xsha[:, :].opt()], outs=[xga[:, :].opt()]), inc='cc')
        cc_xa = p.cnt['cc']
        p.op('gpsimd', lambda e: e.collective_compute(
            "AllGather", ALU.bypass, replica_groups=RG,
            ins=[xshb[:, :].opt()], outs=[xgb[:, :].opt()]), inc='cc')
        cc_x = p.cnt['cc']
        p.wait('sync', 'cc', cc_xa)
        p.wait('gpsimd', 'cc', cc_xa)
        ve_proj0 = p.cnt['ve']

        def load_xa(g, hc, buf, q, sem):
            r, gg = g // 2, g % 2
            src = xga if gg == 0 else xgb
            p.dma(q, buf[:, :], src[2048 * r + 128 * hc:2048 * r + 128 * (hc + 1), :], sem)

        def load_wst(hc, buf, sem, q='gpsimd'):
            p.dma(q, buf[:, :], wqkvf_bf[128 * hc:128 * (hc + 1), :], sem)

        p.wait('vector', 'dhb', bounce_done)
        for pair in range(2):
            p.op('vector', lambda e, pair=pair: e.memset(
                Vsb[pair][:, 0:8320].rearrange('p (c w) -> p c w', w=130)[:, :, 64:65], 1.0))
            p.op('vector', lambda e, pair=pair: e.memset(
                Vsb[pair][:, 0:8320].rearrange('p (c w) -> p c w', w=130)[:, :, 129:130], 1.0), inc='ve')
        p.wait('sync', 'dhb', p.cnt['dhb'])
        load_xa(0, 0, xat[0], 'sync', 'dhb')
        p.wait('gpsimd', 'dhb', p.cnt['dhb'])
        p.wait('gpsimd', 'dga', p.cnt['dga'])
        load_wst(0, wst[0], 'dga')
        gorder = [g for g in range(16) if g % 2 == 0] + [g for g in range(16) if g % 2 == 1]
        for gi, g in enumerate(gorder):
            for hc in range(NCH):
                i = gi * NCH + hc
                b = i % 2
                xsem = 'dhb' if b == 0 else 'dha'
                wsem = 'dga' if b == 0 else 'dsa'
                nx = p.cnt[xsem]
                nw = p.cnt[wsem]
                if i + 1 < 16 * NCH:
                    ng, nhc = (g, hc + 1) if hc + 1 < NCH else (gorder[gi + 1], 0)
                    nxs = 'dha' if b == 0 else 'dhb'
                    nws = 'dsa' if b == 0 else 'dga'
                    nwq = 'scalar' if b == 0 else 'gpsimd'
                    p.wait('sync', 'te', p.cnt['te'])
                    p.wait('sync', nxs, p.cnt[nxs])
                    if ng % 2 == 1 and g % 2 == 0:
                        p.wait('sync', 'cc', cc_x)
                    load_xa(ng, nhc, xat[1 - b], 'sync', nxs)
                    p.wait(nwq, 'te', p.cnt['te'])
                    p.wait(nwq, nws, p.cnt[nws])
                    load_wst(nhc, wst[1 - b], nws, nwq)
                p.wait('tensor', xsem, nx)
                p.wait('tensor', wsem, nw)
                if hc == 0:
                    p.wait('tensor', 've', ve_proj0 if g == 0 else p.cnt['ve'])
                for si in range(4):
                    p.op('tensor', lambda e, si=si, b=b, hc=hc: e.matmul(
                        PSa[si][:, :], wst[b][:, 128 * si:128 * (si + 1)], xat[b][:, :],
                        start=(hc == 0), stop=(hc == NCH - 1)))
                for si in range(4):
                    p.op('tensor', lambda e, si=si, b=b, hc=hc: e.matmul(
                        PSa[4 + si][0:128, 0:260], xat[b][:, 128 * si:128 * (si + 1)],
                        wst[b][:, 512:772],
                        start=(hc == 0), stop=(hc == NCH - 1)),
                        inc=('te' if si == 3 else None))
            p.wait('vector', 'te')
            pr, gg = g // 2, g % 2
            for pair in range(2):
                p.op('vector', lambda e, pair=pair, g=g: e.tensor_scalar_mul(
                    Qsb[pair][:, 512 * g:512 * (g + 1)], PSa[pair][:, :], 0.125),
                    drain=(pair == 0))
                p.op('vector', lambda e, pair=pair, g=g: e.tensor_copy(
                    Ksb[pair][:, 512 * g:512 * (g + 1)], PSa[2 + pair][:, :]), drain=False)
            for si in range(4):
                ch = 4 * g + si
                for pair in range(2):
                    p.op('vector', lambda e, si=si, pair=pair, ch=ch: e.tensor_copy(
                        Vsb[pair][:, 130 * ch:130 * ch + 64],
                        PSa[4 + si][0:128, 128 * pair:128 * pair + 64]), drain=False)
                    p.op('vector', lambda e, si=si, pair=pair, ch=ch: e.tensor_copy(
                        Vsb[pair][:, 130 * ch + 65:130 * ch + 129],
                        PSa[4 + si][0:128, 128 * pair + 64:128 * (pair + 1)]), drain=False)
                p.op('vector', lambda e, si=si: e.tensor_copy(
                    ftmp[:, :], PSa[4 + si][0:128, 256:260]), inc='ve')
                pgrp = 32 * (g // 4)
                p.wait('tensor', 've')
                p.op('tensor', lambda e: e.transpose(
                    PSa[0][0:4, 0:128], ftmp[:, :], ident_sb[:, :]), inc='te')
                p.wait('vector', 'te')
                p.wait('vector', 'dq', p.cnt['dq'])
                p.op('vector', lambda e: e.tensor_copy(ft2[:, :], PSa[0][0:4, 0:128]), inc='ve')
                p.wait('gpsimd', 've')
                col0 = (512 * g + 128 * si) % 2048
                p.dma('gpsimd', Ptil[pgrp:pgrp + 4, col0:col0 + 128], ft2[:, :], 'dq')
        p.wait('vector', 'dgc', g_loaded)
        p.wait('vector', 'dq', p.cnt['dq'])
        p.wait('scalar', 'dq', p.cnt['dq'])
        p.wait('vector', 'te', p.cnt['te'])
        p.op('vector', lambda e: e.memzero(zz8[:, :]))
        p.op('vector', lambda e: e.memzero(initc[:, :]), inc='ve')
        for grp in range(4):
            rs = slice(32 * grp, 32 * grp + 4)
            p.op('vector', lambda e, rs=rs: e.tensor_scalar(
                pcum[rs, :], Ptil[rs, :], bfc_sb[rs, 0:1], 0.0, ALU.add, ALU.min), inc='ve')
            p.wait('scalar', 've')
            p.op('scalar', lambda e, rs=rs: e.activation(
                Ptil[rs, :], Ptil[rs, :], AF.Abs, bias=bfc_sb[rs, 0:1]))
            p.op('scalar', lambda e, rs=rs: e.activation(
                Ptil[rs, :], Ptil[rs, :], AF.Exp, scale=-1.0))
            p.op('scalar', lambda e, rs=rs: e.activation(
                Ptil[rs, :], Ptil[rs, :], AF.Ln, bias=cst_sb[rs, 1:2]), inc='sc')
            p.wait('vector', 'sc')
            p.op('vector', lambda e, rs=rs: e.tensor_sub(
                Ptil[rs, :], Ptil[rs, :], pcum[rs, :]), inc='ve')
        for grp in range(4):
            rs = slice(32 * grp, 32 * grp + 4)
            if grp > 0:
                p.wait('gpsimd', 've', p.cnt['ve'])
                p.dma('gpsimd', initc[32 * grp:32 * grp + 4, 0:1],
                      pcum[32 * (grp - 1):32 * (grp - 1) + 4, 2047:2048], 'dq')
                p.wait('vector', 'dq', p.cnt['dq'])
            p.op('vector', lambda e, rs=rs: e.tensor_tensor_scan(
                pcum[rs, :], Ptil[rs, :], zz8[rs, :], initc[rs, 0:1],
                ALU.add, ALU.add), inc='ve')
        p.wait('tensor', 've')
        for k in range(64):
            grp, col = k // 16, (k % 16) * 128
            p.op('tensor', lambda e, grp=grp, col=col: e.transpose(
                PSa[1][0:128, 0:4], pcum[32 * grp:32 * grp + 4, col:col + 128],
                ident_sb[32 * grp:32 * grp + 4, 32 * grp:32 * grp + 4],
                tile_position=(32 * grp, 0)), inc='te')
            p.wait('vector', 'te')
            p.op('vector', lambda e, k=k: e.tensor_copy(fct[:, 4 * k:4 * (k + 1)],
                                                        PSa[1][0:128, 0:4]), inc='ve')
            p.wait('tensor', 've')
        p.wait('scalar', 've', p.cnt['ve'])
        fq3_rdy = {}
        fq3_free = {0: 0, 1: 0}
        prep_ve_end = 0

        def emit_prep(J):
            nonlocal prep_ve_end
            pr, jq = divmod(J, 16)
            grp, col = jq // 4, (jq % 4) * 512
            pb = J % 2
            p.wait('gpsimd', 've', prep_ve_end)
            p.wait('gpsimd', 'te', fq3_free[pb])
            p.wait('gpsimd', 'dq', p.cnt['dq'])
            for h, hp in ((0, 96), (1, 32)):
                row = 32 * grp + 2 * pr + h
                p.dma('gpsimd', rcpb[hp:hp + 1, :], pcum[row:row + 1, col:col + 512], 'dq')
            vrow = p.cnt['dq']
            p.wait('vector', 'dq', vrow)
            dma0 = {}
            for nm, dr in (('n', True), ('hi', True), ('r', True),
                           ('mid', True), ('r2', True), ('lo', True)):
                for h, hp in ((0, 96), (1, 32)):
                    hs = slice(hp, hp + 1)
                    if nm == 'n':
                        p.op('vector', lambda e, hs=hs: e.tensor_single_scalar(
                            rcpbB[hs, :], rcpb[hs, :], -1.0, ALU.mult), drain=(h == 0))
                    elif nm == 'hi':
                        p.op('vector', lambda e, hs=hs: e.tensor_copy(
                            hiT[hs, :], rcpbB[hs, :]), inc='ve', drain=(h == 0))
                        if h == 1:
                            hi_ve = p.cnt['ve']
                    elif nm == 'r':
                        p.op('vector', lambda e, hs=hs: e.tensor_sub(
                            rcpb[hs, :], rcpbB[hs, :], hiT[hs, :]), drain=(h == 0))
                    elif nm == 'mid':
                        p.op('vector', lambda e, hs=hs: e.tensor_copy(
                            miT[hs, :], rcpb[hs, :]), inc='ve', drain=(h == 0))
                        if h == 1:
                            mid_ve = p.cnt['ve']
                    elif nm == 'r2':
                        p.op('vector', lambda e, hs=hs: e.tensor_sub(
                            rcpbB[hs, :], rcpb[hs, :], miT[hs, :]), drain=(h == 0))
                    else:
                        if h == 0:
                            p.wait('vector', 'dq', dma0['hi'])
                        p.op('vector', lambda e, hs=hs: e.tensor_copy(
                            hiT[hs, :], rcpbB[hs, :]), inc='ve', drain=(h == 0))
                if nm == 'hi' or nm == 'mid':
                    mark = hi_ve if nm == 'hi' else mid_ve
                    ri = 0 if nm == 'hi' else 1
                    p.wait('gpsimd', 've', mark)
                    p.wait('gpsimd', 'dq', p.cnt['dq'])
                    for h, hp in ((0, 96), (1, 32)):
                        src = hiT if nm == 'hi' else miT
                        p.dma('gpsimd', fq3w[pb][ri:ri + 1, 512 * h:512 * (h + 1)],
                              src[hp:hp + 1, :], 'dq')
                    dma0[nm] = p.cnt['dq']
            lo_ve = p.cnt['ve']
            prep_ve_end = lo_ve
            p.wait('gpsimd', 've', lo_ve)
            p.wait('gpsimd', 'dq', p.cnt['dq'])
            for h, hp in ((0, 96), (1, 32)):
                p.dma('gpsimd', fq3w[pb][2:3, 512 * h:512 * (h + 1)], hiT[hp:hp + 1, :], 'dq')
            fq3_rdy[J] = p.cnt['dq']

        emit_prep(0)
        for pair in range(2):
            for j in range(16):
                J = 16 * pair + j
                ve_qb = p.cnt['ve']
                if J + 1 < 32:
                    emit_prep(J + 1)
                pb = J % 2
                nk = 4 * (j + 1)
                sc_mark_f = {}
                p.wait('tensor', 've', ve_qb)

                def emit_lpv(k, nk):
                    kpar = k % 3
                    mc = 128 * (k - 4 * j) if k >= 4 * j else 0
                    p.op('tensor', lambda e, pair=pair, kpar=kpar, k=k, mc=mc, nk=nk: e.matmul(
                        PSa[6][0:65, mc:512], Vsb[pair][:, 130 * k:130 * k + 65],
                        ptb[kpar][:, mc:512], start=(k == 0), stop=(k == nk - 1)))
                    p.op('tensor', lambda e, pair=pair, kpar=kpar, k=k, mc=mc, nk=nk: e.matmul(
                        PSa[7][0:65, mc:512], Vsb[pair][:, 130 * k + 65:130 * (k + 1)],
                        ptb[3 + kpar][:, mc:512],
                        start=(k == 0), stop=(k == nk - 1)), inc='te')

                for k in range(nk + 2):
                    if k < nk:
                        kpar = k % 3
                        STW = PBa[kpar]
                        STA, STB = PSa[kpar * 2], PSa[kpar * 2 + 1]
                        mc = 128 * (k - 4 * j) if k >= 4 * j else 0
                        p.op('tensor', lambda e, STA=STA, pair=pair, k=k, j=j, mc=mc: e.matmul(
                            STA[:, mc:512], Ksb[pair][0:64, 128 * k:128 * (k + 1)],
                            Qsb[pair][0:64, 512 * j + mc:512 * (j + 1)],
                            start=True, stop=False))
                        p.op('tensor', lambda e, STB=STB, pair=pair, k=k, j=j, mc=mc: e.matmul(
                            STB[:, mc:512], Ksb[pair][64:128, 128 * k:128 * (k + 1)],
                            Qsb[pair][64:128, 512 * j + mc:512 * (j + 1)],
                            tile_position=(64, 0), start=True, stop=False))
                        if k == 0:
                            p.wait('tensor', 'dq', fq3_rdy[J])
                        p.op('tensor', lambda e, STA=STA, pb=pb, mc=mc: e.matmul(
                            STA[:, mc:512], onesb_sb[0:3, 0:128],
                            fq3w[pb][0:3, mc:512], start=False, stop=True))
                        p.op('tensor', lambda e, STB=STB, pb=pb, mc=mc: e.matmul(
                            STB[:, mc:512], onesb_sb[0:3, 0:128],
                            fq3w[pb][0:3, 512 + mc:1024], start=False, stop=True), inc='te')
                        te_k = p.cnt['te']
                    if k >= 2:
                        p.wait('tensor', 'sc', sc_mark_f[k - 2])
                        if k - 2 == 0:
                            p.wait('tensor', 've', ve_qb)
                        emit_lpv(k - 2, nk)
                    if k >= nk:
                        continue
                    if k >= 4 * j:
                        p.wait('vector', 'te', te_k)
                        p.op('vector', lambda e, STA=STA, mc=mc: e.tensor_add(
                            STA[:, mc:mc + 128], STA[:, mc:mc + 128], trif_sb[:, :]))
                        p.op('vector', lambda e, STB=STB, mc=mc: e.tensor_add(
                            STB[:, mc:mc + 128], STB[:, mc:mc + 128], trif_sb[:, :]),
                            inc='ve', drain=False)
                        p.wait('scalar', 've', p.cnt['ve'])
                    else:
                        p.wait('scalar', 'te', te_k)
                    p.op('scalar', lambda e, STA=STA, kpar=kpar, k=k, pair=pair, mc=mc: e.activation(
                        ptb[kpar][:, mc:512], STA[:, mc:512], AF.Exp,
                        bias=fct[:, 4 * k + 2 * pair:4 * k + 2 * pair + 1]))
                    p.op('scalar', lambda e, STB=STB, kpar=kpar, k=k, pair=pair, mc=mc: e.activation(
                        ptb[3 + kpar][:, mc:512], STB[:, mc:512], AF.Exp,
                        bias=fct[:, 4 * k + 2 * pair + 1:4 * k + 2 * pair + 2]), inc='sc', drain=False)
                    sc_mark_f[k] = p.cnt['sc']
                fq3_free[pb] = p.cnt['te']
                p.wait('vector', 'te')
                p.wait('vector', 'dq', p.cnt['dq'])
                p.wait('scalar', 'te', p.cnt['te'])
                p.op('scalar', lambda e: e.activation(dvb[64:65, :], PSa[6][64:65, :], AF.Ln))
                p.op('scalar', lambda e: e.activation(rcpb[64:65, :], dvb[64:65, :], AF.Exp,
                                                      scale=-1.0))
                p.op('scalar', lambda e: e.activation(dvb2[64:65, :], PSa[7][64:65, :], AF.Ln, drain=False)
                     if False else e.activation(dvb2[64:65, :], PSa[7][64:65, :], AF.Ln))
                p.op('scalar', lambda e: e.activation(rcpbB[64:65, :], dvb2[64:65, :], AF.Exp,
                                                      scale=-1.0), inc='sc')
                p.wait('vector', 'sc')
                p.op('vector', lambda e: e.nop(), inc='ve')
                p.wait('gpsimd', 've')
                p.dma('gpsimd', rcpb[0:1, :], rcpb[64:65, :], 'dq')
                p.dma('gpsimd', rcpbB[0:1, :], rcpbB[64:65, :], 'dq')
                p.wait('tensor', 'dq', p.cnt['dq'])
                p.wait('tensor', 've', p.cnt['ve'])
                p.wait('tensor', 'sc', p.cnt['sc'])
                p.op('tensor', lambda e: e.matmul(PSa[0][0:64, :], onesf_sb[0:1, 0:64],
                                                  rcpb[0:1, :], start=True, stop=True))
                p.op('tensor', lambda e: e.matmul(PSa[1][0:64, :], onesf_sb[0:1, 0:64],
                                                  rcpbB[0:1, :], start=True, stop=True), inc='te')
                p.wait('vector', 'te')
                p.op('vector', lambda e: e.tensor_copy(dvb[0:64, :], PSa[0][0:64, :]))
                p.op('vector', lambda e: e.tensor_copy(dvb2[0:64, :], PSa[1][0:64, :]))
                p.op('vector', lambda e: e.tensor_mul(onrm_sb[0:64, :], PSa[6][0:64, :], dvb[0:64, :]))
                p.op('vector', lambda e: e.tensor_mul(onrm2_sb[0:64, :], PSa[7][0:64, :],
                                                      dvb2[0:64, :]), inc='ve')
                p.wait('gpsimd', 've')
                p.dma('gpsimd', obnc[128 * pair:128 * pair + 64, 512 * j:512 * (j + 1)],
                      onrm_sb[0:64, :], 'dq')
                p.dma('gpsimd', obnc[128 * pair + 64:128 * (pair + 1), 512 * j:512 * (j + 1)],
                      onrm2_sb[0:64, :], 'dq')
        p.wait('gpsimd', 'dq', p.cnt['dq'])
        p.op('gpsimd', lambda e: e.collective_compute(
            "AllGather", ALU.bypass, replica_groups=RG,
            ins=[obnc[:, :].opt()], outs=[ogath[:, :].opt()]), inc='cc')
        cc_o = p.cnt['cc']
        p.wait('sync', 'cc', cc_o)
        p.wait('sync', 'te', p.cnt['te'])
        p.wait('sync', 'dhb', p.cnt['dhb'])
        cell = {}
        p.emit('sync', lambda e: cell.__setitem__('off', e.partition_id()))
        for c in range(NCH):
            def oload(e, c=c):
                src = ogath[128 * c:128 * (c + 1), bass.ts(cell['off'], TOK)]
                return e.dma_start(out=oown[:, TOK * c:TOK * (c + 1)], in_=src)
            p.op('sync', oload, inc='dhb', n=16)
        oo_done = p.cnt['dhb']

        def load_wo(dst, h, sem):
            for c in range(NCH):
                p.dma('gpsimd', dst[:, 128 * c:128 * (c + 1)],
                      WobT[128 * c:128 * (c + 1), 128 * h:128 * (h + 1)], sem)

        p.wait('gpsimd', 'te', p.cnt['te'])
        p.wait('gpsimd', 'dga', p.cnt['dga'])
        load_wo(fgw[0], 0, 'dga')
        prev_te = p.cnt['te']
        for oc in range(NCH):
            b = oc % 2
            sem_b = 'dga' if b == 0 else 'dgb'
            need = p.cnt[sem_b]
            if oc + 1 < NCH:
                nsem = 'dgb' if b == 0 else 'dga'
                p.wait('gpsimd', 'te', prev_te)
                p.wait('gpsimd', nsem, p.cnt[nsem])
                load_wo(fgw[1 - b], oc + 1, nsem)
            p.wait('tensor', sem_b, need)
            p.wait('tensor', 'dhb', oo_done)
            for g in range(NTG):
                p.wait('tensor', 've', p.cnt['ve'])
                for c in range(NCH):
                    p.op('tensor', lambda e, g=g, c=c, b=b: e.matmul(
                        PSa[g][:, :], fgw[b][:, 128 * c:128 * (c + 1)],
                        oown[:, TOK * c + 512 * g: TOK * c + 512 * (g + 1)],
                        start=(c == 0), stop=(c == NCH - 1)),
                        inc=('te' if c == NCH - 1 else None))
                p.wait('vector', 'te')
                p.op('vector', lambda e, oc=oc, g=g: e.tensor_add(
                    R[:, TOK * oc + 512 * g: TOK * oc + 512 * (g + 1)],
                    R[:, TOK * oc + 512 * g: TOK * oc + 512 * (g + 1)], PSa[g][:, :]), inc='ve')
            prev_te = p.cnt['te']

    for c in range(NCH):
        b = c % 2
        gsem = 'dga' if b == 0 else 'dgb'
        p.wait('gpsimd', gsem, p.cnt[gsem])
        p.wait('gpsimd', 'dhb', p.cnt['dhb'])
        p.dma('gpsimd', qwb[b][:, :], WqaT[128 * c:128 * (c + 1), :], gsem)
        p.wait('sync', gsem, p.cnt[gsem])
        p.dma('sync', wqa_bf[:, 128 * c:128 * (c + 1), :].transpose([1, 0, 2]),
              qwb[b].rearrange('p (h w) -> p h w', w=128)[:, :, :], 'dhb')
    for c in range(NCH):
        b = c % 2
        gsem = 'dga' if b == 0 else 'dgb'
        p.wait('gpsimd', gsem, p.cnt[gsem])
        p.wait('gpsimd', 'dhb', p.cnt['dhb'])
        p.dma('gpsimd', qwb[b][:, :], WoaT[128 * c:128 * (c + 1), :], gsem)
        p.wait('sync', gsem, p.cnt[gsem])
        p.dma('sync', woa_bf[:, 128 * c:128 * (c + 1), :].transpose([1, 0, 2]),
              qwb[b].rearrange('p (h w) -> p h w', w=128)[:, :, :], 'dhb')
    wqa_done = p.cnt['dhb']
    qsteps = quant_steps() if 'q' in phases else []
    bank_weights_prep()
    layernorm()
    tap_xln = mktap('xln', [128, NCH * TOK], BF16)
    if tap_xln is not None:
        p.wait('gpsimd', 've')
        p.dma('gpsimd', tap_xln[:, :], xln[:, :], 'st')
    if 'a' in phases:
        attn_a(qsteps)
    else:
        for st in qsteps:
            st()
    if 'f' in phases:
        layernorm()
        ffn()
    if 'b' in phases:
        layernorm()
        attn_b()
    if 'g' in phases:
        layernorm()
        ffn()

    p.wait('sync', 've')
    p.wait('sync', 'te')
    for c in range(NCH):
        p.dma('sync', y[128 * c:128 * (c + 1), :], R[:, TOK * c:TOK * (c + 1)], 'st')
    p.wait('gpsimd', 'st')
    p.build()
    return nc


def kernel(**inputs):
    nc = build_nc(phases=('q', 'a', 'f', 'b', 'g'))
    in_maps = [host_inputs(inputs, c) for c in range(NCORES)]
    res = run_bass_kernel_spmd(nc, in_maps, core_ids=list(range(NCORES)))
    out = np.concatenate([r['y'].T for r in res.results], axis=0)
    return out.reshape(1, S, D).astype(np.float32)


# version 16
# speedup vs baseline: 1.1929x; 1.0455x over previous
import sys, os
if '/opt/trn_rl_repo' not in sys.path:
    sys.path.insert(0, '/opt/trn_rl_repo')
import numpy as np
import ml_dtypes
import concourse.bass as bass
import concourse.mybir as mybir
from concourse.bass_utils import run_bass_kernel_spmd
from concourse import dve_ops

F32 = mybir.dt.float32
BF16 = mybir.dt.bfloat16
AF = mybir.ActivationFunctionType
ALU = mybir.AluOpType
BF = ml_dtypes.bfloat16

NCORES = 8
S = 8192
D = 2048
TOK = S // NCORES
NCH = D // 128
NTG = TOK // 512
HA = 16
NBK = 4
HB = 32
HBL = HB // NCORES
HD = 64
FH = 8192
FH2 = 16384
LNEPS = 1e-5
RMSEPS = 1.1920929e-7
NEGBIG = -1.0e9


def _c(a):
    return np.ascontiguousarray(a)


def host_inputs(inp, core):
    d = {}
    x = np.asarray(inp['L0_attn_a_hidden_states'])
    sl = slice(TOK * core, TOK * (core + 1))
    d['xT'] = _c(x[0, sl, :].T)

    d['WqaT'] = _c(np.asarray(inp['L0_attn_a_Wq']).T)
    d['WoaT'] = _c(np.asarray(inp['L0_attn_a_Wo']).T)
    mem = np.asarray(inp['L0_attn_a_memories'])
    d['memH'] = _c(mem.transpose(1, 0, 2, 3))
    mn = np.asarray(inp['L0_attn_a_memory_norms'])
    d['mnT'] = _c(mn.transpose(1, 2, 0).reshape(128 * HA // 128, 128, NBK)
                  .reshape(HA, 128, NBK))
    d['mnFlat'] = _c(mn.reshape(NBK, HA * 128))
    bw = np.asarray(inp['L0_attn_a_bank_weights'])
    d['bwT'] = _c(bw.T)

    hsl = slice(HBL * HD * core, HBL * HD * (core + 1))
    d['Wqkvf'] = _c(np.concatenate([
        np.asarray(inp['L0_attn_b_Wq'])[hsl, :].T,
        np.asarray(inp['L0_attn_b_Wk'])[hsl, :].T,
        np.asarray(inp['L0_attn_b_Wv'])[hsl, :].T,
        np.asarray(inp['L0_attn_b_Wf'])[HBL * core: HBL * (core + 1), :].T,
    ], axis=1))
    bfv = np.asarray(inp['L0_attn_b_bf'])[HBL * core: HBL * (core + 1)]
    bff = np.zeros((128, 1), dtype=np.float32)
    for grp in range(4):
        bff[32 * grp:32 * grp + 4, 0] = bfv
    d['bf_full'] = bff
    d['WobT'] = _c(np.asarray(inp['L0_attn_b_Wo']).T)

    wguT = np.asarray(inp['L0_ffn_w_gate_up']).T
    d['wguS'] = _c(wguT[256 * core: 256 * (core + 1), :])
    wpT = np.asarray(inp['L0_ffn_w_proj']).T
    d['wpS'] = _c(wpT[1024 * core: 1024 * (core + 1), :])

    d['ident_f32'] = np.eye(128, dtype=np.float32)
    d['ones_f32'] = np.ones((128, 512), dtype=np.float32)
    d['ones_bf'] = np.ones((128, 512), dtype=BF)
    tri = np.tril(np.ones((128, 128), dtype=np.float32), -1)
    d['Utri_bf'] = tri.astype(BF)
    d['negI_bf'] = (NEGBIG * np.eye(128, dtype=np.float32)).astype(BF)
    d['trif'] = NEGBIG * tri
    sel = np.zeros((4, 512), dtype=np.float32)
    for n in range(4):
        sel[n, 128 * n:128 * (n + 1)] = 1.0
    d['selG'] = sel.astype(BF)
    ind2 = np.zeros((128, 2), dtype=np.float32)
    ind2[:64, 0] = 1.0
    ind2[64:, 1] = 1.0
    d['ind2'] = ind2.astype(BF)
    e2 = np.zeros((2, 128), dtype=np.float32)
    e2[0, :64] = 1.0
    e2[1, 64:] = 1.0
    d['e2_bf'] = e2.astype(BF)
    d['e2_f32'] = e2
    cst = np.zeros((128, 8), dtype=np.float32)
    cst[:, 0] = LNEPS
    cst[:, 1] = 1.0
    cst[:, 2] = RMSEPS
    d['cst'] = cst
    return d


class P:
    def __init__(self, nc):
        self.nc = nc
        self.streams = {k: [] for k in ('tensor', 'vector', 'scalar', 'sync', 'gpsimd')}
        self.sem = {}
        self.cnt = {}
        from contextlib import ExitStack
        self.stack = ExitStack()

    def mksem(self, *names):
        for n in names:
            self.sem[n] = self.stack.enter_context(self.nc.semaphore(n))
            self.cnt[n] = 0

    def emit(self, eng, f):
        self.streams[eng].append(f)

    def wait(self, eng, sname, val=None):
        v = self.cnt[sname] if val is None else val
        s = self.sem[sname]
        if v > 0:
            self.emit(eng, lambda e, s=s, v=v: e.wait_ge(s, v))
        return v

    def op(self, eng, f, inc=None, n=1, drain=True):
        if drain and eng in ('vector', 'scalar'):
            self.emit(eng, lambda e: e.drain())
        if inc is not None:
            self.cnt[inc] += n
            s = self.sem[inc]
            self.emit(eng, lambda e, f=f, s=s, n=n: f(e).then_inc(s, n))
        else:
            self.emit(eng, lambda e, f=f: f(e))

    def dma(self, q, out, in_, sname):
        self.op(q, lambda e, o=out, i=in_: e.dma_start(out=o, in_=i), inc=sname, n=16,
                drain=False)
        return self.cnt[sname]

    def build(self):
        nc = self.nc
        with nc.Block() as block:
            @block.tensor
            def _(e):
                for f in self.streams['tensor']:
                    f(e)

            @block.vector
            def _(e):
                for f in self.streams['vector']:
                    f(e)

            @block.scalar
            def _(e):
                for f in self.streams['scalar']:
                    f(e)

            @block.sync
            def _(e):
                for f in self.streams['sync']:
                    f(e)

            @block.gpsimd
            def _(e):
                for f in self.streams['gpsimd']:
                    f(e)
        self.stack.close()


def build_nc(phases=('q', 'a', 'f'), debug_taps=()):
    nc = bass.Bass("TRN2", target_bir_lowering=False, debug=False)
    p = P(nc)
    p.mksem('dh', 'dha', 'dhb', 'dga', 'dgb', 'dgc', 'dsa', 'dsb', 'dso', 'te', 've', 'sc', 'cc', 'st', 'dq')

    io = {}
    def din(name, shape, dtype=F32):
        io[name] = nc.dram_tensor(name, list(shape), dtype, kind="ExternalInput").ap()
        return io[name]

    xT = din('xT', [D, TOK])
    WqaT = din('WqaT', [D, D]); WoaT = din('WoaT', [D, D])
    memH = din('memH', [HA, NBK, 128, 128]); mnT = din('mnT', [HA, 128, NBK])
    mnFlat = din('mnFlat', [NBK, HA * 128]); bwT = din('bwT', [NBK, HA])
    Wqkvf = din('Wqkvf', [D, 772]); bf_full = din('bf_full', [128, 1])
    WobT = din('WobT', [D, D])
    wguS = din('wguS', [256, FH2]); wpS = din('wpS', [1024, D])
    ident_f32 = din('ident_f32', [128, 128])
    ones_f32 = din('ones_f32', [128, 512]); ones_bf = din('ones_bf', [128, 512], BF16)
    Utri_bf = din('Utri_bf', [128, 128], BF16); negI_bf = din('negI_bf', [128, 128], BF16)
    trif = din('trif', [128, 128])
    selG = din('selG', [4, 512], BF16)
    ind2 = din('ind2', [128, 2], BF16); e2_bf = din('e2_bf', [2, 128], BF16)
    e2_f32 = din('e2_f32', [2, 128]); cst = din('cst', [128, 8])

    y = nc.dram_tensor('y', [D, TOK], F32, kind="ExternalOutput").ap()
    wqgu_sh = nc.dram_tensor('wqgu_sh', [128, 128, 2, 128], BF16).ap()
    wqgu_all = nc.dram_tensor('wqgu_all', [8 * 128, 128, 2, 128], BF16, addr_space='Shared').ap()
    wqp_sh = nc.dram_tensor('wqp_sh', [8, 4, 128, 512], BF16).ap()
    wqp_all = nc.dram_tensor('wqp_all', [64, 4, 128, 512], BF16, addr_space='Shared').ap()
    xsha = nc.dram_tensor('xsha', [D, TOK // 2], BF16).ap()
    xshb = nc.dram_tensor('xshb', [D, TOK // 2], BF16).ap()
    xga = nc.dram_tensor('xga', [8 * D, TOK // 2], BF16, addr_space='Shared').ap()
    xgb = nc.dram_tensor('xgb', [8 * D, TOK // 2], BF16, addr_space='Shared').ap()
    obnc = nc.dram_tensor('obnc', [256, S], BF16).ap()
    ogath = nc.dram_tensor('ogath', [8 * 256, S], BF16, addr_space='Shared').ap()
    wqkvf_bf = nc.dram_tensor('wqkvf_bf', [D, 772], BF16).ap()
    wqa_bf = nc.dram_tensor('wqa_bf', [HA, D, 128], BF16).ap()
    woa_bf = nc.dram_tensor('woa_bf', [NCH, D, 128], BF16).ap()
    taps = {}
    for t in debug_taps:
        pass

    def mktap(name, shape, dtype=F32):
        if name in debug_taps:
            taps[name] = nc.dram_tensor('tap_' + name, list(shape), dtype,
                                        kind="ExternalOutput").ap()
            return taps[name]
        return None

    SB = {}
    SBASE = 16512
    def sb(name, shape, dtype, off):
        h = nc.alloc_sbuf_tensor_at(name, list(shape), dtype, offset=SBASE + off)
        SB[name] = h
        return h.ap()

    R = sb('R', [128, NCH * TOK], F32, 0)
    xln = sb('xln', [128, NCH * TOK], BF16, 65536)
    comb = sb('comb', [128, HA * TOK], BF16, 98304)
    hbuf = sb('hbuf', [128, (FH // 128) * 512], BF16, 98304)
    Ksb = [sb(f'Ksb{i}', [128, S], BF16, 98816 + 16384 * i) for i in range(2)]
    Qsb = [sb(f'Qsb{i}', [128, S], BF16, 131584 + 16384 * i) for i in range(2)]
    Vsb = [sb(f'Vsb{i}', [128, (S // 128) * 130], BF16, 65536 + 16640 * i) for i in range(2)]
    W0 = 163840
    wab = [sb(f'wab{i}', [128, NCH * 128], BF16, W0 + 4096 * i) for i in range(2)]
    mem_sb = [sb(f'mem_sb{i}', [128, NBK * 128], BF16, W0 + 8192 + 1024 * i) for i in range(2)]
    mnT_sb = sb('mnT_sb', [128, HA * NBK], BF16, W0 + 10240)
    smallc = W0 + 10496
    bw_sb = sb('bw_sb', [4, HA], F32, smallc)
    wact4 = sb('wact4', [4, HA], F32, smallc + 128)
    act_sb = sb('act_sb', [4, 1], F32, smallc + 320)
    g_sb = sb('g_sb', [4, 512], BF16, smallc + 384)
    ident_sb = sb('ident_sb', [128, 128], F32, smallc + 1408)
    onesf_sb = sb('onesf_sb', [128, 512], F32, smallc + 1920)
    onesb_sb = sb('onesb_sb', [128, 512], BF16, smallc + 3968)
    selG_sb = sb('selG_sb', [4, 512], BF16, smallc + 4992)
    utri_sb = sb('utri_sb', [128, 128], BF16, smallc + 6144)
    negI_sb = sb('negI_sb', [128, 128], BF16, smallc + 6400)
    trif_sb = sb('trif_sb', [128, 128], F32, 172288)
    cst_sb = sb('cst_sb', [128, 8], F32, smallc + 6016)
    lnc = smallc + 6656
    mnf_sb = sb('mnf_sb', [4, HA * 128], F32, lnc)
    xb_sb = [sb(f'xb_sb{i}', [128, 512], BF16, lnc + 1024 * i) for i in range(2)]
    xsq_sb = [sb(f'xsq_sb{i}', [128, 512], BF16, lnc + 2048 + 1024 * i) for i in range(2)]
    sigp = [sb(f'sigp{i}', [128, 512], BF16, lnc + 10240 + 1024 * i) for i in range(2)]
    row_sb = sb('row_sb', [1, 512 * 5], F32, lnc + 4096)
    tmpa = sb('tmpa', [128, 512], F32, lnc + 14336)
    tmpb = sb('tmpb', [128, 512], F32, lnc + 16384)
    qwb = [sb(f'qwb{i}', [128, 2048], BF16, 131840 + 4096 * i) for i in range(2)]
    qt1 = sb('qt1', [128, 512], BF16, 131840 + 8192)
    qt2 = [sb('qt2a', [128, 512], BF16, 131840 + 9216), sb('qt2b', [128, 512], BF16, 131840 + 22016)]
    qt3 = sb('qt3', [128, 512], BF16, 131840 + 10240)
    qsd = sb('qsd', [2, 512], BF16, 131840 + 11264)
    qrcp = sb('qrcp', [2, 512], F32, 131840 + 12288)
    qsdf = sb('qsdf', [2, 512], F32, 131840 + 15872)
    qscr = sb('qscr', [2, 512], F32, 131840 + 17920)
    ascr = sb('ascr', [128, 512], F32, 131840 + 19968)
    ind2_sb = sb('ind2_sb', [128, 2], BF16, 131840 + 14336)
    e2b_sb = sb('e2b_sb', [2, 128], BF16, 131840 + 14848)
    e2f_sb = sb('e2f_sb', [2, 128], F32, 131840 + 15360)
    fgw = [sb('fgw0', [128, 2048], BF16, 0 + 163840), sb('fgw1', [128, 2048], BF16, 4096 + 163840),
           sb('fgw2', [128, 2048], BF16, 203520), sb('fgw3', [128, 2048], BF16, 207616)]
    fpw = [sb('fpw0', [128, 512], BF16, 172032), sb('fpw1', [128, 512], BF16, 173056)]
    sil_sb = [sb('sil_sb0', [128, 512], BF16, lnc + 18432), sb('sil_sb1', [128, 512], BF16, lnc + 19456)]
    hsq_sb = [sb('hsq_sb0', [128, 512], BF16, lnc + 20480), sb('hsq_sb1', [128, 512], BF16, lnc + 21504)]
    xat = [sb(f'xat{i}', [128, 512], BF16, lnc + 1024 * i) for i in range(4)]
    Ptil = sb('Ptil', [128, 2048], F32, lnc + 14336)
    zz8 = sb('zz8', [128, 2048], F32, lnc + 4096)
    pcum = sb('pcum', [128, 2048], F32, 203520)
    wst = [sb(f'wst{i}', [128, 772], BF16, lnc + 4096 + 1664 * i) for i in range(2)]
    ptb = [sb(f'ptb{i}', [128, 512], BF16, lnc + 10240 + 1024 * i) for i in range(6)]
    rank1A = sb('rank1A', [128, 512], F32, 167936)
    rank1B = sb('rank1B', [128, 512], F32, 170240)
    fq3w = [sb(f'fq3w{b}', [3, 1024], BF16, 164352 + 2048 * b) for b in range(2)]
    fqbw = sb('fqbw', [128, 1024], F32, lnc + 16384)
    hiT = sb('hiT', [97, 512], BF16, 168448)
    miT = sb('miT', [97, 512], BF16, 169472)
    rcpb = sb('rcpb', [128, 512], F32, lnc)
    rcpbB = sb('rcpbB', [128, 512], F32, lnc + 2048)
    dvb = sb('dvb', [128, 512], F32, lnc + 4096)
    dvb2 = sb('dvb2', [128, 512], F32, lnc + 6144)
    onrm_sb = sb('onrm_sb', [128, 512], BF16, lnc + 8192)
    onrm2_sb = sb('onrm2_sb', [128, 512], BF16, lnc + 9216)
    fct = sb('fct', [128, 256], F32, 211712)
    bfc_sb = sb('bfc_sb', [128, 1], F32, 212736)
    initc = sb('initc', [128, 1], F32, 212768)
    ftmp = sb('ftmp', [128, 4], F32, 174080)
    ft2 = sb('ft2', [4, 128], F32, lnc + 12288)
    oown = sb('oown', [128, NCH * 1024], BF16, 65536)
    assert 16512 + 212800 <= 229344

    PB = [nc.alloc_psum_tensor(f'PB{i}', [128, 1024], F32) for i in range(3)]
    PS = [nc.alloc_psum_tensor(f'PS{i}', [128, 512], F32) for i in range(2)]
    PSa = [PB[0].ap()[:, 0:512], PB[0].ap()[:, 512:1024],
           PB[1].ap()[:, 0:512], PB[1].ap()[:, 512:1024],
           PB[2].ap()[:, 0:512], PB[2].ap()[:, 512:1024]] + [t.ap() for t in PS]
    PBa = [t.ap() for t in PB]

    for dst, src in ((ident_sb, ident_f32), (onesf_sb, ones_f32), (onesb_sb, ones_bf),
                     (selG_sb, selG), (bw_sb, bwT), (mnf_sb, mnFlat), (cst_sb, cst),
                     (utri_sb, Utri_bf), (negI_sb, negI_bf)):
        p.dma('sync', dst[:, :], src[:, :], 'dh')
    for h in range(HA):
        p.dma('gpsimd', mnT_sb[:, NBK * h:NBK * (h + 1)], mnT[h, :, :], 'dgc')
    p.dma('gpsimd', bfc_sb[:, :], bf_full[:, :], 'dgc')
    p.dma('gpsimd', ind2_sb[:, :], ind2[:, :], 'dgc')
    p.dma('gpsimd', e2b_sb[:, :], e2_bf[:, :], 'dgc')
    p.dma('gpsimd', e2f_sb[:, :], e2_f32[:, :], 'dgc')
    for c in range(NCH):
        p.dma('sync', R[:, TOK * c:TOK * (c + 1)], xT[128 * c:128 * (c + 1), :], 'dh')
    c_loaded = p.cnt['dh']
    g_loaded = p.cnt['dgc']

    def rcp_fast(out, in_, scratch, inc=None):
        p.op('vector', lambda e: e.reciprocal_approx_fast(out=scratch, in_=in_))
        p.op('vector', lambda e: e._custom_dve(
            dve_ops.RECIPROCAL_APPROX_NR, out=out, in0=in_, in1=scratch, s0=2.0), inc=inc)

    def bank_weights_prep():
        p.wait('vector', 'dh', c_loaded)
        p.wait('vector', 'dgc', g_loaded)
        p.op('vector', lambda e: e.reduce_sum(tmpb[0:4, 0:1], mnf_sb[:, :], mybir.AxisListType.X))
        p.op('vector', lambda e: e.tensor_single_scalar(act_sb[:, :], tmpb[0:4, 0:1], 1e-6, ALU.is_ge), inc='ve')
        p.wait('scalar', 'dh', c_loaded)
        p.op('scalar', lambda e: e.activation(tmpa[0:4, 0:HA], bw_sb[:, :], AF.Exp), inc='sc')
        p.wait('tensor', 'sc')
        p.op('tensor', lambda e: e.matmul(PSa[6][0:1, 0:HA], onesf_sb[0:4, 0:1], tmpa[0:4, 0:HA],
                                          start=True, stop=True), inc='te')
        p.wait('vector', 'te')
        p.op('vector', lambda e: e.reciprocal(row_sb[0:1, 0:HA], PSa[6][0:1, 0:HA]), inc='ve')
        p.wait('tensor', 've')
        p.op('tensor', lambda e: e.matmul(PSa[7][0:4, 0:HA], onesf_sb[0:1, 0:4], row_sb[0:1, 0:HA],
                                          start=True, stop=True), inc='te')
        p.wait('vector', 'te')
        p.op('vector', lambda e: e.scalar_tensor_tensor(wact4[:, :], tmpa[0:4, 0:HA], act_sb[:, :],
                                                        PSa[7][0:4, 0:HA], ALU.mult, ALU.mult), inc='ve')

    def layernorm():
        te0 = p.cnt['te']
        p.wait('vector', 'dhb', p.cnt['dhb'])
        for g in range(NTG):
            gv0 = p.cnt['ve']
            for c in range(NCH):
                b = c % 2
                p.wait('vector', 'te', p.cnt['te'])
                p.op('vector', lambda e, c=c, g=g, b=b: e.tensor_copy(
                    xb_sb[b][:, :], R[:, TOK * c + 512 * g: TOK * c + 512 * (g + 1)]), inc='ve')
                p.wait('scalar', 've')
                p.op('scalar', lambda e, b=b: e.activation(xsq_sb[b][:, :], xb_sb[b][:, :], AF.Square), inc='sc')
                p.wait('tensor', 've')
                p.wait('tensor', 'sc')
                p.op('tensor', lambda e, b=b, c=c: e.matmul(
                    PSa[6][0:1, :], onesb_sb[:, 0:1], xb_sb[b][:, :], start=(c == 0), stop=(c == NCH - 1)))
                p.op('tensor', lambda e, b=b, c=c: e.matmul(
                    PSa[7][0:1, :], onesb_sb[:, 0:1], xsq_sb[b][:, :], start=(c == 0), stop=(c == NCH - 1)),
                    inc='te')
            p.wait('vector', 'te')
            m = row_sb[0:1, 0:512]; rstd = row_sb[0:1, 512:1024]; mr = row_sb[0:1, 1024:1536]
            var = row_sb[0:1, 1536:2048]; m2 = row_sb[0:1, 2048:2560]
            p.op('vector', lambda e: e.tensor_scalar_mul(m, PSa[6][0:1, :], 1.0 / D))
            p.op('vector', lambda e: e.tensor_scalar_mul(var, PSa[7][0:1, :], 1.0 / D))
            p.op('vector', lambda e: e.tensor_mul(m2, m, m))
            p.op('vector', lambda e: e.tensor_sub(var, var, m2), inc='ve')
            p.wait('scalar', 've')
            p.op('scalar', lambda e: e.activation(var, var, AF.Ln, bias=cst_sb[0:1, 0:1]))
            p.op('scalar', lambda e: e.activation(rstd, var, AF.Exp, scale=-0.5), inc='sc')
            p.wait('vector', 'sc')
            p.op('vector', lambda e: e.tensor_mul(mr, m, rstd), inc='ve')
            p.wait('tensor', 've')
            p.op('tensor', lambda e: e.matmul(PSa[4][:, :], onesf_sb[0:1, 0:128], rstd, start=True, stop=True))
            p.op('tensor', lambda e: e.matmul(PSa[5][:, :], onesf_sb[0:1, 0:128], mr, start=True, stop=True),
                 inc='te')
            p.wait('vector', 'te')
            for c in range(NCH):
                p.op('vector', lambda e, c=c, g=g: e.tensor_mul(
                    tmpa[:, :], R[:, TOK * c + 512 * g: TOK * c + 512 * (g + 1)], PSa[4][:, :]))
                p.op('vector', lambda e, c=c, g=g: e.tensor_sub(
                    xln[:, TOK * c + 512 * g: TOK * c + 512 * (g + 1)], tmpa[:, :], PSa[5][:, :]), inc='ve')

    def attn_a(qsteps=()):
        qsteps = list(qsteps)
        def load_w(dst, srcB, h, sem):
            p.dma('sync', dst[:, :],
                  srcB[h, :, :].rearrange('(c p) w -> p c w', p=128)[:, :, :], sem)

        def load_mem(dst, h, sem):
            for n in range(NBK):
                p.dma('gpsimd', dst[:, 128 * n:128 * (n + 1)], memH[h, n, :, :], sem)

        p.wait('sync', 'dhb', wqa_done)
        p.wait('sync', 'dha', p.cnt['dha'])
        load_w(wab[0], wqa_bf, 0, 'dha')
        p.wait('gpsimd', 'dga', p.cnt['dga'])
        load_mem(mem_sb[0], 0, 'dga')
        xln_ready = p.cnt['ve']
        prev_te = 0
        sig_done = {}
        for h in range(HA):
            b = h % 2
            sem_b = 'dha' if b == 0 else 'dhb'
            gsem_b = 'dga' if b == 0 else 'dgb'
            need = p.cnt[sem_b]
            gneed = p.cnt[gsem_b]
            if h + 1 < HA:
                nsem = 'dhb' if b == 0 else 'dha'
                gsem = 'dgb' if b == 0 else 'dga'
                p.wait('sync', 'te', prev_te)
                p.wait('sync', nsem, p.cnt[nsem])
                load_w(wab[1 - b], wqa_bf, h + 1, nsem)
                p.wait('gpsimd', 'te', prev_te)
                p.wait('gpsimd', gsem, p.cnt[gsem])
                load_mem(mem_sb[1 - b], h + 1, gsem)
            p.wait('tensor', sem_b, need)
            p.wait('tensor', gsem_b, gneed)
            p.wait('tensor', 've', max(sig_done.get(h - 1, 0), xln_ready))
            for g in range(NTG):
                for c in range(NCH):
                    p.op('tensor', lambda e, g=g, c=c, b=b: e.matmul(
                        PSa[g][:, :], wab[b][:, 128 * c:128 * (c + 1)],
                        xln[:, TOK * c + 512 * g: TOK * c + 512 * (g + 1)],
                        start=(c == 0), stop=(c == NCH - 1)),
                        inc=('te' if (g == NTG - 1 and c == NCH - 1) else None))
            te_q = p.cnt['te']
            for g in range(NTG):
                p.wait('vector', 'te', te_q)
                p.op('vector', lambda e, g=g: e.tensor_single_scalar(tmpa[:, :], PSa[g][:, :], 0.0, ALU.min),
                     inc='ve')
                p.wait('scalar', 've')
                p.op('scalar', lambda e: e.activation(tmpb[:, :], tmpa[:, :], AF.Exp), inc='sc')
                p.wait('vector', 'sc')
                p.op('vector', lambda e, g=g: e.scalar_tensor_tensor(
                    xb_sb[g][:, :], PSa[g][:, :], 0.0, tmpb[:, :], ALU.max, ALU.add), inc='ve')
                ve_sig = p.cnt['ve']
                if g == NTG - 1:
                    sig_done[h] = ve_sig
                p.wait('tensor', 've', ve_sig)
                p.op('tensor', lambda e, g=g, h=h: e.matmul(
                    PSa[6][0:4, :], mnT_sb[:, NBK * h: NBK * (h + 1)], xb_sb[g][:, :],
                    start=True, stop=True), inc='te')
                te_n = p.cnt['te']
                p.wait('vector', 'te', te_n)
                p.op('vector', lambda e: e.tensor_single_scalar(tmpa[0:4, :], PSa[6][0:4, :], 1e-6, ALU.max),
                     inc='ve')
                p.wait('scalar', 've')
                p.op('scalar', lambda e: e.activation(ascr[0:4, :], tmpa[0:4, :], AF.Ln))
                p.op('scalar', lambda e: e.activation(tmpb[0:4, :], ascr[0:4, :], AF.Exp,
                                                      scale=-1.0), inc='sc')
                p.wait('vector', 'sc')
                p.op('vector', lambda e, h=h: e.tensor_scalar_mul(
                    g_sb[:, :], tmpb[0:4, :], wact4[0:4, h:h + 1]), inc='ve')
                ve_g = p.cnt['ve']
                sigp_ve = {}
                for n in range(NBK):
                    pb = 4 + (n % 2)
                    p.wait('tensor', 've', ve_g if n == 0 else sigp_ve.get(n - 2, 0))
                    p.op('tensor', lambda e, n=n, pb=pb: e.matmul(
                        PSa[pb][:, :], selG_sb[:, 128 * n:128 * (n + 1)], g_sb[:, :],
                        start=True, stop=True), inc='te')
                    p.wait('vector', 'te')
                    p.op('vector', lambda e, g=g, pb=pb, n=n: e.tensor_mul(
                        sigp[n % 2][:, :], xb_sb[g][:, :], PSa[pb][:, :]), inc='ve')
                    sigp_ve[n] = p.cnt['ve']
                    p.wait('tensor', 've', sigp_ve[n])
                    p.op('tensor', lambda e, g=g, n=n, b=b: e.matmul(
                        PSa[2 + g][:, :], mem_sb[b][:, 128 * n:128 * (n + 1)], sigp[n % 2][:, :],
                        start=(n == 0), stop=(n == NBK - 1)), inc='te')
                p.wait('vector', 'te')
                p.op('vector', lambda e, g=g, h=h: e.tensor_copy(
                    comb[:, TOK * h + 512 * g: TOK * h + 512 * (g + 1)], PSa[2 + g][:, :]), inc='ve')
            prev_te = p.cnt['te']
            for _ in range(2):
                if qsteps:
                    qsteps.pop(0)()
        while qsteps:
            qsteps.pop(0)()
        comb_done = p.cnt['ve']
        p.wait('sync', 'te', p.cnt['te'])
        p.wait('sync', 'dha', p.cnt['dha'])
        load_w(wab[0], woa_bf, 0, 'dha')
        prev_te = 0
        for oc in range(NCH):
            b = oc % 2
            sem_b = 'dha' if b == 0 else 'dhb'
            need = p.cnt[sem_b]
            if oc + 1 < NCH:
                nsem = 'dhb' if b == 0 else 'dha'
                p.wait('sync', 'te', prev_te)
                p.wait('sync', nsem, p.cnt[nsem])
                load_w(wab[1 - b], woa_bf, oc + 1, nsem)
            p.wait('tensor', sem_b, need)
            p.wait('tensor', 've', comb_done)
            for g in range(NTG):
                p.wait('tensor', 've', p.cnt['ve'])
                for c in range(NCH):
                    p.op('tensor', lambda e, g=g, c=c, b=b: e.matmul(
                        PSa[g][:, :], wab[b][:, 128 * c:128 * (c + 1)],
                        comb[:, TOK * c + 512 * g: TOK * c + 512 * (g + 1)],
                        start=(c == 0), stop=(c == NCH - 1)),
                        inc=('te' if c == NCH - 1 else None))
                p.wait('vector', 'te')
                p.op('vector', lambda e, oc=oc, g=g: e.tensor_add(
                    R[:, TOK * oc + 512 * g: TOK * oc + 512 * (g + 1)],
                    R[:, TOK * oc + 512 * g: TOK * oc + 512 * (g + 1)], PSa[g][:, :]), inc='ve')
            prev_te = p.cnt['te']

    def quant_steps():
        RG = [list(range(NCORES))]
        steps = []

        def do_piece(bufb, out_fn):
            dso_mark = {-1: p.cnt['dso']}
            for st in range(4):
                qt2c = qt2[st % 2]
                wsl = qwb[bufb][:, 512 * st:512 * (st + 1)]
                p.wait('vector', 'dso', dso_mark.get(st - 2, dso_mark.get(-1, 0)))
                p.op('vector', lambda e: e.nop(), inc='ve')
                p.wait('scalar', 've')
                p.wait('scalar', 'te', p.cnt['te'])
                p.op('scalar', lambda e, wsl=wsl: e.activation(qt1[:, :], wsl, AF.Abs), inc='sc')
                p.wait('tensor', 'sc')
                p.op('tensor', lambda e: e.matmul(PSa[6][0:2, :], ind2_sb[:, :], qt1[:, :],
                                                  start=True, stop=True), inc='te')
                p.wait('vector', 'te')
                p.op('vector', lambda e: e.tensor_scalar(
                    qsd[:, :], PSa[6][0:2, :], 1.0 / 64, 1e-8, ALU.mult, ALU.max), inc='ve')
                p.wait('scalar', 've')
                p.op('scalar', lambda e: e.activation(qsdf[:, :], qsd[:, :], AF.Ln))
                p.op('scalar', lambda e: e.activation(qrcp[:, :], qsdf[:, :], AF.Exp,
                                                      scale=-1.0), inc='sc')
                p.wait('tensor', 've')
                p.wait('tensor', 'sc')
                p.op('tensor', lambda e: e.matmul(PSa[4][:, :], e2f_sb[:, :], qrcp[:, :],
                                                  start=True, stop=True))
                p.op('tensor', lambda e: e.matmul(PSa[5][:, :], e2b_sb[:, :], qsd[:, :],
                                                  start=True, stop=True), inc='te')
                p.wait('vector', 'te')
                p.op('vector', lambda e, wsl=wsl, qt2c=qt2c: e.tensor_mul(qt2c[:, :], wsl, PSa[4][:, :]))
                p.op('vector', lambda e, qt2c=qt2c: e.tensor_single_scalar(
                    qt1[:, :], qt2c[:, :], -0.5, ALU.is_lt))
                p.op('vector', lambda e, qt2c=qt2c: e.scalar_tensor_tensor(
                    qt3[:, :], qt2c[:, :], 0.5, qt1[:, :], ALU.is_gt, ALU.subtract))
                p.op('vector', lambda e, qt2c=qt2c: e.tensor_mul(
                    qt2c[:, :], qt3[:, :], PSa[5][:, :]), inc='ve')
                p.wait('scalar', 've')
                p.wait('scalar', 'dso', p.cnt['dso'])
                out_fn(st, qt2c)
                dso_mark[st] = p.cnt['dso']

        def gu_out(rc, cp, st, qt2c):
            hc0 = (2048 * cp + 512 * st) // 128
            dst = wqgu_sh[hc0:hc0 + 4, :, rc, :].transpose([1, 0, 2])
            p.dma('scalar', dst, qt2c.rearrange('p (a h) -> p a h', h=128)[:, :, :], 'dso')

        def p_out(rc, st, qt2c):
            p.dma('scalar', wqp_sh[rc, st, :, :], qt2c[:, :], 'dso')

        def precast(c0, c1):
            p.wait('gpsimd', 'dh', p.cnt['dh'])
            p.wait('gpsimd', 've', p.cnt['ve'])
            for c in range(c0, c1):
                b = c % 2
                sem = 'dga' if b == 0 else 'dgb'
                p.wait('gpsimd', sem, p.cnt[sem])
                p.wait('gpsimd', 'dhb', p.cnt['dhb'])
                p.dma('gpsimd', wst[b][:, :], Wqkvf[128 * c:128 * (c + 1), :], sem)
                p.wait('sync', sem, p.cnt[sem])
                p.dma('sync', wqkvf_bf[128 * c:128 * (c + 1), :], wst[b][:, :], 'dhb')
        steps.append(lambda: precast(0, 8))
        steps.append(lambda: precast(8, 16))

        pieces = [(rc, cp) for rc in range(2) for cp in range(8)]

        def gu_step(i):
            rc, cp = pieces[i]
            b = i % 2
            sem_b = 'dga' if b == 0 else 'dgb'
            if i == 0:
                p.wait('tensor', 'dgc', g_loaded)
                p.wait('gpsimd', 'dga', p.cnt['dga'])
                p.wait('gpsimd', 'dhb', p.cnt['dhb'])
                p.dma('gpsimd', qwb[0][:, :], wguS[0:128, 0:2048], 'dga')
            need = p.cnt[sem_b]
            if i + 1 < len(pieces):
                nrc, ncp = pieces[i + 1]
                nsem = 'dgb' if b == 0 else 'dga'
                p.wait('gpsimd', 've', p.cnt['ve'])
                p.wait('gpsimd', nsem, p.cnt[nsem])
                p.dma('gpsimd', qwb[1 - b][:, :],
                      wguS[128 * nrc:128 * (nrc + 1), 2048 * ncp:2048 * (ncp + 1)],
                      nsem)
            p.wait('vector', sem_b, need)
            do_piece(b, lambda st, qt2c, rc=rc, cp=cp: gu_out(rc, cp, st, qt2c))

        for i in range(len(pieces)):
            steps.append(lambda i=i: gu_step(i))

        def ag_gu():
            p.wait('gpsimd', 'dso', p.cnt['dso'])
            p.op('gpsimd', lambda e: e.collective_compute(
                "AllGather", ALU.bypass, replica_groups=RG,
                ins=[wqgu_sh[:, :, :, :].opt()], outs=[wqgu_all[:, :, :, :].opt()]), inc='cc')
        steps.append(ag_gu)

        def p_step(rc):
            b = rc % 2
            sem_b = 'dga' if b == 0 else 'dgb'
            if rc == 0:
                p.wait('gpsimd', 'dga', p.cnt['dga'])
                p.dma('gpsimd', qwb[0][:, :], wpS[0:128, :], 'dga')
            need = p.cnt[sem_b]
            if rc + 1 < 8:
                nsem = 'dgb' if b == 0 else 'dga'
                p.wait('gpsimd', 've', p.cnt['ve'])
                p.wait('gpsimd', nsem, p.cnt[nsem])
                p.dma('gpsimd', qwb[1 - b][:, :], wpS[128 * (rc + 1):128 * (rc + 2), :], nsem)
            p.wait('vector', sem_b, need)
            do_piece(b, lambda st, qt2c, rc=rc: p_out(rc, st, qt2c))

        for rc in range(8):
            steps.append(lambda rc=rc: p_step(rc))

        def ag_p():
            p.wait('gpsimd', 'dso', p.cnt['dso'])
            p.op('gpsimd', lambda e: e.collective_compute(
                "AllGather", ALU.bypass, replica_groups=RG,
                ins=[wqp_sh[:, :, :, :].opt()], outs=[wqp_all[:, :, :, :].opt()]), inc='cc')
        steps.append(ag_p)
        return steps

    def ffn():
        p.wait('sync', 'cc', 1)
        p.wait('gpsimd', 'cc', 1)
        p.wait('sync', 'te', p.cnt['te'])
        p.wait('gpsimd', 'te', p.cnt['te'])
        p.wait('vector', 'dhb', p.cnt['dhb'])
        p.wait('scalar', 'dhb', p.cnt['dhb'])
        p.wait('sync', 've', p.cnt['ve'])
        p.wait('gpsimd', 've', p.cnt['ve'])
        for T in range(NTG):
            ve_xln = p.cnt['ve']
            def load_gu(hcl, gbuf, ubuf, sems):
                for r in range(8):
                    q, sem = (('sync', sems[0]) if r < 3 else
                              ('scalar', sems[2]) if r < 6 else ('gpsimd', sems[1]))
                    p.dma(q, gbuf[:, 256 * r:256 * (r + 1)],
                          wqgu_all[128 * r + hcl, :, :, :], sem)
                    p.dma(q, ubuf[:, 256 * r:256 * (r + 1)],
                          wqgu_all[128 * r + hcl + 64, :, :, :], sem)

            sc_mark = {}
            ve_mark = {}
            rms_te = {}
            p.wait('sync', 'dha', p.cnt['dha'])
            p.wait('gpsimd', 'dga', p.cnt['dga'])
            p.wait('scalar', 'dsa', p.cnt['dsa'])
            load_gu(0, fgw[0], fgw[1], ('dha', 'dga', 'dsa'))
            for hcl in range(64):
                b = hcl % 2
                sems = ('dha', 'dga', 'dsa') if b == 0 else ('dhb', 'dgb', 'dsb')
                nh = p.cnt[sems[0]]
                ng = p.cnt[sems[1]]
                ns = p.cnt[sems[2]]
                if hcl + 1 < 64:
                    nsems = ('dhb', 'dgb', 'dsb') if b == 0 else ('dha', 'dga', 'dsa')
                    p.wait('sync', 'te', p.cnt['te'])
                    p.wait('gpsimd', 'te', p.cnt['te'])
                    p.wait('scalar', 'te', p.cnt['te'])
                    p.wait('sync', nsems[0], p.cnt[nsems[0]])
                    p.wait('gpsimd', nsems[1], p.cnt[nsems[1]])
                    p.wait('scalar', nsems[2], p.cnt[nsems[2]])
                    load_gu(hcl + 1, fgw[2 * (1 - b)], fgw[2 * (1 - b) + 1], nsems)
                p.wait('tensor', sems[0], nh)
                p.wait('tensor', sems[1], ng)
                p.wait('tensor', sems[2], ns)
                p.wait('tensor', 'sc', sc_mark[hcl - 2] if hcl >= 2 else 0)
                p.wait('tensor', 've', ve_mark[hcl - 2] if hcl >= 2 else ve_xln)
                for c in range(NCH):
                    p.op('tensor', lambda e, c=c, b=b, T=T: e.matmul(
                        PSa[2 * b][:, :], fgw[2 * b][:, 128 * c:128 * (c + 1)],
                        xln[:, TOK * c + 512 * T: TOK * c + 512 * (T + 1)],
                        start=(c == 0), stop=(c == NCH - 1)))
                for c in range(NCH):
                    p.op('tensor', lambda e, c=c, b=b, T=T: e.matmul(
                        PSa[2 * b + 1][:, :], fgw[2 * b + 1][:, 128 * c:128 * (c + 1)],
                        xln[:, TOK * c + 512 * T: TOK * c + 512 * (T + 1)],
                        start=(c == 0), stop=(c == NCH - 1)), inc=('te' if c == NCH - 1 else None))
                p.wait('scalar', 'te')
                if hcl >= 2:
                    p.wait('scalar', 've', ve_mark[hcl - 2])
                p.op('scalar', lambda e, b=b: e.activation(sil_sb[b][:, :], PSa[2 * b][:, :], AF.Silu),
                     inc='sc', drain=False)
                sc_mark[hcl] = p.cnt['sc']
                p.wait('vector', 'sc')
                p.op('vector', lambda e, b=b, hcl=hcl: e.tensor_mul(
                    hbuf[:, 512 * hcl:512 * (hcl + 1)], sil_sb[b][:, :], PSa[2 * b + 1][:, :]), drain=False)
                if hcl >= 2:
                    p.wait('vector', 'te', rms_te.get(hcl - 2, 0))
                p.op('vector', lambda e, b=b, hcl=hcl: e.tensor_mul(
                    hsq_sb[b][:, :], hbuf[:, 512 * hcl:512 * (hcl + 1)],
                    hbuf[:, 512 * hcl:512 * (hcl + 1)]), inc='ve')
                ve_mark[hcl] = p.cnt['ve']
                if hcl > 0:
                    p.wait('tensor', 've', ve_mark[hcl - 1])
                    p.op('tensor', lambda e, hcl=hcl: e.matmul(
                        PSa[6][0:1, :], onesb_sb[:, 0:1], hsq_sb[(hcl - 1) % 2][:, :],
                        start=(hcl - 1 == 0), stop=False), inc='te')
                    rms_te[hcl - 1] = p.cnt['te']
                if hcl == 63:
                    p.wait('tensor', 've', ve_mark[63])
                    p.op('tensor', lambda e: e.matmul(
                        PSa[6][0:1, :], onesb_sb[:, 0:1], hsq_sb[63 % 2][:, :],
                        start=False, stop=True), inc='te')
            p.wait('vector', 'te')
            rr = row_sb[0:1, 0:512]
            p.op('vector', lambda e: e.tensor_scalar_mul(rr, PSa[6][0:1, :], 1.0 / FH), inc='ve')
            p.wait('scalar', 've')
            p.op('scalar', lambda e: e.activation(rr, rr, AF.Ln, bias=cst_sb[0:1, 2:3]))
            p.op('scalar', lambda e: e.activation(rr, rr, AF.Exp, scale=-0.5), inc='sc')
            p.wait('tensor', 'sc')
            p.op('tensor', lambda e: e.matmul(PSa[7][:, :], onesf_sb[0:1, 0:128], rr,
                                              start=True, stop=True), inc='te')
            p.wait('vector', 'te')
            p.op('vector', lambda e: e.tensor_copy(tmpb[:, :], PSa[7][:, :]), inc='ve')
            p.wait('sync', 'cc', 2)
            p.wait('gpsimd', 'cc', 2)

            def load_p(dq, hc, buf):
                q, sem = ('sync', 'dha') if hc % 2 == 0 else ('gpsimd', 'dga')
                p.dma(q, buf[:, :], wqp_all[hc, dq, :, :], sem)

            p.wait('sync', 'dha', p.cnt['dha'])
            load_p(0, 0, fpw[0])
            for dq in range(4):
                for hc in range(64):
                    b = hc % 2
                    sem_b = 'dha' if b == 0 else 'dga'
                    need = p.cnt[sem_b]
                    if dq * 64 + hc + 1 < 256:
                        ndq, nhc = (dq, hc + 1) if hc + 1 < 64 else (dq + 1, 0)
                        nq, nsem = ('sync', 'dha') if nhc % 2 == 0 else ('gpsimd', 'dga')
                        p.wait('sync', 'te', p.cnt['te'])
                        p.wait('gpsimd', 'te', p.cnt['te'])
                        p.wait(nq, nsem, p.cnt[nsem])
                        load_p(ndq, nhc, fpw[1 - b])
                    p.wait('tensor', sem_b, need)
                    if hc == 0:
                        p.wait('tensor', 've', p.cnt['ve'])
                    for cc in range(4):
                        p.op('tensor', lambda e, cc=cc, b=b, hc=hc: e.matmul(
                            PSa[cc][:, :], fpw[b][:, 128 * cc:128 * (cc + 1)],
                            hbuf[:, 512 * hc:512 * (hc + 1)],
                            start=(hc == 0), stop=(hc == 63)),
                            inc=('te' if cc == 3 else None))
                p.wait('vector', 'te')
                for cc in range(4):
                    dchunk = 4 * dq + cc
                    p.op('vector', lambda e, cc=cc: e.tensor_mul(tmpa[:, :], PSa[cc][:, :], tmpb[:, :]))
                    p.op('vector', lambda e, dchunk=dchunk, T=T: e.tensor_add(
                        R[:, TOK * dchunk + 512 * T: TOK * dchunk + 512 * (T + 1)],
                        R[:, TOK * dchunk + 512 * T: TOK * dchunk + 512 * (T + 1)],
                        tmpa[:, :]), inc='ve')

    def attn_b():
        RG = [list(range(NCORES))]
        p.wait('sync', 've', p.cnt['ve'])
        p.wait('sync', 'te', p.cnt['te'])
        p.dma('sync', trif_sb[:, :], trif[:, :], 'dhb')
        for c in range(NCH):
            p.dma('sync', xsha[128 * c:128 * (c + 1), :], xln[:, TOK * c:TOK * c + 512], 'dhb')
            p.dma('sync', xshb[128 * c:128 * (c + 1), :], xln[:, TOK * c + 512:TOK * (c + 1)], 'dhb')
        bounce_done = p.cnt['dhb']
        p.wait('gpsimd', 'dhb', bounce_done)
        p.op('gpsimd', lambda e: e.collective_compute(
            "AllGather", ALU.bypass, replica_groups=RG,
            ins=[xsha[:, :].opt()], outs=[xga[:, :].opt()]), inc='cc')
        cc_xa = p.cnt['cc']
        p.op('gpsimd', lambda e: e.collective_compute(
            "AllGather", ALU.bypass, replica_groups=RG,
            ins=[xshb[:, :].opt()], outs=[xgb[:, :].opt()]), inc='cc')
        cc_x = p.cnt['cc']
        p.wait('sync', 'cc', cc_xa)
        p.wait('gpsimd', 'cc', cc_xa)
        ve_proj0 = p.cnt['ve']

        def load_xa(g, hc, buf, q, sem):
            r, gg = g // 2, g % 2
            src = xga if gg == 0 else xgb
            p.dma(q, buf[:, :], src[2048 * r + 128 * hc:2048 * r + 128 * (hc + 1), :], sem)

        def load_wst(hc, buf, sem, q='gpsimd'):
            p.dma(q, buf[:, :], wqkvf_bf[128 * hc:128 * (hc + 1), :], sem)

        p.wait('vector', 'dhb', bounce_done)
        for pair in range(2):
            p.op('vector', lambda e, pair=pair: e.memset(
                Vsb[pair][:, 0:8320].rearrange('p (c w) -> p c w', w=130)[:, :, 64:65], 1.0))
            p.op('vector', lambda e, pair=pair: e.memset(
                Vsb[pair][:, 0:8320].rearrange('p (c w) -> p c w', w=130)[:, :, 129:130], 1.0), inc='ve')
        p.wait('sync', 'dhb', p.cnt['dhb'])
        load_xa(0, 0, xat[0], 'sync', 'dhb')
        p.wait('gpsimd', 'dhb', p.cnt['dhb'])
        p.wait('gpsimd', 'dga', p.cnt['dga'])
        load_wst(0, wst[0], 'dga')
        gorder = [g for g in range(16) if g % 2 == 0] + [g for g in range(16) if g % 2 == 1]
        for gi, g in enumerate(gorder):
            for hc in range(NCH):
                i = gi * NCH + hc
                b = i % 2
                xsem = 'dhb' if b == 0 else 'dha'
                wsem = 'dga' if b == 0 else 'dsa'
                nx = p.cnt[xsem]
                nw = p.cnt[wsem]
                if i + 1 < 16 * NCH:
                    ng, nhc = (g, hc + 1) if hc + 1 < NCH else (gorder[gi + 1], 0)
                    nxs = 'dha' if b == 0 else 'dhb'
                    nws = 'dsa' if b == 0 else 'dga'
                    nwq = 'scalar' if b == 0 else 'gpsimd'
                    p.wait('sync', 'te', p.cnt['te'])
                    p.wait('sync', nxs, p.cnt[nxs])
                    if ng % 2 == 1 and g % 2 == 0:
                        p.wait('sync', 'cc', cc_x)
                    load_xa(ng, nhc, xat[1 - b], 'sync', nxs)
                    p.wait(nwq, 'te', p.cnt['te'])
                    p.wait(nwq, nws, p.cnt[nws])
                    load_wst(nhc, wst[1 - b], nws, nwq)
                p.wait('tensor', xsem, nx)
                p.wait('tensor', wsem, nw)
                if hc == 0:
                    p.wait('tensor', 've', ve_proj0 if g == 0 else p.cnt['ve'])
                for si in range(4):
                    p.op('tensor', lambda e, si=si, b=b, hc=hc: e.matmul(
                        PSa[si][:, :], wst[b][:, 128 * si:128 * (si + 1)], xat[b][:, :],
                        start=(hc == 0), stop=(hc == NCH - 1)))
                for si in range(4):
                    p.op('tensor', lambda e, si=si, b=b, hc=hc: e.matmul(
                        PSa[4 + si][0:128, 0:260], xat[b][:, 128 * si:128 * (si + 1)],
                        wst[b][:, 512:772],
                        start=(hc == 0), stop=(hc == NCH - 1)),
                        inc=('te' if si == 3 else None))
            p.wait('vector', 'te')
            pr, gg = g // 2, g % 2
            for pair in range(2):
                p.op('vector', lambda e, pair=pair, g=g: e.tensor_scalar_mul(
                    Qsb[pair][:, 512 * g:512 * (g + 1)], PSa[pair][:, :], 0.125),
                    drain=(pair == 0))
                p.op('vector', lambda e, pair=pair, g=g: e.tensor_copy(
                    Ksb[pair][:, 512 * g:512 * (g + 1)], PSa[2 + pair][:, :]), drain=False)
            for si in range(4):
                ch = 4 * g + si
                for pair in range(2):
                    p.op('vector', lambda e, si=si, pair=pair, ch=ch: e.tensor_copy(
                        Vsb[pair][:, 130 * ch:130 * ch + 64],
                        PSa[4 + si][0:128, 128 * pair:128 * pair + 64]), drain=False)
                    p.op('vector', lambda e, si=si, pair=pair, ch=ch: e.tensor_copy(
                        Vsb[pair][:, 130 * ch + 65:130 * ch + 129],
                        PSa[4 + si][0:128, 128 * pair + 64:128 * (pair + 1)]), drain=False)
                p.op('vector', lambda e, si=si: e.tensor_copy(
                    ftmp[:, :], PSa[4 + si][0:128, 256:260]), inc='ve')
                pgrp = 32 * (g // 4)
                p.wait('tensor', 've')
                p.op('tensor', lambda e: e.transpose(
                    PSa[0][0:4, 0:128], ftmp[:, :], ident_sb[:, :]), inc='te')
                p.wait('vector', 'te')
                p.wait('vector', 'dq', p.cnt['dq'])
                p.op('vector', lambda e: e.tensor_copy(ft2[:, :], PSa[0][0:4, 0:128]), inc='ve')
                p.wait('gpsimd', 've')
                col0 = (512 * g + 128 * si) % 2048
                p.dma('gpsimd', Ptil[pgrp:pgrp + 4, col0:col0 + 128], ft2[:, :], 'dq')
        p.wait('vector', 'dgc', g_loaded)
        p.wait('vector', 'dq', p.cnt['dq'])
        p.wait('scalar', 'dq', p.cnt['dq'])
        p.wait('vector', 'te', p.cnt['te'])
        p.op('vector', lambda e: e.memzero(zz8[:, :]))
        p.op('vector', lambda e: e.memzero(initc[:, :]), inc='ve')
        for grp in range(4):
            rs = slice(32 * grp, 32 * grp + 4)
            p.op('vector', lambda e, rs=rs: e.tensor_scalar(
                pcum[rs, :], Ptil[rs, :], bfc_sb[rs, 0:1], 0.0, ALU.add, ALU.min), inc='ve')
            p.wait('scalar', 've')
            p.op('scalar', lambda e, rs=rs: e.activation(
                Ptil[rs, :], Ptil[rs, :], AF.Abs, bias=bfc_sb[rs, 0:1]))
            p.op('scalar', lambda e, rs=rs: e.activation(
                Ptil[rs, :], Ptil[rs, :], AF.Exp, scale=-1.0))
            p.op('scalar', lambda e, rs=rs: e.activation(
                Ptil[rs, :], Ptil[rs, :], AF.Ln, bias=cst_sb[rs, 1:2]), inc='sc')
            p.wait('vector', 'sc')
            p.op('vector', lambda e, rs=rs: e.tensor_sub(
                Ptil[rs, :], Ptil[rs, :], pcum[rs, :]), inc='ve')
        for grp in range(4):
            rs = slice(32 * grp, 32 * grp + 4)
            if grp > 0:
                p.wait('gpsimd', 've', p.cnt['ve'])
                p.dma('gpsimd', initc[32 * grp:32 * grp + 4, 0:1],
                      pcum[32 * (grp - 1):32 * (grp - 1) + 4, 2047:2048], 'dq')
                p.wait('vector', 'dq', p.cnt['dq'])
            p.op('vector', lambda e, rs=rs: e.tensor_tensor_scan(
                pcum[rs, :], Ptil[rs, :], zz8[rs, :], initc[rs, 0:1],
                ALU.add, ALU.add), inc='ve')
        p.wait('tensor', 've')
        for k in range(64):
            grp, col = k // 16, (k % 16) * 128
            p.op('tensor', lambda e, grp=grp, col=col: e.transpose(
                PSa[1][0:128, 0:4], pcum[32 * grp:32 * grp + 4, col:col + 128],
                ident_sb[32 * grp:32 * grp + 4, 32 * grp:32 * grp + 4],
                tile_position=(32 * grp, 0)), inc='te')
            p.wait('vector', 'te')
            p.op('vector', lambda e, k=k: e.tensor_copy(fct[:, 4 * k:4 * (k + 1)],
                                                        PSa[1][0:128, 0:4]), inc='ve')
            p.wait('tensor', 've')
        p.wait('scalar', 've', p.cnt['ve'])
        fq3_rdy = {}
        fq3_free = {0: 0, 1: 0}
        prep_ve_end = 0

        def emit_prep(J):
            nonlocal prep_ve_end
            pr, jq = divmod(J, 16)
            grp, col = jq // 4, (jq % 4) * 512
            pb = J % 2
            p.wait('gpsimd', 've', prep_ve_end)
            p.wait('gpsimd', 'te', fq3_free[pb])
            p.wait('gpsimd', 'dq', p.cnt['dq'])
            for h, hp in ((0, 96), (1, 32)):
                row = 32 * grp + 2 * pr + h
                p.dma('gpsimd', rcpb[hp:hp + 1, :], pcum[row:row + 1, col:col + 512], 'dq')
            vrow = p.cnt['dq']
            p.wait('vector', 'dq', vrow)
            dma0 = {}
            for nm, dr in (('n', True), ('hi', True), ('r', True),
                           ('mid', True), ('r2', True), ('lo', True)):
                for h, hp in ((0, 96), (1, 32)):
                    hs = slice(hp, hp + 1)
                    if nm == 'n':
                        p.op('vector', lambda e, hs=hs: e.tensor_single_scalar(
                            rcpbB[hs, :], rcpb[hs, :], -1.0, ALU.mult), drain=(h == 0))
                    elif nm == 'hi':
                        p.op('vector', lambda e, hs=hs: e.tensor_copy(
                            hiT[hs, :], rcpbB[hs, :]), inc='ve', drain=(h == 0))
                        if h == 1:
                            hi_ve = p.cnt['ve']
                    elif nm == 'r':
                        p.op('vector', lambda e, hs=hs: e.tensor_sub(
                            rcpb[hs, :], rcpbB[hs, :], hiT[hs, :]), drain=(h == 0))
                    elif nm == 'mid':
                        p.op('vector', lambda e, hs=hs: e.tensor_copy(
                            miT[hs, :], rcpb[hs, :]), inc='ve', drain=(h == 0))
                        if h == 1:
                            mid_ve = p.cnt['ve']
                    elif nm == 'r2':
                        p.op('vector', lambda e, hs=hs: e.tensor_sub(
                            rcpbB[hs, :], rcpb[hs, :], miT[hs, :]), drain=(h == 0))
                    else:
                        if h == 0:
                            p.wait('vector', 'dq', dma0['hi'])
                        p.op('vector', lambda e, hs=hs: e.tensor_copy(
                            hiT[hs, :], rcpbB[hs, :]), inc='ve', drain=(h == 0))
                if nm == 'hi' or nm == 'mid':
                    mark = hi_ve if nm == 'hi' else mid_ve
                    ri = 0 if nm == 'hi' else 1
                    p.wait('gpsimd', 've', mark)
                    p.wait('gpsimd', 'dq', p.cnt['dq'])
                    for h, hp in ((0, 96), (1, 32)):
                        src = hiT if nm == 'hi' else miT
                        p.dma('gpsimd', fq3w[pb][ri:ri + 1, 512 * h:512 * (h + 1)],
                              src[hp:hp + 1, :], 'dq')
                    dma0[nm] = p.cnt['dq']
            lo_ve = p.cnt['ve']
            prep_ve_end = lo_ve
            p.wait('gpsimd', 've', lo_ve)
            p.wait('gpsimd', 'dq', p.cnt['dq'])
            for h, hp in ((0, 96), (1, 32)):
                p.dma('gpsimd', fq3w[pb][2:3, 512 * h:512 * (h + 1)], hiT[hp:hp + 1, :], 'dq')
            fq3_rdy[J] = p.cnt['dq']

        emit_prep(0)
        for pair in range(2):
            for j in range(16):
                J = 16 * pair + j
                ve_qb = p.cnt['ve']
                if J + 1 < 32:
                    emit_prep(J + 1)
                pb = J % 2
                nk = 4 * (j + 1)
                sc_mark_f = {}
                p.wait('tensor', 've', ve_qb)
                p.wait('tensor', 'dq', fq3_rdy[J])
                p.op('tensor', lambda e, pb=pb: e.matmul(
                    PSa[4][:, :], onesb_sb[0:3, 0:128], fq3w[pb][0:3, 0:512],
                    start=True, stop=True))
                p.op('tensor', lambda e, pb=pb: e.matmul(
                    PSa[5][:, :], onesb_sb[0:3, 0:128], fq3w[pb][0:3, 512:1024],
                    start=True, stop=True), inc='te')
                p.wait('vector', 'te', p.cnt['te'])
                p.op('vector', lambda e: e.tensor_copy(fqbw[:, 0:512], PSa[4][:, :]))
                p.op('vector', lambda e: e.tensor_copy(fqbw[:, 512:1024], PSa[5][:, :]),
                     inc='ve', drain=False)
                r1_ve = p.cnt['ve']

                def emit_lpv(k, nk):
                    kpar = k % 3
                    mc = 128 * (k - 4 * j) if k >= 4 * j else 0
                    p.op('tensor', lambda e, pair=pair, kpar=kpar, k=k, mc=mc, nk=nk: e.matmul(
                        PSa[6][0:65, mc:512], Vsb[pair][:, 130 * k:130 * k + 65],
                        ptb[kpar][:, mc:512], start=(k == 0), stop=(k == nk - 1)))
                    p.op('tensor', lambda e, pair=pair, kpar=kpar, k=k, mc=mc, nk=nk: e.matmul(
                        PSa[7][0:65, mc:512], Vsb[pair][:, 130 * k + 65:130 * (k + 1)],
                        ptb[3 + kpar][:, mc:512],
                        start=(k == 0), stop=(k == nk - 1)), inc='te')

                for k in range(nk + 2):
                    if k < nk:
                        kpar = k % 3
                        STW = PBa[kpar]
                        STA, STB = PSa[kpar * 2], PSa[kpar * 2 + 1]
                        mc = 128 * (k - 4 * j) if k >= 4 * j else 0
                        use_r3 = (k % 2 == 0) or k >= 4 * j or k < 2
                        if k == 2:
                            p.wait('tensor', 've', r1_ve)
                        p.op('tensor', lambda e, STA=STA, pair=pair, k=k, j=j, mc=mc,
                             u=use_r3: e.matmul(
                            STA[:, mc:512], Ksb[pair][0:64, 128 * k:128 * (k + 1)],
                            Qsb[pair][0:64, 512 * j + mc:512 * (j + 1)],
                            start=True, stop=not u))
                        if use_r3:
                            p.op('tensor', lambda e, STB=STB, pair=pair, k=k, j=j, mc=mc: e.matmul(
                                STB[:, mc:512], Ksb[pair][64:128, 128 * k:128 * (k + 1)],
                                Qsb[pair][64:128, 512 * j + mc:512 * (j + 1)],
                                tile_position=(64, 0), start=True, stop=False))
                            p.op('tensor', lambda e, STA=STA, pb=pb, mc=mc: e.matmul(
                                STA[:, mc:512], onesb_sb[0:3, 0:128],
                                fq3w[pb][0:3, mc:512], start=False, stop=True))
                            p.op('tensor', lambda e, STB=STB, pb=pb, mc=mc: e.matmul(
                                STB[:, mc:512], onesb_sb[0:3, 0:128],
                                fq3w[pb][0:3, 512 + mc:1024], start=False, stop=True), inc='te')
                        else:
                            p.op('tensor', lambda e, STB=STB, pair=pair, k=k, j=j, mc=mc: e.matmul(
                                STB[:, mc:512], Ksb[pair][64:128, 128 * k:128 * (k + 1)],
                                Qsb[pair][64:128, 512 * j + mc:512 * (j + 1)],
                                tile_position=(64, 0), start=True, stop=True), inc='te')
                        te_k = p.cnt['te']
                    if k >= 2:
                        p.wait('tensor', 'sc', sc_mark_f[k - 2])
                        if k - 2 == 0:
                            p.wait('tensor', 've', ve_qb)
                        emit_lpv(k - 2, nk)
                    if k >= nk:
                        continue
                    if k >= 4 * j:
                        p.wait('vector', 'te', te_k)
                        p.op('vector', lambda e, STA=STA, mc=mc: e.tensor_add(
                            STA[:, mc:mc + 128], STA[:, mc:mc + 128], trif_sb[:, :]))
                        p.op('vector', lambda e, STB=STB, mc=mc: e.tensor_add(
                            STB[:, mc:mc + 128], STB[:, mc:mc + 128], trif_sb[:, :]),
                            inc='ve', drain=False)
                        p.wait('scalar', 've', p.cnt['ve'])
                    elif not use_r3:
                        p.wait('vector', 'te', te_k)
                        p.op('vector', lambda e, STW=STW: e.tensor_add(
                            STW[:, :], STW[:, :], fqbw[:, :]), inc='ve', drain=(k == 3))
                        p.wait('scalar', 've', p.cnt['ve'])
                    else:
                        p.wait('scalar', 'te', te_k)
                    p.op('scalar', lambda e, STA=STA, kpar=kpar, k=k, pair=pair, mc=mc: e.activation(
                        ptb[kpar][:, mc:512], STA[:, mc:512], AF.Exp,
                        bias=fct[:, 4 * k + 2 * pair:4 * k + 2 * pair + 1]))
                    p.op('scalar', lambda e, STB=STB, kpar=kpar, k=k, pair=pair, mc=mc: e.activation(
                        ptb[3 + kpar][:, mc:512], STB[:, mc:512], AF.Exp,
                        bias=fct[:, 4 * k + 2 * pair + 1:4 * k + 2 * pair + 2]), inc='sc', drain=False)
                    sc_mark_f[k] = p.cnt['sc']
                fq3_free[pb] = p.cnt['te']
                p.wait('vector', 'te')
                p.wait('vector', 'dq', p.cnt['dq'])
                p.wait('scalar', 'te', p.cnt['te'])
                p.op('scalar', lambda e: e.activation(dvb[64:65, :], PSa[6][64:65, :], AF.Ln))
                p.op('scalar', lambda e: e.activation(rcpb[64:65, :], dvb[64:65, :], AF.Exp,
                                                      scale=-1.0))
                p.op('scalar', lambda e: e.activation(dvb2[64:65, :], PSa[7][64:65, :], AF.Ln, drain=False)
                     if False else e.activation(dvb2[64:65, :], PSa[7][64:65, :], AF.Ln))
                p.op('scalar', lambda e: e.activation(rcpbB[64:65, :], dvb2[64:65, :], AF.Exp,
                                                      scale=-1.0), inc='sc')
                p.wait('vector', 'sc')
                p.op('vector', lambda e: e.nop(), inc='ve')
                p.wait('gpsimd', 've')
                p.dma('gpsimd', rcpb[0:1, :], rcpb[64:65, :], 'dq')
                p.dma('gpsimd', rcpbB[0:1, :], rcpbB[64:65, :], 'dq')
                p.wait('tensor', 'dq', p.cnt['dq'])
                p.wait('tensor', 've', p.cnt['ve'])
                p.wait('tensor', 'sc', p.cnt['sc'])
                p.op('tensor', lambda e: e.matmul(PSa[0][0:64, :], onesf_sb[0:1, 0:64],
                                                  rcpb[0:1, :], start=True, stop=True))
                p.op('tensor', lambda e: e.matmul(PSa[1][0:64, :], onesf_sb[0:1, 0:64],
                                                  rcpbB[0:1, :], start=True, stop=True), inc='te')
                p.wait('vector', 'te')
                p.op('vector', lambda e: e.tensor_copy(dvb[0:64, :], PSa[0][0:64, :]))
                p.op('vector', lambda e: e.tensor_copy(dvb2[0:64, :], PSa[1][0:64, :]))
                p.op('vector', lambda e: e.tensor_mul(onrm_sb[0:64, :], PSa[6][0:64, :], dvb[0:64, :]))
                p.op('vector', lambda e: e.tensor_mul(onrm2_sb[0:64, :], PSa[7][0:64, :],
                                                      dvb2[0:64, :]), inc='ve')
                p.wait('gpsimd', 've')
                p.dma('gpsimd', obnc[128 * pair:128 * pair + 64, 512 * j:512 * (j + 1)],
                      onrm_sb[0:64, :], 'dq')
                p.dma('gpsimd', obnc[128 * pair + 64:128 * (pair + 1), 512 * j:512 * (j + 1)],
                      onrm2_sb[0:64, :], 'dq')
        p.wait('gpsimd', 'dq', p.cnt['dq'])
        p.op('gpsimd', lambda e: e.collective_compute(
            "AllGather", ALU.bypass, replica_groups=RG,
            ins=[obnc[:, :].opt()], outs=[ogath[:, :].opt()]), inc='cc')
        cc_o = p.cnt['cc']
        p.wait('sync', 'cc', cc_o)
        p.wait('sync', 'te', p.cnt['te'])
        p.wait('sync', 'dhb', p.cnt['dhb'])
        cell = {}
        p.emit('sync', lambda e: cell.__setitem__('off', e.partition_id()))
        for c in range(NCH):
            def oload(e, c=c):
                src = ogath[128 * c:128 * (c + 1), bass.ts(cell['off'], TOK)]
                return e.dma_start(out=oown[:, TOK * c:TOK * (c + 1)], in_=src)
            p.op('sync', oload, inc='dhb', n=16)
        oo_done = p.cnt['dhb']

        def load_wo(dst, h, sem):
            for c in range(NCH):
                p.dma('gpsimd', dst[:, 128 * c:128 * (c + 1)],
                      WobT[128 * c:128 * (c + 1), 128 * h:128 * (h + 1)], sem)

        p.wait('gpsimd', 'te', p.cnt['te'])
        p.wait('gpsimd', 'dga', p.cnt['dga'])
        load_wo(fgw[0], 0, 'dga')
        prev_te = p.cnt['te']
        for oc in range(NCH):
            b = oc % 2
            sem_b = 'dga' if b == 0 else 'dgb'
            need = p.cnt[sem_b]
            if oc + 1 < NCH:
                nsem = 'dgb' if b == 0 else 'dga'
                p.wait('gpsimd', 'te', prev_te)
                p.wait('gpsimd', nsem, p.cnt[nsem])
                load_wo(fgw[1 - b], oc + 1, nsem)
            p.wait('tensor', sem_b, need)
            p.wait('tensor', 'dhb', oo_done)
            for g in range(NTG):
                p.wait('tensor', 've', p.cnt['ve'])
                for c in range(NCH):
                    p.op('tensor', lambda e, g=g, c=c, b=b: e.matmul(
                        PSa[g][:, :], fgw[b][:, 128 * c:128 * (c + 1)],
                        oown[:, TOK * c + 512 * g: TOK * c + 512 * (g + 1)],
                        start=(c == 0), stop=(c == NCH - 1)),
                        inc=('te' if c == NCH - 1 else None))
                p.wait('vector', 'te')
                p.op('vector', lambda e, oc=oc, g=g: e.tensor_add(
                    R[:, TOK * oc + 512 * g: TOK * oc + 512 * (g + 1)],
                    R[:, TOK * oc + 512 * g: TOK * oc + 512 * (g + 1)], PSa[g][:, :]), inc='ve')
            prev_te = p.cnt['te']

    for c in range(NCH):
        b = c % 2
        gsem = 'dga' if b == 0 else 'dgb'
        p.wait('gpsimd', gsem, p.cnt[gsem])
        p.wait('gpsimd', 'dhb', p.cnt['dhb'])
        p.dma('gpsimd', qwb[b][:, :], WqaT[128 * c:128 * (c + 1), :], gsem)
        p.wait('sync', gsem, p.cnt[gsem])
        p.dma('sync', wqa_bf[:, 128 * c:128 * (c + 1), :].transpose([1, 0, 2]),
              qwb[b].rearrange('p (h w) -> p h w', w=128)[:, :, :], 'dhb')
    for c in range(NCH):
        b = c % 2
        gsem = 'dga' if b == 0 else 'dgb'
        p.wait('gpsimd', gsem, p.cnt[gsem])
        p.wait('gpsimd', 'dhb', p.cnt['dhb'])
        p.dma('gpsimd', qwb[b][:, :], WoaT[128 * c:128 * (c + 1), :], gsem)
        p.wait('sync', gsem, p.cnt[gsem])
        p.dma('sync', woa_bf[:, 128 * c:128 * (c + 1), :].transpose([1, 0, 2]),
              qwb[b].rearrange('p (h w) -> p h w', w=128)[:, :, :], 'dhb')
    wqa_done = p.cnt['dhb']
    qsteps = quant_steps() if 'q' in phases else []
    bank_weights_prep()
    layernorm()
    tap_xln = mktap('xln', [128, NCH * TOK], BF16)
    if tap_xln is not None:
        p.wait('gpsimd', 've')
        p.dma('gpsimd', tap_xln[:, :], xln[:, :], 'st')
    if 'a' in phases:
        attn_a(qsteps)
    else:
        for st in qsteps:
            st()
    if 'f' in phases:
        layernorm()
        ffn()
    if 'b' in phases:
        layernorm()
        attn_b()
    if 'g' in phases:
        layernorm()
        ffn()

    p.wait('sync', 've')
    p.wait('sync', 'te')
    for c in range(NCH):
        p.dma('sync', y[128 * c:128 * (c + 1), :], R[:, TOK * c:TOK * (c + 1)], 'st')
    p.wait('gpsimd', 'st')
    p.build()
    return nc


def kernel(**inputs):
    nc = build_nc(phases=('q', 'a', 'f', 'b', 'g'))
    in_maps = [host_inputs(inputs, c) for c in range(NCORES)]
    res = run_bass_kernel_spmd(nc, in_maps, core_ids=list(range(NCORES)))
    out = np.concatenate([r['y'].T for r in res.results], axis=0)
    return out.reshape(1, S, D).astype(np.float32)


# version 17
# speedup vs baseline: 1.2059x; 1.0109x over previous
import sys, os
if '/opt/trn_rl_repo' not in sys.path:
    sys.path.insert(0, '/opt/trn_rl_repo')
import numpy as np
import ml_dtypes
import concourse.bass as bass
import concourse.mybir as mybir
from concourse.bass_utils import run_bass_kernel_spmd
from concourse import dve_ops

F32 = mybir.dt.float32
BF16 = mybir.dt.bfloat16
AF = mybir.ActivationFunctionType
ALU = mybir.AluOpType
BF = ml_dtypes.bfloat16

NCORES = 8
S = 8192
D = 2048
TOK = S // NCORES
NCH = D // 128
NTG = TOK // 512
HA = 16
NBK = 4
HB = 32
HBL = HB // NCORES
HD = 64
FH = 8192
FH2 = 16384
LNEPS = 1e-5
RMSEPS = 1.1920929e-7
NEGBIG = -1.0e9


def _c(a):
    return np.ascontiguousarray(a)


def host_inputs(inp, core):
    d = {}
    x = np.asarray(inp['L0_attn_a_hidden_states'])
    sl = slice(TOK * core, TOK * (core + 1))
    d['xT'] = _c(x[0, sl, :].T)

    d['WqaT'] = _c(np.asarray(inp['L0_attn_a_Wq']).T)
    d['WoaT'] = _c(np.asarray(inp['L0_attn_a_Wo']).T)
    mem = np.asarray(inp['L0_attn_a_memories'])
    d['memH'] = _c(mem.transpose(1, 0, 2, 3))
    mn = np.asarray(inp['L0_attn_a_memory_norms'])
    d['mnT'] = _c(mn.transpose(1, 2, 0).reshape(128 * HA // 128, 128, NBK)
                  .reshape(HA, 128, NBK))
    d['mnFlat'] = _c(mn.reshape(NBK, HA * 128))
    bw = np.asarray(inp['L0_attn_a_bank_weights'])
    d['bwT'] = _c(bw.T)

    hsl = slice(HBL * HD * core, HBL * HD * (core + 1))
    d['Wqkvf'] = _c(np.concatenate([
        np.asarray(inp['L0_attn_b_Wq'])[hsl, :].T,
        np.asarray(inp['L0_attn_b_Wk'])[hsl, :].T,
        np.asarray(inp['L0_attn_b_Wv'])[hsl, :].T,
        np.asarray(inp['L0_attn_b_Wf'])[HBL * core: HBL * (core + 1), :].T,
    ], axis=1))
    bfv = np.asarray(inp['L0_attn_b_bf'])[HBL * core: HBL * (core + 1)]
    bff = np.zeros((128, 1), dtype=np.float32)
    for grp in range(4):
        bff[32 * grp:32 * grp + 4, 0] = bfv
    d['bf_full'] = bff
    d['WobT'] = _c(np.asarray(inp['L0_attn_b_Wo']).T)

    wguT = np.asarray(inp['L0_ffn_w_gate_up']).T
    d['wguS'] = _c(wguT[256 * core: 256 * (core + 1), :])
    wpT = np.asarray(inp['L0_ffn_w_proj']).T
    d['wpS'] = _c(wpT[1024 * core: 1024 * (core + 1), :])

    d['ident_f32'] = np.eye(128, dtype=np.float32)
    d['ones_f32'] = np.ones((128, 512), dtype=np.float32)
    d['ones_bf'] = np.ones((128, 512), dtype=BF)
    tri = np.tril(np.ones((128, 128), dtype=np.float32), -1)
    d['Utri_bf'] = tri.astype(BF)
    d['negI_bf'] = (NEGBIG * np.eye(128, dtype=np.float32)).astype(BF)
    d['trif'] = NEGBIG * tri
    sel = np.zeros((4, 512), dtype=np.float32)
    for n in range(4):
        sel[n, 128 * n:128 * (n + 1)] = 1.0
    d['selG'] = sel.astype(BF)
    ind2 = np.zeros((128, 2), dtype=np.float32)
    ind2[:64, 0] = 1.0
    ind2[64:, 1] = 1.0
    d['ind2'] = ind2.astype(BF)
    e2 = np.zeros((2, 128), dtype=np.float32)
    e2[0, :64] = 1.0
    e2[1, 64:] = 1.0
    d['e2_bf'] = e2.astype(BF)
    d['e2_f32'] = e2
    cst = np.zeros((128, 8), dtype=np.float32)
    cst[:, 0] = LNEPS
    cst[:, 1] = 1.0
    cst[:, 2] = RMSEPS
    d['cst'] = cst
    return d


class P:
    def __init__(self, nc):
        self.nc = nc
        self.streams = {k: [] for k in ('tensor', 'vector', 'scalar', 'sync', 'gpsimd')}
        self.sem = {}
        self.cnt = {}
        from contextlib import ExitStack
        self.stack = ExitStack()

    def mksem(self, *names):
        for n in names:
            self.sem[n] = self.stack.enter_context(self.nc.semaphore(n))
            self.cnt[n] = 0

    def emit(self, eng, f):
        self.streams[eng].append(f)

    def wait(self, eng, sname, val=None):
        v = self.cnt[sname] if val is None else val
        s = self.sem[sname]
        if v > 0:
            self.emit(eng, lambda e, s=s, v=v: e.wait_ge(s, v))
        return v

    def op(self, eng, f, inc=None, n=1, drain=True):
        if drain and eng in ('vector', 'scalar'):
            self.emit(eng, lambda e: e.drain())
        if inc is not None:
            self.cnt[inc] += n
            s = self.sem[inc]
            self.emit(eng, lambda e, f=f, s=s, n=n: f(e).then_inc(s, n))
        else:
            self.emit(eng, lambda e, f=f: f(e))

    def dma(self, q, out, in_, sname):
        self.op(q, lambda e, o=out, i=in_: e.dma_start(out=o, in_=i), inc=sname, n=16,
                drain=False)
        return self.cnt[sname]

    def build(self):
        nc = self.nc
        with nc.Block() as block:
            @block.tensor
            def _(e):
                for f in self.streams['tensor']:
                    f(e)

            @block.vector
            def _(e):
                for f in self.streams['vector']:
                    f(e)

            @block.scalar
            def _(e):
                for f in self.streams['scalar']:
                    f(e)

            @block.sync
            def _(e):
                for f in self.streams['sync']:
                    f(e)

            @block.gpsimd
            def _(e):
                for f in self.streams['gpsimd']:
                    f(e)
        self.stack.close()


def build_nc(phases=('q', 'a', 'f'), debug_taps=()):
    nc = bass.Bass("TRN2", target_bir_lowering=False, debug=False)
    p = P(nc)
    p.mksem('dh', 'dha', 'dhb', 'dga', 'dgb', 'dgc', 'dsa', 'dsb', 'dso', 'te', 've', 'sc', 'cc', 'st', 'dq')

    io = {}
    def din(name, shape, dtype=F32):
        io[name] = nc.dram_tensor(name, list(shape), dtype, kind="ExternalInput").ap()
        return io[name]

    xT = din('xT', [D, TOK])
    WqaT = din('WqaT', [D, D]); WoaT = din('WoaT', [D, D])
    memH = din('memH', [HA, NBK, 128, 128]); mnT = din('mnT', [HA, 128, NBK])
    mnFlat = din('mnFlat', [NBK, HA * 128]); bwT = din('bwT', [NBK, HA])
    Wqkvf = din('Wqkvf', [D, 772]); bf_full = din('bf_full', [128, 1])
    WobT = din('WobT', [D, D])
    wguS = din('wguS', [256, FH2]); wpS = din('wpS', [1024, D])
    ident_f32 = din('ident_f32', [128, 128])
    ones_f32 = din('ones_f32', [128, 512]); ones_bf = din('ones_bf', [128, 512], BF16)
    Utri_bf = din('Utri_bf', [128, 128], BF16); negI_bf = din('negI_bf', [128, 128], BF16)
    trif = din('trif', [128, 128])
    selG = din('selG', [4, 512], BF16)
    ind2 = din('ind2', [128, 2], BF16); e2_bf = din('e2_bf', [2, 128], BF16)
    e2_f32 = din('e2_f32', [2, 128]); cst = din('cst', [128, 8])

    y = nc.dram_tensor('y', [D, TOK], F32, kind="ExternalOutput").ap()
    wqgu_sh = nc.dram_tensor('wqgu_sh', [128, 128, 2, 128], BF16).ap()
    wqgu_all = nc.dram_tensor('wqgu_all', [8 * 128, 128, 2, 128], BF16, addr_space='Shared').ap()
    wqp_sh = nc.dram_tensor('wqp_sh', [8, 4, 128, 512], BF16).ap()
    wqp_all = nc.dram_tensor('wqp_all', [64, 4, 128, 512], BF16, addr_space='Shared').ap()
    xsha = nc.dram_tensor('xsha', [D, TOK // 2], BF16).ap()
    xshb = nc.dram_tensor('xshb', [D, TOK // 2], BF16).ap()
    xga = nc.dram_tensor('xga', [8 * D, TOK // 2], BF16, addr_space='Shared').ap()
    xgb = nc.dram_tensor('xgb', [8 * D, TOK // 2], BF16, addr_space='Shared').ap()
    obnc = nc.dram_tensor('obnc', [256, S], BF16).ap()
    ogath = nc.dram_tensor('ogath', [8 * 256, S], BF16, addr_space='Shared').ap()
    wqkvf_bf = nc.dram_tensor('wqkvf_bf', [D, 772], BF16).ap()
    wqa_bf = nc.dram_tensor('wqa_bf', [HA, D, 128], BF16).ap()
    woa_bf = nc.dram_tensor('woa_bf', [NCH, D, 128], BF16).ap()
    taps = {}
    for t in debug_taps:
        pass

    def mktap(name, shape, dtype=F32):
        if name in debug_taps:
            taps[name] = nc.dram_tensor('tap_' + name, list(shape), dtype,
                                        kind="ExternalOutput").ap()
            return taps[name]
        return None

    SB = {}
    SBASE = 16512
    def sb(name, shape, dtype, off):
        h = nc.alloc_sbuf_tensor_at(name, list(shape), dtype, offset=SBASE + off)
        SB[name] = h
        return h.ap()

    R = sb('R', [128, NCH * TOK], F32, 0)
    xln = sb('xln', [128, NCH * TOK], BF16, 65536)
    comb = sb('comb', [128, HA * TOK], BF16, 98304)
    hbuf = sb('hbuf', [128, (FH // 128) * 512], BF16, 98304)
    Ksb = [sb(f'Ksb{i}', [128, S], BF16, 98816 + 16384 * i) for i in range(2)]
    Qsb = [sb(f'Qsb{i}', [128, S], BF16, 131584 + 16384 * i) for i in range(2)]
    Vsb = [sb(f'Vsb{i}', [128, (S // 128) * 130], BF16, 65536 + 16640 * i) for i in range(2)]
    W0 = 163840
    wab = [sb(f'wab{i}', [128, NCH * 128], BF16, W0 + 4096 * i) for i in range(2)]
    mem_sb = [sb(f'mem_sb{i}', [128, NBK * 128], BF16, W0 + 8192 + 1024 * i) for i in range(2)]
    mnT_sb = sb('mnT_sb', [128, HA * NBK], BF16, W0 + 10240)
    smallc = W0 + 10496
    bw_sb = sb('bw_sb', [4, HA], F32, smallc)
    wact4 = sb('wact4', [4, HA], F32, smallc + 128)
    act_sb = sb('act_sb', [4, 1], F32, smallc + 320)
    g_sb = sb('g_sb', [4, 512], BF16, smallc + 384)
    ident_sb = sb('ident_sb', [128, 128], F32, smallc + 1408)
    onesf_sb = sb('onesf_sb', [128, 512], F32, smallc + 1920)
    onesb_sb = sb('onesb_sb', [128, 512], BF16, smallc + 3968)
    selG_sb = sb('selG_sb', [4, 512], BF16, smallc + 4992)
    utri_sb = sb('utri_sb', [128, 128], BF16, smallc + 6144)
    negI_sb = sb('negI_sb', [128, 128], BF16, smallc + 6400)
    trif_sb = sb('trif_sb', [128, 128], F32, 172288)
    cst_sb = sb('cst_sb', [128, 8], F32, smallc + 6016)
    lnc = smallc + 6656
    mnf_sb = sb('mnf_sb', [4, HA * 128], F32, lnc)
    xb_sb = [sb(f'xb_sb{i}', [128, 512], BF16, lnc + 1024 * i) for i in range(2)]
    xsq_sb = [sb(f'xsq_sb{i}', [128, 512], BF16, lnc + 2048 + 1024 * i) for i in range(2)]
    sigp = [sb(f'sigp{i}', [128, 512], BF16, lnc + 10240 + 1024 * i) for i in range(2)]
    row_sb = sb('row_sb', [1, 512 * 5], F32, lnc + 4096)
    tmpa = sb('tmpa', [128, 512], F32, lnc + 14336)
    tmpb = sb('tmpb', [128, 512], F32, lnc + 16384)
    qwb = [sb(f'qwb{i}', [128, 2048], BF16, 131840 + 4096 * i) for i in range(2)]
    qt1 = sb('qt1', [128, 512], BF16, 131840 + 8192)
    qt2 = [sb('qt2a', [128, 512], BF16, 131840 + 9216), sb('qt2b', [128, 512], BF16, 131840 + 22016)]
    qt3 = sb('qt3', [128, 512], BF16, 131840 + 10240)
    qsd = sb('qsd', [2, 512], BF16, 131840 + 11264)
    qrcp = sb('qrcp', [2, 512], F32, 131840 + 12288)
    qsdf = sb('qsdf', [2, 512], F32, 131840 + 15872)
    qscr = sb('qscr', [2, 512], F32, 131840 + 17920)
    ascr = sb('ascr', [128, 512], F32, 131840 + 19968)
    ind2_sb = sb('ind2_sb', [128, 2], BF16, 131840 + 14336)
    e2b_sb = sb('e2b_sb', [2, 128], BF16, 131840 + 14848)
    e2f_sb = sb('e2f_sb', [2, 128], F32, 131840 + 15360)
    fgw = [sb('fgw0', [128, 2048], BF16, 0 + 163840), sb('fgw1', [128, 2048], BF16, 4096 + 163840),
           sb('fgw2', [128, 2048], BF16, 203520), sb('fgw3', [128, 2048], BF16, 207616)]
    fpw = [sb('fpw0', [128, 512], BF16, 172032), sb('fpw1', [128, 512], BF16, 173056)]
    sil_sb = [sb('sil_sb0', [128, 512], BF16, lnc + 18432), sb('sil_sb1', [128, 512], BF16, lnc + 19456)]
    hsq_sb = [sb('hsq_sb0', [128, 512], BF16, lnc + 20480), sb('hsq_sb1', [128, 512], BF16, lnc + 21504)]
    xat = [sb(f'xat{i}', [128, 512], BF16, lnc + 1024 * i) for i in range(4)]
    Ptil = sb('Ptil', [128, 2048], F32, lnc + 14336)
    zz8 = sb('zz8', [128, 2048], F32, lnc + 4096)
    pcum = sb('pcum', [128, 2048], F32, 203520)
    wst = [sb(f'wst{i}', [128, 772], BF16, lnc + 4096 + 1664 * i) for i in range(2)]
    ptb = [sb(f'ptb{i}', [128, 512], BF16, lnc + 10240 + 1024 * i) for i in range(6)]
    rank1A = sb('rank1A', [128, 512], F32, 167936)
    rank1B = sb('rank1B', [128, 512], F32, 170240)
    fq3w = [sb(f'fq3w{b}', [3, 1024], BF16, 164352 + 2048 * b) for b in range(2)]
    fqbw = sb('fqbw', [128, 1024], F32, lnc + 16384)
    hiT = sb('hiT', [97, 512], BF16, 168448)
    miT = sb('miT', [97, 512], BF16, 169472)
    rcpb = sb('rcpb', [128, 512], F32, lnc)
    rcpbB = sb('rcpbB', [128, 512], F32, lnc + 2048)
    dvb = sb('dvb', [128, 512], F32, lnc + 4096)
    dvb2 = sb('dvb2', [128, 512], F32, lnc + 6144)
    onrm_sb = sb('onrm_sb', [128, 512], BF16, lnc + 8192)
    onrm2_sb = sb('onrm2_sb', [128, 512], BF16, lnc + 9216)
    fct = sb('fct', [128, 256], F32, 211712)
    bfc_sb = sb('bfc_sb', [128, 1], F32, 212736)
    initc = sb('initc', [128, 1], F32, 212768)
    ftmp = sb('ftmp', [128, 4], F32, 174080)
    ft2 = sb('ft2', [4, 128], F32, lnc + 12288)
    oown = sb('oown', [128, NCH * 1024], BF16, 65536)
    assert 16512 + 212800 <= 229344

    PB = [nc.alloc_psum_tensor(f'PB{i}', [128, 1024], F32) for i in range(3)]
    PS = [nc.alloc_psum_tensor(f'PS{i}', [128, 512], F32) for i in range(2)]
    PSa = [PB[0].ap()[:, 0:512], PB[0].ap()[:, 512:1024],
           PB[1].ap()[:, 0:512], PB[1].ap()[:, 512:1024],
           PB[2].ap()[:, 0:512], PB[2].ap()[:, 512:1024]] + [t.ap() for t in PS]
    PBa = [t.ap() for t in PB]

    for dst, src in ((ident_sb, ident_f32), (onesf_sb, ones_f32), (onesb_sb, ones_bf),
                     (selG_sb, selG), (bw_sb, bwT), (mnf_sb, mnFlat), (cst_sb, cst),
                     (utri_sb, Utri_bf), (negI_sb, negI_bf)):
        p.dma('sync', dst[:, :], src[:, :], 'dh')
    for h in range(HA):
        p.dma('gpsimd', mnT_sb[:, NBK * h:NBK * (h + 1)], mnT[h, :, :], 'dgc')
    p.dma('gpsimd', bfc_sb[:, :], bf_full[:, :], 'dgc')
    p.dma('gpsimd', ind2_sb[:, :], ind2[:, :], 'dgc')
    p.dma('gpsimd', e2b_sb[:, :], e2_bf[:, :], 'dgc')
    p.dma('gpsimd', e2f_sb[:, :], e2_f32[:, :], 'dgc')
    for c in range(NCH):
        p.dma('sync', R[:, TOK * c:TOK * (c + 1)], xT[128 * c:128 * (c + 1), :], 'dh')
    c_loaded = p.cnt['dh']
    g_loaded = p.cnt['dgc']

    def rcp_fast(out, in_, scratch, inc=None):
        p.op('vector', lambda e: e.reciprocal_approx_fast(out=scratch, in_=in_))
        p.op('vector', lambda e: e._custom_dve(
            dve_ops.RECIPROCAL_APPROX_NR, out=out, in0=in_, in1=scratch, s0=2.0), inc=inc)

    def bank_weights_prep():
        p.wait('vector', 'dh', c_loaded)
        p.wait('vector', 'dgc', g_loaded)
        p.op('vector', lambda e: e.reduce_sum(tmpb[0:4, 0:1], mnf_sb[:, :], mybir.AxisListType.X))
        p.op('vector', lambda e: e.tensor_single_scalar(act_sb[:, :], tmpb[0:4, 0:1], 1e-6, ALU.is_ge), inc='ve')
        p.wait('scalar', 'dh', c_loaded)
        p.op('scalar', lambda e: e.activation(tmpa[0:4, 0:HA], bw_sb[:, :], AF.Exp), inc='sc')
        p.wait('tensor', 'sc')
        p.op('tensor', lambda e: e.matmul(PSa[6][0:1, 0:HA], onesf_sb[0:4, 0:1], tmpa[0:4, 0:HA],
                                          start=True, stop=True), inc='te')
        p.wait('vector', 'te')
        p.op('vector', lambda e: e.reciprocal(row_sb[0:1, 0:HA], PSa[6][0:1, 0:HA]), inc='ve')
        p.wait('tensor', 've')
        p.op('tensor', lambda e: e.matmul(PSa[7][0:4, 0:HA], onesf_sb[0:1, 0:4], row_sb[0:1, 0:HA],
                                          start=True, stop=True), inc='te')
        p.wait('vector', 'te')
        p.op('vector', lambda e: e.scalar_tensor_tensor(wact4[:, :], tmpa[0:4, 0:HA], act_sb[:, :],
                                                        PSa[7][0:4, 0:HA], ALU.mult, ALU.mult), inc='ve')

    def layernorm():
        te0 = p.cnt['te']
        p.wait('vector', 'dhb', p.cnt['dhb'])
        for g in range(NTG):
            gv0 = p.cnt['ve']
            for c in range(NCH):
                b = c % 2
                p.wait('vector', 'te', p.cnt['te'])
                p.op('vector', lambda e, c=c, g=g, b=b: e.tensor_copy(
                    xb_sb[b][:, :], R[:, TOK * c + 512 * g: TOK * c + 512 * (g + 1)]), inc='ve')
                p.wait('scalar', 've')
                p.op('scalar', lambda e, b=b: e.activation(xsq_sb[b][:, :], xb_sb[b][:, :], AF.Square), inc='sc')
                p.wait('tensor', 've')
                p.wait('tensor', 'sc')
                p.op('tensor', lambda e, b=b, c=c: e.matmul(
                    PSa[6][0:1, :], onesb_sb[:, 0:1], xb_sb[b][:, :], start=(c == 0), stop=(c == NCH - 1)))
                p.op('tensor', lambda e, b=b, c=c: e.matmul(
                    PSa[7][0:1, :], onesb_sb[:, 0:1], xsq_sb[b][:, :], start=(c == 0), stop=(c == NCH - 1)),
                    inc='te')
            p.wait('vector', 'te')
            m = row_sb[0:1, 0:512]; rstd = row_sb[0:1, 512:1024]; mr = row_sb[0:1, 1024:1536]
            var = row_sb[0:1, 1536:2048]; m2 = row_sb[0:1, 2048:2560]
            p.op('vector', lambda e: e.tensor_scalar_mul(m, PSa[6][0:1, :], 1.0 / D))
            p.op('vector', lambda e: e.tensor_scalar_mul(var, PSa[7][0:1, :], 1.0 / D))
            p.op('vector', lambda e: e.tensor_mul(m2, m, m))
            p.op('vector', lambda e: e.tensor_sub(var, var, m2), inc='ve')
            p.wait('scalar', 've')
            p.op('scalar', lambda e: e.activation(var, var, AF.Ln, bias=cst_sb[0:1, 0:1]))
            p.op('scalar', lambda e: e.activation(rstd, var, AF.Exp, scale=-0.5), inc='sc')
            p.wait('vector', 'sc')
            p.op('vector', lambda e: e.tensor_mul(mr, m, rstd), inc='ve')
            p.wait('tensor', 've')
            p.op('tensor', lambda e: e.matmul(PSa[4][:, :], onesf_sb[0:1, 0:128], rstd, start=True, stop=True))
            p.op('tensor', lambda e: e.matmul(PSa[5][:, :], onesf_sb[0:1, 0:128], mr, start=True, stop=True),
                 inc='te')
            p.wait('vector', 'te')
            for c in range(NCH):
                p.op('vector', lambda e, c=c, g=g: e.tensor_mul(
                    tmpa[:, :], R[:, TOK * c + 512 * g: TOK * c + 512 * (g + 1)], PSa[4][:, :]))
                p.op('vector', lambda e, c=c, g=g: e.tensor_sub(
                    xln[:, TOK * c + 512 * g: TOK * c + 512 * (g + 1)], tmpa[:, :], PSa[5][:, :]), inc='ve')

    def attn_a(qsteps=()):
        qsteps = list(qsteps)
        def load_w(dst, srcB, h, sem):
            p.dma('sync', dst[:, :],
                  srcB[h, :, :].rearrange('(c p) w -> p c w', p=128)[:, :, :], sem)

        def load_mem(dst, h, sem):
            for n in range(NBK):
                p.dma('gpsimd', dst[:, 128 * n:128 * (n + 1)], memH[h, n, :, :], sem)

        p.wait('sync', 'dhb', wqa_done)
        p.wait('sync', 'dha', p.cnt['dha'])
        load_w(wab[0], wqa_bf, 0, 'dha')
        p.wait('gpsimd', 'dga', p.cnt['dga'])
        load_mem(mem_sb[0], 0, 'dga')
        xln_ready = p.cnt['ve']
        prev_te = 0
        sig_done = {}
        for h in range(HA):
            b = h % 2
            sem_b = 'dha' if b == 0 else 'dhb'
            gsem_b = 'dga' if b == 0 else 'dgb'
            need = p.cnt[sem_b]
            gneed = p.cnt[gsem_b]
            if h + 1 < HA:
                nsem = 'dhb' if b == 0 else 'dha'
                gsem = 'dgb' if b == 0 else 'dga'
                p.wait('sync', 'te', prev_te)
                p.wait('sync', nsem, p.cnt[nsem])
                load_w(wab[1 - b], wqa_bf, h + 1, nsem)
                p.wait('gpsimd', 'te', prev_te)
                p.wait('gpsimd', gsem, p.cnt[gsem])
                load_mem(mem_sb[1 - b], h + 1, gsem)
            p.wait('tensor', sem_b, need)
            p.wait('tensor', gsem_b, gneed)
            p.wait('tensor', 've', max(sig_done.get(h - 1, 0), xln_ready))
            for g in range(NTG):
                for c in range(NCH):
                    p.op('tensor', lambda e, g=g, c=c, b=b: e.matmul(
                        PSa[g][:, :], wab[b][:, 128 * c:128 * (c + 1)],
                        xln[:, TOK * c + 512 * g: TOK * c + 512 * (g + 1)],
                        start=(c == 0), stop=(c == NCH - 1)),
                        inc=('te' if (g == NTG - 1 and c == NCH - 1) else None))
            te_q = p.cnt['te']
            for g in range(NTG):
                p.wait('vector', 'te', te_q)
                p.op('vector', lambda e, g=g: e.tensor_single_scalar(tmpa[:, :], PSa[g][:, :], 0.0, ALU.min),
                     inc='ve')
                p.wait('scalar', 've')
                p.op('scalar', lambda e: e.activation(tmpb[:, :], tmpa[:, :], AF.Exp), inc='sc')
                p.wait('vector', 'sc')
                p.op('vector', lambda e, g=g: e.scalar_tensor_tensor(
                    xb_sb[g][:, :], PSa[g][:, :], 0.0, tmpb[:, :], ALU.max, ALU.add), inc='ve')
                ve_sig = p.cnt['ve']
                if g == NTG - 1:
                    sig_done[h] = ve_sig
                p.wait('tensor', 've', ve_sig)
                p.op('tensor', lambda e, g=g, h=h: e.matmul(
                    PSa[6][0:4, :], mnT_sb[:, NBK * h: NBK * (h + 1)], xb_sb[g][:, :],
                    start=True, stop=True), inc='te')
                te_n = p.cnt['te']
                p.wait('vector', 'te', te_n)
                p.op('vector', lambda e: e.tensor_single_scalar(tmpa[0:4, :], PSa[6][0:4, :], 1e-6, ALU.max),
                     inc='ve')
                p.wait('scalar', 've')
                p.op('scalar', lambda e: e.activation(ascr[0:4, :], tmpa[0:4, :], AF.Ln))
                p.op('scalar', lambda e: e.activation(tmpb[0:4, :], ascr[0:4, :], AF.Exp,
                                                      scale=-1.0), inc='sc')
                p.wait('vector', 'sc')
                p.op('vector', lambda e, h=h: e.tensor_scalar_mul(
                    g_sb[:, :], tmpb[0:4, :], wact4[0:4, h:h + 1]), inc='ve')
                ve_g = p.cnt['ve']
                sigp_ve = {}
                for n in range(NBK):
                    pb = 4 + (n % 2)
                    p.wait('tensor', 've', ve_g if n == 0 else sigp_ve.get(n - 2, 0))
                    p.op('tensor', lambda e, n=n, pb=pb: e.matmul(
                        PSa[pb][:, :], selG_sb[:, 128 * n:128 * (n + 1)], g_sb[:, :],
                        start=True, stop=True), inc='te')
                    p.wait('vector', 'te')
                    p.op('vector', lambda e, g=g, pb=pb, n=n: e.tensor_mul(
                        sigp[n % 2][:, :], xb_sb[g][:, :], PSa[pb][:, :]), inc='ve')
                    sigp_ve[n] = p.cnt['ve']
                    p.wait('tensor', 've', sigp_ve[n])
                    p.op('tensor', lambda e, g=g, n=n, b=b: e.matmul(
                        PSa[2 + g][:, :], mem_sb[b][:, 128 * n:128 * (n + 1)], sigp[n % 2][:, :],
                        start=(n == 0), stop=(n == NBK - 1)), inc='te')
                p.wait('vector', 'te')
                p.op('vector', lambda e, g=g, h=h: e.tensor_copy(
                    comb[:, TOK * h + 512 * g: TOK * h + 512 * (g + 1)], PSa[2 + g][:, :]), inc='ve')
            prev_te = p.cnt['te']
            for _ in range(2):
                if qsteps:
                    qsteps.pop(0)()
        while qsteps:
            qsteps.pop(0)()
        comb_done = p.cnt['ve']
        p.wait('sync', 'te', p.cnt['te'])
        p.wait('sync', 'dha', p.cnt['dha'])
        load_w(wab[0], woa_bf, 0, 'dha')
        prev_te = 0
        for oc in range(NCH):
            b = oc % 2
            sem_b = 'dha' if b == 0 else 'dhb'
            need = p.cnt[sem_b]
            if oc + 1 < NCH:
                nsem = 'dhb' if b == 0 else 'dha'
                p.wait('sync', 'te', prev_te)
                p.wait('sync', nsem, p.cnt[nsem])
                load_w(wab[1 - b], woa_bf, oc + 1, nsem)
            p.wait('tensor', sem_b, need)
            p.wait('tensor', 've', comb_done)
            for g in range(NTG):
                p.wait('tensor', 've', p.cnt['ve'])
                for c in range(NCH):
                    p.op('tensor', lambda e, g=g, c=c, b=b: e.matmul(
                        PSa[g][:, :], wab[b][:, 128 * c:128 * (c + 1)],
                        comb[:, TOK * c + 512 * g: TOK * c + 512 * (g + 1)],
                        start=(c == 0), stop=(c == NCH - 1)),
                        inc=('te' if c == NCH - 1 else None))
                p.wait('vector', 'te')
                p.op('vector', lambda e, oc=oc, g=g: e.tensor_add(
                    R[:, TOK * oc + 512 * g: TOK * oc + 512 * (g + 1)],
                    R[:, TOK * oc + 512 * g: TOK * oc + 512 * (g + 1)], PSa[g][:, :]), inc='ve')
            prev_te = p.cnt['te']

    def quant_steps():
        RG = [list(range(NCORES))]
        steps = []

        def do_piece(bufb, out_fn):
            dso_mark = {-1: p.cnt['dso']}
            for st in range(4):
                qt2c = qt2[st % 2]
                wsl = qwb[bufb][:, 512 * st:512 * (st + 1)]
                p.wait('vector', 'dso', dso_mark.get(st - 2, dso_mark.get(-1, 0)))
                p.op('vector', lambda e: e.nop(), inc='ve')
                p.wait('scalar', 've')
                p.wait('scalar', 'te', p.cnt['te'])
                p.op('scalar', lambda e, wsl=wsl: e.activation(qt1[:, :], wsl, AF.Abs),
                     inc='sc', drain=False)
                p.wait('tensor', 'sc')
                p.op('tensor', lambda e: e.matmul(PSa[6][0:2, :], ind2_sb[:, :], qt1[:, :],
                                                  start=True, stop=True), inc='te')
                p.wait('vector', 'te')
                p.op('vector', lambda e: e.tensor_scalar(
                    qsd[:, :], PSa[6][0:2, :], 1.0 / 64, 1e-8, ALU.mult, ALU.max), inc='ve')
                p.wait('scalar', 've')
                p.op('scalar', lambda e: e.activation(qsdf[:, :], qsd[:, :], AF.Ln), drain=False)
                p.op('scalar', lambda e: e.activation(qrcp[:, :], qsdf[:, :], AF.Exp,
                                                      scale=-1.0), inc='sc')
                p.wait('tensor', 've')
                p.wait('tensor', 'sc')
                p.op('tensor', lambda e: e.matmul(PSa[4][:, :], e2f_sb[:, :], qrcp[:, :],
                                                  start=True, stop=True))
                p.op('tensor', lambda e: e.matmul(PSa[5][:, :], e2b_sb[:, :], qsd[:, :],
                                                  start=True, stop=True), inc='te')
                p.wait('vector', 'te')
                p.op('vector', lambda e, wsl=wsl, qt2c=qt2c: e.tensor_mul(qt2c[:, :], wsl, PSa[4][:, :]))
                p.op('vector', lambda e, qt2c=qt2c: e.tensor_single_scalar(
                    qt1[:, :], qt2c[:, :], -0.5, ALU.is_lt))
                p.op('vector', lambda e, qt2c=qt2c: e.scalar_tensor_tensor(
                    qt3[:, :], qt2c[:, :], 0.5, qt1[:, :], ALU.is_gt, ALU.subtract))
                p.op('vector', lambda e, qt2c=qt2c: e.tensor_mul(
                    qt2c[:, :], qt3[:, :], PSa[5][:, :]), inc='ve')
                p.wait('scalar', 've')
                p.wait('scalar', 'dso', p.cnt['dso'])
                out_fn(st, qt2c)
                dso_mark[st] = p.cnt['dso']

        def gu_out(rc, cp, st, qt2c):
            hc0 = (2048 * cp + 512 * st) // 128
            dst = wqgu_sh[hc0:hc0 + 4, :, rc, :].transpose([1, 0, 2])
            p.dma('scalar', dst, qt2c.rearrange('p (a h) -> p a h', h=128)[:, :, :], 'dso')

        def p_out(rc, st, qt2c):
            p.dma('scalar', wqp_sh[rc, st, :, :], qt2c[:, :], 'dso')

        def precast(c0, c1):
            p.wait('gpsimd', 'dh', p.cnt['dh'])
            p.wait('gpsimd', 've', p.cnt['ve'])
            for c in range(c0, c1):
                b = c % 2
                sem = 'dga' if b == 0 else 'dgb'
                p.wait('gpsimd', sem, p.cnt[sem])
                p.wait('gpsimd', 'dhb', p.cnt['dhb'])
                p.dma('gpsimd', wst[b][:, :], Wqkvf[128 * c:128 * (c + 1), :], sem)
                p.wait('sync', sem, p.cnt[sem])
                p.dma('sync', wqkvf_bf[128 * c:128 * (c + 1), :], wst[b][:, :], 'dhb')
        steps.append(lambda: precast(0, 8))
        steps.append(lambda: precast(8, 16))

        pieces = [(rc, cp) for rc in range(2) for cp in range(8)]

        def gu_step(i):
            rc, cp = pieces[i]
            b = i % 2
            sem_b = 'dga' if b == 0 else 'dgb'
            if i == 0:
                p.wait('tensor', 'dgc', g_loaded)
                p.wait('gpsimd', 'dga', p.cnt['dga'])
                p.wait('gpsimd', 'dhb', p.cnt['dhb'])
                p.dma('gpsimd', qwb[0][:, :], wguS[0:128, 0:2048], 'dga')
            need = p.cnt[sem_b]
            if i + 1 < len(pieces):
                nrc, ncp = pieces[i + 1]
                nsem = 'dgb' if b == 0 else 'dga'
                p.wait('gpsimd', 've', p.cnt['ve'])
                p.wait('gpsimd', nsem, p.cnt[nsem])
                p.dma('gpsimd', qwb[1 - b][:, :],
                      wguS[128 * nrc:128 * (nrc + 1), 2048 * ncp:2048 * (ncp + 1)],
                      nsem)
            p.wait('vector', sem_b, need)
            do_piece(b, lambda st, qt2c, rc=rc, cp=cp: gu_out(rc, cp, st, qt2c))

        for i in range(len(pieces)):
            steps.append(lambda i=i: gu_step(i))

        def ag_gu():
            p.wait('gpsimd', 'dso', p.cnt['dso'])
            p.op('gpsimd', lambda e: e.collective_compute(
                "AllGather", ALU.bypass, replica_groups=RG,
                ins=[wqgu_sh[:, :, :, :].opt()], outs=[wqgu_all[:, :, :, :].opt()]), inc='cc')
        steps.append(ag_gu)

        def p_step(rc):
            b = rc % 2
            sem_b = 'dga' if b == 0 else 'dgb'
            if rc == 0:
                p.wait('gpsimd', 'dga', p.cnt['dga'])
                p.dma('gpsimd', qwb[0][:, :], wpS[0:128, :], 'dga')
            need = p.cnt[sem_b]
            if rc + 1 < 8:
                nsem = 'dgb' if b == 0 else 'dga'
                p.wait('gpsimd', 've', p.cnt['ve'])
                p.wait('gpsimd', nsem, p.cnt[nsem])
                p.dma('gpsimd', qwb[1 - b][:, :], wpS[128 * (rc + 1):128 * (rc + 2), :], nsem)
            p.wait('vector', sem_b, need)
            do_piece(b, lambda st, qt2c, rc=rc: p_out(rc, st, qt2c))

        for rc in range(8):
            steps.append(lambda rc=rc: p_step(rc))

        def ag_p():
            p.wait('gpsimd', 'dso', p.cnt['dso'])
            p.op('gpsimd', lambda e: e.collective_compute(
                "AllGather", ALU.bypass, replica_groups=RG,
                ins=[wqp_sh[:, :, :, :].opt()], outs=[wqp_all[:, :, :, :].opt()]), inc='cc')
        steps.append(ag_p)
        return steps

    def ffn():
        p.wait('sync', 'cc', 1)
        p.wait('gpsimd', 'cc', 1)
        p.wait('sync', 'te', p.cnt['te'])
        p.wait('gpsimd', 'te', p.cnt['te'])
        p.wait('vector', 'dhb', p.cnt['dhb'])
        p.wait('scalar', 'dhb', p.cnt['dhb'])
        p.wait('sync', 've', p.cnt['ve'])
        p.wait('gpsimd', 've', p.cnt['ve'])
        for T in range(NTG):
            ve_xln = p.cnt['ve']
            def load_gu(hcl, gbuf, ubuf, sems):
                for r in range(8):
                    q, sem = (('sync', sems[0]) if r < 3 else
                              ('scalar', sems[2]) if r < 6 else ('gpsimd', sems[1]))
                    p.dma(q, gbuf[:, 256 * r:256 * (r + 1)],
                          wqgu_all[128 * r + hcl, :, :, :], sem)
                    p.dma(q, ubuf[:, 256 * r:256 * (r + 1)],
                          wqgu_all[128 * r + hcl + 64, :, :, :], sem)

            sc_mark = {}
            ve_mark = {}
            rms_te = {}
            p.wait('sync', 'dha', p.cnt['dha'])
            p.wait('gpsimd', 'dga', p.cnt['dga'])
            p.wait('scalar', 'dsa', p.cnt['dsa'])
            load_gu(0, fgw[0], fgw[1], ('dha', 'dga', 'dsa'))
            for hcl in range(64):
                b = hcl % 2
                sems = ('dha', 'dga', 'dsa') if b == 0 else ('dhb', 'dgb', 'dsb')
                nh = p.cnt[sems[0]]
                ng = p.cnt[sems[1]]
                ns = p.cnt[sems[2]]
                if hcl + 1 < 64:
                    nsems = ('dhb', 'dgb', 'dsb') if b == 0 else ('dha', 'dga', 'dsa')
                    p.wait('sync', 'te', p.cnt['te'])
                    p.wait('gpsimd', 'te', p.cnt['te'])
                    p.wait('scalar', 'te', p.cnt['te'])
                    p.wait('sync', nsems[0], p.cnt[nsems[0]])
                    p.wait('gpsimd', nsems[1], p.cnt[nsems[1]])
                    p.wait('scalar', nsems[2], p.cnt[nsems[2]])
                    load_gu(hcl + 1, fgw[2 * (1 - b)], fgw[2 * (1 - b) + 1], nsems)
                p.wait('tensor', sems[0], nh)
                p.wait('tensor', sems[1], ng)
                p.wait('tensor', sems[2], ns)
                p.wait('tensor', 'sc', sc_mark[hcl - 2] if hcl >= 2 else 0)
                p.wait('tensor', 've', ve_mark[hcl - 2] if hcl >= 2 else ve_xln)
                for c in range(NCH):
                    p.op('tensor', lambda e, c=c, b=b, T=T: e.matmul(
                        PSa[2 * b][:, :], fgw[2 * b][:, 128 * c:128 * (c + 1)],
                        xln[:, TOK * c + 512 * T: TOK * c + 512 * (T + 1)],
                        start=(c == 0), stop=(c == NCH - 1)))
                for c in range(NCH):
                    p.op('tensor', lambda e, c=c, b=b, T=T: e.matmul(
                        PSa[2 * b + 1][:, :], fgw[2 * b + 1][:, 128 * c:128 * (c + 1)],
                        xln[:, TOK * c + 512 * T: TOK * c + 512 * (T + 1)],
                        start=(c == 0), stop=(c == NCH - 1)), inc=('te' if c == NCH - 1 else None))
                p.wait('scalar', 'te')
                if hcl >= 2:
                    p.wait('scalar', 've', ve_mark[hcl - 2])
                p.op('scalar', lambda e, b=b: e.activation(sil_sb[b][:, :], PSa[2 * b][:, :], AF.Silu),
                     inc='sc', drain=False)
                sc_mark[hcl] = p.cnt['sc']
                p.wait('vector', 'sc')
                p.op('vector', lambda e, b=b, hcl=hcl: e.tensor_mul(
                    hbuf[:, 512 * hcl:512 * (hcl + 1)], sil_sb[b][:, :], PSa[2 * b + 1][:, :]), drain=False)
                if hcl >= 2:
                    p.wait('vector', 'te', rms_te.get(hcl - 2, 0))
                p.op('vector', lambda e, b=b, hcl=hcl: e.tensor_mul(
                    hsq_sb[b][:, :], hbuf[:, 512 * hcl:512 * (hcl + 1)],
                    hbuf[:, 512 * hcl:512 * (hcl + 1)]), inc='ve')
                ve_mark[hcl] = p.cnt['ve']
                if hcl > 0:
                    p.wait('tensor', 've', ve_mark[hcl - 1])
                    p.op('tensor', lambda e, hcl=hcl: e.matmul(
                        PSa[6][0:1, :], onesb_sb[:, 0:1], hsq_sb[(hcl - 1) % 2][:, :],
                        start=(hcl - 1 == 0), stop=False), inc='te')
                    rms_te[hcl - 1] = p.cnt['te']
                if hcl == 63:
                    p.wait('tensor', 've', ve_mark[63])
                    p.op('tensor', lambda e: e.matmul(
                        PSa[6][0:1, :], onesb_sb[:, 0:1], hsq_sb[63 % 2][:, :],
                        start=False, stop=True), inc='te')
            p.wait('vector', 'te')
            rr = row_sb[0:1, 0:512]
            p.op('vector', lambda e: e.tensor_scalar_mul(rr, PSa[6][0:1, :], 1.0 / FH), inc='ve')
            p.wait('scalar', 've')
            p.op('scalar', lambda e: e.activation(rr, rr, AF.Ln, bias=cst_sb[0:1, 2:3]))
            p.op('scalar', lambda e: e.activation(rr, rr, AF.Exp, scale=-0.5), inc='sc')
            p.wait('tensor', 'sc')
            p.op('tensor', lambda e: e.matmul(PSa[7][:, :], onesf_sb[0:1, 0:128], rr,
                                              start=True, stop=True), inc='te')
            p.wait('vector', 'te')
            p.op('vector', lambda e: e.tensor_copy(tmpb[:, :], PSa[7][:, :]), inc='ve')
            p.wait('sync', 'cc', 2)
            p.wait('gpsimd', 'cc', 2)

            def load_p(dq, hc, buf):
                q, sem = ('sync', 'dha') if hc % 2 == 0 else ('gpsimd', 'dga')
                p.dma(q, buf[:, :], wqp_all[hc, dq, :, :], sem)

            p.wait('sync', 'dha', p.cnt['dha'])
            load_p(0, 0, fpw[0])
            for dq in range(4):
                for hc in range(64):
                    b = hc % 2
                    sem_b = 'dha' if b == 0 else 'dga'
                    need = p.cnt[sem_b]
                    if dq * 64 + hc + 1 < 256:
                        ndq, nhc = (dq, hc + 1) if hc + 1 < 64 else (dq + 1, 0)
                        nq, nsem = ('sync', 'dha') if nhc % 2 == 0 else ('gpsimd', 'dga')
                        p.wait('sync', 'te', p.cnt['te'])
                        p.wait('gpsimd', 'te', p.cnt['te'])
                        p.wait(nq, nsem, p.cnt[nsem])
                        load_p(ndq, nhc, fpw[1 - b])
                    p.wait('tensor', sem_b, need)
                    if hc == 0:
                        p.wait('tensor', 've', p.cnt['ve'])
                    for cc in range(4):
                        p.op('tensor', lambda e, cc=cc, b=b, hc=hc: e.matmul(
                            PSa[cc][:, :], fpw[b][:, 128 * cc:128 * (cc + 1)],
                            hbuf[:, 512 * hc:512 * (hc + 1)],
                            start=(hc == 0), stop=(hc == 63)),
                            inc=('te' if cc == 3 else None))
                p.wait('vector', 'te')
                for cc in range(4):
                    dchunk = 4 * dq + cc
                    p.op('vector', lambda e, cc=cc: e.tensor_mul(tmpa[:, :], PSa[cc][:, :], tmpb[:, :]))
                    p.op('vector', lambda e, dchunk=dchunk, T=T: e.tensor_add(
                        R[:, TOK * dchunk + 512 * T: TOK * dchunk + 512 * (T + 1)],
                        R[:, TOK * dchunk + 512 * T: TOK * dchunk + 512 * (T + 1)],
                        tmpa[:, :]), inc='ve')

    def attn_b():
        RG = [list(range(NCORES))]
        p.wait('sync', 've', p.cnt['ve'])
        p.wait('sync', 'te', p.cnt['te'])
        p.dma('sync', trif_sb[:, :], trif[:, :], 'dhb')
        for c in range(NCH):
            p.dma('sync', xsha[128 * c:128 * (c + 1), :], xln[:, TOK * c:TOK * c + 512], 'dhb')
            p.dma('sync', xshb[128 * c:128 * (c + 1), :], xln[:, TOK * c + 512:TOK * (c + 1)], 'dhb')
        bounce_done = p.cnt['dhb']
        p.wait('gpsimd', 'dhb', bounce_done)
        p.op('gpsimd', lambda e: e.collective_compute(
            "AllGather", ALU.bypass, replica_groups=RG,
            ins=[xsha[:, :].opt()], outs=[xga[:, :].opt()]), inc='cc')
        cc_xa = p.cnt['cc']
        p.op('gpsimd', lambda e: e.collective_compute(
            "AllGather", ALU.bypass, replica_groups=RG,
            ins=[xshb[:, :].opt()], outs=[xgb[:, :].opt()]), inc='cc')
        cc_x = p.cnt['cc']
        p.wait('sync', 'cc', cc_xa)
        p.wait('gpsimd', 'cc', cc_xa)
        ve_proj0 = p.cnt['ve']

        def load_xa(g, hc, buf, q, sem):
            r, gg = g // 2, g % 2
            src = xga if gg == 0 else xgb
            p.dma(q, buf[:, :], src[2048 * r + 128 * hc:2048 * r + 128 * (hc + 1), :], sem)

        def load_wst(hc, buf, sem, q='gpsimd'):
            p.dma(q, buf[:, :], wqkvf_bf[128 * hc:128 * (hc + 1), :], sem)

        p.wait('vector', 'dhb', bounce_done)
        for pair in range(2):
            p.op('vector', lambda e, pair=pair: e.memset(
                Vsb[pair][:, 0:8320].rearrange('p (c w) -> p c w', w=130)[:, :, 64:65], 1.0))
            p.op('vector', lambda e, pair=pair: e.memset(
                Vsb[pair][:, 0:8320].rearrange('p (c w) -> p c w', w=130)[:, :, 129:130], 1.0), inc='ve')
        p.wait('sync', 'dhb', p.cnt['dhb'])
        load_xa(0, 0, xat[0], 'sync', 'dhb')
        p.wait('gpsimd', 'dhb', p.cnt['dhb'])
        p.wait('gpsimd', 'dga', p.cnt['dga'])
        load_wst(0, wst[0], 'dga')
        gorder = [g for g in range(16) if g % 2 == 0] + [g for g in range(16) if g % 2 == 1]
        for gi, g in enumerate(gorder):
            for hc in range(NCH):
                i = gi * NCH + hc
                b = i % 2
                xsem = 'dhb' if b == 0 else 'dha'
                wsem = 'dga' if b == 0 else 'dsa'
                nx = p.cnt[xsem]
                nw = p.cnt[wsem]
                if i + 1 < 16 * NCH:
                    ng, nhc = (g, hc + 1) if hc + 1 < NCH else (gorder[gi + 1], 0)
                    nxs = 'dha' if b == 0 else 'dhb'
                    nws = 'dsa' if b == 0 else 'dga'
                    nwq = 'scalar' if b == 0 else 'gpsimd'
                    p.wait('sync', 'te', p.cnt['te'])
                    p.wait('sync', nxs, p.cnt[nxs])
                    if ng % 2 == 1 and g % 2 == 0:
                        p.wait('sync', 'cc', cc_x)
                    load_xa(ng, nhc, xat[1 - b], 'sync', nxs)
                    p.wait(nwq, 'te', p.cnt['te'])
                    p.wait(nwq, nws, p.cnt[nws])
                    load_wst(nhc, wst[1 - b], nws, nwq)
                p.wait('tensor', xsem, nx)
                p.wait('tensor', wsem, nw)
                if hc == 0:
                    p.wait('tensor', 've', ve_proj0 if g == 0 else p.cnt['ve'])
                for si in range(4):
                    p.op('tensor', lambda e, si=si, b=b, hc=hc: e.matmul(
                        PSa[si][:, :], wst[b][:, 128 * si:128 * (si + 1)], xat[b][:, :],
                        start=(hc == 0), stop=(hc == NCH - 1)))
                for si in range(4):
                    p.op('tensor', lambda e, si=si, b=b, hc=hc: e.matmul(
                        PSa[4 + si][0:128, 0:260], xat[b][:, 128 * si:128 * (si + 1)],
                        wst[b][:, 512:772],
                        start=(hc == 0), stop=(hc == NCH - 1)),
                        inc=('te' if si == 3 else None))
            p.wait('vector', 'te')
            pr, gg = g // 2, g % 2
            for pair in range(2):
                p.op('vector', lambda e, pair=pair, g=g: e.tensor_scalar_mul(
                    Qsb[pair][:, 512 * g:512 * (g + 1)], PSa[pair][:, :], 0.125),
                    drain=(pair == 0))
                p.op('vector', lambda e, pair=pair, g=g: e.tensor_copy(
                    Ksb[pair][:, 512 * g:512 * (g + 1)], PSa[2 + pair][:, :]), drain=False)
            for si in range(4):
                ch = 4 * g + si
                for pair in range(2):
                    p.op('vector', lambda e, si=si, pair=pair, ch=ch: e.tensor_copy(
                        Vsb[pair][:, 130 * ch:130 * ch + 64],
                        PSa[4 + si][0:128, 128 * pair:128 * pair + 64]), drain=False)
                    p.op('vector', lambda e, si=si, pair=pair, ch=ch: e.tensor_copy(
                        Vsb[pair][:, 130 * ch + 65:130 * ch + 129],
                        PSa[4 + si][0:128, 128 * pair + 64:128 * (pair + 1)]), drain=False)
                p.op('vector', lambda e, si=si: e.tensor_copy(
                    ftmp[:, :], PSa[4 + si][0:128, 256:260]), inc='ve')
                pgrp = 32 * (g // 4)
                p.wait('tensor', 've')
                p.op('tensor', lambda e: e.transpose(
                    PSa[0][0:4, 0:128], ftmp[:, :], ident_sb[:, :]), inc='te')
                p.wait('vector', 'te')
                p.wait('vector', 'dq', p.cnt['dq'])
                p.op('vector', lambda e: e.tensor_copy(ft2[:, :], PSa[0][0:4, 0:128]), inc='ve')
                p.wait('gpsimd', 've')
                col0 = (512 * g + 128 * si) % 2048
                p.dma('gpsimd', Ptil[pgrp:pgrp + 4, col0:col0 + 128], ft2[:, :], 'dq')
        p.wait('vector', 'dgc', g_loaded)
        p.wait('vector', 'dq', p.cnt['dq'])
        p.wait('scalar', 'dq', p.cnt['dq'])
        p.wait('vector', 'te', p.cnt['te'])
        p.op('vector', lambda e: e.memzero(zz8[:, :]))
        p.op('vector', lambda e: e.memzero(initc[:, :]), inc='ve')
        for grp in range(4):
            rs = slice(32 * grp, 32 * grp + 4)
            p.op('vector', lambda e, rs=rs: e.tensor_scalar(
                pcum[rs, :], Ptil[rs, :], bfc_sb[rs, 0:1], 0.0, ALU.add, ALU.min), inc='ve')
            p.wait('scalar', 've')
            p.op('scalar', lambda e, rs=rs: e.activation(
                Ptil[rs, :], Ptil[rs, :], AF.Abs, bias=bfc_sb[rs, 0:1]))
            p.op('scalar', lambda e, rs=rs: e.activation(
                Ptil[rs, :], Ptil[rs, :], AF.Exp, scale=-1.0))
            p.op('scalar', lambda e, rs=rs: e.activation(
                Ptil[rs, :], Ptil[rs, :], AF.Ln, bias=cst_sb[rs, 1:2]), inc='sc')
            p.wait('vector', 'sc')
            p.op('vector', lambda e, rs=rs: e.tensor_sub(
                Ptil[rs, :], Ptil[rs, :], pcum[rs, :]), inc='ve')
        for grp in range(4):
            rs = slice(32 * grp, 32 * grp + 4)
            if grp > 0:
                p.wait('gpsimd', 've', p.cnt['ve'])
                p.dma('gpsimd', initc[32 * grp:32 * grp + 4, 0:1],
                      pcum[32 * (grp - 1):32 * (grp - 1) + 4, 2047:2048], 'dq')
                p.wait('vector', 'dq', p.cnt['dq'])
            p.op('vector', lambda e, rs=rs: e.tensor_tensor_scan(
                pcum[rs, :], Ptil[rs, :], zz8[rs, :], initc[rs, 0:1],
                ALU.add, ALU.add), inc='ve')
        p.wait('tensor', 've')
        for k in range(64):
            grp, col = k // 16, (k % 16) * 128
            p.op('tensor', lambda e, grp=grp, col=col: e.transpose(
                PSa[1][0:128, 0:4], pcum[32 * grp:32 * grp + 4, col:col + 128],
                ident_sb[32 * grp:32 * grp + 4, 32 * grp:32 * grp + 4],
                tile_position=(32 * grp, 0)), inc='te')
            p.wait('vector', 'te')
            p.op('vector', lambda e, k=k: e.tensor_copy(fct[:, 4 * k:4 * (k + 1)],
                                                        PSa[1][0:128, 0:4]), inc='ve')
            p.wait('tensor', 've')
        p.wait('scalar', 've', p.cnt['ve'])
        fq3_rdy = {}
        fq3_free = {0: 0, 1: 0}
        prep_ve_end = 0

        def emit_prep(J):
            nonlocal prep_ve_end
            pr, jq = divmod(J, 16)
            grp, col = jq // 4, (jq % 4) * 512
            pb = J % 2
            p.wait('gpsimd', 've', prep_ve_end)
            p.wait('gpsimd', 'te', fq3_free[pb])
            p.wait('gpsimd', 'dq', p.cnt['dq'])
            for h, hp in ((0, 96), (1, 32)):
                row = 32 * grp + 2 * pr + h
                p.dma('gpsimd', rcpb[hp:hp + 1, :], pcum[row:row + 1, col:col + 512], 'dq')
            vrow = p.cnt['dq']
            p.wait('vector', 'dq', vrow)
            dma0 = {}
            for nm, dr in (('n', True), ('hi', True), ('r', True),
                           ('mid', True), ('r2', True), ('lo', True)):
                for h, hp in ((0, 96), (1, 32)):
                    hs = slice(hp, hp + 1)
                    if nm == 'n':
                        p.op('vector', lambda e, hs=hs: e.tensor_single_scalar(
                            rcpbB[hs, :], rcpb[hs, :], -1.0, ALU.mult), drain=(h == 0))
                    elif nm == 'hi':
                        p.op('vector', lambda e, hs=hs: e.tensor_copy(
                            hiT[hs, :], rcpbB[hs, :]), inc='ve', drain=(h == 0))
                        if h == 1:
                            hi_ve = p.cnt['ve']
                    elif nm == 'r':
                        p.op('vector', lambda e, hs=hs: e.tensor_sub(
                            rcpb[hs, :], rcpbB[hs, :], hiT[hs, :]), drain=(h == 0))
                    elif nm == 'mid':
                        p.op('vector', lambda e, hs=hs: e.tensor_copy(
                            miT[hs, :], rcpb[hs, :]), inc='ve', drain=(h == 0))
                        if h == 1:
                            mid_ve = p.cnt['ve']
                    elif nm == 'r2':
                        p.op('vector', lambda e, hs=hs: e.tensor_sub(
                            rcpbB[hs, :], rcpb[hs, :], miT[hs, :]), drain=(h == 0))
                    else:
                        if h == 0:
                            p.wait('vector', 'dq', dma0['hi'])
                        p.op('vector', lambda e, hs=hs: e.tensor_copy(
                            hiT[hs, :], rcpbB[hs, :]), inc='ve', drain=(h == 0))
                if nm == 'hi' or nm == 'mid':
                    mark = hi_ve if nm == 'hi' else mid_ve
                    ri = 0 if nm == 'hi' else 1
                    p.wait('gpsimd', 've', mark)
                    p.wait('gpsimd', 'dq', p.cnt['dq'])
                    for h, hp in ((0, 96), (1, 32)):
                        src = hiT if nm == 'hi' else miT
                        p.dma('gpsimd', fq3w[pb][ri:ri + 1, 512 * h:512 * (h + 1)],
                              src[hp:hp + 1, :], 'dq')
                    dma0[nm] = p.cnt['dq']
            lo_ve = p.cnt['ve']
            prep_ve_end = lo_ve
            p.wait('gpsimd', 've', lo_ve)
            p.wait('gpsimd', 'dq', p.cnt['dq'])
            for h, hp in ((0, 96), (1, 32)):
                p.dma('gpsimd', fq3w[pb][2:3, 512 * h:512 * (h + 1)], hiT[hp:hp + 1, :], 'dq')
            fq3_rdy[J] = p.cnt['dq']

        emit_prep(0)
        for pair in range(2):
            for j in range(16):
                J = 16 * pair + j
                ve_qb = p.cnt['ve']
                if J + 1 < 32:
                    emit_prep(J + 1)
                pb = J % 2
                nk = 4 * (j + 1)
                sc_mark_f = {}
                p.wait('tensor', 've', ve_qb)
                p.wait('tensor', 'dq', fq3_rdy[J])
                p.op('tensor', lambda e, pb=pb: e.matmul(
                    PSa[4][:, :], onesb_sb[0:3, 0:128], fq3w[pb][0:3, 0:512],
                    start=True, stop=True))
                p.op('tensor', lambda e, pb=pb: e.matmul(
                    PSa[5][:, :], onesb_sb[0:3, 0:128], fq3w[pb][0:3, 512:1024],
                    start=True, stop=True), inc='te')
                p.wait('vector', 'te', p.cnt['te'])
                p.op('vector', lambda e: e.tensor_copy(fqbw[:, 0:512], PSa[4][:, :]))
                p.op('vector', lambda e: e.tensor_copy(fqbw[:, 512:1024], PSa[5][:, :]),
                     inc='ve', drain=False)
                r1_ve = p.cnt['ve']

                def emit_lpv(k, nk):
                    kpar = k % 3
                    mc = 128 * (k - 4 * j) if k >= 4 * j else 0
                    p.op('tensor', lambda e, pair=pair, kpar=kpar, k=k, mc=mc, nk=nk: e.matmul(
                        PSa[6][0:65, mc:512], Vsb[pair][:, 130 * k:130 * k + 65],
                        ptb[kpar][:, mc:512], start=(k == 0), stop=(k == nk - 1)))
                    p.op('tensor', lambda e, pair=pair, kpar=kpar, k=k, mc=mc, nk=nk: e.matmul(
                        PSa[7][0:65, mc:512], Vsb[pair][:, 130 * k + 65:130 * (k + 1)],
                        ptb[3 + kpar][:, mc:512],
                        start=(k == 0), stop=(k == nk - 1)), inc='te')

                for k in range(nk + 2):
                    if k < nk:
                        kpar = k % 3
                        STW = PBa[kpar]
                        STA, STB = PSa[kpar * 2], PSa[kpar * 2 + 1]
                        mc = 128 * (k - 4 * j) if k >= 4 * j else 0
                        use_r3 = (k % 2 == 0) or k >= 4 * j or k < 2
                        if k == 2:
                            p.wait('tensor', 've', r1_ve)
                        p.op('tensor', lambda e, STA=STA, pair=pair, k=k, j=j, mc=mc,
                             u=use_r3: e.matmul(
                            STA[:, mc:512], Ksb[pair][0:64, 128 * k:128 * (k + 1)],
                            Qsb[pair][0:64, 512 * j + mc:512 * (j + 1)],
                            start=True, stop=not u))
                        if use_r3:
                            p.op('tensor', lambda e, STB=STB, pair=pair, k=k, j=j, mc=mc: e.matmul(
                                STB[:, mc:512], Ksb[pair][64:128, 128 * k:128 * (k + 1)],
                                Qsb[pair][64:128, 512 * j + mc:512 * (j + 1)],
                                tile_position=(64, 0), start=True, stop=False))
                            p.op('tensor', lambda e, STA=STA, pb=pb, mc=mc: e.matmul(
                                STA[:, mc:512], onesb_sb[0:3, 0:128],
                                fq3w[pb][0:3, mc:512], start=False, stop=True))
                            p.op('tensor', lambda e, STB=STB, pb=pb, mc=mc: e.matmul(
                                STB[:, mc:512], onesb_sb[0:3, 0:128],
                                fq3w[pb][0:3, 512 + mc:1024], start=False, stop=True), inc='te')
                        else:
                            p.op('tensor', lambda e, STB=STB, pair=pair, k=k, j=j, mc=mc: e.matmul(
                                STB[:, mc:512], Ksb[pair][64:128, 128 * k:128 * (k + 1)],
                                Qsb[pair][64:128, 512 * j + mc:512 * (j + 1)],
                                tile_position=(64, 0), start=True, stop=True), inc='te')
                        te_k = p.cnt['te']
                    if k >= 2:
                        p.wait('tensor', 'sc', sc_mark_f[k - 2])
                        if k - 2 == 0:
                            p.wait('tensor', 've', ve_qb)
                        emit_lpv(k - 2, nk)
                    if k >= nk:
                        continue
                    if k >= 4 * j:
                        p.wait('vector', 'te', te_k)
                        p.op('vector', lambda e, STA=STA, mc=mc: e.tensor_add(
                            STA[:, mc:mc + 128], STA[:, mc:mc + 128], trif_sb[:, :]))
                        p.op('vector', lambda e, STB=STB, mc=mc: e.tensor_add(
                            STB[:, mc:mc + 128], STB[:, mc:mc + 128], trif_sb[:, :]),
                            inc='ve', drain=False)
                        p.wait('scalar', 've', p.cnt['ve'])
                    elif not use_r3:
                        p.wait('vector', 'te', te_k)
                        p.op('vector', lambda e, STW=STW: e.tensor_add(
                            STW[:, :], STW[:, :], fqbw[:, :]), inc='ve', drain=(k == 3))
                        p.wait('scalar', 've', p.cnt['ve'])
                    else:
                        p.wait('scalar', 'te', te_k)
                    p.op('scalar', lambda e, STA=STA, kpar=kpar, k=k, pair=pair, mc=mc: e.activation(
                        ptb[kpar][:, mc:512], STA[:, mc:512], AF.Exp,
                        bias=fct[:, 4 * k + 2 * pair:4 * k + 2 * pair + 1]), drain=False)
                    p.op('scalar', lambda e, STB=STB, kpar=kpar, k=k, pair=pair, mc=mc: e.activation(
                        ptb[3 + kpar][:, mc:512], STB[:, mc:512], AF.Exp,
                        bias=fct[:, 4 * k + 2 * pair + 1:4 * k + 2 * pair + 2]), inc='sc', drain=False)
                    sc_mark_f[k] = p.cnt['sc']
                fq3_free[pb] = p.cnt['te']
                p.wait('vector', 'te')
                p.wait('vector', 'dq', p.cnt['dq'])
                p.wait('scalar', 'te', p.cnt['te'])
                p.op('scalar', lambda e: e.activation(dvb[64:65, :], PSa[6][64:65, :], AF.Ln))
                p.op('scalar', lambda e: e.activation(rcpb[64:65, :], dvb[64:65, :], AF.Exp,
                                                      scale=-1.0))
                p.op('scalar', lambda e: e.activation(dvb2[64:65, :], PSa[7][64:65, :], AF.Ln, drain=False)
                     if False else e.activation(dvb2[64:65, :], PSa[7][64:65, :], AF.Ln))
                p.op('scalar', lambda e: e.activation(rcpbB[64:65, :], dvb2[64:65, :], AF.Exp,
                                                      scale=-1.0), inc='sc')
                p.wait('vector', 'sc')
                p.op('vector', lambda e: e.nop(), inc='ve')
                p.wait('gpsimd', 've')
                p.dma('gpsimd', rcpb[0:1, :], rcpb[64:65, :], 'dq')
                p.dma('gpsimd', rcpbB[0:1, :], rcpbB[64:65, :], 'dq')
                p.wait('tensor', 'dq', p.cnt['dq'])
                p.wait('tensor', 've', p.cnt['ve'])
                p.wait('tensor', 'sc', p.cnt['sc'])
                p.op('tensor', lambda e: e.matmul(PSa[0][0:64, :], onesf_sb[0:1, 0:64],
                                                  rcpb[0:1, :], start=True, stop=True))
                p.op('tensor', lambda e: e.matmul(PSa[1][0:64, :], onesf_sb[0:1, 0:64],
                                                  rcpbB[0:1, :], start=True, stop=True), inc='te')
                p.wait('vector', 'te')
                p.op('vector', lambda e: e.tensor_copy(dvb[0:64, :], PSa[0][0:64, :]))
                p.op('vector', lambda e: e.tensor_copy(dvb2[0:64, :], PSa[1][0:64, :]))
                p.op('vector', lambda e: e.tensor_mul(onrm_sb[0:64, :], PSa[6][0:64, :], dvb[0:64, :]))
                p.op('vector', lambda e: e.tensor_mul(onrm2_sb[0:64, :], PSa[7][0:64, :],
                                                      dvb2[0:64, :]), inc='ve')
                p.wait('gpsimd', 've')
                p.dma('gpsimd', obnc[128 * pair:128 * pair + 64, 512 * j:512 * (j + 1)],
                      onrm_sb[0:64, :], 'dq')
                p.dma('gpsimd', obnc[128 * pair + 64:128 * (pair + 1), 512 * j:512 * (j + 1)],
                      onrm2_sb[0:64, :], 'dq')
        p.wait('gpsimd', 'dq', p.cnt['dq'])
        p.op('gpsimd', lambda e: e.collective_compute(
            "AllGather", ALU.bypass, replica_groups=RG,
            ins=[obnc[:, :].opt()], outs=[ogath[:, :].opt()]), inc='cc')
        cc_o = p.cnt['cc']
        p.wait('sync', 'cc', cc_o)
        p.wait('sync', 'te', p.cnt['te'])
        p.wait('sync', 'dhb', p.cnt['dhb'])
        cell = {}
        p.emit('sync', lambda e: cell.__setitem__('off', e.partition_id()))
        for c in range(NCH):
            def oload(e, c=c):
                src = ogath[128 * c:128 * (c + 1), bass.ts(cell['off'], TOK)]
                return e.dma_start(out=oown[:, TOK * c:TOK * (c + 1)], in_=src)
            p.op('sync', oload, inc='dhb', n=16)
        oo_done = p.cnt['dhb']

        def load_wo(dst, h, sem):
            for c in range(NCH):
                p.dma('gpsimd', dst[:, 128 * c:128 * (c + 1)],
                      WobT[128 * c:128 * (c + 1), 128 * h:128 * (h + 1)], sem)

        p.wait('gpsimd', 'te', p.cnt['te'])
        p.wait('gpsimd', 'dga', p.cnt['dga'])
        load_wo(fgw[0], 0, 'dga')
        prev_te = p.cnt['te']
        for oc in range(NCH):
            b = oc % 2
            sem_b = 'dga' if b == 0 else 'dgb'
            need = p.cnt[sem_b]
            if oc + 1 < NCH:
                nsem = 'dgb' if b == 0 else 'dga'
                p.wait('gpsimd', 'te', prev_te)
                p.wait('gpsimd', nsem, p.cnt[nsem])
                load_wo(fgw[1 - b], oc + 1, nsem)
            p.wait('tensor', sem_b, need)
            p.wait('tensor', 'dhb', oo_done)
            for g in range(NTG):
                p.wait('tensor', 've', p.cnt['ve'])
                for c in range(NCH):
                    p.op('tensor', lambda e, g=g, c=c, b=b: e.matmul(
                        PSa[g][:, :], fgw[b][:, 128 * c:128 * (c + 1)],
                        oown[:, TOK * c + 512 * g: TOK * c + 512 * (g + 1)],
                        start=(c == 0), stop=(c == NCH - 1)),
                        inc=('te' if c == NCH - 1 else None))
                p.wait('vector', 'te')
                p.op('vector', lambda e, oc=oc, g=g: e.tensor_add(
                    R[:, TOK * oc + 512 * g: TOK * oc + 512 * (g + 1)],
                    R[:, TOK * oc + 512 * g: TOK * oc + 512 * (g + 1)], PSa[g][:, :]), inc='ve')
            prev_te = p.cnt['te']

    for c in range(NCH):
        b = c % 2
        gsem = 'dga' if b == 0 else 'dgb'
        p.wait('gpsimd', gsem, p.cnt[gsem])
        p.wait('gpsimd', 'dhb', p.cnt['dhb'])
        p.dma('gpsimd', qwb[b][:, :], WqaT[128 * c:128 * (c + 1), :], gsem)
        p.wait('sync', gsem, p.cnt[gsem])
        p.dma('sync', wqa_bf[:, 128 * c:128 * (c + 1), :].transpose([1, 0, 2]),
              qwb[b].rearrange('p (h w) -> p h w', w=128)[:, :, :], 'dhb')
    for c in range(NCH):
        b = c % 2
        gsem = 'dga' if b == 0 else 'dgb'
        p.wait('gpsimd', gsem, p.cnt[gsem])
        p.wait('gpsimd', 'dhb', p.cnt['dhb'])
        p.dma('gpsimd', qwb[b][:, :], WoaT[128 * c:128 * (c + 1), :], gsem)
        p.wait('sync', gsem, p.cnt[gsem])
        p.dma('sync', woa_bf[:, 128 * c:128 * (c + 1), :].transpose([1, 0, 2]),
              qwb[b].rearrange('p (h w) -> p h w', w=128)[:, :, :], 'dhb')
    wqa_done = p.cnt['dhb']
    qsteps = quant_steps() if 'q' in phases else []
    bank_weights_prep()
    layernorm()
    tap_xln = mktap('xln', [128, NCH * TOK], BF16)
    if tap_xln is not None:
        p.wait('gpsimd', 've')
        p.dma('gpsimd', tap_xln[:, :], xln[:, :], 'st')
    if 'a' in phases:
        attn_a(qsteps)
    else:
        for st in qsteps:
            st()
    if 'f' in phases:
        layernorm()
        ffn()
    if 'b' in phases:
        layernorm()
        attn_b()
    if 'g' in phases:
        layernorm()
        ffn()

    p.wait('sync', 've')
    p.wait('sync', 'te')
    for c in range(NCH):
        p.dma('sync', y[128 * c:128 * (c + 1), :], R[:, TOK * c:TOK * (c + 1)], 'st')
    p.wait('gpsimd', 'st')
    p.build()
    return nc


def kernel(**inputs):
    nc = build_nc(phases=('q', 'a', 'f', 'b', 'g'))
    in_maps = [host_inputs(inputs, c) for c in range(NCORES)]
    res = run_bass_kernel_spmd(nc, in_maps, core_ids=list(range(NCORES)))
    out = np.concatenate([r['y'].T for r in res.results], axis=0)
    return out.reshape(1, S, D).astype(np.float32)


# version 18
# speedup vs baseline: 1.2151x; 1.0076x over previous
import sys, os
if '/opt/trn_rl_repo' not in sys.path:
    sys.path.insert(0, '/opt/trn_rl_repo')
import numpy as np
import ml_dtypes
import concourse.bass as bass
import concourse.mybir as mybir
from concourse.bass_utils import run_bass_kernel_spmd
from concourse import dve_ops

F32 = mybir.dt.float32
BF16 = mybir.dt.bfloat16
AF = mybir.ActivationFunctionType
ALU = mybir.AluOpType
BF = ml_dtypes.bfloat16

NCORES = 8
S = 8192
D = 2048
TOK = S // NCORES
NCH = D // 128
NTG = TOK // 512
HA = 16
NBK = 4
HB = 32
HBL = HB // NCORES
HD = 64
FH = 8192
FH2 = 16384
LNEPS = 1e-5
RMSEPS = 1.1920929e-7
NEGBIG = -1.0e9


def _c(a):
    return np.ascontiguousarray(a)


def host_inputs(inp, core):
    d = {}
    x = np.asarray(inp['L0_attn_a_hidden_states'])
    sl = slice(TOK * core, TOK * (core + 1))
    d['xT'] = _c(x[0, sl, :].T)

    d['WqaT'] = _c(np.asarray(inp['L0_attn_a_Wq']).T)
    d['WoaT'] = _c(np.asarray(inp['L0_attn_a_Wo']).T)
    mem = np.asarray(inp['L0_attn_a_memories'])
    d['memH'] = _c(mem.transpose(1, 0, 2, 3))
    mn = np.asarray(inp['L0_attn_a_memory_norms'])
    d['mnT'] = _c(mn.transpose(1, 2, 0).reshape(128 * HA // 128, 128, NBK)
                  .reshape(HA, 128, NBK))
    d['mnFlat'] = _c(mn.reshape(NBK, HA * 128))
    bw = np.asarray(inp['L0_attn_a_bank_weights'])
    d['bwT'] = _c(bw.T)

    hsl = slice(HBL * HD * core, HBL * HD * (core + 1))
    d['Wqkvf'] = _c(np.concatenate([
        np.asarray(inp['L0_attn_b_Wq'])[hsl, :].T,
        np.asarray(inp['L0_attn_b_Wk'])[hsl, :].T,
        np.asarray(inp['L0_attn_b_Wv'])[hsl, :].T,
        np.asarray(inp['L0_attn_b_Wf'])[HBL * core: HBL * (core + 1), :].T,
    ], axis=1))
    bfv = np.asarray(inp['L0_attn_b_bf'])[HBL * core: HBL * (core + 1)]
    bff = np.zeros((128, 1), dtype=np.float32)
    for grp in range(4):
        bff[32 * grp:32 * grp + 4, 0] = bfv
    d['bf_full'] = bff
    d['WobT'] = _c(np.asarray(inp['L0_attn_b_Wo']).T)

    wguT = np.asarray(inp['L0_ffn_w_gate_up']).T
    d['wguS'] = _c(wguT[256 * core: 256 * (core + 1), :])
    wpT = np.asarray(inp['L0_ffn_w_proj']).T
    d['wpS'] = _c(wpT[1024 * core: 1024 * (core + 1), :])

    d['ident_f32'] = np.eye(128, dtype=np.float32)
    d['ones_f32'] = np.ones((128, 512), dtype=np.float32)
    d['ones_bf'] = np.ones((128, 512), dtype=BF)
    tri = np.tril(np.ones((128, 128), dtype=np.float32), -1)
    d['Utri_bf'] = tri.astype(BF)
    d['negI_bf'] = (NEGBIG * np.eye(128, dtype=np.float32)).astype(BF)
    d['trif'] = NEGBIG * tri
    sel = np.zeros((4, 512), dtype=np.float32)
    for n in range(4):
        sel[n, 128 * n:128 * (n + 1)] = 1.0
    d['selG'] = sel.astype(BF)
    ind2 = np.zeros((128, 2), dtype=np.float32)
    ind2[:64, 0] = 1.0
    ind2[64:, 1] = 1.0
    d['ind2'] = ind2.astype(BF)
    e2 = np.zeros((2, 128), dtype=np.float32)
    e2[0, :64] = 1.0
    e2[1, 64:] = 1.0
    d['e2_bf'] = e2.astype(BF)
    d['e2_f32'] = e2
    cst = np.zeros((128, 8), dtype=np.float32)
    cst[:, 0] = LNEPS
    cst[:, 1] = 1.0
    cst[:, 2] = RMSEPS
    d['cst'] = cst
    return d


class P:
    def __init__(self, nc):
        self.nc = nc
        self.streams = {k: [] for k in ('tensor', 'vector', 'scalar', 'sync', 'gpsimd')}
        self.sem = {}
        self.cnt = {}
        from contextlib import ExitStack
        self.stack = ExitStack()

    def mksem(self, *names):
        for n in names:
            self.sem[n] = self.stack.enter_context(self.nc.semaphore(n))
            self.cnt[n] = 0

    def emit(self, eng, f):
        self.streams[eng].append(f)

    def wait(self, eng, sname, val=None):
        v = self.cnt[sname] if val is None else val
        s = self.sem[sname]
        if v > 0:
            self.emit(eng, lambda e, s=s, v=v: e.wait_ge(s, v))
        return v

    def op(self, eng, f, inc=None, n=1, drain=True):
        if drain and eng in ('vector', 'scalar'):
            self.emit(eng, lambda e: e.drain())
        if inc is not None:
            self.cnt[inc] += n
            s = self.sem[inc]
            self.emit(eng, lambda e, f=f, s=s, n=n: f(e).then_inc(s, n))
        else:
            self.emit(eng, lambda e, f=f: f(e))

    def dma(self, q, out, in_, sname):
        self.op(q, lambda e, o=out, i=in_: e.dma_start(out=o, in_=i), inc=sname, n=16,
                drain=False)
        return self.cnt[sname]

    def build(self):
        nc = self.nc
        with nc.Block() as block:
            @block.tensor
            def _(e):
                for f in self.streams['tensor']:
                    f(e)

            @block.vector
            def _(e):
                for f in self.streams['vector']:
                    f(e)

            @block.scalar
            def _(e):
                for f in self.streams['scalar']:
                    f(e)

            @block.sync
            def _(e):
                for f in self.streams['sync']:
                    f(e)

            @block.gpsimd
            def _(e):
                for f in self.streams['gpsimd']:
                    f(e)
        self.stack.close()


def build_nc(phases=('q', 'a', 'f'), debug_taps=()):
    nc = bass.Bass("TRN2", target_bir_lowering=False, debug=False)
    p = P(nc)
    p.mksem('dh', 'dha', 'dhb', 'dga', 'dgb', 'dgc', 'dsa', 'dsb', 'dso', 'te', 've', 'sc', 'cc', 'st', 'dq')

    io = {}
    def din(name, shape, dtype=F32):
        io[name] = nc.dram_tensor(name, list(shape), dtype, kind="ExternalInput").ap()
        return io[name]

    xT = din('xT', [D, TOK])
    WqaT = din('WqaT', [D, D]); WoaT = din('WoaT', [D, D])
    memH = din('memH', [HA, NBK, 128, 128]); mnT = din('mnT', [HA, 128, NBK])
    mnFlat = din('mnFlat', [NBK, HA * 128]); bwT = din('bwT', [NBK, HA])
    Wqkvf = din('Wqkvf', [D, 772]); bf_full = din('bf_full', [128, 1])
    WobT = din('WobT', [D, D])
    wguS = din('wguS', [256, FH2]); wpS = din('wpS', [1024, D])
    ident_f32 = din('ident_f32', [128, 128])
    ones_f32 = din('ones_f32', [128, 512]); ones_bf = din('ones_bf', [128, 512], BF16)
    Utri_bf = din('Utri_bf', [128, 128], BF16); negI_bf = din('negI_bf', [128, 128], BF16)
    trif = din('trif', [128, 128])
    selG = din('selG', [4, 512], BF16)
    ind2 = din('ind2', [128, 2], BF16); e2_bf = din('e2_bf', [2, 128], BF16)
    e2_f32 = din('e2_f32', [2, 128]); cst = din('cst', [128, 8])

    y = nc.dram_tensor('y', [D, TOK], F32, kind="ExternalOutput").ap()
    wqgu_sh = nc.dram_tensor('wqgu_sh', [128, 128, 2, 128], BF16).ap()
    wqgu_all = nc.dram_tensor('wqgu_all', [8 * 128, 128, 2, 128], BF16, addr_space='Shared').ap()
    wqp_sh = nc.dram_tensor('wqp_sh', [8, 4, 128, 512], BF16).ap()
    wqp_all = nc.dram_tensor('wqp_all', [64, 4, 128, 512], BF16, addr_space='Shared').ap()
    xsha = nc.dram_tensor('xsha', [D, TOK // 2], BF16).ap()
    xshb = nc.dram_tensor('xshb', [D, TOK // 2], BF16).ap()
    xga = nc.dram_tensor('xga', [8 * D, TOK // 2], BF16, addr_space='Shared').ap()
    xgb = nc.dram_tensor('xgb', [8 * D, TOK // 2], BF16, addr_space='Shared').ap()
    obnc = nc.dram_tensor('obnc', [256, S], BF16).ap()
    ogath = nc.dram_tensor('ogath', [8 * 256, S], BF16, addr_space='Shared').ap()
    wqkvf_bf = nc.dram_tensor('wqkvf_bf', [D, 772], BF16).ap()
    wqa_bf = nc.dram_tensor('wqa_bf', [HA, D, 128], BF16).ap()
    woa_bf = nc.dram_tensor('woa_bf', [NCH, D, 128], BF16).ap()
    taps = {}
    for t in debug_taps:
        pass

    def mktap(name, shape, dtype=F32):
        if name in debug_taps:
            taps[name] = nc.dram_tensor('tap_' + name, list(shape), dtype,
                                        kind="ExternalOutput").ap()
            return taps[name]
        return None

    SB = {}
    SBASE = 16512
    def sb(name, shape, dtype, off):
        h = nc.alloc_sbuf_tensor_at(name, list(shape), dtype, offset=SBASE + off)
        SB[name] = h
        return h.ap()

    R = sb('R', [128, NCH * TOK], F32, 0)
    xln = sb('xln', [128, NCH * TOK], BF16, 65536)
    comb = sb('comb', [128, HA * TOK], BF16, 98304)
    hbuf = sb('hbuf', [128, (FH // 128) * 512], BF16, 98304)
    Ksb = [sb(f'Ksb{i}', [128, S], BF16, 98816 + 16384 * i) for i in range(2)]
    Qsb = [sb(f'Qsb{i}', [128, S], BF16, 131584 + 16384 * i) for i in range(2)]
    Vsb = [sb(f'Vsb{i}', [128, (S // 128) * 130], BF16, 65536 + 16640 * i) for i in range(2)]
    W0 = 163840
    wab = [sb(f'wab{i}', [128, NCH * 128], BF16, W0 + 4096 * i) for i in range(2)]
    mem_sb = [sb(f'mem_sb{i}', [128, NBK * 128], BF16, W0 + 8192 + 1024 * i) for i in range(2)]
    mnT_sb = sb('mnT_sb', [128, HA * NBK], BF16, W0 + 10240)
    smallc = W0 + 10496
    bw_sb = sb('bw_sb', [4, HA], F32, smallc)
    wact4 = sb('wact4', [4, HA], F32, smallc + 128)
    act_sb = sb('act_sb', [4, 1], F32, smallc + 320)
    g_sb = sb('g_sb', [4, 512], BF16, smallc + 384)
    ident_sb = sb('ident_sb', [128, 128], F32, smallc + 1408)
    onesf_sb = sb('onesf_sb', [128, 512], F32, smallc + 1920)
    onesb_sb = sb('onesb_sb', [128, 512], BF16, smallc + 3968)
    selG_sb = sb('selG_sb', [4, 512], BF16, smallc + 4992)
    utri_sb = sb('utri_sb', [128, 128], BF16, smallc + 6144)
    negI_sb = sb('negI_sb', [128, 128], BF16, smallc + 6400)
    trif_sb = sb('trif_sb', [128, 128], F32, 172288)
    cst_sb = sb('cst_sb', [128, 8], F32, smallc + 6016)
    lnc = smallc + 6656
    mnf_sb = sb('mnf_sb', [4, HA * 128], F32, lnc)
    xb_sb = [sb(f'xb_sb{i}', [128, 512], BF16, lnc + 1024 * i) for i in range(2)]
    xsq_sb = [sb(f'xsq_sb{i}', [128, 512], BF16, lnc + 2048 + 1024 * i) for i in range(2)]
    sigp = [sb(f'sigp{i}', [128, 512], BF16, lnc + 10240 + 1024 * i) for i in range(2)]
    row_sb = sb('row_sb', [1, 512 * 5], F32, lnc + 4096)
    tmpa = sb('tmpa', [128, 512], F32, lnc + 14336)
    tmpb = sb('tmpb', [128, 512], F32, lnc + 16384)
    qwb = [sb(f'qwb{i}', [128, 2048], BF16, 131840 + 4096 * i) for i in range(2)]
    qt1 = sb('qt1', [128, 512], BF16, 131840 + 8192)
    qt2 = [sb('qt2a', [128, 512], BF16, 131840 + 9216), sb('qt2b', [128, 512], BF16, 131840 + 22016)]
    qt3 = sb('qt3', [128, 512], BF16, 131840 + 10240)
    qsd = sb('qsd', [2, 512], BF16, 131840 + 11264)
    qrcp = sb('qrcp', [2, 512], F32, 131840 + 12288)
    qsdf = sb('qsdf', [2, 512], F32, 131840 + 15872)
    qscr = sb('qscr', [2, 512], F32, 131840 + 17920)
    ascr = sb('ascr', [128, 512], F32, 131840 + 19968)
    ind2_sb = sb('ind2_sb', [128, 2], BF16, 131840 + 14336)
    e2b_sb = sb('e2b_sb', [2, 128], BF16, 131840 + 14848)
    e2f_sb = sb('e2f_sb', [2, 128], F32, 131840 + 15360)
    fgw = [sb('fgw0', [128, 2048], BF16, 0 + 163840), sb('fgw1', [128, 2048], BF16, 4096 + 163840),
           sb('fgw2', [128, 2048], BF16, 203520), sb('fgw3', [128, 2048], BF16, 207616)]
    fpw = [sb('fpw0', [128, 512], BF16, 172032), sb('fpw1', [128, 512], BF16, 173056)]
    sil_sb = [sb('sil_sb0', [128, 512], BF16, lnc + 18432), sb('sil_sb1', [128, 512], BF16, lnc + 19456)]
    hsq_sb = [sb('hsq_sb0', [128, 512], BF16, lnc + 20480), sb('hsq_sb1', [128, 512], BF16, lnc + 21504)]
    xat = [sb(f'xat{i}', [128, 512], BF16, lnc + 1024 * i) for i in range(4)]
    Ptil = sb('Ptil', [128, 2048], F32, lnc + 14336)
    zz8 = sb('zz8', [128, 2048], F32, lnc + 4096)
    pcum = sb('pcum', [128, 2048], F32, 203520)
    wst = [sb(f'wst{i}', [128, 772], BF16, lnc + 4096 + 1664 * i) for i in range(2)]
    ptb = [sb(f'ptb{i}', [128, 512], BF16, lnc + 10240 + 1024 * i) for i in range(6)]
    rank1A = sb('rank1A', [128, 512], F32, 167936)
    rank1B = sb('rank1B', [128, 512], F32, 170240)
    fq3w = [sb(f'fq3w{b}', [3, 1024], BF16, 164352 + 2048 * b) for b in range(2)]
    fqbw = sb('fqbw', [128, 1024], F32, lnc + 16384)
    hiT = sb('hiT', [97, 512], BF16, 168448)
    miT = sb('miT', [97, 512], BF16, 169472)
    rcpb = sb('rcpb', [128, 512], F32, lnc)
    rcpbB = sb('rcpbB', [128, 512], F32, lnc + 2048)
    dvb = sb('dvb', [128, 512], F32, lnc + 4096)
    dvb2 = sb('dvb2', [128, 512], F32, lnc + 6144)
    onrm_sb = sb('onrm_sb', [128, 512], BF16, lnc + 8192)
    onrm2_sb = sb('onrm2_sb', [128, 512], BF16, lnc + 9216)
    fct = sb('fct', [128, 256], F32, 211712)
    bfc_sb = sb('bfc_sb', [128, 1], F32, 212736)
    initc = sb('initc', [128, 1], F32, 212768)
    ftmp = sb('ftmp', [128, 4], F32, 174080)
    ft2 = sb('ft2', [4, 128], F32, lnc + 12288)
    oown = sb('oown', [128, NCH * 1024], BF16, 65536)
    assert 16512 + 212800 <= 229344

    PB = [nc.alloc_psum_tensor(f'PB{i}', [128, 1024], F32) for i in range(3)]
    PS = [nc.alloc_psum_tensor(f'PS{i}', [128, 512], F32) for i in range(2)]
    PSa = [PB[0].ap()[:, 0:512], PB[0].ap()[:, 512:1024],
           PB[1].ap()[:, 0:512], PB[1].ap()[:, 512:1024],
           PB[2].ap()[:, 0:512], PB[2].ap()[:, 512:1024]] + [t.ap() for t in PS]
    PBa = [t.ap() for t in PB]

    for dst, src in ((ident_sb, ident_f32), (onesf_sb, ones_f32), (onesb_sb, ones_bf),
                     (selG_sb, selG), (bw_sb, bwT), (mnf_sb, mnFlat), (cst_sb, cst),
                     (utri_sb, Utri_bf), (negI_sb, negI_bf)):
        p.dma('sync', dst[:, :], src[:, :], 'dh')
    for h in range(HA):
        p.dma('gpsimd', mnT_sb[:, NBK * h:NBK * (h + 1)], mnT[h, :, :], 'dgc')
    p.dma('gpsimd', bfc_sb[:, :], bf_full[:, :], 'dgc')
    p.dma('gpsimd', ind2_sb[:, :], ind2[:, :], 'dgc')
    p.dma('gpsimd', e2b_sb[:, :], e2_bf[:, :], 'dgc')
    p.dma('gpsimd', e2f_sb[:, :], e2_f32[:, :], 'dgc')
    for c in range(NCH):
        p.dma('sync', R[:, TOK * c:TOK * (c + 1)], xT[128 * c:128 * (c + 1), :], 'dh')
    c_loaded = p.cnt['dh']
    g_loaded = p.cnt['dgc']

    def rcp_fast(out, in_, scratch, inc=None):
        p.op('vector', lambda e: e.reciprocal_approx_fast(out=scratch, in_=in_))
        p.op('vector', lambda e: e._custom_dve(
            dve_ops.RECIPROCAL_APPROX_NR, out=out, in0=in_, in1=scratch, s0=2.0), inc=inc)

    def bank_weights_prep():
        p.wait('vector', 'dh', c_loaded)
        p.wait('vector', 'dgc', g_loaded)
        p.op('vector', lambda e: e.reduce_sum(tmpb[0:4, 0:1], mnf_sb[:, :], mybir.AxisListType.X))
        p.op('vector', lambda e: e.tensor_single_scalar(act_sb[:, :], tmpb[0:4, 0:1], 1e-6, ALU.is_ge), inc='ve')
        p.wait('scalar', 'dh', c_loaded)
        p.op('scalar', lambda e: e.activation(tmpa[0:4, 0:HA], bw_sb[:, :], AF.Exp), inc='sc')
        p.wait('tensor', 'sc')
        p.op('tensor', lambda e: e.matmul(PSa[6][0:1, 0:HA], onesf_sb[0:4, 0:1], tmpa[0:4, 0:HA],
                                          start=True, stop=True), inc='te')
        p.wait('vector', 'te')
        p.op('vector', lambda e: e.reciprocal(row_sb[0:1, 0:HA], PSa[6][0:1, 0:HA]), inc='ve')
        p.wait('tensor', 've')
        p.op('tensor', lambda e: e.matmul(PSa[7][0:4, 0:HA], onesf_sb[0:1, 0:4], row_sb[0:1, 0:HA],
                                          start=True, stop=True), inc='te')
        p.wait('vector', 'te')
        p.op('vector', lambda e: e.scalar_tensor_tensor(wact4[:, :], tmpa[0:4, 0:HA], act_sb[:, :],
                                                        PSa[7][0:4, 0:HA], ALU.mult, ALU.mult), inc='ve')

    def layernorm():
        te0 = p.cnt['te']
        p.wait('vector', 'dhb', p.cnt['dhb'])
        for g in range(NTG):
            gv0 = p.cnt['ve']
            for c in range(NCH):
                b = c % 2
                p.wait('vector', 'te', p.cnt['te'])
                p.op('vector', lambda e, c=c, g=g, b=b: e.tensor_copy(
                    xb_sb[b][:, :], R[:, TOK * c + 512 * g: TOK * c + 512 * (g + 1)]), inc='ve')
                p.wait('scalar', 've')
                p.op('scalar', lambda e, b=b: e.activation(xsq_sb[b][:, :], xb_sb[b][:, :], AF.Square), inc='sc')
                p.wait('tensor', 've')
                p.wait('tensor', 'sc')
                p.op('tensor', lambda e, b=b, c=c: e.matmul(
                    PSa[6][0:1, :], onesb_sb[:, 0:1], xb_sb[b][:, :], start=(c == 0), stop=(c == NCH - 1)))
                p.op('tensor', lambda e, b=b, c=c: e.matmul(
                    PSa[7][0:1, :], onesb_sb[:, 0:1], xsq_sb[b][:, :], start=(c == 0), stop=(c == NCH - 1)),
                    inc='te')
            p.wait('vector', 'te')
            m = row_sb[0:1, 0:512]; rstd = row_sb[0:1, 512:1024]; mr = row_sb[0:1, 1024:1536]
            var = row_sb[0:1, 1536:2048]; m2 = row_sb[0:1, 2048:2560]
            p.op('vector', lambda e: e.tensor_scalar_mul(m, PSa[6][0:1, :], 1.0 / D))
            p.op('vector', lambda e: e.tensor_scalar_mul(var, PSa[7][0:1, :], 1.0 / D))
            p.op('vector', lambda e: e.tensor_mul(m2, m, m))
            p.op('vector', lambda e: e.tensor_sub(var, var, m2), inc='ve')
            p.wait('scalar', 've')
            p.op('scalar', lambda e: e.activation(var, var, AF.Ln, bias=cst_sb[0:1, 0:1]))
            p.op('scalar', lambda e: e.activation(rstd, var, AF.Exp, scale=-0.5), inc='sc')
            p.wait('vector', 'sc')
            p.op('vector', lambda e: e.tensor_mul(mr, m, rstd), inc='ve')
            p.wait('tensor', 've')
            p.op('tensor', lambda e: e.matmul(PSa[4][:, :], onesf_sb[0:1, 0:128], rstd, start=True, stop=True))
            p.op('tensor', lambda e: e.matmul(PSa[5][:, :], onesf_sb[0:1, 0:128], mr, start=True, stop=True),
                 inc='te')
            p.wait('vector', 'te')
            for c in range(NCH):
                p.op('vector', lambda e, c=c, g=g: e.tensor_mul(
                    tmpa[:, :], R[:, TOK * c + 512 * g: TOK * c + 512 * (g + 1)], PSa[4][:, :]))
                p.op('vector', lambda e, c=c, g=g: e.tensor_sub(
                    xln[:, TOK * c + 512 * g: TOK * c + 512 * (g + 1)], tmpa[:, :], PSa[5][:, :]), inc='ve')

    def attn_a(qsteps=()):
        qsteps = list(qsteps)
        def load_w(dst, srcB, h, sem):
            p.dma('sync', dst[:, :],
                  srcB[h, :, :].rearrange('(c p) w -> p c w', p=128)[:, :, :], sem)

        def load_mem(dst, h, sem):
            for n in range(NBK):
                p.dma('gpsimd', dst[:, 128 * n:128 * (n + 1)], memH[h, n, :, :], sem)

        p.wait('sync', 'dhb', wqa_done)
        p.wait('sync', 'dha', p.cnt['dha'])
        load_w(wab[0], wqa_bf, 0, 'dha')
        p.wait('gpsimd', 'dga', p.cnt['dga'])
        load_mem(mem_sb[0], 0, 'dga')
        xln_ready = p.cnt['ve']
        prev_te = 0
        sig_done = {}
        for h in range(HA):
            b = h % 2
            sem_b = 'dha' if b == 0 else 'dhb'
            gsem_b = 'dga' if b == 0 else 'dgb'
            need = p.cnt[sem_b]
            gneed = p.cnt[gsem_b]
            if h + 1 < HA:
                nsem = 'dhb' if b == 0 else 'dha'
                gsem = 'dgb' if b == 0 else 'dga'
                p.wait('sync', 'te', prev_te)
                p.wait('sync', nsem, p.cnt[nsem])
                load_w(wab[1 - b], wqa_bf, h + 1, nsem)
                p.wait('gpsimd', 'te', prev_te)
                p.wait('gpsimd', gsem, p.cnt[gsem])
                load_mem(mem_sb[1 - b], h + 1, gsem)
            p.wait('tensor', sem_b, need)
            p.wait('tensor', gsem_b, gneed)
            p.wait('tensor', 've', max(sig_done.get(h - 1, 0), xln_ready))
            for g in range(NTG):
                for c in range(NCH):
                    p.op('tensor', lambda e, g=g, c=c, b=b: e.matmul(
                        PSa[g][:, :], wab[b][:, 128 * c:128 * (c + 1)],
                        xln[:, TOK * c + 512 * g: TOK * c + 512 * (g + 1)],
                        start=(c == 0), stop=(c == NCH - 1)),
                        inc=('te' if (g == NTG - 1 and c == NCH - 1) else None))
            te_q = p.cnt['te']
            for g in range(NTG):
                p.wait('vector', 'te', te_q)
                p.op('vector', lambda e, g=g: e.tensor_single_scalar(tmpa[:, :], PSa[g][:, :], 0.0, ALU.min),
                     inc='ve')
                p.wait('scalar', 've')
                p.op('scalar', lambda e: e.activation(tmpb[:, :], tmpa[:, :], AF.Exp), inc='sc')
                p.wait('vector', 'sc')
                p.op('vector', lambda e, g=g: e.scalar_tensor_tensor(
                    xb_sb[g][:, :], PSa[g][:, :], 0.0, tmpb[:, :], ALU.max, ALU.add), inc='ve')
                ve_sig = p.cnt['ve']
                if g == NTG - 1:
                    sig_done[h] = ve_sig
                p.wait('tensor', 've', ve_sig)
                p.op('tensor', lambda e, g=g, h=h: e.matmul(
                    PSa[6][0:4, :], mnT_sb[:, NBK * h: NBK * (h + 1)], xb_sb[g][:, :],
                    start=True, stop=True), inc='te')
                te_n = p.cnt['te']
                p.wait('vector', 'te', te_n)
                p.op('vector', lambda e: e.tensor_single_scalar(tmpa[0:4, :], PSa[6][0:4, :], 1e-6, ALU.max),
                     inc='ve')
                p.wait('scalar', 've')
                p.op('scalar', lambda e: e.activation(ascr[0:4, :], tmpa[0:4, :], AF.Ln))
                p.op('scalar', lambda e: e.activation(tmpb[0:4, :], ascr[0:4, :], AF.Exp,
                                                      scale=-1.0), inc='sc')
                p.wait('vector', 'sc')
                p.op('vector', lambda e, h=h: e.tensor_scalar_mul(
                    g_sb[:, :], tmpb[0:4, :], wact4[0:4, h:h + 1]), inc='ve')
                ve_g = p.cnt['ve']
                sigp_ve = {}
                for n in range(NBK):
                    pb = 4 + (n % 2)
                    p.wait('tensor', 've', ve_g if n == 0 else sigp_ve.get(n - 2, 0))
                    p.op('tensor', lambda e, n=n, pb=pb: e.matmul(
                        PSa[pb][:, :], selG_sb[:, 128 * n:128 * (n + 1)], g_sb[:, :],
                        start=True, stop=True), inc='te')
                    p.wait('vector', 'te')
                    p.op('vector', lambda e, g=g, pb=pb, n=n: e.tensor_mul(
                        sigp[n % 2][:, :], xb_sb[g][:, :], PSa[pb][:, :]), inc='ve')
                    sigp_ve[n] = p.cnt['ve']
                    p.wait('tensor', 've', sigp_ve[n])
                    p.op('tensor', lambda e, g=g, n=n, b=b: e.matmul(
                        PSa[2 + g][:, :], mem_sb[b][:, 128 * n:128 * (n + 1)], sigp[n % 2][:, :],
                        start=(n == 0), stop=(n == NBK - 1)), inc='te')
                p.wait('vector', 'te')
                p.op('vector', lambda e, g=g, h=h: e.tensor_copy(
                    comb[:, TOK * h + 512 * g: TOK * h + 512 * (g + 1)], PSa[2 + g][:, :]), inc='ve')
            prev_te = p.cnt['te']
            for _ in range(2):
                if qsteps:
                    qsteps.pop(0)()
        while qsteps:
            qsteps.pop(0)()
        comb_done = p.cnt['ve']
        p.wait('sync', 'te', p.cnt['te'])
        p.wait('sync', 'dha', p.cnt['dha'])
        load_w(wab[0], woa_bf, 0, 'dha')
        prev_te = 0
        for oc in range(NCH):
            b = oc % 2
            sem_b = 'dha' if b == 0 else 'dhb'
            need = p.cnt[sem_b]
            if oc + 1 < NCH:
                nsem = 'dhb' if b == 0 else 'dha'
                p.wait('sync', 'te', prev_te)
                p.wait('sync', nsem, p.cnt[nsem])
                load_w(wab[1 - b], woa_bf, oc + 1, nsem)
            p.wait('tensor', sem_b, need)
            p.wait('tensor', 've', comb_done)
            for g in range(NTG):
                p.wait('tensor', 've', p.cnt['ve'])
                for c in range(NCH):
                    p.op('tensor', lambda e, g=g, c=c, b=b: e.matmul(
                        PSa[g][:, :], wab[b][:, 128 * c:128 * (c + 1)],
                        comb[:, TOK * c + 512 * g: TOK * c + 512 * (g + 1)],
                        start=(c == 0), stop=(c == NCH - 1)),
                        inc=('te' if c == NCH - 1 else None))
                p.wait('vector', 'te')
                p.op('vector', lambda e, oc=oc, g=g: e.tensor_add(
                    R[:, TOK * oc + 512 * g: TOK * oc + 512 * (g + 1)],
                    R[:, TOK * oc + 512 * g: TOK * oc + 512 * (g + 1)], PSa[g][:, :]), inc='ve')
            prev_te = p.cnt['te']

    def quant_steps():
        RG = [list(range(NCORES))]
        steps = []

        def do_piece(bufb, out_fn):
            dso_mark = {-1: p.cnt['dso']}
            for st in range(4):
                qt2c = qt2[st % 2]
                wsl = qwb[bufb][:, 512 * st:512 * (st + 1)]
                p.wait('vector', 'dso', dso_mark.get(st - 2, dso_mark.get(-1, 0)))
                p.op('vector', lambda e: e.nop(), inc='ve')
                p.wait('scalar', 've')
                p.wait('scalar', 'te', p.cnt['te'])
                p.op('scalar', lambda e, wsl=wsl: e.activation(qt1[:, :], wsl, AF.Abs),
                     inc='sc', drain=False)
                p.wait('tensor', 'sc')
                p.op('tensor', lambda e: e.matmul(PSa[6][0:2, :], ind2_sb[:, :], qt1[:, :],
                                                  start=True, stop=True), inc='te')
                p.wait('vector', 'te')
                p.op('vector', lambda e: e.tensor_scalar(
                    qsd[:, :], PSa[6][0:2, :], 1.0 / 64, 1e-8, ALU.mult, ALU.max), inc='ve')
                p.wait('scalar', 've')
                p.op('scalar', lambda e: e.activation(qsdf[:, :], qsd[:, :], AF.Ln), drain=False)
                p.op('scalar', lambda e: e.activation(qrcp[:, :], qsdf[:, :], AF.Exp,
                                                      scale=-1.0), inc='sc')
                p.wait('tensor', 've')
                p.wait('tensor', 'sc')
                p.op('tensor', lambda e: e.matmul(PSa[4][:, :], e2f_sb[:, :], qrcp[:, :],
                                                  start=True, stop=True))
                p.op('tensor', lambda e: e.matmul(PSa[5][:, :], e2b_sb[:, :], qsd[:, :],
                                                  start=True, stop=True), inc='te')
                p.wait('vector', 'te')
                p.op('vector', lambda e, wsl=wsl, qt2c=qt2c: e.tensor_mul(qt2c[:, :], wsl, PSa[4][:, :]))
                p.op('vector', lambda e, qt2c=qt2c: e.tensor_single_scalar(
                    qt1[:, :], qt2c[:, :], -0.5, ALU.is_lt))
                p.op('vector', lambda e, qt2c=qt2c: e.scalar_tensor_tensor(
                    qt3[:, :], qt2c[:, :], 0.5, qt1[:, :], ALU.is_gt, ALU.subtract))
                p.op('vector', lambda e, qt2c=qt2c: e.tensor_mul(
                    qt2c[:, :], qt3[:, :], PSa[5][:, :]), inc='ve')
                p.wait('scalar', 've')
                p.wait('scalar', 'dso', p.cnt['dso'])
                out_fn(st, qt2c)
                dso_mark[st] = p.cnt['dso']

        def gu_out(rc, cp, st, qt2c):
            hc0 = (2048 * cp + 512 * st) // 128
            dst = wqgu_sh[hc0:hc0 + 4, :, rc, :].transpose([1, 0, 2])
            p.dma('scalar', dst, qt2c.rearrange('p (a h) -> p a h', h=128)[:, :, :], 'dso')

        def p_out(rc, st, qt2c):
            p.dma('scalar', wqp_sh[rc, st, :, :], qt2c[:, :], 'dso')

        def precast(c0, c1):
            p.wait('gpsimd', 'dh', p.cnt['dh'])
            p.wait('gpsimd', 've', p.cnt['ve'])
            for c in range(c0, c1):
                b = c % 2
                sem = 'dga' if b == 0 else 'dgb'
                p.wait('gpsimd', sem, p.cnt[sem])
                p.wait('gpsimd', 'dhb', p.cnt['dhb'])
                p.dma('gpsimd', wst[b][:, :], Wqkvf[128 * c:128 * (c + 1), :], sem)
                p.wait('sync', sem, p.cnt[sem])
                p.dma('sync', wqkvf_bf[128 * c:128 * (c + 1), :], wst[b][:, :], 'dhb')
        steps.append(lambda: precast(0, 8))
        steps.append(lambda: precast(8, 16))

        pieces = [(rc, cp) for rc in range(2) for cp in range(8)]

        def gu_step(i):
            rc, cp = pieces[i]
            b = i % 2
            sem_b = 'dga' if b == 0 else 'dgb'
            if i == 0:
                p.wait('tensor', 'dgc', g_loaded)
                p.wait('gpsimd', 'dga', p.cnt['dga'])
                p.wait('gpsimd', 'dhb', p.cnt['dhb'])
                p.dma('gpsimd', qwb[0][:, :], wguS[0:128, 0:2048], 'dga')
            need = p.cnt[sem_b]
            if i + 1 < len(pieces):
                nrc, ncp = pieces[i + 1]
                nsem = 'dgb' if b == 0 else 'dga'
                p.wait('gpsimd', 've', p.cnt['ve'])
                p.wait('gpsimd', nsem, p.cnt[nsem])
                p.dma('gpsimd', qwb[1 - b][:, :],
                      wguS[128 * nrc:128 * (nrc + 1), 2048 * ncp:2048 * (ncp + 1)],
                      nsem)
            p.wait('vector', sem_b, need)
            do_piece(b, lambda st, qt2c, rc=rc, cp=cp: gu_out(rc, cp, st, qt2c))

        for i in range(len(pieces)):
            steps.append(lambda i=i: gu_step(i))

        def ag_gu():
            p.wait('gpsimd', 'dso', p.cnt['dso'])
            p.op('gpsimd', lambda e: e.collective_compute(
                "AllGather", ALU.bypass, replica_groups=RG,
                ins=[wqgu_sh[:, :, :, :].opt()], outs=[wqgu_all[:, :, :, :].opt()]), inc='cc')
        steps.append(ag_gu)

        def p_step(rc):
            b = rc % 2
            sem_b = 'dga' if b == 0 else 'dgb'
            if rc == 0:
                p.wait('gpsimd', 'dga', p.cnt['dga'])
                p.dma('gpsimd', qwb[0][:, :], wpS[0:128, :], 'dga')
            need = p.cnt[sem_b]
            if rc + 1 < 8:
                nsem = 'dgb' if b == 0 else 'dga'
                p.wait('gpsimd', 've', p.cnt['ve'])
                p.wait('gpsimd', nsem, p.cnt[nsem])
                p.dma('gpsimd', qwb[1 - b][:, :], wpS[128 * (rc + 1):128 * (rc + 2), :], nsem)
            p.wait('vector', sem_b, need)
            do_piece(b, lambda st, qt2c, rc=rc: p_out(rc, st, qt2c))

        for rc in range(8):
            steps.append(lambda rc=rc: p_step(rc))

        def ag_p():
            p.wait('gpsimd', 'dso', p.cnt['dso'])
            p.op('gpsimd', lambda e: e.collective_compute(
                "AllGather", ALU.bypass, replica_groups=RG,
                ins=[wqp_sh[:, :, :, :].opt()], outs=[wqp_all[:, :, :, :].opt()]), inc='cc')
        steps.append(ag_p)
        return steps

    def ffn():
        p.wait('sync', 'cc', 1)
        p.wait('gpsimd', 'cc', 1)
        p.wait('sync', 'te', p.cnt['te'])
        p.wait('gpsimd', 'te', p.cnt['te'])
        p.wait('vector', 'dhb', p.cnt['dhb'])
        p.wait('scalar', 'dhb', p.cnt['dhb'])
        p.wait('sync', 've', p.cnt['ve'])
        p.wait('gpsimd', 've', p.cnt['ve'])
        for T in range(NTG):
            ve_xln = p.cnt['ve']
            def load_gu(hcl, gbuf, ubuf, sems):
                for r in range(8):
                    q, sem = (('sync', sems[0]) if r < 3 else
                              ('scalar', sems[2]) if r < 6 else ('gpsimd', sems[1]))
                    p.dma(q, gbuf[:, 256 * r:256 * (r + 1)],
                          wqgu_all[128 * r + hcl, :, :, :], sem)
                    p.dma(q, ubuf[:, 256 * r:256 * (r + 1)],
                          wqgu_all[128 * r + hcl + 64, :, :, :], sem)

            sc_mark = {}
            ve_mark = {}
            rms_te = {}
            p.wait('sync', 'dha', p.cnt['dha'])
            p.wait('gpsimd', 'dga', p.cnt['dga'])
            p.wait('scalar', 'dsa', p.cnt['dsa'])
            load_gu(0, fgw[0], fgw[1], ('dha', 'dga', 'dsa'))
            for hcl in range(64):
                b = hcl % 2
                sems = ('dha', 'dga', 'dsa') if b == 0 else ('dhb', 'dgb', 'dsb')
                nh = p.cnt[sems[0]]
                ng = p.cnt[sems[1]]
                ns = p.cnt[sems[2]]
                if hcl + 1 < 64:
                    nsems = ('dhb', 'dgb', 'dsb') if b == 0 else ('dha', 'dga', 'dsa')
                    p.wait('sync', 'te', p.cnt['te'])
                    p.wait('gpsimd', 'te', p.cnt['te'])
                    p.wait('scalar', 'te', p.cnt['te'])
                    p.wait('sync', nsems[0], p.cnt[nsems[0]])
                    p.wait('gpsimd', nsems[1], p.cnt[nsems[1]])
                    p.wait('scalar', nsems[2], p.cnt[nsems[2]])
                    load_gu(hcl + 1, fgw[2 * (1 - b)], fgw[2 * (1 - b) + 1], nsems)
                p.wait('tensor', sems[0], nh)
                p.wait('tensor', sems[1], ng)
                p.wait('tensor', sems[2], ns)
                p.wait('tensor', 'sc', sc_mark[hcl - 2] if hcl >= 2 else 0)
                p.wait('tensor', 've', ve_mark[hcl - 2] if hcl >= 2 else ve_xln)
                for c in range(NCH):
                    p.op('tensor', lambda e, c=c, b=b, T=T: e.matmul(
                        PSa[2 * b][:, :], fgw[2 * b][:, 128 * c:128 * (c + 1)],
                        xln[:, TOK * c + 512 * T: TOK * c + 512 * (T + 1)],
                        start=(c == 0), stop=(c == NCH - 1)))
                for c in range(NCH):
                    p.op('tensor', lambda e, c=c, b=b, T=T: e.matmul(
                        PSa[2 * b + 1][:, :], fgw[2 * b + 1][:, 128 * c:128 * (c + 1)],
                        xln[:, TOK * c + 512 * T: TOK * c + 512 * (T + 1)],
                        start=(c == 0), stop=(c == NCH - 1)), inc=('te' if c == NCH - 1 else None))
                p.wait('scalar', 'te')
                if hcl >= 2:
                    p.wait('scalar', 've', ve_mark[hcl - 2])
                p.op('scalar', lambda e, b=b: e.activation(sil_sb[b][:, :], PSa[2 * b][:, :], AF.Silu),
                     inc='sc', drain=False)
                sc_mark[hcl] = p.cnt['sc']
                p.wait('vector', 'sc')
                p.op('vector', lambda e, b=b, hcl=hcl: e.tensor_mul(
                    hbuf[:, 512 * hcl:512 * (hcl + 1)], sil_sb[b][:, :], PSa[2 * b + 1][:, :]), drain=False)
                if hcl >= 2:
                    p.wait('vector', 'te', rms_te.get(hcl - 2, 0))
                p.op('vector', lambda e, b=b, hcl=hcl: e.tensor_mul(
                    hsq_sb[b][:, :], hbuf[:, 512 * hcl:512 * (hcl + 1)],
                    hbuf[:, 512 * hcl:512 * (hcl + 1)]), inc='ve')
                ve_mark[hcl] = p.cnt['ve']
                if hcl > 0:
                    p.wait('tensor', 've', ve_mark[hcl - 1])
                    p.op('tensor', lambda e, hcl=hcl: e.matmul(
                        PSa[6][0:1, :], onesb_sb[:, 0:1], hsq_sb[(hcl - 1) % 2][:, :],
                        start=(hcl - 1 == 0), stop=False), inc='te')
                    rms_te[hcl - 1] = p.cnt['te']
                if hcl == 63:
                    p.wait('tensor', 've', ve_mark[63])
                    p.op('tensor', lambda e: e.matmul(
                        PSa[6][0:1, :], onesb_sb[:, 0:1], hsq_sb[63 % 2][:, :],
                        start=False, stop=True), inc='te')
            p.wait('vector', 'te')
            rr = row_sb[0:1, 0:512]
            p.op('vector', lambda e: e.tensor_scalar_mul(rr, PSa[6][0:1, :], 1.0 / FH), inc='ve')
            p.wait('scalar', 've')
            p.op('scalar', lambda e: e.activation(rr, rr, AF.Ln, bias=cst_sb[0:1, 2:3]))
            p.op('scalar', lambda e: e.activation(rr, rr, AF.Exp, scale=-0.5), inc='sc')
            p.wait('tensor', 'sc')
            p.op('tensor', lambda e: e.matmul(PSa[7][:, :], onesf_sb[0:1, 0:128], rr,
                                              start=True, stop=True), inc='te')
            p.wait('vector', 'te')
            p.op('vector', lambda e: e.tensor_copy(tmpb[:, :], PSa[7][:, :]), inc='ve')
            p.wait('sync', 'cc', 2)
            p.wait('gpsimd', 'cc', 2)

            def load_p(dq, hc, buf):
                q, sem = ('sync', 'dha') if hc % 2 == 0 else ('gpsimd', 'dga')
                p.dma(q, buf[:, :], wqp_all[hc, dq, :, :], sem)

            p.wait('sync', 'dha', p.cnt['dha'])
            load_p(0, 0, fpw[0])
            for dq in range(4):
                for hc in range(64):
                    b = hc % 2
                    sem_b = 'dha' if b == 0 else 'dga'
                    need = p.cnt[sem_b]
                    if dq * 64 + hc + 1 < 256:
                        ndq, nhc = (dq, hc + 1) if hc + 1 < 64 else (dq + 1, 0)
                        nq, nsem = ('sync', 'dha') if nhc % 2 == 0 else ('gpsimd', 'dga')
                        p.wait('sync', 'te', p.cnt['te'])
                        p.wait('gpsimd', 'te', p.cnt['te'])
                        p.wait(nq, nsem, p.cnt[nsem])
                        load_p(ndq, nhc, fpw[1 - b])
                    p.wait('tensor', sem_b, need)
                    if hc == 0:
                        p.wait('tensor', 've', p.cnt['ve'])
                    for cc in range(4):
                        p.op('tensor', lambda e, cc=cc, b=b, hc=hc: e.matmul(
                            PSa[cc][:, :], fpw[b][:, 128 * cc:128 * (cc + 1)],
                            hbuf[:, 512 * hc:512 * (hc + 1)],
                            start=(hc == 0), stop=(hc == 63)),
                            inc=('te' if cc == 3 else None))
                p.wait('vector', 'te')
                for cc in range(4):
                    dchunk = 4 * dq + cc
                    p.op('vector', lambda e, cc=cc: e.tensor_mul(tmpa[:, :], PSa[cc][:, :], tmpb[:, :]))
                    p.op('vector', lambda e, dchunk=dchunk, T=T: e.tensor_add(
                        R[:, TOK * dchunk + 512 * T: TOK * dchunk + 512 * (T + 1)],
                        R[:, TOK * dchunk + 512 * T: TOK * dchunk + 512 * (T + 1)],
                        tmpa[:, :]), inc='ve')

    def attn_b():
        RG = [list(range(NCORES))]
        p.wait('sync', 've', p.cnt['ve'])
        p.wait('sync', 'te', p.cnt['te'])
        p.dma('sync', trif_sb[:, :], trif[:, :], 'dhb')
        for c in range(NCH):
            p.dma('sync', xsha[128 * c:128 * (c + 1), :], xln[:, TOK * c:TOK * c + 512], 'dhb')
            p.dma('sync', xshb[128 * c:128 * (c + 1), :], xln[:, TOK * c + 512:TOK * (c + 1)], 'dhb')
        bounce_done = p.cnt['dhb']
        p.wait('gpsimd', 'dhb', bounce_done)
        p.op('gpsimd', lambda e: e.collective_compute(
            "AllGather", ALU.bypass, replica_groups=RG,
            ins=[xsha[:, :].opt()], outs=[xga[:, :].opt()]), inc='cc')
        cc_xa = p.cnt['cc']
        p.op('gpsimd', lambda e: e.collective_compute(
            "AllGather", ALU.bypass, replica_groups=RG,
            ins=[xshb[:, :].opt()], outs=[xgb[:, :].opt()]), inc='cc')
        cc_x = p.cnt['cc']
        p.wait('sync', 'cc', cc_xa)
        p.wait('gpsimd', 'cc', cc_xa)
        ve_proj0 = p.cnt['ve']

        def load_xa(g, hc, buf, q, sem):
            r, gg = g // 2, g % 2
            src = xga if gg == 0 else xgb
            p.dma(q, buf[:, :], src[2048 * r + 128 * hc:2048 * r + 128 * (hc + 1), :], sem)

        def load_wst(hc, buf, sem, q='gpsimd'):
            p.dma(q, buf[:, :], wqkvf_bf[128 * hc:128 * (hc + 1), :], sem)

        p.wait('vector', 'dhb', bounce_done)
        for pair in range(2):
            p.op('vector', lambda e, pair=pair: e.memset(
                Vsb[pair][:, 0:8320].rearrange('p (c w) -> p c w', w=130)[:, :, 64:65], 1.0))
            p.op('vector', lambda e, pair=pair: e.memset(
                Vsb[pair][:, 0:8320].rearrange('p (c w) -> p c w', w=130)[:, :, 129:130], 1.0), inc='ve')
        p.wait('sync', 'dhb', p.cnt['dhb'])
        load_xa(0, 0, xat[0], 'sync', 'dhb')
        p.wait('gpsimd', 'dhb', p.cnt['dhb'])
        p.wait('gpsimd', 'dga', p.cnt['dga'])
        load_wst(0, wst[0], 'dga')
        gorder = [g for g in range(16) if g % 2 == 0] + [g for g in range(16) if g % 2 == 1]
        for gi, g in enumerate(gorder):
            for hc in range(NCH):
                i = gi * NCH + hc
                b = i % 2
                xsem = 'dhb' if b == 0 else 'dha'
                wsem = 'dga' if b == 0 else 'dsa'
                nx = p.cnt[xsem]
                nw = p.cnt[wsem]
                if i + 1 < 16 * NCH:
                    ng, nhc = (g, hc + 1) if hc + 1 < NCH else (gorder[gi + 1], 0)
                    nxs = 'dha' if b == 0 else 'dhb'
                    nws = 'dsa' if b == 0 else 'dga'
                    nwq = 'scalar' if b == 0 else 'gpsimd'
                    p.wait('sync', 'te', p.cnt['te'])
                    p.wait('sync', nxs, p.cnt[nxs])
                    if ng % 2 == 1 and g % 2 == 0:
                        p.wait('sync', 'cc', cc_x)
                    load_xa(ng, nhc, xat[1 - b], 'sync', nxs)
                    p.wait(nwq, 'te', p.cnt['te'])
                    p.wait(nwq, nws, p.cnt[nws])
                    load_wst(nhc, wst[1 - b], nws, nwq)
                p.wait('tensor', xsem, nx)
                p.wait('tensor', wsem, nw)
                if hc == 0:
                    p.wait('tensor', 've', ve_proj0 if g == 0 else p.cnt['ve'])
                for si in range(4):
                    p.op('tensor', lambda e, si=si, b=b, hc=hc: e.matmul(
                        PSa[si][:, :], wst[b][:, 128 * si:128 * (si + 1)], xat[b][:, :],
                        start=(hc == 0), stop=(hc == NCH - 1)))
                for si in range(4):
                    p.op('tensor', lambda e, si=si, b=b, hc=hc: e.matmul(
                        PSa[4 + si][0:128, 0:260], xat[b][:, 128 * si:128 * (si + 1)],
                        wst[b][:, 512:772],
                        start=(hc == 0), stop=(hc == NCH - 1)),
                        inc=('te' if si == 3 else None))
            p.wait('vector', 'te')
            pr, gg = g // 2, g % 2
            for pair in range(2):
                p.op('vector', lambda e, pair=pair, g=g: e.tensor_scalar_mul(
                    Qsb[pair][:, 512 * g:512 * (g + 1)], PSa[pair][:, :], 0.125),
                    drain=(pair == 0))
                p.op('vector', lambda e, pair=pair, g=g: e.tensor_copy(
                    Ksb[pair][:, 512 * g:512 * (g + 1)], PSa[2 + pair][:, :]), drain=False)
            for si in range(4):
                ch = 4 * g + si
                for pair in range(2):
                    p.op('vector', lambda e, si=si, pair=pair, ch=ch: e.tensor_copy(
                        Vsb[pair][:, 130 * ch:130 * ch + 64],
                        PSa[4 + si][0:128, 128 * pair:128 * pair + 64]), drain=False)
                    p.op('vector', lambda e, si=si, pair=pair, ch=ch: e.tensor_copy(
                        Vsb[pair][:, 130 * ch + 65:130 * ch + 129],
                        PSa[4 + si][0:128, 128 * pair + 64:128 * (pair + 1)]), drain=False)
                p.op('vector', lambda e, si=si: e.tensor_copy(
                    ftmp[:, :], PSa[4 + si][0:128, 256:260]), inc='ve')
                pgrp = 32 * (g // 4)
                p.wait('tensor', 've')
                p.op('tensor', lambda e: e.transpose(
                    PSa[0][0:4, 0:128], ftmp[:, :], ident_sb[:, :]), inc='te')
                p.wait('vector', 'te')
                p.wait('vector', 'dq', p.cnt['dq'])
                p.op('vector', lambda e: e.tensor_copy(ft2[:, :], PSa[0][0:4, 0:128]), inc='ve')
                p.wait('gpsimd', 've')
                col0 = (512 * g + 128 * si) % 2048
                p.dma('gpsimd', Ptil[pgrp:pgrp + 4, col0:col0 + 128], ft2[:, :], 'dq')
        p.wait('vector', 'dgc', g_loaded)
        p.wait('vector', 'dq', p.cnt['dq'])
        p.wait('scalar', 'dq', p.cnt['dq'])
        p.wait('vector', 'te', p.cnt['te'])
        p.op('vector', lambda e: e.memzero(zz8[:, :]))
        p.op('vector', lambda e: e.memzero(initc[:, :]), inc='ve')
        for grp in range(4):
            rs = slice(32 * grp, 32 * grp + 4)
            p.op('vector', lambda e, rs=rs: e.tensor_scalar(
                pcum[rs, :], Ptil[rs, :], bfc_sb[rs, 0:1], 0.0, ALU.add, ALU.min), inc='ve')
            p.wait('scalar', 've')
            p.op('scalar', lambda e, rs=rs: e.activation(
                Ptil[rs, :], Ptil[rs, :], AF.Abs, bias=bfc_sb[rs, 0:1]))
            p.op('scalar', lambda e, rs=rs: e.activation(
                Ptil[rs, :], Ptil[rs, :], AF.Exp, scale=-1.0))
            p.op('scalar', lambda e, rs=rs: e.activation(
                Ptil[rs, :], Ptil[rs, :], AF.Ln, bias=cst_sb[rs, 1:2]), inc='sc')
            p.wait('vector', 'sc')
            p.op('vector', lambda e, rs=rs: e.tensor_sub(
                Ptil[rs, :], Ptil[rs, :], pcum[rs, :]), inc='ve')
        for grp in range(4):
            rs = slice(32 * grp, 32 * grp + 4)
            if grp > 0:
                p.wait('gpsimd', 've', p.cnt['ve'])
                p.dma('gpsimd', initc[32 * grp:32 * grp + 4, 0:1],
                      pcum[32 * (grp - 1):32 * (grp - 1) + 4, 2047:2048], 'dq')
                p.wait('vector', 'dq', p.cnt['dq'])
            p.op('vector', lambda e, rs=rs: e.tensor_tensor_scan(
                pcum[rs, :], Ptil[rs, :], zz8[rs, :], initc[rs, 0:1],
                ALU.add, ALU.add), inc='ve')
        p.wait('tensor', 've')
        for k in range(64):
            grp, col = k // 16, (k % 16) * 128
            p.op('tensor', lambda e, grp=grp, col=col: e.transpose(
                PSa[1][0:128, 0:4], pcum[32 * grp:32 * grp + 4, col:col + 128],
                ident_sb[32 * grp:32 * grp + 4, 32 * grp:32 * grp + 4],
                tile_position=(32 * grp, 0)), inc='te')
            p.wait('vector', 'te')
            p.op('vector', lambda e, k=k: e.tensor_copy(fct[:, 4 * k:4 * (k + 1)],
                                                        PSa[1][0:128, 0:4]), inc='ve')
            p.wait('tensor', 've')
        p.wait('scalar', 've', p.cnt['ve'])
        fq3_rdy = {}
        fq3_free = {0: 0, 1: 0}
        prep_ve_end = 0

        def emit_prep(J):
            nonlocal prep_ve_end
            pr, jq = divmod(J, 16)
            grp, col = jq // 4, (jq % 4) * 512
            pb = J % 2
            p.wait('gpsimd', 've', prep_ve_end)
            p.wait('gpsimd', 'te', fq3_free[pb])
            p.wait('gpsimd', 'dq', p.cnt['dq'])
            for h, hp in ((0, 96), (1, 32)):
                row = 32 * grp + 2 * pr + h
                p.dma('gpsimd', rcpb[hp:hp + 1, :], pcum[row:row + 1, col:col + 512], 'dq')
            vrow = p.cnt['dq']
            p.wait('vector', 'dq', vrow)
            dma0 = {}
            for nm, dr in (('n', True), ('hi', True), ('r', True),
                           ('mid', True), ('r2', True), ('lo', True)):
                for h, hp in ((0, 96), (1, 32)):
                    hs = slice(hp, hp + 1)
                    if nm == 'n':
                        p.op('vector', lambda e, hs=hs: e.tensor_single_scalar(
                            rcpbB[hs, :], rcpb[hs, :], -1.0, ALU.mult), drain=(h == 0))
                    elif nm == 'hi':
                        p.op('vector', lambda e, hs=hs: e.tensor_copy(
                            hiT[hs, :], rcpbB[hs, :]), inc='ve', drain=(h == 0))
                        if h == 1:
                            hi_ve = p.cnt['ve']
                    elif nm == 'r':
                        p.op('vector', lambda e, hs=hs: e.tensor_sub(
                            rcpb[hs, :], rcpbB[hs, :], hiT[hs, :]), drain=(h == 0))
                    elif nm == 'mid':
                        p.op('vector', lambda e, hs=hs: e.tensor_copy(
                            miT[hs, :], rcpb[hs, :]), inc='ve', drain=(h == 0))
                        if h == 1:
                            mid_ve = p.cnt['ve']
                    elif nm == 'r2':
                        p.op('vector', lambda e, hs=hs: e.tensor_sub(
                            rcpbB[hs, :], rcpb[hs, :], miT[hs, :]), drain=(h == 0))
                    else:
                        if h == 0:
                            p.wait('vector', 'dq', dma0['hi'])
                        p.op('vector', lambda e, hs=hs: e.tensor_copy(
                            hiT[hs, :], rcpbB[hs, :]), inc='ve', drain=(h == 0))
                if nm == 'hi' or nm == 'mid':
                    mark = hi_ve if nm == 'hi' else mid_ve
                    ri = 0 if nm == 'hi' else 1
                    p.wait('gpsimd', 've', mark)
                    p.wait('gpsimd', 'dq', p.cnt['dq'])
                    for h, hp in ((0, 96), (1, 32)):
                        src = hiT if nm == 'hi' else miT
                        p.dma('gpsimd', fq3w[pb][ri:ri + 1, 512 * h:512 * (h + 1)],
                              src[hp:hp + 1, :], 'dq')
                    dma0[nm] = p.cnt['dq']
            lo_ve = p.cnt['ve']
            prep_ve_end = lo_ve
            p.wait('gpsimd', 've', lo_ve)
            p.wait('gpsimd', 'dq', p.cnt['dq'])
            for h, hp in ((0, 96), (1, 32)):
                p.dma('gpsimd', fq3w[pb][2:3, 512 * h:512 * (h + 1)], hiT[hp:hp + 1, :], 'dq')
            fq3_rdy[J] = p.cnt['dq']

        emit_prep(0)
        for pair in range(2):
            for j in range(16):
                J = 16 * pair + j
                ve_qb = p.cnt['ve']
                if J + 1 < 32:
                    emit_prep(J + 1)
                pb = J % 2
                nk = 4 * (j + 1)
                sc_mark_f = {}
                p.wait('tensor', 've', ve_qb)
                p.wait('tensor', 'dq', fq3_rdy[J])
                p.op('tensor', lambda e, pb=pb: e.matmul(
                    PSa[4][:, :], onesb_sb[0:3, 0:128], fq3w[pb][0:3, 0:512],
                    start=True, stop=True))
                p.op('tensor', lambda e, pb=pb: e.matmul(
                    PSa[5][:, :], onesb_sb[0:3, 0:128], fq3w[pb][0:3, 512:1024],
                    start=True, stop=True), inc='te')
                p.wait('vector', 'te', p.cnt['te'])
                p.op('vector', lambda e: e.tensor_copy(fqbw[:, 0:512], PSa[4][:, :]))
                p.op('vector', lambda e: e.tensor_copy(fqbw[:, 512:1024], PSa[5][:, :]),
                     inc='ve', drain=False)
                r1_ve = p.cnt['ve']

                def emit_lpv(k, nk):
                    kpar = k % 3
                    mc = 128 * (k - 4 * j) if k >= 4 * j else 0
                    p.op('tensor', lambda e, pair=pair, kpar=kpar, k=k, mc=mc, nk=nk: e.matmul(
                        PSa[6][0:65, mc:512], Vsb[pair][:, 130 * k:130 * k + 65],
                        ptb[kpar][:, mc:512], start=(k == 0), stop=(k == nk - 1)))
                    p.op('tensor', lambda e, pair=pair, kpar=kpar, k=k, mc=mc, nk=nk: e.matmul(
                        PSa[7][0:65, mc:512], Vsb[pair][:, 130 * k + 65:130 * (k + 1)],
                        ptb[3 + kpar][:, mc:512],
                        start=(k == 0), stop=(k == nk - 1)), inc='te')

                for k in range(nk + 2):
                    if k < nk:
                        kpar = k % 3
                        STW = PBa[kpar]
                        STA, STB = PSa[kpar * 2], PSa[kpar * 2 + 1]
                        mc = 128 * (k - 4 * j) if k >= 4 * j else 0
                        use_r3 = (k % 4 == 0) or k >= 4 * j or k < 2
                        if k == 2:
                            p.wait('tensor', 've', r1_ve)
                        p.op('tensor', lambda e, STA=STA, pair=pair, k=k, j=j, mc=mc,
                             u=use_r3: e.matmul(
                            STA[:, mc:512], Ksb[pair][0:64, 128 * k:128 * (k + 1)],
                            Qsb[pair][0:64, 512 * j + mc:512 * (j + 1)],
                            start=True, stop=not u))
                        if use_r3:
                            p.op('tensor', lambda e, STB=STB, pair=pair, k=k, j=j, mc=mc: e.matmul(
                                STB[:, mc:512], Ksb[pair][64:128, 128 * k:128 * (k + 1)],
                                Qsb[pair][64:128, 512 * j + mc:512 * (j + 1)],
                                tile_position=(64, 0), start=True, stop=False))
                            p.op('tensor', lambda e, STA=STA, pb=pb, mc=mc: e.matmul(
                                STA[:, mc:512], onesb_sb[0:3, 0:128],
                                fq3w[pb][0:3, mc:512], start=False, stop=True))
                            p.op('tensor', lambda e, STB=STB, pb=pb, mc=mc: e.matmul(
                                STB[:, mc:512], onesb_sb[0:3, 0:128],
                                fq3w[pb][0:3, 512 + mc:1024], start=False, stop=True), inc='te')
                        else:
                            p.op('tensor', lambda e, STB=STB, pair=pair, k=k, j=j, mc=mc: e.matmul(
                                STB[:, mc:512], Ksb[pair][64:128, 128 * k:128 * (k + 1)],
                                Qsb[pair][64:128, 512 * j + mc:512 * (j + 1)],
                                tile_position=(64, 0), start=True, stop=True), inc='te')
                        te_k = p.cnt['te']
                    if k >= 2:
                        p.wait('tensor', 'sc', sc_mark_f[k - 2])
                        if k - 2 == 0:
                            p.wait('tensor', 've', ve_qb)
                        emit_lpv(k - 2, nk)
                    if k >= nk:
                        continue
                    if k >= 4 * j:
                        p.wait('vector', 'te', te_k)
                        p.op('vector', lambda e, STA=STA, mc=mc: e.tensor_add(
                            STA[:, mc:mc + 128], STA[:, mc:mc + 128], trif_sb[:, :]))
                        p.op('vector', lambda e, STB=STB, mc=mc: e.tensor_add(
                            STB[:, mc:mc + 128], STB[:, mc:mc + 128], trif_sb[:, :]),
                            inc='ve', drain=False)
                        p.wait('scalar', 've', p.cnt['ve'])
                    elif not use_r3:
                        p.wait('vector', 'te', te_k)
                        p.op('vector', lambda e, STW=STW: e.tensor_add(
                            STW[:, :], STW[:, :], fqbw[:, :]), inc='ve', drain=(k == 3))
                        p.wait('scalar', 've', p.cnt['ve'])
                    else:
                        p.wait('scalar', 'te', te_k)
                    p.op('scalar', lambda e, STA=STA, kpar=kpar, k=k, pair=pair, mc=mc: e.activation(
                        ptb[kpar][:, mc:512], STA[:, mc:512], AF.Exp,
                        bias=fct[:, 4 * k + 2 * pair:4 * k + 2 * pair + 1]), drain=False)
                    p.op('scalar', lambda e, STB=STB, kpar=kpar, k=k, pair=pair, mc=mc: e.activation(
                        ptb[3 + kpar][:, mc:512], STB[:, mc:512], AF.Exp,
                        bias=fct[:, 4 * k + 2 * pair + 1:4 * k + 2 * pair + 2]), inc='sc', drain=False)
                    sc_mark_f[k] = p.cnt['sc']
                fq3_free[pb] = p.cnt['te']
                p.wait('vector', 'te')
                p.wait('vector', 'dq', p.cnt['dq'])
                p.wait('scalar', 'te', p.cnt['te'])
                p.op('scalar', lambda e: e.activation(dvb[64:65, :], PSa[6][64:65, :], AF.Ln))
                p.op('scalar', lambda e: e.activation(rcpb[64:65, :], dvb[64:65, :], AF.Exp,
                                                      scale=-1.0))
                p.op('scalar', lambda e: e.activation(dvb2[64:65, :], PSa[7][64:65, :], AF.Ln, drain=False)
                     if False else e.activation(dvb2[64:65, :], PSa[7][64:65, :], AF.Ln))
                p.op('scalar', lambda e: e.activation(rcpbB[64:65, :], dvb2[64:65, :], AF.Exp,
                                                      scale=-1.0), inc='sc')
                p.wait('vector', 'sc')
                p.op('vector', lambda e: e.nop(), inc='ve')
                p.wait('gpsimd', 've')
                p.dma('gpsimd', rcpb[0:1, :], rcpb[64:65, :], 'dq')
                p.dma('gpsimd', rcpbB[0:1, :], rcpbB[64:65, :], 'dq')
                p.wait('tensor', 'dq', p.cnt['dq'])
                p.wait('tensor', 've', p.cnt['ve'])
                p.wait('tensor', 'sc', p.cnt['sc'])
                p.op('tensor', lambda e: e.matmul(PSa[0][0:64, :], onesf_sb[0:1, 0:64],
                                                  rcpb[0:1, :], start=True, stop=True))
                p.op('tensor', lambda e: e.matmul(PSa[1][0:64, :], onesf_sb[0:1, 0:64],
                                                  rcpbB[0:1, :], start=True, stop=True), inc='te')
                p.wait('vector', 'te')
                p.op('vector', lambda e: e.tensor_copy(dvb[0:64, :], PSa[0][0:64, :]))
                p.op('vector', lambda e: e.tensor_copy(dvb2[0:64, :], PSa[1][0:64, :]))
                p.op('vector', lambda e: e.tensor_mul(onrm_sb[0:64, :], PSa[6][0:64, :], dvb[0:64, :]))
                p.op('vector', lambda e: e.tensor_mul(onrm2_sb[0:64, :], PSa[7][0:64, :],
                                                      dvb2[0:64, :]), inc='ve')
                p.wait('gpsimd', 've')
                p.dma('gpsimd', obnc[128 * pair:128 * pair + 64, 512 * j:512 * (j + 1)],
                      onrm_sb[0:64, :], 'dq')
                p.dma('gpsimd', obnc[128 * pair + 64:128 * (pair + 1), 512 * j:512 * (j + 1)],
                      onrm2_sb[0:64, :], 'dq')
        p.wait('gpsimd', 'dq', p.cnt['dq'])
        p.op('gpsimd', lambda e: e.collective_compute(
            "AllGather", ALU.bypass, replica_groups=RG,
            ins=[obnc[:, :].opt()], outs=[ogath[:, :].opt()]), inc='cc')
        cc_o = p.cnt['cc']
        p.wait('sync', 'cc', cc_o)
        p.wait('sync', 'te', p.cnt['te'])
        p.wait('sync', 'dhb', p.cnt['dhb'])
        cell = {}
        p.emit('sync', lambda e: cell.__setitem__('off', e.partition_id()))
        for c in range(NCH):
            def oload(e, c=c):
                src = ogath[128 * c:128 * (c + 1), bass.ts(cell['off'], TOK)]
                return e.dma_start(out=oown[:, TOK * c:TOK * (c + 1)], in_=src)
            p.op('sync', oload, inc='dhb', n=16)
        oo_done = p.cnt['dhb']

        def load_wo(dst, h, sem):
            for c in range(NCH):
                p.dma('gpsimd', dst[:, 128 * c:128 * (c + 1)],
                      WobT[128 * c:128 * (c + 1), 128 * h:128 * (h + 1)], sem)

        p.wait('gpsimd', 'te', p.cnt['te'])
        p.wait('gpsimd', 'dga', p.cnt['dga'])
        load_wo(fgw[0], 0, 'dga')
        prev_te = p.cnt['te']
        for oc in range(NCH):
            b = oc % 2
            sem_b = 'dga' if b == 0 else 'dgb'
            need = p.cnt[sem_b]
            if oc + 1 < NCH:
                nsem = 'dgb' if b == 0 else 'dga'
                p.wait('gpsimd', 'te', prev_te)
                p.wait('gpsimd', nsem, p.cnt[nsem])
                load_wo(fgw[1 - b], oc + 1, nsem)
            p.wait('tensor', sem_b, need)
            p.wait('tensor', 'dhb', oo_done)
            for g in range(NTG):
                p.wait('tensor', 've', p.cnt['ve'])
                for c in range(NCH):
                    p.op('tensor', lambda e, g=g, c=c, b=b: e.matmul(
                        PSa[g][:, :], fgw[b][:, 128 * c:128 * (c + 1)],
                        oown[:, TOK * c + 512 * g: TOK * c + 512 * (g + 1)],
                        start=(c == 0), stop=(c == NCH - 1)),
                        inc=('te' if c == NCH - 1 else None))
                p.wait('vector', 'te')
                p.op('vector', lambda e, oc=oc, g=g: e.tensor_add(
                    R[:, TOK * oc + 512 * g: TOK * oc + 512 * (g + 1)],
                    R[:, TOK * oc + 512 * g: TOK * oc + 512 * (g + 1)], PSa[g][:, :]), inc='ve')
            prev_te = p.cnt['te']

    for c in range(NCH):
        b = c % 2
        gsem = 'dga' if b == 0 else 'dgb'
        p.wait('gpsimd', gsem, p.cnt[gsem])
        p.wait('gpsimd', 'dhb', p.cnt['dhb'])
        p.dma('gpsimd', qwb[b][:, :], WqaT[128 * c:128 * (c + 1), :], gsem)
        p.wait('sync', gsem, p.cnt[gsem])
        p.dma('sync', wqa_bf[:, 128 * c:128 * (c + 1), :].transpose([1, 0, 2]),
              qwb[b].rearrange('p (h w) -> p h w', w=128)[:, :, :], 'dhb')
    for c in range(NCH):
        b = c % 2
        gsem = 'dga' if b == 0 else 'dgb'
        p.wait('gpsimd', gsem, p.cnt[gsem])
        p.wait('gpsimd', 'dhb', p.cnt['dhb'])
        p.dma('gpsimd', qwb[b][:, :], WoaT[128 * c:128 * (c + 1), :], gsem)
        p.wait('sync', gsem, p.cnt[gsem])
        p.dma('sync', woa_bf[:, 128 * c:128 * (c + 1), :].transpose([1, 0, 2]),
              qwb[b].rearrange('p (h w) -> p h w', w=128)[:, :, :], 'dhb')
    wqa_done = p.cnt['dhb']
    qsteps = quant_steps() if 'q' in phases else []
    bank_weights_prep()
    layernorm()
    tap_xln = mktap('xln', [128, NCH * TOK], BF16)
    if tap_xln is not None:
        p.wait('gpsimd', 've')
        p.dma('gpsimd', tap_xln[:, :], xln[:, :], 'st')
    if 'a' in phases:
        attn_a(qsteps)
    else:
        for st in qsteps:
            st()
    if 'f' in phases:
        layernorm()
        ffn()
    if 'b' in phases:
        layernorm()
        attn_b()
    if 'g' in phases:
        layernorm()
        ffn()

    p.wait('sync', 've')
    p.wait('sync', 'te')
    for c in range(NCH):
        p.dma('sync', y[128 * c:128 * (c + 1), :], R[:, TOK * c:TOK * (c + 1)], 'st')
    p.wait('gpsimd', 'st')
    p.build()
    return nc


def kernel(**inputs):
    nc = build_nc(phases=('q', 'a', 'f', 'b', 'g'))
    in_maps = [host_inputs(inputs, c) for c in range(NCORES)]
    res = run_bass_kernel_spmd(nc, in_maps, core_ids=list(range(NCORES)))
    out = np.concatenate([r['y'].T for r in res.results], axis=0)
    return out.reshape(1, S, D).astype(np.float32)


# version 19
# speedup vs baseline: 1.2196x; 1.0038x over previous
import sys, os
if '/opt/trn_rl_repo' not in sys.path:
    sys.path.insert(0, '/opt/trn_rl_repo')
import numpy as np
import ml_dtypes
import concourse.bass as bass
import concourse.mybir as mybir
from concourse.bass_utils import run_bass_kernel_spmd
from concourse import dve_ops

F32 = mybir.dt.float32
BF16 = mybir.dt.bfloat16
AF = mybir.ActivationFunctionType
ALU = mybir.AluOpType
BF = ml_dtypes.bfloat16

NCORES = 8
S = 8192
D = 2048
TOK = S // NCORES
NCH = D // 128
NTG = TOK // 512
HA = 16
NBK = 4
HB = 32
HBL = HB // NCORES
HD = 64
FH = 8192
FH2 = 16384
LNEPS = 1e-5
RMSEPS = 1.1920929e-7
NEGBIG = -1.0e9


def _c(a):
    return np.ascontiguousarray(a)


def host_inputs(inp, core):
    d = {}
    x = np.asarray(inp['L0_attn_a_hidden_states'])
    sl = slice(TOK * core, TOK * (core + 1))
    d['xT'] = _c(x[0, sl, :].T)

    d['WqaT'] = _c(np.asarray(inp['L0_attn_a_Wq']).T)
    d['WoaT'] = _c(np.asarray(inp['L0_attn_a_Wo']).T)
    mem = np.asarray(inp['L0_attn_a_memories'])
    d['memH'] = _c(mem.transpose(1, 0, 2, 3))
    mn = np.asarray(inp['L0_attn_a_memory_norms'])
    d['mnT'] = _c(mn.transpose(1, 2, 0).reshape(128 * HA // 128, 128, NBK)
                  .reshape(HA, 128, NBK))
    d['mnFlat'] = _c(mn.reshape(NBK, HA * 128))
    bw = np.asarray(inp['L0_attn_a_bank_weights'])
    d['bwT'] = _c(bw.T)

    hsl = slice(HBL * HD * core, HBL * HD * (core + 1))
    d['Wqkvf'] = _c(np.concatenate([
        np.asarray(inp['L0_attn_b_Wq'])[hsl, :].T,
        np.asarray(inp['L0_attn_b_Wk'])[hsl, :].T,
        np.asarray(inp['L0_attn_b_Wv'])[hsl, :].T,
        np.asarray(inp['L0_attn_b_Wf'])[HBL * core: HBL * (core + 1), :].T,
    ], axis=1))
    bfv = np.asarray(inp['L0_attn_b_bf'])[HBL * core: HBL * (core + 1)]
    bff = np.zeros((128, 1), dtype=np.float32)
    for grp in range(4):
        bff[32 * grp:32 * grp + 4, 0] = bfv
    d['bf_full'] = bff
    d['WobT'] = _c(np.asarray(inp['L0_attn_b_Wo']).T)

    wguT = np.asarray(inp['L0_ffn_w_gate_up']).T
    d['wguS'] = _c(wguT[256 * core: 256 * (core + 1), :])
    wpT = np.asarray(inp['L0_ffn_w_proj']).T
    d['wpS'] = _c(wpT[1024 * core: 1024 * (core + 1), :])

    d['ident_f32'] = np.eye(128, dtype=np.float32)
    d['ones_f32'] = np.ones((128, 512), dtype=np.float32)
    d['ones_bf'] = np.ones((128, 512), dtype=BF)
    tri = np.tril(np.ones((128, 128), dtype=np.float32), -1)
    d['Utri_bf'] = tri.astype(BF)
    d['negI_bf'] = (NEGBIG * np.eye(128, dtype=np.float32)).astype(BF)
    d['trif'] = NEGBIG * tri
    sel = np.zeros((4, 512), dtype=np.float32)
    for n in range(4):
        sel[n, 128 * n:128 * (n + 1)] = 1.0
    d['selG'] = sel.astype(BF)
    ind2 = np.zeros((128, 2), dtype=np.float32)
    ind2[:64, 0] = 1.0
    ind2[64:, 1] = 1.0
    d['ind2'] = ind2.astype(BF)
    e2 = np.zeros((2, 128), dtype=np.float32)
    e2[0, :64] = 1.0
    e2[1, 64:] = 1.0
    d['e2_bf'] = e2.astype(BF)
    d['e2_f32'] = e2
    cst = np.zeros((128, 8), dtype=np.float32)
    cst[:, 0] = LNEPS
    cst[:, 1] = 1.0
    cst[:, 2] = RMSEPS
    d['cst'] = cst
    return d


class P:
    def __init__(self, nc):
        self.nc = nc
        self.streams = {k: [] for k in ('tensor', 'vector', 'scalar', 'sync', 'gpsimd')}
        self.sem = {}
        self.cnt = {}
        from contextlib import ExitStack
        self.stack = ExitStack()

    def mksem(self, *names):
        for n in names:
            self.sem[n] = self.stack.enter_context(self.nc.semaphore(n))
            self.cnt[n] = 0

    def emit(self, eng, f):
        self.streams[eng].append(f)

    def wait(self, eng, sname, val=None):
        v = self.cnt[sname] if val is None else val
        s = self.sem[sname]
        if v > 0:
            self.emit(eng, lambda e, s=s, v=v: e.wait_ge(s, v))
        return v

    def op(self, eng, f, inc=None, n=1, drain=True):
        if drain and eng in ('vector', 'scalar'):
            self.emit(eng, lambda e: e.drain())
        if inc is not None:
            self.cnt[inc] += n
            s = self.sem[inc]
            self.emit(eng, lambda e, f=f, s=s, n=n: f(e).then_inc(s, n))
        else:
            self.emit(eng, lambda e, f=f: f(e))

    def dma(self, q, out, in_, sname):
        self.op(q, lambda e, o=out, i=in_: e.dma_start(out=o, in_=i), inc=sname, n=16,
                drain=False)
        return self.cnt[sname]

    def build(self):
        nc = self.nc
        with nc.Block() as block:
            @block.tensor
            def _(e):
                for f in self.streams['tensor']:
                    f(e)

            @block.vector
            def _(e):
                for f in self.streams['vector']:
                    f(e)

            @block.scalar
            def _(e):
                for f in self.streams['scalar']:
                    f(e)

            @block.sync
            def _(e):
                for f in self.streams['sync']:
                    f(e)

            @block.gpsimd
            def _(e):
                for f in self.streams['gpsimd']:
                    f(e)
        self.stack.close()


def build_nc(phases=('q', 'a', 'f'), debug_taps=()):
    nc = bass.Bass("TRN2", target_bir_lowering=False, debug=False)
    p = P(nc)
    p.mksem('dh', 'dha', 'dhb', 'dga', 'dgb', 'dgc', 'dsa', 'dsb', 'dso', 'te', 've', 'sc', 'cc', 'st', 'dq')

    io = {}
    def din(name, shape, dtype=F32):
        io[name] = nc.dram_tensor(name, list(shape), dtype, kind="ExternalInput").ap()
        return io[name]

    xT = din('xT', [D, TOK])
    WqaT = din('WqaT', [D, D]); WoaT = din('WoaT', [D, D])
    memH = din('memH', [HA, NBK, 128, 128]); mnT = din('mnT', [HA, 128, NBK])
    mnFlat = din('mnFlat', [NBK, HA * 128]); bwT = din('bwT', [NBK, HA])
    Wqkvf = din('Wqkvf', [D, 772]); bf_full = din('bf_full', [128, 1])
    WobT = din('WobT', [D, D])
    wguS = din('wguS', [256, FH2]); wpS = din('wpS', [1024, D])
    ident_f32 = din('ident_f32', [128, 128])
    ones_f32 = din('ones_f32', [128, 512]); ones_bf = din('ones_bf', [128, 512], BF16)
    Utri_bf = din('Utri_bf', [128, 128], BF16); negI_bf = din('negI_bf', [128, 128], BF16)
    trif = din('trif', [128, 128])
    selG = din('selG', [4, 512], BF16)
    ind2 = din('ind2', [128, 2], BF16); e2_bf = din('e2_bf', [2, 128], BF16)
    e2_f32 = din('e2_f32', [2, 128]); cst = din('cst', [128, 8])

    y = nc.dram_tensor('y', [D, TOK], F32, kind="ExternalOutput").ap()
    wqgu_sh = nc.dram_tensor('wqgu_sh', [128, 128, 2, 128], BF16).ap()
    wqgu_all = nc.dram_tensor('wqgu_all', [8 * 128, 128, 2, 128], BF16, addr_space='Shared').ap()
    wqp_sh = nc.dram_tensor('wqp_sh', [8, 4, 128, 512], BF16).ap()
    wqp_all = nc.dram_tensor('wqp_all', [64, 4, 128, 512], BF16, addr_space='Shared').ap()
    xsha = nc.dram_tensor('xsha', [D, TOK // 2], BF16).ap()
    xshb = nc.dram_tensor('xshb', [D, TOK // 2], BF16).ap()
    xga = nc.dram_tensor('xga', [8 * D, TOK // 2], BF16, addr_space='Shared').ap()
    xgb = nc.dram_tensor('xgb', [8 * D, TOK // 2], BF16, addr_space='Shared').ap()
    obnc = nc.dram_tensor('obnc', [256, S], BF16).ap()
    ogath = nc.dram_tensor('ogath', [8 * 256, S], BF16, addr_space='Shared').ap()
    wqkvf_bf = nc.dram_tensor('wqkvf_bf', [D, 772], BF16).ap()
    wqa_bf = nc.dram_tensor('wqa_bf', [HA, D, 128], BF16).ap()
    woa_bf = nc.dram_tensor('woa_bf', [NCH, D, 128], BF16).ap()
    taps = {}
    for t in debug_taps:
        pass

    def mktap(name, shape, dtype=F32):
        if name in debug_taps:
            taps[name] = nc.dram_tensor('tap_' + name, list(shape), dtype,
                                        kind="ExternalOutput").ap()
            return taps[name]
        return None

    SB = {}
    SBASE = 16512
    def sb(name, shape, dtype, off):
        h = nc.alloc_sbuf_tensor_at(name, list(shape), dtype, offset=SBASE + off)
        SB[name] = h
        return h.ap()

    R = sb('R', [128, NCH * TOK], F32, 0)
    xln = sb('xln', [128, NCH * TOK], BF16, 65536)
    comb = sb('comb', [128, HA * TOK], BF16, 98304)
    hbuf = sb('hbuf', [128, (FH // 128) * 512], BF16, 98304)
    Ksb = [sb(f'Ksb{i}', [128, S], BF16, 98816 + 16384 * i) for i in range(2)]
    Qsb = [sb(f'Qsb{i}', [128, S], BF16, 131584 + 16384 * i) for i in range(2)]
    Vsb = [sb(f'Vsb{i}', [128, (S // 128) * 130], BF16, 65536 + 16640 * i) for i in range(2)]
    W0 = 163840
    wab = [sb(f'wab{i}', [128, NCH * 128], BF16, W0 + 4096 * i) for i in range(2)]
    mem_sb = [sb(f'mem_sb{i}', [128, NBK * 128], BF16, W0 + 8192 + 1024 * i) for i in range(2)]
    mnT_sb = sb('mnT_sb', [128, HA * NBK], BF16, W0 + 10240)
    smallc = W0 + 10496
    bw_sb = sb('bw_sb', [4, HA], F32, smallc)
    wact4 = sb('wact4', [4, HA], F32, smallc + 128)
    act_sb = sb('act_sb', [4, 1], F32, smallc + 320)
    g_sb = sb('g_sb', [4, 512], BF16, smallc + 384)
    ident_sb = sb('ident_sb', [128, 128], F32, smallc + 1408)
    onesf_sb = sb('onesf_sb', [128, 512], F32, smallc + 1920)
    onesb_sb = sb('onesb_sb', [128, 512], BF16, smallc + 3968)
    selG_sb = sb('selG_sb', [4, 512], BF16, smallc + 4992)
    utri_sb = sb('utri_sb', [128, 128], BF16, smallc + 6144)
    negI_sb = sb('negI_sb', [128, 128], BF16, smallc + 6400)
    trif_sb = sb('trif_sb', [128, 128], F32, 172288)
    cst_sb = sb('cst_sb', [128, 8], F32, smallc + 6016)
    lnc = smallc + 6656
    mnf_sb = sb('mnf_sb', [4, HA * 128], F32, lnc)
    xb_sb = [sb(f'xb_sb{i}', [128, 512], BF16, lnc + 1024 * i) for i in range(2)]
    xsq_sb = [sb(f'xsq_sb{i}', [128, 512], BF16, lnc + 2048 + 1024 * i) for i in range(2)]
    sigp = [sb(f'sigp{i}', [128, 512], BF16, lnc + 10240 + 1024 * i) for i in range(2)]
    row_sb = sb('row_sb', [1, 512 * 5], F32, lnc + 4096)
    tmpa = sb('tmpa', [128, 512], F32, lnc + 14336)
    tmpb = sb('tmpb', [128, 512], F32, lnc + 16384)
    qwb = [sb(f'qwb{i}', [128, 2048], BF16, 131840 + 4096 * i) for i in range(2)]
    qt1 = sb('qt1', [128, 512], BF16, 131840 + 8192)
    qt2 = [sb('qt2a', [128, 512], BF16, 131840 + 9216), sb('qt2b', [128, 512], BF16, 131840 + 22016)]
    qt3 = sb('qt3', [128, 512], BF16, 131840 + 10240)
    qsd = sb('qsd', [2, 512], BF16, 131840 + 11264)
    qrcp = sb('qrcp', [2, 512], F32, 131840 + 12288)
    qsdf = sb('qsdf', [2, 512], F32, 131840 + 15872)
    qscr = sb('qscr', [2, 512], F32, 131840 + 17920)
    ascr = sb('ascr', [128, 512], F32, 131840 + 19968)
    ind2_sb = sb('ind2_sb', [128, 2], BF16, 131840 + 14336)
    e2b_sb = sb('e2b_sb', [2, 128], BF16, 131840 + 14848)
    e2f_sb = sb('e2f_sb', [2, 128], F32, 131840 + 15360)
    fgw = [sb('fgw0', [128, 2048], BF16, 0 + 163840), sb('fgw1', [128, 2048], BF16, 4096 + 163840),
           sb('fgw2', [128, 2048], BF16, 203520), sb('fgw3', [128, 2048], BF16, 207616)]
    fpw = [sb('fpw0', [128, 512], BF16, 172032), sb('fpw1', [128, 512], BF16, 173056)]
    sil_sb = [sb('sil_sb0', [128, 512], BF16, lnc + 18432), sb('sil_sb1', [128, 512], BF16, lnc + 19456)]
    hsq_sb = [sb('hsq_sb0', [128, 512], BF16, lnc + 20480), sb('hsq_sb1', [128, 512], BF16, lnc + 21504)]
    xat = [sb(f'xat{i}', [128, 512], BF16, lnc + 1024 * i) for i in range(4)]
    Ptil = sb('Ptil', [128, 2048], F32, lnc + 14336)
    zz8 = sb('zz8', [128, 2048], F32, lnc + 4096)
    pcum = sb('pcum', [128, 2048], F32, 203520)
    wst = [sb(f'wst{i}', [128, 772], BF16, lnc + 4096 + 1664 * i) for i in range(2)]
    ptb = [sb(f'ptb{i}', [128, 512], BF16, lnc + 10240 + 1024 * i) for i in range(6)]
    rank1A = sb('rank1A', [128, 512], F32, 167936)
    rank1B = sb('rank1B', [128, 512], F32, 170240)
    fq3w = [sb(f'fq3w{b}', [3, 1024], BF16, 164352 + 2048 * b) for b in range(2)]
    fqbw = sb('fqbw', [128, 1024], F32, lnc + 16384)
    hiT = sb('hiT', [97, 512], BF16, 168448)
    miT = sb('miT', [97, 512], BF16, 169472)
    rcpb = sb('rcpb', [128, 512], F32, lnc)
    rcpbB = sb('rcpbB', [128, 512], F32, lnc + 2048)
    dvb = sb('dvb', [128, 512], F32, lnc + 4096)
    dvb2 = sb('dvb2', [128, 512], F32, lnc + 6144)
    onrm_sb = sb('onrm_sb', [128, 512], BF16, lnc + 8192)
    onrm2_sb = sb('onrm2_sb', [128, 512], BF16, lnc + 9216)
    fct = sb('fct', [128, 256], F32, 211712)
    bfc_sb = sb('bfc_sb', [128, 1], F32, 212736)
    initc = sb('initc', [128, 1], F32, 212768)
    ftmp = sb('ftmp', [128, 4], F32, 174080)
    ft2 = sb('ft2', [4, 128], F32, lnc + 12288)
    oown = sb('oown', [128, NCH * 1024], BF16, 65536)
    assert 16512 + 212800 <= 229344

    PB = [nc.alloc_psum_tensor(f'PB{i}', [128, 1024], F32) for i in range(3)]
    PS = [nc.alloc_psum_tensor(f'PS{i}', [128, 512], F32) for i in range(2)]
    PSa = [PB[0].ap()[:, 0:512], PB[0].ap()[:, 512:1024],
           PB[1].ap()[:, 0:512], PB[1].ap()[:, 512:1024],
           PB[2].ap()[:, 0:512], PB[2].ap()[:, 512:1024]] + [t.ap() for t in PS]
    PBa = [t.ap() for t in PB]

    for dst, src in ((ident_sb, ident_f32), (onesf_sb, ones_f32), (onesb_sb, ones_bf),
                     (selG_sb, selG), (bw_sb, bwT), (mnf_sb, mnFlat), (cst_sb, cst),
                     (utri_sb, Utri_bf), (negI_sb, negI_bf)):
        p.dma('sync', dst[:, :], src[:, :], 'dh')
    for h in range(HA):
        p.dma('gpsimd', mnT_sb[:, NBK * h:NBK * (h + 1)], mnT[h, :, :], 'dgc')
    p.dma('gpsimd', bfc_sb[:, :], bf_full[:, :], 'dgc')
    p.dma('gpsimd', ind2_sb[:, :], ind2[:, :], 'dgc')
    p.dma('gpsimd', e2b_sb[:, :], e2_bf[:, :], 'dgc')
    p.dma('gpsimd', e2f_sb[:, :], e2_f32[:, :], 'dgc')
    for c in range(NCH):
        p.dma('sync', R[:, TOK * c:TOK * (c + 1)], xT[128 * c:128 * (c + 1), :], 'dh')
    c_loaded = p.cnt['dh']
    g_loaded = p.cnt['dgc']

    def rcp_fast(out, in_, scratch, inc=None):
        p.op('vector', lambda e: e.reciprocal_approx_fast(out=scratch, in_=in_))
        p.op('vector', lambda e: e._custom_dve(
            dve_ops.RECIPROCAL_APPROX_NR, out=out, in0=in_, in1=scratch, s0=2.0), inc=inc)

    def bank_weights_prep():
        p.wait('vector', 'dh', c_loaded)
        p.wait('vector', 'dgc', g_loaded)
        p.op('vector', lambda e: e.reduce_sum(tmpb[0:4, 0:1], mnf_sb[:, :], mybir.AxisListType.X))
        p.op('vector', lambda e: e.tensor_single_scalar(act_sb[:, :], tmpb[0:4, 0:1], 1e-6, ALU.is_ge), inc='ve')
        p.wait('scalar', 'dh', c_loaded)
        p.op('scalar', lambda e: e.activation(tmpa[0:4, 0:HA], bw_sb[:, :], AF.Exp), inc='sc')
        p.wait('tensor', 'sc')
        p.op('tensor', lambda e: e.matmul(PSa[6][0:1, 0:HA], onesf_sb[0:4, 0:1], tmpa[0:4, 0:HA],
                                          start=True, stop=True), inc='te')
        p.wait('vector', 'te')
        p.op('vector', lambda e: e.reciprocal(row_sb[0:1, 0:HA], PSa[6][0:1, 0:HA]), inc='ve')
        p.wait('tensor', 've')
        p.op('tensor', lambda e: e.matmul(PSa[7][0:4, 0:HA], onesf_sb[0:1, 0:4], row_sb[0:1, 0:HA],
                                          start=True, stop=True), inc='te')
        p.wait('vector', 'te')
        p.op('vector', lambda e: e.scalar_tensor_tensor(wact4[:, :], tmpa[0:4, 0:HA], act_sb[:, :],
                                                        PSa[7][0:4, 0:HA], ALU.mult, ALU.mult), inc='ve')

    def layernorm():
        te0 = p.cnt['te']
        p.wait('vector', 'dhb', p.cnt['dhb'])
        for g in range(NTG):
            gv0 = p.cnt['ve']
            for c in range(NCH):
                b = c % 2
                p.wait('vector', 'te', p.cnt['te'])
                p.op('vector', lambda e, c=c, g=g, b=b: e.tensor_copy(
                    xb_sb[b][:, :], R[:, TOK * c + 512 * g: TOK * c + 512 * (g + 1)]), inc='ve')
                p.wait('scalar', 've')
                p.op('scalar', lambda e, b=b: e.activation(xsq_sb[b][:, :], xb_sb[b][:, :], AF.Square), inc='sc')
                p.wait('tensor', 've')
                p.wait('tensor', 'sc')
                p.op('tensor', lambda e, b=b, c=c: e.matmul(
                    PSa[6][0:1, :], onesb_sb[:, 0:1], xb_sb[b][:, :], start=(c == 0), stop=(c == NCH - 1)))
                p.op('tensor', lambda e, b=b, c=c: e.matmul(
                    PSa[7][0:1, :], onesb_sb[:, 0:1], xsq_sb[b][:, :], start=(c == 0), stop=(c == NCH - 1)),
                    inc='te')
            p.wait('vector', 'te')
            m = row_sb[0:1, 0:512]; rstd = row_sb[0:1, 512:1024]; mr = row_sb[0:1, 1024:1536]
            var = row_sb[0:1, 1536:2048]; m2 = row_sb[0:1, 2048:2560]
            p.op('vector', lambda e: e.tensor_scalar_mul(m, PSa[6][0:1, :], 1.0 / D))
            p.op('vector', lambda e: e.tensor_scalar_mul(var, PSa[7][0:1, :], 1.0 / D))
            p.op('vector', lambda e: e.tensor_mul(m2, m, m))
            p.op('vector', lambda e: e.tensor_sub(var, var, m2), inc='ve')
            p.wait('scalar', 've')
            p.op('scalar', lambda e: e.activation(var, var, AF.Ln, bias=cst_sb[0:1, 0:1]))
            p.op('scalar', lambda e: e.activation(rstd, var, AF.Exp, scale=-0.5), inc='sc')
            p.wait('vector', 'sc')
            p.op('vector', lambda e: e.tensor_mul(mr, m, rstd), inc='ve')
            p.wait('tensor', 've')
            p.op('tensor', lambda e: e.matmul(PSa[4][:, :], onesf_sb[0:1, 0:128], rstd, start=True, stop=True))
            p.op('tensor', lambda e: e.matmul(PSa[5][:, :], onesf_sb[0:1, 0:128], mr, start=True, stop=True),
                 inc='te')
            p.wait('vector', 'te')
            for c in range(NCH):
                p.op('vector', lambda e, c=c, g=g: e.tensor_mul(
                    tmpa[:, :], R[:, TOK * c + 512 * g: TOK * c + 512 * (g + 1)], PSa[4][:, :]))
                p.op('vector', lambda e, c=c, g=g: e.tensor_sub(
                    xln[:, TOK * c + 512 * g: TOK * c + 512 * (g + 1)], tmpa[:, :], PSa[5][:, :]), inc='ve')

    def attn_a(qsteps=()):
        qsteps = list(qsteps)
        def load_w(dst, srcB, h, sem):
            p.dma('sync', dst[:, :],
                  srcB[h, :, :].rearrange('(c p) w -> p c w', p=128)[:, :, :], sem)

        def load_mem(dst, h, sem):
            for n in range(NBK):
                p.dma('gpsimd', dst[:, 128 * n:128 * (n + 1)], memH[h, n, :, :], sem)

        p.wait('sync', 'dhb', wqa_done)
        p.wait('sync', 'dha', p.cnt['dha'])
        load_w(wab[0], wqa_bf, 0, 'dha')
        p.wait('gpsimd', 'dga', p.cnt['dga'])
        load_mem(mem_sb[0], 0, 'dga')
        xln_ready = p.cnt['ve']
        prev_te = 0
        sig_done = {}
        for h in range(HA):
            b = h % 2
            sem_b = 'dha' if b == 0 else 'dhb'
            gsem_b = 'dga' if b == 0 else 'dgb'
            need = p.cnt[sem_b]
            gneed = p.cnt[gsem_b]
            if h + 1 < HA:
                nsem = 'dhb' if b == 0 else 'dha'
                gsem = 'dgb' if b == 0 else 'dga'
                p.wait('sync', 'te', prev_te)
                p.wait('sync', nsem, p.cnt[nsem])
                load_w(wab[1 - b], wqa_bf, h + 1, nsem)
                p.wait('gpsimd', 'te', prev_te)
                p.wait('gpsimd', gsem, p.cnt[gsem])
                load_mem(mem_sb[1 - b], h + 1, gsem)
            p.wait('tensor', sem_b, need)
            p.wait('tensor', gsem_b, gneed)
            p.wait('tensor', 've', max(sig_done.get(h - 1, 0), xln_ready))
            for g in range(NTG):
                for c in range(NCH):
                    p.op('tensor', lambda e, g=g, c=c, b=b: e.matmul(
                        PSa[g][:, :], wab[b][:, 128 * c:128 * (c + 1)],
                        xln[:, TOK * c + 512 * g: TOK * c + 512 * (g + 1)],
                        start=(c == 0), stop=(c == NCH - 1)),
                        inc=('te' if (g == NTG - 1 and c == NCH - 1) else None))
            te_q = p.cnt['te']
            for g in range(NTG):
                p.wait('vector', 'te', te_q)
                p.op('vector', lambda e, g=g: e.tensor_single_scalar(tmpa[:, :], PSa[g][:, :], 0.0, ALU.min),
                     inc='ve')
                p.wait('scalar', 've')
                p.op('scalar', lambda e: e.activation(tmpb[:, :], tmpa[:, :], AF.Exp), inc='sc')
                p.wait('vector', 'sc')
                p.op('vector', lambda e, g=g: e.scalar_tensor_tensor(
                    xb_sb[g][:, :], PSa[g][:, :], 0.0, tmpb[:, :], ALU.max, ALU.add), inc='ve')
                ve_sig = p.cnt['ve']
                if g == NTG - 1:
                    sig_done[h] = ve_sig
                p.wait('tensor', 've', ve_sig)
                p.op('tensor', lambda e, g=g, h=h: e.matmul(
                    PSa[6][0:4, :], mnT_sb[:, NBK * h: NBK * (h + 1)], xb_sb[g][:, :],
                    start=True, stop=True), inc='te')
                te_n = p.cnt['te']
                p.wait('vector', 'te', te_n)
                p.op('vector', lambda e: e.tensor_single_scalar(tmpa[0:4, :], PSa[6][0:4, :], 1e-6, ALU.max),
                     inc='ve')
                p.wait('scalar', 've')
                p.op('scalar', lambda e: e.activation(ascr[0:4, :], tmpa[0:4, :], AF.Ln))
                p.op('scalar', lambda e: e.activation(tmpb[0:4, :], ascr[0:4, :], AF.Exp,
                                                      scale=-1.0), inc='sc')
                p.wait('vector', 'sc')
                p.op('vector', lambda e, h=h: e.tensor_scalar_mul(
                    g_sb[:, :], tmpb[0:4, :], wact4[0:4, h:h + 1]), inc='ve')
                ve_g = p.cnt['ve']
                sigp_ve = {}
                for n in range(NBK):
                    pb = 4 + (n % 2)
                    p.wait('tensor', 've', ve_g if n == 0 else sigp_ve.get(n - 2, 0))
                    p.op('tensor', lambda e, n=n, pb=pb: e.matmul(
                        PSa[pb][:, :], selG_sb[:, 128 * n:128 * (n + 1)], g_sb[:, :],
                        start=True, stop=True), inc='te')
                    p.wait('vector', 'te')
                    p.op('vector', lambda e, g=g, pb=pb, n=n: e.tensor_mul(
                        sigp[n % 2][:, :], xb_sb[g][:, :], PSa[pb][:, :]), inc='ve')
                    sigp_ve[n] = p.cnt['ve']
                    p.wait('tensor', 've', sigp_ve[n])
                    p.op('tensor', lambda e, g=g, n=n, b=b: e.matmul(
                        PSa[2 + g][:, :], mem_sb[b][:, 128 * n:128 * (n + 1)], sigp[n % 2][:, :],
                        start=(n == 0), stop=(n == NBK - 1)), inc='te')
                p.wait('vector', 'te')
                p.op('vector', lambda e, g=g, h=h: e.tensor_copy(
                    comb[:, TOK * h + 512 * g: TOK * h + 512 * (g + 1)], PSa[2 + g][:, :]), inc='ve')
            prev_te = p.cnt['te']
            for _ in range(2):
                if qsteps:
                    qsteps.pop(0)()
        while qsteps:
            qsteps.pop(0)()
        comb_done = p.cnt['ve']
        p.wait('sync', 'te', p.cnt['te'])
        p.wait('sync', 'dha', p.cnt['dha'])
        load_w(wab[0], woa_bf, 0, 'dha')
        prev_te = 0
        for oc in range(NCH):
            b = oc % 2
            sem_b = 'dha' if b == 0 else 'dhb'
            need = p.cnt[sem_b]
            if oc + 1 < NCH:
                nsem = 'dhb' if b == 0 else 'dha'
                p.wait('sync', 'te', prev_te)
                p.wait('sync', nsem, p.cnt[nsem])
                load_w(wab[1 - b], woa_bf, oc + 1, nsem)
            p.wait('tensor', sem_b, need)
            p.wait('tensor', 've', comb_done)
            for g in range(NTG):
                p.wait('tensor', 've', p.cnt['ve'])
                for c in range(NCH):
                    p.op('tensor', lambda e, g=g, c=c, b=b: e.matmul(
                        PSa[g][:, :], wab[b][:, 128 * c:128 * (c + 1)],
                        comb[:, TOK * c + 512 * g: TOK * c + 512 * (g + 1)],
                        start=(c == 0), stop=(c == NCH - 1)),
                        inc=('te' if c == NCH - 1 else None))
                p.wait('vector', 'te')
                p.op('vector', lambda e, oc=oc, g=g: e.tensor_add(
                    R[:, TOK * oc + 512 * g: TOK * oc + 512 * (g + 1)],
                    R[:, TOK * oc + 512 * g: TOK * oc + 512 * (g + 1)], PSa[g][:, :]), inc='ve')
            prev_te = p.cnt['te']

    def quant_steps():
        RG = [list(range(NCORES))]
        steps = []

        def do_piece(bufb, out_fn):
            dso_mark = {-1: p.cnt['dso']}
            for st in range(4):
                qt2c = qt2[st % 2]
                wsl = qwb[bufb][:, 512 * st:512 * (st + 1)]
                p.wait('vector', 'dso', dso_mark.get(st - 2, dso_mark.get(-1, 0)))
                p.op('vector', lambda e: e.nop(), inc='ve')
                p.wait('scalar', 've')
                p.wait('scalar', 'te', p.cnt['te'])
                p.op('scalar', lambda e, wsl=wsl: e.activation(qt1[:, :], wsl, AF.Abs),
                     inc='sc', drain=False)
                p.wait('tensor', 'sc')
                p.op('tensor', lambda e: e.matmul(PSa[6][0:2, :], ind2_sb[:, :], qt1[:, :],
                                                  start=True, stop=True), inc='te')
                p.wait('vector', 'te')
                p.op('vector', lambda e: e.tensor_scalar(
                    qsd[:, :], PSa[6][0:2, :], 1.0 / 64, 1e-8, ALU.mult, ALU.max), inc='ve')
                p.wait('scalar', 've')
                p.op('scalar', lambda e: e.activation(qsdf[:, :], qsd[:, :], AF.Ln), drain=False)
                p.op('scalar', lambda e: e.activation(qrcp[:, :], qsdf[:, :], AF.Exp,
                                                      scale=-1.0), inc='sc')
                p.wait('tensor', 've')
                p.wait('tensor', 'sc')
                p.op('tensor', lambda e: e.matmul(PSa[4][:, :], e2f_sb[:, :], qrcp[:, :],
                                                  start=True, stop=True))
                p.op('tensor', lambda e: e.matmul(PSa[5][:, :], e2b_sb[:, :], qsd[:, :],
                                                  start=True, stop=True), inc='te')
                p.wait('vector', 'te')
                p.op('vector', lambda e, wsl=wsl, qt2c=qt2c: e.tensor_mul(qt2c[:, :], wsl, PSa[4][:, :]))
                p.op('vector', lambda e, qt2c=qt2c: e.tensor_single_scalar(
                    qt1[:, :], qt2c[:, :], -0.5, ALU.is_lt))
                p.op('vector', lambda e, qt2c=qt2c: e.scalar_tensor_tensor(
                    qt3[:, :], qt2c[:, :], 0.5, qt1[:, :], ALU.is_gt, ALU.subtract))
                p.op('vector', lambda e, qt2c=qt2c: e.tensor_mul(
                    qt2c[:, :], qt3[:, :], PSa[5][:, :]), inc='ve')
                p.wait('scalar', 've')
                p.wait('scalar', 'dso', p.cnt['dso'])
                out_fn(st, qt2c)
                dso_mark[st] = p.cnt['dso']

        def gu_out(rc, cp, st, qt2c):
            hc0 = (2048 * cp + 512 * st) // 128
            dst = wqgu_sh[hc0:hc0 + 4, :, rc, :].transpose([1, 0, 2])
            p.dma('scalar', dst, qt2c.rearrange('p (a h) -> p a h', h=128)[:, :, :], 'dso')

        def p_out(rc, st, qt2c):
            p.dma('scalar', wqp_sh[rc, st, :, :], qt2c[:, :], 'dso')

        def precast(c0, c1):
            p.wait('gpsimd', 'dh', p.cnt['dh'])
            p.wait('gpsimd', 've', p.cnt['ve'])
            for c in range(c0, c1):
                b = c % 2
                sem = 'dga' if b == 0 else 'dgb'
                p.wait('gpsimd', sem, p.cnt[sem])
                p.wait('gpsimd', 'dhb', p.cnt['dhb'])
                p.dma('gpsimd', wst[b][:, :], Wqkvf[128 * c:128 * (c + 1), :], sem)
                p.wait('sync', sem, p.cnt[sem])
                p.dma('sync', wqkvf_bf[128 * c:128 * (c + 1), :], wst[b][:, :], 'dhb')
        steps.append(lambda: precast(0, 8))
        steps.append(lambda: precast(8, 16))

        pieces = [(rc, cp) for rc in range(2) for cp in range(8)]

        def gu_step(i):
            rc, cp = pieces[i]
            b = i % 2
            sem_b = 'dga' if b == 0 else 'dgb'
            if i == 0:
                p.wait('tensor', 'dgc', g_loaded)
                p.wait('gpsimd', 'dga', p.cnt['dga'])
                p.wait('gpsimd', 'dhb', p.cnt['dhb'])
                p.dma('gpsimd', qwb[0][:, :], wguS[0:128, 0:2048], 'dga')
            need = p.cnt[sem_b]
            if i + 1 < len(pieces):
                nrc, ncp = pieces[i + 1]
                nsem = 'dgb' if b == 0 else 'dga'
                p.wait('gpsimd', 've', p.cnt['ve'])
                p.wait('gpsimd', nsem, p.cnt[nsem])
                p.dma('gpsimd', qwb[1 - b][:, :],
                      wguS[128 * nrc:128 * (nrc + 1), 2048 * ncp:2048 * (ncp + 1)],
                      nsem)
            p.wait('vector', sem_b, need)
            do_piece(b, lambda st, qt2c, rc=rc, cp=cp: gu_out(rc, cp, st, qt2c))

        for i in range(len(pieces)):
            steps.append(lambda i=i: gu_step(i))

        def ag_gu():
            p.wait('gpsimd', 'dso', p.cnt['dso'])
            p.op('gpsimd', lambda e: e.collective_compute(
                "AllGather", ALU.bypass, replica_groups=RG,
                ins=[wqgu_sh[:, :, :, :].opt()], outs=[wqgu_all[:, :, :, :].opt()]), inc='cc')
        steps.append(ag_gu)

        def p_step(rc):
            b = rc % 2
            sem_b = 'dga' if b == 0 else 'dgb'
            if rc == 0:
                p.wait('gpsimd', 'dga', p.cnt['dga'])
                p.dma('gpsimd', qwb[0][:, :], wpS[0:128, :], 'dga')
            need = p.cnt[sem_b]
            if rc + 1 < 8:
                nsem = 'dgb' if b == 0 else 'dga'
                p.wait('gpsimd', 've', p.cnt['ve'])
                p.wait('gpsimd', nsem, p.cnt[nsem])
                p.dma('gpsimd', qwb[1 - b][:, :], wpS[128 * (rc + 1):128 * (rc + 2), :], nsem)
            p.wait('vector', sem_b, need)
            do_piece(b, lambda st, qt2c, rc=rc: p_out(rc, st, qt2c))

        for rc in range(8):
            steps.append(lambda rc=rc: p_step(rc))

        def ag_p():
            p.wait('gpsimd', 'dso', p.cnt['dso'])
            p.op('gpsimd', lambda e: e.collective_compute(
                "AllGather", ALU.bypass, replica_groups=RG,
                ins=[wqp_sh[:, :, :, :].opt()], outs=[wqp_all[:, :, :, :].opt()]), inc='cc')
        steps.append(ag_p)
        return steps

    def ffn():
        p.wait('sync', 'cc', 1)
        p.wait('gpsimd', 'cc', 1)
        p.wait('sync', 'te', p.cnt['te'])
        p.wait('gpsimd', 'te', p.cnt['te'])
        p.wait('vector', 'dhb', p.cnt['dhb'])
        p.wait('scalar', 'dhb', p.cnt['dhb'])
        p.wait('sync', 've', p.cnt['ve'])
        p.wait('gpsimd', 've', p.cnt['ve'])
        for T in range(NTG):
            ve_xln = p.cnt['ve']
            def load_gu(hcl, gbuf, ubuf, sems):
                for r in range(8):
                    q, sem = (('sync', sems[0]) if r < 3 else
                              ('scalar', sems[2]) if r < 6 else ('gpsimd', sems[1]))
                    p.dma(q, gbuf[:, 256 * r:256 * (r + 1)],
                          wqgu_all[128 * r + hcl, :, :, :], sem)
                    p.dma(q, ubuf[:, 256 * r:256 * (r + 1)],
                          wqgu_all[128 * r + hcl + 64, :, :, :], sem)

            sc_mark = {}
            ve_mark = {}
            rms_te = {}
            p.wait('sync', 'dha', p.cnt['dha'])
            p.wait('gpsimd', 'dga', p.cnt['dga'])
            p.wait('scalar', 'dsa', p.cnt['dsa'])
            load_gu(0, fgw[0], fgw[1], ('dha', 'dga', 'dsa'))
            for hcl in range(64):
                b = hcl % 2
                sems = ('dha', 'dga', 'dsa') if b == 0 else ('dhb', 'dgb', 'dsb')
                nh = p.cnt[sems[0]]
                ng = p.cnt[sems[1]]
                ns = p.cnt[sems[2]]
                if hcl + 1 < 64:
                    nsems = ('dhb', 'dgb', 'dsb') if b == 0 else ('dha', 'dga', 'dsa')
                    p.wait('sync', 'te', p.cnt['te'])
                    p.wait('gpsimd', 'te', p.cnt['te'])
                    p.wait('scalar', 'te', p.cnt['te'])
                    p.wait('sync', nsems[0], p.cnt[nsems[0]])
                    p.wait('gpsimd', nsems[1], p.cnt[nsems[1]])
                    p.wait('scalar', nsems[2], p.cnt[nsems[2]])
                    load_gu(hcl + 1, fgw[2 * (1 - b)], fgw[2 * (1 - b) + 1], nsems)
                p.wait('tensor', sems[0], nh)
                p.wait('tensor', sems[1], ng)
                p.wait('tensor', sems[2], ns)
                p.wait('tensor', 'sc', sc_mark[hcl - 2] if hcl >= 2 else 0)
                p.wait('tensor', 've', ve_mark[hcl - 2] if hcl >= 2 else ve_xln)
                for c in range(NCH):
                    p.op('tensor', lambda e, c=c, b=b, T=T: e.matmul(
                        PSa[2 * b][:, :], fgw[2 * b][:, 128 * c:128 * (c + 1)],
                        xln[:, TOK * c + 512 * T: TOK * c + 512 * (T + 1)],
                        start=(c == 0), stop=(c == NCH - 1)))
                for c in range(NCH):
                    p.op('tensor', lambda e, c=c, b=b, T=T: e.matmul(
                        PSa[2 * b + 1][:, :], fgw[2 * b + 1][:, 128 * c:128 * (c + 1)],
                        xln[:, TOK * c + 512 * T: TOK * c + 512 * (T + 1)],
                        start=(c == 0), stop=(c == NCH - 1)), inc=('te' if c == NCH - 1 else None))
                p.wait('scalar', 'te')
                if hcl >= 2:
                    p.wait('scalar', 've', ve_mark[hcl - 2])
                p.op('scalar', lambda e, b=b: e.activation(sil_sb[b][:, :], PSa[2 * b][:, :], AF.Silu),
                     inc='sc', drain=False)
                sc_mark[hcl] = p.cnt['sc']
                p.wait('vector', 'sc')
                p.op('vector', lambda e, b=b, hcl=hcl: e.tensor_mul(
                    hbuf[:, 512 * hcl:512 * (hcl + 1)], sil_sb[b][:, :], PSa[2 * b + 1][:, :]), drain=False)
                if hcl >= 2:
                    p.wait('vector', 'te', rms_te.get(hcl - 2, 0))
                p.op('vector', lambda e, b=b, hcl=hcl: e.tensor_mul(
                    hsq_sb[b][:, :], hbuf[:, 512 * hcl:512 * (hcl + 1)],
                    hbuf[:, 512 * hcl:512 * (hcl + 1)]), inc='ve')
                ve_mark[hcl] = p.cnt['ve']
                if hcl > 0:
                    p.wait('tensor', 've', ve_mark[hcl - 1])
                    p.op('tensor', lambda e, hcl=hcl: e.matmul(
                        PSa[6][0:1, :], onesb_sb[:, 0:1], hsq_sb[(hcl - 1) % 2][:, :],
                        start=(hcl - 1 == 0), stop=False), inc='te')
                    rms_te[hcl - 1] = p.cnt['te']
                if hcl == 63:
                    p.wait('tensor', 've', ve_mark[63])
                    p.op('tensor', lambda e: e.matmul(
                        PSa[6][0:1, :], onesb_sb[:, 0:1], hsq_sb[63 % 2][:, :],
                        start=False, stop=True), inc='te')
            p.wait('vector', 'te')
            rr = row_sb[0:1, 0:512]
            p.op('vector', lambda e: e.tensor_scalar_mul(rr, PSa[6][0:1, :], 1.0 / FH), inc='ve')
            p.wait('scalar', 've')
            p.op('scalar', lambda e: e.activation(rr, rr, AF.Ln, bias=cst_sb[0:1, 2:3]))
            p.op('scalar', lambda e: e.activation(rr, rr, AF.Exp, scale=-0.5), inc='sc')
            p.wait('tensor', 'sc')
            p.op('tensor', lambda e: e.matmul(PSa[7][:, :], onesf_sb[0:1, 0:128], rr,
                                              start=True, stop=True), inc='te')
            p.wait('vector', 'te')
            p.op('vector', lambda e: e.tensor_copy(tmpb[:, :], PSa[7][:, :]), inc='ve')
            p.wait('sync', 'cc', 2)
            p.wait('gpsimd', 'cc', 2)

            def load_p(dq, hc, buf):
                q, sem = ('sync', 'dha') if hc % 2 == 0 else ('gpsimd', 'dga')
                p.dma(q, buf[:, :], wqp_all[hc, dq, :, :], sem)

            p.wait('sync', 'dha', p.cnt['dha'])
            load_p(0, 0, fpw[0])
            for dq in range(4):
                for hc in range(64):
                    b = hc % 2
                    sem_b = 'dha' if b == 0 else 'dga'
                    need = p.cnt[sem_b]
                    if dq * 64 + hc + 1 < 256:
                        ndq, nhc = (dq, hc + 1) if hc + 1 < 64 else (dq + 1, 0)
                        nq, nsem = ('sync', 'dha') if nhc % 2 == 0 else ('gpsimd', 'dga')
                        p.wait('sync', 'te', p.cnt['te'])
                        p.wait('gpsimd', 'te', p.cnt['te'])
                        p.wait(nq, nsem, p.cnt[nsem])
                        load_p(ndq, nhc, fpw[1 - b])
                    p.wait('tensor', sem_b, need)
                    if hc == 0:
                        p.wait('tensor', 've', p.cnt['ve'])
                    for cc in range(4):
                        p.op('tensor', lambda e, cc=cc, b=b, hc=hc: e.matmul(
                            PSa[cc][:, :], fpw[b][:, 128 * cc:128 * (cc + 1)],
                            hbuf[:, 512 * hc:512 * (hc + 1)],
                            start=(hc == 0), stop=(hc == 63)),
                            inc=('te' if cc == 3 else None))
                p.wait('vector', 'te')
                for cc in range(4):
                    dchunk = 4 * dq + cc
                    p.op('vector', lambda e, cc=cc: e.tensor_mul(tmpa[:, :], PSa[cc][:, :], tmpb[:, :]))
                    p.op('vector', lambda e, dchunk=dchunk, T=T: e.tensor_add(
                        R[:, TOK * dchunk + 512 * T: TOK * dchunk + 512 * (T + 1)],
                        R[:, TOK * dchunk + 512 * T: TOK * dchunk + 512 * (T + 1)],
                        tmpa[:, :]), inc='ve')

    def attn_b():
        RG = [list(range(NCORES))]
        p.wait('sync', 've', p.cnt['ve'])
        p.wait('sync', 'te', p.cnt['te'])
        p.dma('sync', trif_sb[:, :], trif[:, :], 'dhb')
        for c in range(NCH):
            p.dma('sync', xsha[128 * c:128 * (c + 1), :], xln[:, TOK * c:TOK * c + 512], 'dhb')
            p.dma('sync', xshb[128 * c:128 * (c + 1), :], xln[:, TOK * c + 512:TOK * (c + 1)], 'dhb')
        bounce_done = p.cnt['dhb']
        p.wait('gpsimd', 'dhb', bounce_done)
        p.op('gpsimd', lambda e: e.collective_compute(
            "AllGather", ALU.bypass, replica_groups=RG,
            ins=[xsha[:, :].opt()], outs=[xga[:, :].opt()]), inc='cc')
        cc_xa = p.cnt['cc']
        p.op('gpsimd', lambda e: e.collective_compute(
            "AllGather", ALU.bypass, replica_groups=RG,
            ins=[xshb[:, :].opt()], outs=[xgb[:, :].opt()]), inc='cc')
        cc_x = p.cnt['cc']
        p.wait('sync', 'cc', cc_xa)
        p.wait('gpsimd', 'cc', cc_xa)
        ve_proj0 = p.cnt['ve']

        def load_xa(g, hc, buf, q, sem):
            r, gg = g // 2, g % 2
            src = xga if gg == 0 else xgb
            p.dma(q, buf[:, :], src[2048 * r + 128 * hc:2048 * r + 128 * (hc + 1), :], sem)

        def load_wst(hc, buf, sem, q='gpsimd'):
            p.dma(q, buf[:, :], wqkvf_bf[128 * hc:128 * (hc + 1), :], sem)

        p.wait('vector', 'dhb', bounce_done)
        for pair in range(2):
            p.op('vector', lambda e, pair=pair: e.memset(
                Vsb[pair][:, 0:8320].rearrange('p (c w) -> p c w', w=130)[:, :, 64:65], 1.0))
            p.op('vector', lambda e, pair=pair: e.memset(
                Vsb[pair][:, 0:8320].rearrange('p (c w) -> p c w', w=130)[:, :, 129:130], 1.0), inc='ve')
        p.wait('sync', 'dhb', p.cnt['dhb'])
        load_xa(0, 0, xat[0], 'sync', 'dhb')
        p.wait('gpsimd', 'dhb', p.cnt['dhb'])
        p.wait('gpsimd', 'dga', p.cnt['dga'])
        load_wst(0, wst[0], 'dga')
        gorder = [g for g in range(16) if g % 2 == 0] + [g for g in range(16) if g % 2 == 1]
        for gi, g in enumerate(gorder):
            for hc in range(NCH):
                i = gi * NCH + hc
                b = i % 2
                xsem = 'dhb' if b == 0 else 'dha'
                wsem = 'dga' if b == 0 else 'dsa'
                nx = p.cnt[xsem]
                nw = p.cnt[wsem]
                if i + 1 < 16 * NCH:
                    ng, nhc = (g, hc + 1) if hc + 1 < NCH else (gorder[gi + 1], 0)
                    nxs = 'dha' if b == 0 else 'dhb'
                    nws = 'dsa' if b == 0 else 'dga'
                    nwq = 'scalar' if b == 0 else 'gpsimd'
                    p.wait('sync', 'te', p.cnt['te'])
                    p.wait('sync', nxs, p.cnt[nxs])
                    if ng % 2 == 1 and g % 2 == 0:
                        p.wait('sync', 'cc', cc_x)
                    load_xa(ng, nhc, xat[1 - b], 'sync', nxs)
                    p.wait(nwq, 'te', p.cnt['te'])
                    p.wait(nwq, nws, p.cnt[nws])
                    load_wst(nhc, wst[1 - b], nws, nwq)
                p.wait('tensor', xsem, nx)
                p.wait('tensor', wsem, nw)
                if hc == 0:
                    p.wait('tensor', 've', ve_proj0 if g == 0 else p.cnt['ve'])
                for si in range(4):
                    p.op('tensor', lambda e, si=si, b=b, hc=hc: e.matmul(
                        PSa[si][:, :], wst[b][:, 128 * si:128 * (si + 1)], xat[b][:, :],
                        start=(hc == 0), stop=(hc == NCH - 1)))
                for si in range(4):
                    p.op('tensor', lambda e, si=si, b=b, hc=hc: e.matmul(
                        PSa[4 + si][0:128, 0:260], xat[b][:, 128 * si:128 * (si + 1)],
                        wst[b][:, 512:772],
                        start=(hc == 0), stop=(hc == NCH - 1)),
                        inc=('te' if si == 3 else None))
            p.wait('vector', 'te')
            pr, gg = g // 2, g % 2
            for pair in range(2):
                p.op('vector', lambda e, pair=pair, g=g: e.tensor_scalar_mul(
                    Qsb[pair][:, 512 * g:512 * (g + 1)], PSa[pair][:, :], 0.125),
                    drain=(pair == 0))
                p.op('vector', lambda e, pair=pair, g=g: e.tensor_copy(
                    Ksb[pair][:, 512 * g:512 * (g + 1)], PSa[2 + pair][:, :]), drain=False)
            for si in range(4):
                ch = 4 * g + si
                for pair in range(2):
                    p.op('vector', lambda e, si=si, pair=pair, ch=ch: e.tensor_copy(
                        Vsb[pair][:, 130 * ch:130 * ch + 64],
                        PSa[4 + si][0:128, 128 * pair:128 * pair + 64]), drain=False)
                    p.op('vector', lambda e, si=si, pair=pair, ch=ch: e.tensor_copy(
                        Vsb[pair][:, 130 * ch + 65:130 * ch + 129],
                        PSa[4 + si][0:128, 128 * pair + 64:128 * (pair + 1)]), drain=False)
                p.op('vector', lambda e, si=si: e.tensor_copy(
                    ftmp[:, :], PSa[4 + si][0:128, 256:260]), inc='ve')
                pgrp = 32 * (g // 4)
                p.wait('tensor', 've')
                p.op('tensor', lambda e: e.transpose(
                    PSa[0][0:4, 0:128], ftmp[:, :], ident_sb[:, :]), inc='te')
                p.wait('vector', 'te')
                p.wait('vector', 'dq', p.cnt['dq'])
                p.op('vector', lambda e: e.tensor_copy(ft2[:, :], PSa[0][0:4, 0:128]), inc='ve')
                p.wait('gpsimd', 've')
                col0 = (512 * g + 128 * si) % 2048
                p.dma('gpsimd', Ptil[pgrp:pgrp + 4, col0:col0 + 128], ft2[:, :], 'dq')
        p.wait('vector', 'dgc', g_loaded)
        p.wait('vector', 'dq', p.cnt['dq'])
        p.wait('scalar', 'dq', p.cnt['dq'])
        p.wait('vector', 'te', p.cnt['te'])
        p.op('vector', lambda e: e.memzero(zz8[:, :]))
        p.op('vector', lambda e: e.memzero(initc[:, :]), inc='ve')
        for grp in range(4):
            rs = slice(32 * grp, 32 * grp + 4)
            p.op('vector', lambda e, rs=rs: e.tensor_scalar(
                pcum[rs, :], Ptil[rs, :], bfc_sb[rs, 0:1], 0.0, ALU.add, ALU.min), inc='ve')
            p.wait('scalar', 've')
            p.op('scalar', lambda e, rs=rs: e.activation(
                Ptil[rs, :], Ptil[rs, :], AF.Abs, bias=bfc_sb[rs, 0:1]))
            p.op('scalar', lambda e, rs=rs: e.activation(
                Ptil[rs, :], Ptil[rs, :], AF.Exp, scale=-1.0))
            p.op('scalar', lambda e, rs=rs: e.activation(
                Ptil[rs, :], Ptil[rs, :], AF.Ln, bias=cst_sb[rs, 1:2]), inc='sc')
            p.wait('vector', 'sc')
            p.op('vector', lambda e, rs=rs: e.tensor_sub(
                Ptil[rs, :], Ptil[rs, :], pcum[rs, :]), inc='ve')
        for grp in range(4):
            rs = slice(32 * grp, 32 * grp + 4)
            if grp > 0:
                p.wait('gpsimd', 've', p.cnt['ve'])
                p.dma('gpsimd', initc[32 * grp:32 * grp + 4, 0:1],
                      pcum[32 * (grp - 1):32 * (grp - 1) + 4, 2047:2048], 'dq')
                p.wait('vector', 'dq', p.cnt['dq'])
            p.op('vector', lambda e, rs=rs: e.tensor_tensor_scan(
                pcum[rs, :], Ptil[rs, :], zz8[rs, :], initc[rs, 0:1],
                ALU.add, ALU.add), inc='ve')
        p.wait('tensor', 've')
        for k in range(64):
            grp, col = k // 16, (k % 16) * 128
            p.op('tensor', lambda e, grp=grp, col=col: e.transpose(
                PSa[1][0:128, 0:4], pcum[32 * grp:32 * grp + 4, col:col + 128],
                ident_sb[32 * grp:32 * grp + 4, 32 * grp:32 * grp + 4],
                tile_position=(32 * grp, 0)), inc='te')
            p.wait('vector', 'te')
            p.op('vector', lambda e, k=k: e.tensor_copy(fct[:, 4 * k:4 * (k + 1)],
                                                        PSa[1][0:128, 0:4]), inc='ve')
            p.wait('tensor', 've')
        p.wait('scalar', 've', p.cnt['ve'])
        fq3_rdy = {}
        fq3_free = {0: 0, 1: 0}
        prep_ve_end = 0

        def emit_prep(J):
            nonlocal prep_ve_end
            pr, jq = divmod(J, 16)
            grp, col = jq // 4, (jq % 4) * 512
            pb = J % 2
            p.wait('gpsimd', 've', prep_ve_end)
            p.wait('gpsimd', 'te', fq3_free[pb])
            p.wait('gpsimd', 'dq', p.cnt['dq'])
            for h, hp in ((0, 96), (1, 32)):
                row = 32 * grp + 2 * pr + h
                p.dma('gpsimd', rcpb[hp:hp + 1, :], pcum[row:row + 1, col:col + 512], 'dq')
            vrow = p.cnt['dq']
            p.wait('vector', 'dq', vrow)
            dma0 = {}
            for nm, dr in (('n', True), ('hi', True), ('r', True),
                           ('mid', True), ('r2', True), ('lo', True)):
                for h, hp in ((0, 96), (1, 32)):
                    hs = slice(hp, hp + 1)
                    if nm == 'n':
                        p.op('vector', lambda e, hs=hs: e.tensor_single_scalar(
                            rcpbB[hs, :], rcpb[hs, :], -1.0, ALU.mult), drain=(h == 0))
                    elif nm == 'hi':
                        p.op('vector', lambda e, hs=hs: e.tensor_copy(
                            hiT[hs, :], rcpbB[hs, :]), inc='ve', drain=(h == 0))
                        if h == 1:
                            hi_ve = p.cnt['ve']
                    elif nm == 'r':
                        p.op('vector', lambda e, hs=hs: e.tensor_sub(
                            rcpb[hs, :], rcpbB[hs, :], hiT[hs, :]), drain=(h == 0))
                    elif nm == 'mid':
                        p.op('vector', lambda e, hs=hs: e.tensor_copy(
                            miT[hs, :], rcpb[hs, :]), inc='ve', drain=(h == 0))
                        if h == 1:
                            mid_ve = p.cnt['ve']
                    elif nm == 'r2':
                        p.op('vector', lambda e, hs=hs: e.tensor_sub(
                            rcpbB[hs, :], rcpb[hs, :], miT[hs, :]), drain=(h == 0))
                    else:
                        if h == 0:
                            p.wait('vector', 'dq', dma0['hi'])
                        p.op('vector', lambda e, hs=hs: e.tensor_copy(
                            hiT[hs, :], rcpbB[hs, :]), inc='ve', drain=(h == 0))
                if nm == 'hi' or nm == 'mid':
                    mark = hi_ve if nm == 'hi' else mid_ve
                    ri = 0 if nm == 'hi' else 1
                    p.wait('gpsimd', 've', mark)
                    p.wait('gpsimd', 'dq', p.cnt['dq'])
                    for h, hp in ((0, 96), (1, 32)):
                        src = hiT if nm == 'hi' else miT
                        p.dma('gpsimd', fq3w[pb][ri:ri + 1, 512 * h:512 * (h + 1)],
                              src[hp:hp + 1, :], 'dq')
                    dma0[nm] = p.cnt['dq']
            lo_ve = p.cnt['ve']
            prep_ve_end = lo_ve
            p.wait('gpsimd', 've', lo_ve)
            p.wait('gpsimd', 'dq', p.cnt['dq'])
            for h, hp in ((0, 96), (1, 32)):
                p.dma('gpsimd', fq3w[pb][2:3, 512 * h:512 * (h + 1)], hiT[hp:hp + 1, :], 'dq')
            fq3_rdy[J] = p.cnt['dq']

        emit_prep(0)
        for pair in range(2):
            for j in range(16):
                J = 16 * pair + j
                ve_qb = p.cnt['ve']
                if J + 1 < 32:
                    emit_prep(J + 1)
                pb = J % 2
                nk = 4 * (j + 1)
                sc_mark_f = {}
                p.wait('tensor', 've', ve_qb)
                p.wait('tensor', 'dq', fq3_rdy[J])
                p.op('tensor', lambda e, pb=pb: e.matmul(
                    PSa[4][:, :], onesb_sb[0:3, 0:128], fq3w[pb][0:3, 0:512],
                    start=True, stop=True))
                p.op('tensor', lambda e, pb=pb: e.matmul(
                    PSa[5][:, :], onesb_sb[0:3, 0:128], fq3w[pb][0:3, 512:1024],
                    start=True, stop=True), inc='te')
                p.wait('vector', 'te', p.cnt['te'])
                p.op('vector', lambda e: e.tensor_copy(fqbw[:, 0:512], PSa[4][:, :]))
                p.op('vector', lambda e: e.tensor_copy(fqbw[:, 512:1024], PSa[5][:, :]),
                     inc='ve', drain=False)
                r1_ve = p.cnt['ve']

                def emit_lpv(k, nk):
                    kpar = k % 3
                    mc = 128 * (k - 4 * j) if k >= 4 * j else 0
                    p.op('tensor', lambda e, pair=pair, kpar=kpar, k=k, mc=mc, nk=nk: e.matmul(
                        PSa[6][0:65, mc:512], Vsb[pair][:, 130 * k:130 * k + 65],
                        ptb[kpar][:, mc:512], start=(k == 0), stop=(k == nk - 1)))
                    p.op('tensor', lambda e, pair=pair, kpar=kpar, k=k, mc=mc, nk=nk: e.matmul(
                        PSa[7][0:65, mc:512], Vsb[pair][:, 130 * k + 65:130 * (k + 1)],
                        ptb[3 + kpar][:, mc:512],
                        start=(k == 0), stop=(k == nk - 1)), inc='te')

                for k in range(nk + 2):
                    if k < nk:
                        kpar = k % 3
                        STW = PBa[kpar]
                        STA, STB = PSa[kpar * 2], PSa[kpar * 2 + 1]
                        mc = 128 * (k - 4 * j) if k >= 4 * j else 0
                        use_r3 = (k % 4 == 0) or k >= 4 * j or k < 2
                        if k == 2:
                            p.wait('tensor', 've', r1_ve)
                        p.op('tensor', lambda e, STA=STA, pair=pair, k=k, j=j, mc=mc,
                             u=use_r3: e.matmul(
                            STA[:, mc:512], Ksb[pair][0:64, 128 * k:128 * (k + 1)],
                            Qsb[pair][0:64, 512 * j + mc:512 * (j + 1)],
                            start=True, stop=not u))
                        if use_r3:
                            p.op('tensor', lambda e, STB=STB, pair=pair, k=k, j=j, mc=mc: e.matmul(
                                STB[:, mc:512], Ksb[pair][64:128, 128 * k:128 * (k + 1)],
                                Qsb[pair][64:128, 512 * j + mc:512 * (j + 1)],
                                tile_position=(64, 0), start=True, stop=False))
                            p.op('tensor', lambda e, STA=STA, pb=pb, mc=mc: e.matmul(
                                STA[:, mc:512], onesb_sb[0:3, 0:128],
                                fq3w[pb][0:3, mc:512], start=False, stop=True))
                            p.op('tensor', lambda e, STB=STB, pb=pb, mc=mc: e.matmul(
                                STB[:, mc:512], onesb_sb[0:3, 0:128],
                                fq3w[pb][0:3, 512 + mc:1024], start=False, stop=True), inc='te')
                        else:
                            p.op('tensor', lambda e, STB=STB, pair=pair, k=k, j=j, mc=mc: e.matmul(
                                STB[:, mc:512], Ksb[pair][64:128, 128 * k:128 * (k + 1)],
                                Qsb[pair][64:128, 512 * j + mc:512 * (j + 1)],
                                tile_position=(64, 0), start=True, stop=True), inc='te')
                        te_k = p.cnt['te']
                    if k >= 2:
                        p.wait('tensor', 'sc', sc_mark_f[k - 2])
                        if k - 2 == 0:
                            p.wait('tensor', 've', ve_qb)
                        emit_lpv(k - 2, nk)
                    if k >= nk:
                        continue
                    if k >= 4 * j:
                        p.wait('vector', 'te', te_k)
                        p.op('vector', lambda e, STA=STA, mc=mc: e.tensor_add(
                            STA[:, mc:mc + 128], STA[:, mc:mc + 128], trif_sb[:, :]),
                            drain=False)
                        p.op('vector', lambda e, STB=STB, mc=mc: e.tensor_add(
                            STB[:, mc:mc + 128], STB[:, mc:mc + 128], trif_sb[:, :]),
                            inc='ve', drain=False)
                        p.wait('scalar', 've', p.cnt['ve'])
                    elif not use_r3:
                        p.wait('vector', 'te', te_k)
                        p.op('vector', lambda e, STW=STW: e.tensor_add(
                            STW[:, :], STW[:, :], fqbw[:, :]), inc='ve', drain=(k == 3))
                        p.wait('scalar', 've', p.cnt['ve'])
                    else:
                        p.wait('scalar', 'te', te_k)
                    p.op('scalar', lambda e, STA=STA, kpar=kpar, k=k, pair=pair, mc=mc: e.activation(
                        ptb[kpar][:, mc:512], STA[:, mc:512], AF.Exp,
                        bias=fct[:, 4 * k + 2 * pair:4 * k + 2 * pair + 1]), drain=False)
                    p.op('scalar', lambda e, STB=STB, kpar=kpar, k=k, pair=pair, mc=mc: e.activation(
                        ptb[3 + kpar][:, mc:512], STB[:, mc:512], AF.Exp,
                        bias=fct[:, 4 * k + 2 * pair + 1:4 * k + 2 * pair + 2]), inc='sc', drain=False)
                    sc_mark_f[k] = p.cnt['sc']
                fq3_free[pb] = p.cnt['te']
                p.wait('vector', 'te')
                p.wait('vector', 'dq', p.cnt['dq'])
                p.wait('scalar', 'te', p.cnt['te'])
                p.op('scalar', lambda e: e.activation(dvb[64:65, :], PSa[6][64:65, :], AF.Ln))
                p.op('scalar', lambda e: e.activation(rcpb[64:65, :], dvb[64:65, :], AF.Exp,
                                                      scale=-1.0))
                p.op('scalar', lambda e: e.activation(dvb2[64:65, :], PSa[7][64:65, :], AF.Ln, drain=False)
                     if False else e.activation(dvb2[64:65, :], PSa[7][64:65, :], AF.Ln))
                p.op('scalar', lambda e: e.activation(rcpbB[64:65, :], dvb2[64:65, :], AF.Exp,
                                                      scale=-1.0), inc='sc')
                p.wait('vector', 'sc')
                p.op('vector', lambda e: e.nop(), inc='ve')
                p.wait('gpsimd', 've')
                p.dma('gpsimd', rcpb[0:1, :], rcpb[64:65, :], 'dq')
                p.dma('gpsimd', rcpbB[0:1, :], rcpbB[64:65, :], 'dq')
                p.wait('tensor', 'dq', p.cnt['dq'])
                p.wait('tensor', 've', p.cnt['ve'])
                p.wait('tensor', 'sc', p.cnt['sc'])
                p.op('tensor', lambda e: e.matmul(PSa[0][0:64, :], onesf_sb[0:1, 0:64],
                                                  rcpb[0:1, :], start=True, stop=True))
                p.op('tensor', lambda e: e.matmul(PSa[1][0:64, :], onesf_sb[0:1, 0:64],
                                                  rcpbB[0:1, :], start=True, stop=True), inc='te')
                p.wait('vector', 'te')
                p.op('vector', lambda e: e.tensor_copy(dvb[0:64, :], PSa[0][0:64, :]))
                p.op('vector', lambda e: e.tensor_copy(dvb2[0:64, :], PSa[1][0:64, :]))
                p.op('vector', lambda e: e.tensor_mul(onrm_sb[0:64, :], PSa[6][0:64, :], dvb[0:64, :]))
                p.op('vector', lambda e: e.tensor_mul(onrm2_sb[0:64, :], PSa[7][0:64, :],
                                                      dvb2[0:64, :]), inc='ve')
                p.wait('gpsimd', 've')
                p.dma('gpsimd', obnc[128 * pair:128 * pair + 64, 512 * j:512 * (j + 1)],
                      onrm_sb[0:64, :], 'dq')
                p.dma('gpsimd', obnc[128 * pair + 64:128 * (pair + 1), 512 * j:512 * (j + 1)],
                      onrm2_sb[0:64, :], 'dq')
        p.wait('gpsimd', 'dq', p.cnt['dq'])
        p.op('gpsimd', lambda e: e.collective_compute(
            "AllGather", ALU.bypass, replica_groups=RG,
            ins=[obnc[:, :].opt()], outs=[ogath[:, :].opt()]), inc='cc')
        cc_o = p.cnt['cc']
        p.wait('sync', 'cc', cc_o)
        p.wait('sync', 'te', p.cnt['te'])
        p.wait('sync', 'dhb', p.cnt['dhb'])
        cell = {}
        p.emit('sync', lambda e: cell.__setitem__('off', e.partition_id()))
        for c in range(NCH):
            def oload(e, c=c):
                src = ogath[128 * c:128 * (c + 1), bass.ts(cell['off'], TOK)]
                return e.dma_start(out=oown[:, TOK * c:TOK * (c + 1)], in_=src)
            p.op('sync', oload, inc='dhb', n=16)
        oo_done = p.cnt['dhb']

        def load_wo(dst, h, sem):
            for c in range(NCH):
                p.dma('gpsimd', dst[:, 128 * c:128 * (c + 1)],
                      WobT[128 * c:128 * (c + 1), 128 * h:128 * (h + 1)], sem)

        p.wait('gpsimd', 'te', p.cnt['te'])
        p.wait('gpsimd', 'dga', p.cnt['dga'])
        load_wo(fgw[0], 0, 'dga')
        prev_te = p.cnt['te']
        for oc in range(NCH):
            b = oc % 2
            sem_b = 'dga' if b == 0 else 'dgb'
            need = p.cnt[sem_b]
            if oc + 1 < NCH:
                nsem = 'dgb' if b == 0 else 'dga'
                p.wait('gpsimd', 'te', prev_te)
                p.wait('gpsimd', nsem, p.cnt[nsem])
                load_wo(fgw[1 - b], oc + 1, nsem)
            p.wait('tensor', sem_b, need)
            p.wait('tensor', 'dhb', oo_done)
            for g in range(NTG):
                p.wait('tensor', 've', p.cnt['ve'])
                for c in range(NCH):
                    p.op('tensor', lambda e, g=g, c=c, b=b: e.matmul(
                        PSa[g][:, :], fgw[b][:, 128 * c:128 * (c + 1)],
                        oown[:, TOK * c + 512 * g: TOK * c + 512 * (g + 1)],
                        start=(c == 0), stop=(c == NCH - 1)),
                        inc=('te' if c == NCH - 1 else None))
                p.wait('vector', 'te')
                p.op('vector', lambda e, oc=oc, g=g: e.tensor_add(
                    R[:, TOK * oc + 512 * g: TOK * oc + 512 * (g + 1)],
                    R[:, TOK * oc + 512 * g: TOK * oc + 512 * (g + 1)], PSa[g][:, :]), inc='ve')
            prev_te = p.cnt['te']

    for c in range(NCH):
        b = c % 2
        gsem = 'dga' if b == 0 else 'dgb'
        p.wait('gpsimd', gsem, p.cnt[gsem])
        p.wait('gpsimd', 'dhb', p.cnt['dhb'])
        p.dma('gpsimd', qwb[b][:, :], WqaT[128 * c:128 * (c + 1), :], gsem)
        p.wait('sync', gsem, p.cnt[gsem])
        p.dma('sync', wqa_bf[:, 128 * c:128 * (c + 1), :].transpose([1, 0, 2]),
              qwb[b].rearrange('p (h w) -> p h w', w=128)[:, :, :], 'dhb')
    for c in range(NCH):
        b = c % 2
        gsem = 'dga' if b == 0 else 'dgb'
        p.wait('gpsimd', gsem, p.cnt[gsem])
        p.wait('gpsimd', 'dhb', p.cnt['dhb'])
        p.dma('gpsimd', qwb[b][:, :], WoaT[128 * c:128 * (c + 1), :], gsem)
        p.wait('sync', gsem, p.cnt[gsem])
        p.dma('sync', woa_bf[:, 128 * c:128 * (c + 1), :].transpose([1, 0, 2]),
              qwb[b].rearrange('p (h w) -> p h w', w=128)[:, :, :], 'dhb')
    wqa_done = p.cnt['dhb']
    qsteps = quant_steps() if 'q' in phases else []
    bank_weights_prep()
    layernorm()
    tap_xln = mktap('xln', [128, NCH * TOK], BF16)
    if tap_xln is not None:
        p.wait('gpsimd', 've')
        p.dma('gpsimd', tap_xln[:, :], xln[:, :], 'st')
    if 'a' in phases:
        attn_a(qsteps)
    else:
        for st in qsteps:
            st()
    if 'f' in phases:
        layernorm()
        ffn()
    if 'b' in phases:
        layernorm()
        attn_b()
    if 'g' in phases:
        layernorm()
        ffn()

    p.wait('sync', 've')
    p.wait('sync', 'te')
    for c in range(NCH):
        p.dma('sync', y[128 * c:128 * (c + 1), :], R[:, TOK * c:TOK * (c + 1)], 'st')
    p.wait('gpsimd', 'st')
    p.build()
    return nc


def kernel(**inputs):
    nc = build_nc(phases=('q', 'a', 'f', 'b', 'g'))
    in_maps = [host_inputs(inputs, c) for c in range(NCORES)]
    res = run_bass_kernel_spmd(nc, in_maps, core_ids=list(range(NCORES)))
    out = np.concatenate([r['y'].T for r in res.results], axis=0)
    return out.reshape(1, S, D).astype(np.float32)
